# version 04
# baseline (speedup 1.0000x reference)
import numpy as np
from contextlib import ExitStack
import concourse.bass as bass
import concourse.mybir as mybir

F32 = mybir.dt.float32
BF16 = mybir.dt.bfloat16
AF = mybir.ActivationFunctionType
ALU = mybir.AluOpType
AX = mybir.AxisListType

D = 1024
S = 4096
B = 4
DC = D // 128
NT = S // 128
EPS_LN = 1e-5

ENGINES = ("tensor", "vector", "scalar", "gpsimd", "sync")


def cdiv(a, b):
    return (a + b - 1) // b


class P:

    def __init__(self, name):
        self.nc = bass.Bass("TRN2", target_bir_lowering=False)
        self.ctx = ExitStack()
        self.block = self.ctx.enter_context(self.nc.Block())
        self.name = name
        self._uid = 0
        self.esem = {e: self.ctx.enter_context(self.nc.semaphore("es_" + e))
                     for e in ENGINES + ("cc",)}
        self.ecnt = {e: 0 for e in ENGINES + ("cc",)}
        self.res = {}
        self.fences = {"gpsimd": [0], "sync": [0]}
        self.waited = {e: {} for e in ENGINES}
        self.safe = {e: 0 for e in ENGINES}
        self._eng_obj = {}
        self._open_section = None
        self.ps = [self.ctx.enter_context(self.nc.psum_tensor(f"pool_ps{i}", [128, 512], F32))
                   for i in range(7)]
        self.psT = self.ctx.enter_context(self.nc.psum_tensor("pool_psT", [128, 1024], BF16))

    def uid(self):
        self._uid += 1
        return self._uid

    def dram(self, name, shape, dtype=BF16, kind="Internal"):
        if kind == "Internal":
            name = f"{name}_{self.uid()}"
        return self.nc.dram_tensor(name, list(shape), dtype, kind=kind).ap()

    def inp(self, name, shape, dtype=F32):
        return self.dram(name, shape, dtype, kind="ExternalInput")

    def outp(self, name, shape, dtype=BF16):
        return self.dram(name, shape, dtype, kind="ExternalOutput")

    def sb(self, name, shape, dtype=BF16):
        return self.ctx.enter_context(self.nc.sbuf_tensor(f"{name}_{self.uid()}", list(shape), dtype))

    def psum(self, name, shape, dtype=F32):
        return self.ctx.enter_context(self.nc.psum_tensor(f"{name}_{self.uid()}", list(shape), dtype))

    def ring(self, name, n, shape, dtype=BF16):
        if name == "swX8":
            return self.ring("swX16", 32, shape, dtype)[:n]
        k = (name, n, tuple(shape), dtype)
        if not hasattr(self, "_rcache"):
            self._rcache = {}
        if k not in self._rcache:
            self._rcache[k] = [self.sb(f"{name}{i}", shape, dtype) for i in range(n)]
        return self._rcache[k]

    def _engine(self, ename):
        if self._open_section == ename:
            return self._eng_obj[ename]
        holder = {}

        def body(eng):
            holder["eng"] = eng
        getattr(self.block, ename)(body)
        raise RuntimeError("engine sections must be driven via step()")

    def step(self, ename, reads=(), writes=(), dma=False, n_dma=1):
        def run(fn):
            deps = {}

            def need(e, c):
                if e == ename and ename == "tensor":
                    return
                if e == ename and c <= self.safe[ename]:
                    return
                if e in self.fences and e != ename:
                    fs = self.fences[e]
                    f = next((x for x in fs if x >= c), None)
                    if f is None:
                        f = self.ecnt[e]
                        assert f >= c

                        def fbody(eng, e=e, f=f):
                            eng.wait_ge(self.esem[e], f)
                        getattr(self.block, e)(fbody)
                        fs.append(f)
                    c = f
                deps[e] = max(deps.get(e, 0), c)

            for r in reads:
                st = self.res.get(id(r) if not isinstance(r, (str, tuple)) else r)
                if st and st["w"]:
                    need(*st["w"])
            for w in writes:
                key = id(w) if not isinstance(w, (str, tuple)) else w
                st = self.res.get(key)
                if st:
                    if st["w"]:
                        need(*st["w"])
                    for rr in st["r"]:
                        need(*rr)

            emitted = {}

            wm = self.waited[ename]
            for e in list(deps):
                if deps[e] <= wm.get(e, 0):
                    del deps[e]
                else:
                    wm[e] = deps[e]
            deps_log = dict(deps)

            def body(eng):
                for e, c in deps.items():
                    if c > 0:
                        eng.wait_ge(self.esem[e], c)
                ret = fn(eng)
                instrs = ret if isinstance(ret, (list, tuple)) else ([ret] if ret is not None else [])
                if dma:
                    for ins in instrs:
                        ins.then_inc(self.esem[ename], 16)
                    emitted["inc"] = 16 * len(instrs)
                else:
                    if instrs:
                        instrs[-1].then_inc(self.esem[ename], 1)
                        emitted["inc"] = 1
                    else:
                        emitted["inc"] = 0

            getattr(self.block, ename)(body)
            self.ecnt[ename] += emitted.get("inc", 0)
            if getattr(self, "log_steps", False):
                self.step_log.append((ename, dict(deps_log), emitted.get("inc", 0),
                                      self.ecnt[ename]))
            cnt = (ename, self.ecnt[ename])
            for w in writes:
                key = id(w) if not isinstance(w, (str, tuple)) else w
                self.res[key] = {"w": cnt, "r": []}
            for r in reads:
                key = id(r) if not isinstance(r, (str, tuple)) else r
                st = self.res.setdefault(key, {"w": None, "r": []})
                st["r"].append(cnt)
            return None
        return run

    def barrier(self):
        final = dict(self.ecnt)
        for ename in ENGINES:
            def body(eng, ename=ename):
                for e in ENGINES + ("cc",):
                    if final[e] > 0 and final[e] > self.waited[ename].get(e, 0):
                        eng.wait_ge(self.esem[e], final[e])
            getattr(self.block, ename)(body)
            for e in ENGINES + ("cc",):
                if final[e] > 0:
                    self.waited[ename][e] = max(self.waited[ename].get(e, 0), final[e])
        for e in ENGINES:
            self.safe[e] = self.ecnt[e]
        self.res.clear()

    def done(self):
        self.barrier()
        self.ctx.close()
        return self.nc



def host_constants():
    c = {}
    c["ident"] = np.eye(128, dtype=np.float32)
    c["ones_col"] = np.ones((128, 1), dtype=np.float32)
    c["ones_row"] = np.ones((1, 128), dtype=np.float32)
    kk, qq = np.meshgrid(np.arange(128), np.arange(128), indexing="ij")
    c["tri_kq"] = (kk <= qq).astype(np.float32)
    hd = 64
    fraction = 2.0 * np.arange(0, hd // 2) / hd
    timescale = 10000.0 ** fraction
    pos = np.arange(S, dtype=np.float64)
    ang = pos[None, :] / timescale[:, None]
    c["rope_sin"] = np.concatenate([np.sin(ang), np.sin(ang)], 0).astype(np.float32)
    c["rope_cos"] = np.concatenate([np.cos(ang), np.cos(ang)], 0).astype(np.float32)
    return c


def load_consts(p, want=("ident", "ones_col", "ones_row", "tri_kq")):
    shapes = {"ident": (128, 128), "ones_col": (128, 1), "ones_row": (1, 128),
              "tri_kq": (128, 128), "rope_sin": (64, S), "rope_cos": (64, S)}
    if not hasattr(p, "_consts"):
        p._consts = {}
    cs = p._consts
    want = [nm for nm in want if nm + "_bf" not in cs]
    for nm in want:
        ap = p.inp("c_" + nm, shapes[nm], F32)
        bf = p.sb("c_" + nm + "_bf", shapes[nm], BF16)
        cs[nm + "_bf"] = bf

        @p.step("gpsimd", writes=[bf], dma=True)
        def _(g, ap=ap, bf=bf):
            return g.dma_start(out=bf[:, :], in_=ap[:, :])
        if nm in ("ones_row", "ones_col"):
            ff = p.sb("c_" + nm + "_f", shapes[nm], F32)
            cs[nm + "_f"] = ff

            @p.step("sync", writes=[ff], dma=True)
            def _(sy, ap=ap, ff=ff):
                return sy.dma_start(out=ff[:, :], in_=ap[:, :])
    return cs



def sweep_proj_fm(p, W_ap, X_ap, O_ap, F_out, kdim=D, n0=0, n1=S,
                  bias_ap=None, evac=None, out_dt=BF16, o_row0=0, x_row0=0,
                  w_col0=0, nbank=4, acc_scale=None, x_col0=0):
    KC = kdim // 128
    FT = cdiv(F_out, 128)
    NCH = cdiv(n1 - n0, 512)
    wbig = p.ring(f"swW{KC}", 2, [128, KC * 128], BF16)
    if KC <= 8:
        nxb = 2 * KC
        xt = p.ring("swX8", 16, [128, 512], BF16)[:nxb]
    else:
        nxb = KC
        xt = p.ring("swX16", 32, [128, 512], BF16)[:nxb]
    ob = p.ring("swO" + str(out_dt), 3, [128, 512], out_dt)
    bb = p.ring("swB", 2, [128, 1], F32)
    pss = p.ps[:nbank]

    for ft in range(FT):
        fw = min(128, F_out - ft * 128)

        @p.step("gpsimd", writes=[wbig[ft % 2]], dma=True)
        def _(g, ft=ft, fw=fw):
            return [g.dma_start(out=wbig[ft % 2][:, kc * 128:kc * 128 + fw],
                                in_=W_ap[kc * 128:(kc + 1) * 128,
                                         w_col0 + ft * 128:w_col0 + ft * 128 + fw])
                    for kc in range(KC)]
        if bias_ap is not None:
            @p.step("sync", writes=[bb[ft % 2]], dma=True)
            def _(sy, ft=ft, fw=fw):
                return sy.dma_start(out=bb[ft % 2][0:fw, :],
                                    in_=bias_ap[w_col0 + ft * 128:w_col0 + ft * 128 + fw, :])

        for ci in range(NCH):
            a = n0 + ci * 512
            xs = [xt[(ci % (nxb // KC)) * KC + kc] for kc in range(KC)]
            for kc in range(KC):
                @p.step("sync", writes=[xs[kc]], dma=True)
                def _(sy, kc=kc, a=a, xs=xs):
                    return sy.dma_start(out=xs[kc][:, :],
                                        in_=X_ap[x_row0 + kc * 128:x_row0 + (kc + 1) * 128,
                                                 x_col0 + a:x_col0 + a + 512])
            idx = ft * NCH + ci
            ps = pss[idx % nbank]

            @p.step("tensor", reads=[wbig[ft % 2]] + xs, writes=[ps])
            def _(te, ft=ft, fw=fw, ps=ps, xs=xs):
                last = None
                for kc in range(KC):
                    last = te.matmul(ps[0:fw, :], wbig[ft % 2][:, kc * 128:kc * 128 + fw],
                                     xs[kc][:, :], start=(kc == 0), stop=(kc == KC - 1))
                return last

            o = ob[idx % 3]
            if evac is not None:
                evac(p, ps, o, ft, ci, fw)
            else:
                rd = [ps] + ([bb[ft % 2]] if bias_ap is not None else [])

                @p.step("scalar", reads=rd, writes=[o])
                def _(sc, ps=ps, o=o, ft=ft, fw=fw):
                    kw = {}
                    fn = AF.Copy
                    if bias_ap is not None:
                        kw["bias"] = bb[ft % 2][0:fw, :]
                        fn = AF.Identity
                    if acc_scale is not None:
                        kw["scale"] = acc_scale
                        fn = AF.Identity
                    return sc.activation(out=o[0:fw, :], in_=ps[0:fw, :], func=fn, **kw)

            @p.step("gpsimd", reads=[o], dma=True)
            def _(g, o=o, ft=ft, fw=fw, a=a):
                return g.dma_start(out=O_ap[o_row0 + ft * 128:o_row0 + ft * 128 + fw, a:a + 512],
                                   in_=o[0:fw, :])


def sweep_proj_tm(p, X_ap, W_ap, O_ap, F_out, kdim=D, n0=0, n1=S, x_row0=0,
                  bias_ap=None, w_col0=0, out_dt=BF16, evac=None, consts=None,
                  o_col0=0, nbank=4):
    KC = kdim // 128
    FCH = cdiv(F_out, 512)
    nt0, nt1 = n0 // 128, n1 // 128
    wt = p.ring("swX16", 32, [128, 512], BF16)[16:16 + KC]
    assert F_out <= 512, "sweep_proj_tm assumes F_out<=512"
    xt = p.ring(f"stX{KC}", 2, [128, KC * 128], BF16)
    ob = p.ring("stO" + str(out_dt), 3, [128, 512], out_dt)
    pss = p.ps[:nbank]
    brow = p.sb("stBr", [1, 512], BF16) if bias_ap is not None else None

    for kc in range(KC):
        @p.step("gpsimd", writes=[wt[kc]], dma=True)
        def _(g, kc=kc):
            return g.dma_start(out=wt[kc][:, 0:F_out],
                               in_=W_ap[kc * 128:(kc + 1) * 128, w_col0:w_col0 + F_out])
    if bias_ap is not None:
        @p.step("gpsimd", writes=[brow], dma=True)
        def _(g):
            return g.dma_start(out=brow[0:1, 0:F_out],
                               in_=bias_ap[0:1, w_col0:w_col0 + F_out])

    for nt in range(nt0, nt1):
        xv = xt[nt % 2]

        @p.step("sync", writes=[xv], dma=True)
        def _(sy, nt=nt, xv=xv):
            return [sy.dma_start(out=xv[:, kc * 128:(kc + 1) * 128],
                                 in_=X_ap[x_row0 + kc * 128:x_row0 + (kc + 1) * 128,
                                          nt * 128:(nt + 1) * 128])
                    for kc in range(KC)]
        idx = nt - nt0
        ps = pss[idx % nbank]
        rd = [xv] + list(wt[:KC]) + ([brow] if bias_ap is not None else [])

        @p.step("tensor", reads=rd, writes=[ps])
        def _(te, xv=xv, ps=ps):
            last = None
            for kc in range(KC):
                last = te.matmul(ps[:, 0:F_out], xv[:, kc * 128:(kc + 1) * 128],
                                 wt[kc][:, 0:F_out], start=(kc == 0),
                                 stop=(bias_ap is None and kc == KC - 1))
            if bias_ap is not None:
                last = te.matmul(ps[:, 0:F_out], consts["ones_row_bf"][0:1, :],
                                 brow[0:1, 0:F_out], start=False, stop=True)
            return last
        o = ob[idx % 3]
        if evac is not None:
            evac(p, ps, o, nt)
        else:
            @p.step("scalar", reads=[ps], writes=[o])
            def _(sc, ps=ps, o=o):
                return sc.activation(out=o[:, 0:F_out], in_=ps[:, 0:F_out], func=AF.Copy)

        @p.step("gpsimd", reads=[o], dma=True)
        def _(g, o=o, nt=nt):
            return g.dma_start(out=O_ap[nt * 128:(nt + 1) * 128, o_col0:o_col0 + F_out],
                               in_=o[:, 0:F_out])


def sweep_ln_fm(p, X_ap, O_ap, consts, n0=0, n1=S, kind="ln", gamma_ap=None,
                scale=1.0, out_dt=BF16, also_tm_ap=None):
    NCH = (n1 - n0) // 512
    xt = p.ring("swX8", 16, [128, 512], BF16)
    sq = p.ring("lnSq", 2, [128, 512], BF16)
    ob = p.ring("lnO", 3, [128, 512], out_dt)
    stats = p.ps[4]
    _lnrows = p.ring("lnrows", 5, [1, 512], F32)
    mu_r, var_r, std_r, r_r, mr_r = _lnrows
    bc_r, bc_mr = p.ps[5], p.ps[6]
    bcs = p.ring("f32buf", 2, [128, 1024], F32)[0]
    gam = None
    if gamma_ap is not None:
        gam = [p.sb("lnG", [128, 1], F32) for _ in range(DC)]
        for dc in range(DC):
            @p.step("sync", writes=[gam[dc]], dma=True)
            def _(sy, dc=dc):
                return sy.dma_start(out=gam[dc][:, :],
                                    in_=gamma_ap[dc * 128:(dc + 1) * 128, :])
    tmp = p.ring("lnT", 2, [128, 512], BF16)
    tout = p.ring("lnTo", 2, [128, 1024], BF16) if also_tm_ap is not None else None

    for ci in range(NCH):
        a = n0 + ci * 512
        xs = [xt[(ci % 2) * DC + dc] for dc in range(DC)]
        for dc in range(DC):
            @p.step("gpsimd", writes=[xs[dc]], dma=True)
            def _(sy, dc=dc, a=a):
                return sy.dma_start(out=xs[dc][:, :], in_=X_ap[dc * 128:(dc + 1) * 128, a:a + 512])
        for dc in range(DC):
            sqb = sq[dc % 2]

            @p.step("scalar", reads=[xs[dc]], writes=[sqb])
            def _(sc, dc=dc, sqb=sqb):
                return sc.activation(out=sqb[:, :], in_=xs[dc][:, :], func=AF.Square)

            @p.step("tensor", reads=[xs[dc], sqb, consts["ones_col_bf"]],
                    writes=[stats])
            def _(te, dc=dc, sqb=sqb):
                te.matmul(stats[0:1, :], consts["ones_col_bf"][:, :], xs[dc][:, :],
                          start=(dc == 0), stop=(dc == DC - 1))
                return te.matmul(stats[64:65, :], consts["ones_col_bf"][:, :], sqb[:, :],
                                 start=(dc == 0), stop=(dc == DC - 1))
        if kind == "ln":
            @p.step("vector", reads=[stats], writes=[mu_r])
            def _(v):
                return v.tensor_scalar(out=mu_r[:, :], in0=stats[0:1, :], scalar1=1.0 / D,
                                       scalar2=None, op0=ALU.mult)

            @p.step("scalar", reads=[mu_r], writes=[var_r])
            def _(sc):
                return sc.activation(out=var_r[:, :], in_=mu_r[:, :], func=AF.Square)

            @p.step("vector", reads=[stats, var_r], writes=[var_r])
            def _(v):
                return v.scalar_tensor_tensor(out=var_r[:, :], in0=stats[64:65, :],
                                              scalar=1.0 / D, in1=var_r[:, :],
                                              op0=ALU.mult, op1=ALU.subtract)

            @p.step("vector", reads=[var_r], writes=[var_r])
            def _(v):
                return v.tensor_scalar(out=var_r[:, :], in0=var_r[:, :], scalar1=EPS_LN,
                                       scalar2=None, op0=ALU.add)

            @p.step("scalar", reads=[var_r], writes=[std_r])
            def _(sc):
                return sc.activation(out=std_r[:, :], in_=var_r[:, :], func=AF.Sqrt)

            @p.step("vector", reads=[std_r], writes=[r_r])
            def _(v):
                return v.reciprocal(out=r_r[:, :], in_=std_r[:, :])

            @p.step("vector", reads=[mu_r, r_r], writes=[mr_r])
            def _(v):
                return v.scalar_tensor_tensor(out=mr_r[:, :], in0=mu_r[:, :], scalar=-1.0,
                                              in1=r_r[:, :], op0=ALU.mult, op1=ALU.mult)
        else:
            @p.step("scalar", reads=[stats], writes=[std_r])
            def _(sc):
                return sc.activation(out=std_r[:, :], in_=stats[64:65, :], func=AF.Sqrt)

            @p.step("vector", reads=[std_r], writes=[std_r])
            def _(v):
                return v.tensor_scalar(out=std_r[:, :], in0=std_r[:, :], scalar1=1e-12,
                                       scalar2=None, op0=ALU.max)

            @p.step("vector", reads=[std_r], writes=[r_r])
            def _(v):
                return v.reciprocal(out=r_r[:, :], in_=std_r[:, :])

            @p.step("vector", reads=[r_r], writes=[r_r])
            def _(v):
                return v.tensor_scalar(out=r_r[:, :], in0=r_r[:, :], scalar1=float(scale),
                                       scalar2=None, op0=ALU.mult)

            @p.step("vector", writes=[mr_r])
            def _(v):
                return v.memset(mr_r[:, :], 0.0)

        @p.step("tensor", reads=[r_r, mr_r, consts["ones_row_f"]], writes=[bc_r, bc_mr])
        def _(te):
            te.matmul(bc_r[:, :], consts["ones_row_f"][0:1, :], r_r[:, :],
                      start=True, stop=True)
            return te.matmul(bc_mr[:, :], consts["ones_row_f"][0:1, :], mr_r[:, :],
                             start=True, stop=True)

        @p.step("vector", reads=[bc_r, bc_mr], writes=[bcs])
        def _(v):
            v.tensor_copy(out=bcs[:, 0:512], in_=bc_r[:, :])
            return v.tensor_copy(out=bcs[:, 512:1024], in_=bc_mr[:, :])

        for dc in range(DC):
            t = tmp[dc % 2]
            o = ob[dc % 3]

            @p.step("vector", reads=[xs[dc], bcs], writes=[t])
            def _(v, dc=dc, t=t):
                return v.tensor_mul(out=t[:, :], in0=xs[dc][:, :], in1=bcs[:, 0:512])

            if kind == "ln":
                @p.step("vector", reads=[t, bcs], writes=[o])
                def _(v, t=t, o=o):
                    return v.tensor_add(out=o[:, :], in0=t[:, :], in1=bcs[:, 512:1024])
            else:
                @p.step("scalar", reads=[t] + ([gam[dc]] if gam else []), writes=[o])
                def _(sc, t=t, o=o, dc=dc):
                    kw = {"scale": gam[dc][:, :]} if gam else {}
                    return sc.activation(out=o[:, :], in_=t[:, :], func=AF.Copy, **kw)

            @p.step("gpsimd", reads=[o], dma=True)
            def _(g, o=o, dc=dc, a=a):
                return g.dma_start(out=O_ap[dc * 128:(dc + 1) * 128, a:a + 512], in_=o[:, :])

            if also_tm_ap is not None:
                tp = p.psT
                to = tout[dc % 2]

                @p.step("tensor", reads=[o, consts["ident_bf"]], writes=[tp])
                def _(te, o=o, tp=tp):
                    last = None
                    for q in range(4):
                        last = te.transpose(tp[:, q * 128 + 0:q * 128 + 128],
                                            o[:, q * 128:(q + 1) * 128],
                                            consts["ident_bf"][:, :])
                    return last

                @p.step("vector", reads=[tp], writes=[to])
                def _(v, tp=tp, to=to):
                    return v.tensor_copy(out=to[:, :], in_=tp[:, :])

                @p.step("gpsimd", reads=[to], dma=True)
                def _(g, to=to, dc=dc, a=a):
                    return [g.dma_start(
                        out=also_tm_ap[a + q * 128:a + (q + 1) * 128,
                                       dc * 128:(dc + 1) * 128],
                        in_=to[:, q * 128:(q + 1) * 128]) for q in range(4)]


def sweep_add_fm(p, A_ap, B_ap, O_ap, n0=0, n1=S, scale_b=None, C_ap=None,
                 scale_c=None, out_dt=BF16, rows=D):
    RT = rows // 128
    RN = 4
    at = p.ring("adA", RN, [128, 512], BF16)
    bt = p.ring("adB", RN, [128, 512], BF16)
    ct = p.ring("adC", RN, [128, 512], BF16) if C_ap is not None else None
    tt = p.ring("adT", RN, [128, 512], F32)
    ob = p.ring("adO", RN, [128, 512], out_dt)
    NCH = (n1 - n0) // 512
    for rt in range(RT):
        for ci in range(NCH):
            a = n0 + ci * 512
            idx = rt * NCH + ci
            av, bv = at[idx % RN], bt[idx % RN]
            o = ob[idx % RN]

            @p.step("gpsimd", writes=[av], dma=True)
            def _(sy, rt=rt, a=a, av=av):
                return sy.dma_start(out=av[:, :], in_=A_ap[rt * 128:(rt + 1) * 128, a:a + 512])

            @p.step("gpsimd", writes=[bv], dma=True)
            def _(sy, rt=rt, a=a, bv=bv):
                return sy.dma_start(out=bv[:, :], in_=B_ap[rt * 128:(rt + 1) * 128, a:a + 512])
            if C_ap is not None:
                cv = ct[idx % RN]

                @p.step("gpsimd", writes=[cv], dma=True)
                def _(sy, rt=rt, a=a, cv=cv):
                    return sy.dma_start(out=cv[:, :], in_=C_ap[rt * 128:(rt + 1) * 128, a:a + 512])
            t = tt[idx % RN]

            @p.step("vector", reads=[av, bv], writes=[t])
            def _(v, av=av, bv=bv, t=t):
                if scale_b is None:
                    return v.tensor_add(out=t[:, :], in0=av[:, :], in1=bv[:, :])
                return v.scalar_tensor_tensor(out=t[:, :], in0=bv[:, :], scalar=float(scale_b),
                                              in1=av[:, :], op0=ALU.mult, op1=ALU.add)
            if C_ap is not None:
                cv = ct[idx % RN]

                @p.step("vector", reads=[t, cv], writes=[o])
                def _(v, t=t, cv=cv, o=o):
                    return v.scalar_tensor_tensor(out=o[:, :], in0=cv[:, :], scalar=float(scale_c),
                                                  in1=t[:, :], op0=ALU.mult, op1=ALU.add)
            else:
                @p.step("vector", reads=[t], writes=[o])
                def _(v, t=t, o=o):
                    return v.tensor_copy(out=o[:, :], in_=t[:, :])

            @p.step("gpsimd", reads=[o], dma=True)
            def _(g, o=o, rt=rt, a=a):
                return g.dma_start(out=O_ap[rt * 128:(rt + 1) * 128, a:a + 512], in_=o[:, :])


def collective_pair(p, in_ap, out_ap, op=ALU.add, groups=None):
    if groups is None:
        groups = [[0, 1], [2, 3], [4, 5], [6, 7]]
    final = dict(p.ecnt)

    def body(g):
        for e in ENGINES:
            if final[e] > 0:
                g.wait_ge(p.esem[e], final[e])
        ins = g.collective_compute(
            "AllReduce", op,
            replica_groups=groups,
            ins=[in_ap.tensor.ap().opt()],
            outs=[out_ap.tensor.ap().opt()],
        )
        ins.then_inc(p.esem["cc"])
    p.block.gpsimd(body)
    p.ecnt["cc"] += 1
    fence = p.ring("ccfence", 1, [128, 128], BF16)[0]
    rr = min(128, out_ap.shape[0])
    cc2 = min(128, out_ap.shape[1])

    @p.step("gpsimd", writes=[fence], dma=True)
    def _(g):
        g.wait_ge(p.esem["cc"], p.ecnt["cc"])
        return g.dma_start(out=fence[0:rr, 0:cc2], in_=out_ap[0:rr, 0:cc2])
    p.barrier()


def wait_cc(p, ename, n=None):
    pass


"""Launch builders. Each build_X() returns (nc, in_names) and a matching
host-side input-prep function make_inputs_X(inputs, consts, core) -> dict.

Core roles: core c -> batch b=c//2, half j=c%2; heads 8j..8j+8, experts 4j..4j+4.
"""

HD = 64
NH_LOC = 8
CHUNK = 128



def phase_tm2fm(p, X_tm_ap, O_fm_ap, consts, in_dt=F32, rows=S, cols=D):
    RT = rows // 128
    CC = cols // 128
    assert cols == D
    xf = p.ring("f32buf", 2, [128, cols], F32)
    x8 = p.ring("swX8", 16, [128, 512], BF16)
    for t in range(RT):
        a = xf[t % 2]
        xb2 = [x8[(t % 2) * 2], x8[(t % 2) * 2 + 1]]
        ob2 = [x8[4 + (t % 2) * 2], x8[4 + (t % 2) * 2 + 1]]

        @p.step("sync", writes=[a], dma=True)
        def _(sy, t=t, a=a):
            return sy.dma_start(out=a[:, :], in_=X_tm_ap[t * 128:(t + 1) * 128, :])

        @p.step("vector", reads=[a], writes=xb2)
        def _(v, a=a, xb2=xb2):
            v.tensor_copy(out=xb2[0][:, :], in_=a[:, 0:512])
            return v.tensor_copy(out=xb2[1][:, :], in_=a[:, 512:1024])

        tpv = p.psT

        @p.step("tensor", reads=xb2 + [consts["ident_bf"]], writes=[tpv])
        def _(te, xb2=xb2, tpv=tpv):
            last = None
            for dc in range(CC):
                last = te.transpose(tpv[:, dc * 128:(dc + 1) * 128],
                                    xb2[dc // 4][:, (dc % 4) * 128:(dc % 4) * 128 + 128],
                                    consts["ident_bf"][:, :])
            return last

        @p.step("vector", reads=[tpv], writes=ob2)
        def _(v, ob2=ob2, tpv=tpv):
            v.tensor_copy(out=ob2[0][:, :], in_=tpv[:, 0:512])
            return v.tensor_copy(out=ob2[1][:, :], in_=tpv[:, 512:1024])

        @p.step("gpsimd", reads=ob2, dma=True)
        def _(g, ob2=ob2, t=t):
            return [g.dma_start(out=O_fm_ap[dc * 128:(dc + 1) * 128, t * 128:(t + 1) * 128],
                                in_=ob2[dc // 4][:, (dc % 4) * 128:(dc % 4) * 128 + 128])
                    for dc in range(CC)]


def evac_elu1(p, ps, o, ft, ci, fw=128):
    adt = p.ring("adT", 4, [128, 512], F32)
    s1 = adt[ci % 2]
    s2 = adt[2 + ci % 2]

    @p.step("vector", reads=[ps], writes=[s1])
    def _(v):
        return v.tensor_scalar(out=s1[0:fw, :], in0=ps[0:fw, :], scalar1=0.0,
                               scalar2=None, op0=ALU.min)

    @p.step("scalar", reads=[s1], writes=[s2])
    def _(sc):
        return sc.activation(out=s2[0:fw, :], in_=s1[0:fw, :], func=AF.Exp)

    @p.step("vector", reads=[ps], writes=[s1])
    def _(v):
        return v.tensor_relu(out=s1[0:fw, :], in_=ps[0:fw, :])

    @p.step("vector", reads=[s1, s2], writes=[o])
    def _(v):
        return v.tensor_add(out=o[0:fw, :], in0=s1[0:fw, :], in1=s2[0:fw, :])



def _io(p, pre, pfx, standalone, debug):
    pre = pre or {}

    def IN(n, sh, dt=F32):
        return pre[n] if n in pre else p.inp(pfx + n, sh, dt)

    def OUT(n, sh, dt=BF16):
        if n in pre:
            return pre[n]
        return p.outp(pfx + n, sh, dt) if standalone else p.dram(pfx + n, sh, dt)
    return IN, OUT



def build_A(debug=False, p=None, pre=None, pfx=""):
    standalone = p is None
    if standalone:
        p = P("A")
    IN, OUT = _io(p, pre, pfx, standalone, debug)
    x_in = IN("x_in", [S, D], F32)
    wq = IN("wq", [D, 512], F32)
    wk = IN("wk", [D, 512], F32)
    wv = IN("wv", [D, 512], F32)
    wo = IN("wo", [512, D], F32)
    consts = {}
    cs = load_consts(p, ("ident", "ones_col", "ones_row", "tri_kq"))
    consts.update(cs)

    mk = p.outp if debug else p.dram
    xfm = mk("xfm", [D, S])
    hs = mk("hs", [D, S])
    qf = mk("qf", [512, S])
    kf = mk("kf", [512, S])
    ktm = mk("ktm", [S, 512])
    vtm = mk("vtm", [S, 512])
    of = mk("of", [512, S])
    cc_in = p.dram("cc_in", [D, S])
    cc_out = p.dram("cc_out", [D, S])
    x1_out = OUT("x1", [D, S])
    hs1_out = OUT("hs1", [D, S])

    phase_tm2fm(p, x_in, xfm, consts)
    p.barrier()
    sweep_ln_fm(p, xfm, hs, consts, kind="ln")
    p.barrier()
    sweep_proj_fm(p, wq, hs, qf, 512, evac=evac_elu1)
    p.barrier()
    sweep_proj_fm(p, wk, hs, kf, 512, evac=evac_elu1)
    p.barrier()

    def evac_elu1_tm(p_, ps, o, nt):
        evac_elu1(p_, ps, o, 0, nt, fw=128)
    sweep_proj_tm(p, hs, wk, ktm, 512, evac=evac_elu1_tm, consts=consts)
    p.barrier()
    sweep_proj_tm(p, hs, wv, vtm, 512, consts=consts)
    p.barrier()

    NCH = S // CHUNK
    kfc = p.ring("laKf", 2, [128, 4 * 128], BF16)
    qfc = p.ring("laQf", 2, [128, 4 * 128], BF16)
    ktc = p.ring("laKt", 2, [128, 512], BF16)
    vtc = p.ring("laVt", 2, [128, 512], BF16)
    psA = [p.ps[4], p.ps[5]]
    psO = [p.ps[0], p.ps[1]]
    psB = [p.ps[2], p.ps[2]]
    psS = [p.ps[3], p.ps[3]]
    amask = p.ring("laM", 2, [128, 128], BF16)
    denr = p.ring("laD", 2, [1, 128], F32)
    bcs = p.ring("laBc", 2, [64, 128], F32)
    otile = p.ring("laOt", 2, [128, 128], BF16)
    _adt = p.ring("adT", 4, [128, 512], F32)
    Sf = [_adt[2 + h // 4][0:64, 65 * (h % 4):65 * (h % 4) + 65] for h in range(NH_LOC)]
    Sb = [p.sb(f"laSb{h}", [128, 65], BF16) for h in range(NH_LOC)]

    for h in range(NH_LOC):
        @p.step("vector", writes=[Sf[h]])
        def _(v, h=h):
            return v.memset(Sf[h][:, :], 0.0)

        @p.step("scalar", reads=[Sf[h]], writes=[Sb[h]])
        def _(sc, h=h):
            sc.activation(out=Sb[h][0:64, :], in_=Sf[h][:, :], func=AF.Copy)
            return sc.activation(out=Sb[h][64:128, :], in_=Sf[h][:, :], func=AF.Copy)

    for c in range(NCH):
        a = c * CHUNK
        kfv, qfv, ktv, vtv = kfc[c % 2], qfc[c % 2], ktc[c % 2], vtc[c % 2]

        @p.step("sync", writes=[kfv], dma=True)
        def _(sy, a=a, kfv=kfv):
            return [sy.dma_start(out=kfv[:, r * 128:(r + 1) * 128],
                                 in_=kf[r * 128:(r + 1) * 128, a:a + 128])
                    for r in range(4)]

        @p.step("sync", writes=[qfv], dma=True)
        def _(sy, a=a, qfv=qfv):
            return [sy.dma_start(out=qfv[:, r * 128:(r + 1) * 128],
                                 in_=qf[r * 128:(r + 1) * 128, a:a + 128])
                    for r in range(4)]

        @p.step("sync", writes=[ktv], dma=True)
        def _(sy, a=a, ktv=ktv):
            return sy.dma_start(out=ktv[:, :], in_=ktm[a:a + 128, :])

        @p.step("sync", writes=[vtv], dma=True)
        def _(sy, a=a, vtv=vtv):
            return sy.dma_start(out=vtv[:, :], in_=vtm[a:a + 128, :])

        for h in range(NH_LOC):
            r, base = h // 2, 64 * (h % 2)
            kslice = kfv[:, r * 128:(r + 1) * 128]
            idx = (c * NH_LOC + h)
            pa = psA[idx % 2]
            po = psO[idx % 2]
            pb = psB[idx % 2]
            pS = psS[idx % 2]
            am = amask[idx % 2]
            dr = denr[idx % 2]
            bc = bcs[idx % 2]
            ot = otile[idx % 2]

            @p.step("tensor", reads=[kfv, qfv], writes=[pa])
            def _(te, r=r, base=base, kfv=kfv, qfv=qfv, pa=pa):
                return te.matmul(pa[:, 0:128],
                                 kfv[base:base + 64, r * 128:(r + 1) * 128],
                                 qfv[base:base + 64, r * 128:(r + 1) * 128],
                                 start=True, stop=True)

            @p.step("vector", reads=[pa, consts["tri_kq_bf"]], writes=[am])
            def _(v, pa=pa, am=am):
                return v.tensor_mul(out=am[:, :], in0=pa[:, 0:128], in1=consts["tri_kq_bf"][:, :])

            @p.step("tensor", reads=[am, vtv, Sb[h], consts["ones_col_bf"], qfv], writes=[po])
            def _(te, h=h, r=r, base=base, am=am, vtv=vtv, qfv=qfv, po=po):
                hl = 64 * h
                te.matmul(po[0:64, 0:128], vtv[:, hl:hl + 64], am[:, :], start=True, stop=False)
                te.matmul(po[0:64, 0:128], Sb[h][base:base + 64, 0:64],
                          qfv[base:base + 64, r * 128:(r + 1) * 128], start=False, stop=True)
                te.matmul(po[64:65, 0:128], consts["ones_col_bf"][:, :], am[:, :],
                          start=True, stop=False)
                return te.matmul(po[64:65, 0:128], Sb[h][base:base + 64, 64:65],
                                 qfv[base:base + 64, r * 128:(r + 1) * 128],
                                 start=False, stop=True)

            @p.step("vector", reads=[po], writes=[dr])
            def _(v, po=po, dr=dr):
                return v.tensor_scalar(out=dr[:, :], in0=po[64:65, 0:128], scalar1=1e-6,
                                       scalar2=None, op0=ALU.add)

            @p.step("vector", reads=[dr], writes=[dr])
            def _(v, dr=dr):
                return v.reciprocal(out=dr[:, :], in_=dr[:, :])

            @p.step("tensor", reads=[dr, consts["ones_row_f"]], writes=[pb])
            def _(te, dr=dr, pb=pb):
                return te.matmul(pb[0:64, 0:128], consts["ones_row_f"][0:1, 0:64], dr[:, :],
                                 start=True, stop=True)

            @p.step("vector", reads=[pb], writes=[bc])
            def _(v, pb=pb, bc=bc):
                return v.tensor_copy(out=bc[:, :], in_=pb[0:64, 0:128])

            @p.step("vector", reads=[po, bc], writes=[ot])
            def _(v, po=po, bc=bc, ot=ot):
                return v.tensor_mul(out=ot[0:64, :], in0=po[0:64, 0:128], in1=bc[:, :])

            @p.step("gpsimd", reads=[ot], dma=True)
            def _(g, ot=ot, h=h, a=a):
                return g.dma_start(out=of[h * 64:(h + 1) * 64, a:a + 128], in_=ot[0:64, :])

            @p.step("tensor", reads=[ktv, vtv, consts["ones_col_bf"]], writes=[pS])
            def _(te, h=h, ktv=ktv, vtv=vtv, pS=pS):
                hl = 64 * h
                te.matmul(pS[0:64, 0:64], ktv[:, hl:hl + 64], vtv[:, hl:hl + 64],
                          start=True, stop=True)
                return te.matmul(pS[0:64, 64:65], ktv[:, hl:hl + 64],
                                 consts["ones_col_bf"][:, :], start=True, stop=True)

            @p.step("vector", reads=[pS, Sf[h]], writes=[Sf[h]])
            def _(v, h=h, pS=pS):
                return v.tensor_add(out=Sf[h][:, :], in0=Sf[h][:, :], in1=pS[0:64, 0:65])

            @p.step("scalar", reads=[Sf[h]], writes=[Sb[h]])
            def _(sc, h=h):
                sc.activation(out=Sb[h][0:64, :], in_=Sf[h][:, :], func=AF.Copy)
                return sc.activation(out=Sb[h][64:128, :], in_=Sf[h][:, :], func=AF.Copy)

    p.barrier()
    sweep_proj_fm(p, wo, of, cc_in, D, kdim=512)
    p.barrier()
    if debug:
        cc_dbg = p.outp("cc_dbg", [D, S])

        @p.step("gpsimd", dma=True)
        def _(g):
            return g.dma_start(out=cc_dbg[:, :], in_=cc_in[:, :])
        p.barrier()
    collective_pair(p, cc_in, cc_out)
    wait_cc(p, "sync")
    wait_cc(p, "gpsimd")
    sweep_add_fm(p, xfm, cc_out, x1_out)
    p.barrier()
    sweep_ln_fm(p, x1_out, hs1_out, consts, kind="ln")
    return p.done() if standalone else p


def make_inputs_A(inp, consts, core):
    b, j = core // 2, core % 2
    x = np.ascontiguousarray(inp["L0_attn_x"][b])
    d = {
        "x_in": x,
        "wq": np.ascontiguousarray(inp["L0_attn_wq"][:, 512 * j:512 * (j + 1)]),
        "wk": np.ascontiguousarray(inp["L0_attn_wk"][:, 512 * j:512 * (j + 1)]),
        "wv": np.ascontiguousarray(inp["L0_attn_wv"][:, 512 * j:512 * (j + 1)]),
        "wo": np.ascontiguousarray(inp["L0_attn_wo"][512 * j:512 * (j + 1), :]),
        "c_ident": consts["ident"], "c_ones_col": consts["ones_col"],
        "c_ones_row": consts["ones_row"], "c_tri_kq": consts["tri_kq"],
    }
    return d


C_STAB0 = 12.0
RS32 = 0.17677669529663687


def build_B(debug=False, p=None, pre=None, pfx=""):
    standalone = p is None
    if standalone:
        p = P("B")
    IN, OUT = _io(p, pre, pfx, standalone, debug)
    hs1 = IN("hs1", [D, S], BF16)
    qw = IN("qw", [D, 512], F32)
    qb = IN("qb", [512, 1], F32)
    kw = IN("kw", [D, 512], F32)
    kb = IN("kb", [512, 1], F32)
    vw = IN("vw", [D, 264], F32)
    vb = IN("vb", [1, 264], F32)
    ow = IN("ow", [256, D], F32)
    ob_ = IN("ob", [D, 1], F32)
    rw = IN("rw", [D, 8], F32)
    rb = IN("rb", [1, 8], F32)
    esel = IN("esel", [8, 8], F32)
    cs = load_consts(p, ("ident", "ones_col", "ones_row"))

    mk = p.outp if debug else p.dram
    qf = mk("qf", [512, S])
    kf = mk("kf", [512, S])
    vtm = mk("vtm", [S, 264])
    aof = mk("aof", [256, S])
    cc_in = p.dram("cc_in", [D, S])
    cc_out = p.dram("cc_out", [D, S])
    mla_out = OUT("mla", [D, S])
    comb_out = OUT("comb", [8, S])

    sweep_proj_fm(p, qw, hs1, qf, 512, bias_ap=qb)
    p.barrier()
    sweep_proj_fm(p, kw, hs1, kf, 512, bias_ap=kb)
    p.barrier()
    sweep_proj_tm(p, hs1, vw, vtm, 264, bias_ap=vb, consts=cs)
    p.barrier()

    nbias = p.sb("nbias", [128, 1], F32)

    @p.step("vector", writes=[nbias])
    def _(v):
        return v.memset(nbias[:, :], -C_STAB0)

    qtiles = p.ring("swX8", 16, [128, 512], BF16)[:8]
    ktiles = p.ring("fbKt", 8, [128, 128], BF16)
    vt = p.ring("fbVt", 2, [128, 264], BF16)
    es = p.ring("swX8", 16, [128, 512], BF16)[8:10]
    drow = p.ring("fbD", 2, [1, 512], F32)
    bcsb = [t[0:32, :] for t in p.ring("adT", 4, [128, 512], F32)[:2]]
    otile = p.ring("swX8", 16, [128, 512], BF16)[10:12]
    psS = [p.ps[0], p.ps[1]]
    psB = p.ps[6]

    for qc in range(8):
        a = qc * 512
        qs = [qtiles[(qc % 2) * 4 + r] for r in range(4)]
        for r in range(4):
            @p.step("sync", writes=[qs[r]], dma=True)
            def _(sy, r=r, a=a, qs=qs):
                return sy.dma_start(out=qs[r][:, :], in_=qf[r * 128:(r + 1) * 128, a:a + 512])
        for hp in range(2):
            for kt in range(NT):
                ks4 = [ktiles[(kt % 2) * 4 + r] for r in range(4)]
                for r in range(4):
                    @p.step("sync", writes=[ks4[r]], dma=True)
                    def _(sy, r=r, kt=kt, ks4=ks4):
                        return sy.dma_start(out=ks4[r][:, :],
                                            in_=kf[r * 128:(r + 1) * 128, kt * 128:(kt + 1) * 128])
                vv = vt[kt % 2]

                @p.step("sync", writes=[vv], dma=True)
                def _(sy, kt=kt, vv=vv):
                    return sy.dma_start(out=vv[:, :], in_=vtm[kt * 128:(kt + 1) * 128, :])

                for g in range(4):
                    h = hp * 4 + g
                    rq, bq = h // 2, 64 * (h % 2)
                    ps_ = psS[g % 2]
                    bank = p.ps[2 + g]

                    @p.step("tensor", reads=[ks4[rq], qs[rq]], writes=[ps_])
                    def _(te, rq=rq, bq=bq, ps_=ps_, ks4=ks4, qs=qs):
                        return te.matmul(ps_[:, :], ks4[rq][bq:bq + 32, :],
                                         qs[rq][bq:bq + 32, :], start=True, stop=True)

                    ev = es[g % 2]

                    @p.step("scalar", reads=[ps_, nbias], writes=[ev])
                    def _(sc, ps_=ps_, ev=ev):
                        return sc.activation(out=ev[:, :], in_=ps_[:, :], func=AF.Exp,
                                             bias=nbias[:, :], scale=RS32)

                    @p.step("tensor", reads=[ev, vv], writes=[bank])
                    def _(te, ev=ev, vv=vv, h=h, kt=kt, bank=bank):
                        return te.matmul(bank[0:33, :], vv[:, 33 * h:33 * h + 33], ev[:, :],
                                         start=(kt == 0), stop=(kt == NT - 1))

            for g in range(4):
                h = hp * 4 + g
                bank = p.ps[2 + g]
                dr = drow[g % 2]
                bc = bcsb[g % 2]
                ot = otile[g % 2]

                @p.step("vector", reads=[bank], writes=[dr])
                def _(v, dr=dr, bank=bank):
                    return v.reciprocal(out=dr[:, :], in_=bank[32:33, :])

                @p.step("tensor", reads=[dr, cs["ones_row_f"]], writes=[psB])
                def _(te, dr=dr):
                    return te.matmul(psB[0:32, :], cs["ones_row_f"][0:1, 0:32], dr[:, :],
                                     start=True, stop=True)

                @p.step("vector", reads=[psB], writes=[bc])
                def _(v, bc=bc):
                    return v.tensor_copy(out=bc[:, :], in_=psB[0:32, :])

                @p.step("vector", reads=[bank, bc], writes=[ot])
                def _(v, bc=bc, ot=ot, bank=bank):
                    return v.tensor_mul(out=ot[0:32, :], in0=bank[0:32, :], in1=bc[:, :])

                @p.step("gpsimd", reads=[ot], dma=True)
                def _(g_, ot=ot, h=h, a=a):
                    return g_.dma_start(out=aof[32 * h:32 * h + 32, a:a + 512], in_=ot[0:32, :])

    p.barrier()
    sweep_proj_fm(p, ow, aof, cc_in, D, kdim=256, bias_ap=ob_)
    p.barrier()
    collective_pair(p, cc_in, cc_out)
    wait_cc(p, "sync")
    wait_cc(p, "gpsimd")
    @p.step("gpsimd", dma=True)
    def _(g):
        return g.dma_start(out=mla_out[:, :], in_=cc_out[:, :])
    p.barrier()

    rwt = [p.sb(f"rwt{kc}", [128, 8], BF16) for kc in range(DC)]
    rbt = p.sb("rbt", [1, 8], BF16)
    for kc in range(DC):
        @p.step("gpsimd", writes=[rwt[kc]], dma=True)
        def _(g, kc=kc):
            return g.dma_start(out=rwt[kc][:, :], in_=rw[kc * 128:(kc + 1) * 128, :])

    @p.step("gpsimd", writes=[rbt], dma=True)
    def _(g):
        return g.dma_start(out=rbt[:, :], in_=rb[:, :])

    mlt = p.ring("t2fXb", 2, [128, D], BF16)
    lg = p.ring("rtL", 2, [128, 8], F32)
    m1 = p.ring("rtm1", 2, [128, 1], F32)
    m2 = p.ring("rtm2", 2, [128, 1], F32)
    den = p.ring("rtden", 2, [128, 1], F32)
    prb = p.ring("rtP", 2, [128, 8], F32)
    sel = p.ring("rtS", 2, [128, 8], F32)
    cmb = p.ring("rtC", 2, [128, 8], BF16)
    cps = p.psT
    cfm = p.ring("rtCf", 2, [8, 128], BF16)
    cfl = p.ring("rtCl", 2, [8, 128], BF16)
    selb = p.sb("selb", [8, 8], BF16)

    @p.step("gpsimd", writes=[selb], dma=True)
    def _(g):
        return g.dma_start(out=selb[:, :], in_=esel[:, :])

    for nt in range(NT):
        ml = mlt[nt % 2]

        @p.step("sync", writes=[ml], dma=True)
        def _(sy, nt=nt, ml=ml):
            return [sy.dma_start(out=ml[:, kc * 128:(kc + 1) * 128],
                                 in_=mla_out[kc * 128:(kc + 1) * 128, nt * 128:(nt + 1) * 128])
                    for kc in range(DC)]
        ps_ = p.ps[4]

        @p.step("tensor", reads=[ml, rbt] + rwt, writes=[ps_])
        def _(te, ml=ml, ps_=ps_):
            for kc in range(DC):
                te.matmul(ps_[:, 0:8], ml[:, kc * 128:(kc + 1) * 128], rwt[kc][:, :],
                          start=(kc == 0), stop=False)
            return te.matmul(ps_[:, 0:8], cs["ones_row_bf"][0:1, :], rbt[0:1, :],
                             start=False, stop=True)
        l, mm1, mm2, dd, pb, se, cb = (lg[nt % 2], m1[nt % 2], m2[nt % 2],
                                       den[nt % 2], prb[nt % 2], sel[nt % 2], cmb[nt % 2])

        @p.step("vector", reads=[ps_], writes=[l])
        def _(v, l=l, ps_=ps_):
            return v.tensor_copy(out=l[:, :], in_=ps_[:, 0:8])

        @p.step("vector", reads=[l], writes=[mm1])
        def _(v, l=l, mm1=mm1):
            return v.reduce_max(out=mm1[:, :], in_=l[:, :], axis=AX.X)

        @p.step("vector", reads=[mm1], writes=[mm1])
        def _(v, mm1=mm1):
            return v.tensor_scalar(out=mm1[:, :], in0=mm1[:, :], scalar1=-1.0,
                                   scalar2=None, op0=ALU.mult)

        @p.step("scalar", reads=[l, mm1], writes=[pb, dd])
        def _(sc, l=l, mm1=mm1, pb=pb, dd=dd):
            return sc.activation(out=pb[:, :], in_=l[:, :], func=AF.Exp,
                                 bias=mm1[:, :], accum_out=dd[:, :])

        @p.step("vector", reads=[dd], writes=[dd])
        def _(v, dd=dd):
            return v.reciprocal(out=dd[:, :], in_=dd[:, :])

        @p.step("vector", reads=[pb, dd], writes=[pb])
        def _(v, pb=pb, dd=dd):
            return v.tensor_scalar(out=pb[:, :], in0=pb[:, :], scalar1=dd[:, :],
                                   scalar2=None, op0=ALU.mult)
        @p.step("vector", reads=[pb], writes=[mm1])
        def _(v, pb=pb, mm1=mm1):
            return v.reduce_max(out=mm1[:, :], in_=pb[:, :], axis=AX.X)

        @p.step("vector", reads=[pb, mm1], writes=[se])
        def _(v, pb=pb, mm1=mm1, se=se):
            return v.tensor_scalar(out=se[:, :], in0=pb[:, :], scalar1=mm1[:, :],
                                   scalar2=None, op0=ALU.is_ge)

        @p.step("vector", reads=[se, pb], writes=[se])
        def _(v, se=se, pb=pb):
            return v.scalar_tensor_tensor(out=se[:, :], in0=se[:, :], scalar=-1e30,
                                          in1=pb[:, :], op0=ALU.mult, op1=ALU.add)

        @p.step("vector", reads=[se], writes=[mm2])
        def _(v, se=se, mm2=mm2):
            return v.reduce_max(out=mm2[:, :], in_=se[:, :], axis=AX.X)

        @p.step("vector", reads=[pb, mm2], writes=[se])
        def _(v, pb=pb, mm2=mm2, se=se):
            return v.tensor_scalar(out=se[:, :], in0=pb[:, :], scalar1=mm2[:, :],
                                   scalar2=None, op0=ALU.is_ge)

        @p.step("vector", reads=[se, pb], writes=[se])
        def _(v, se=se, pb=pb):
            return v.tensor_mul(out=se[:, :], in0=se[:, :], in1=pb[:, :])

        @p.step("vector", reads=[se], writes=[dd])
        def _(v, se=se, dd=dd):
            return v.reduce_sum(out=dd[:, :], in_=se[:, :], axis=AX.X)

        @p.step("vector", reads=[dd], writes=[dd])
        def _(v, dd=dd):
            return v.tensor_scalar(out=dd[:, :], in0=dd[:, :], scalar1=1e-8,
                                   scalar2=None, op0=ALU.add)

        @p.step("vector", reads=[dd], writes=[dd])
        def _(v, dd=dd):
            return v.reciprocal(out=dd[:, :], in_=dd[:, :])

        @p.step("vector", reads=[se, dd], writes=[cb])
        def _(v, se=se, dd=dd, cb=cb):
            return v.tensor_scalar(out=cb[:, :], in0=se[:, :], scalar1=dd[:, :],
                                   scalar2=None, op0=ALU.mult)

        @p.step("tensor", reads=[cb, cs["ident_bf"]], writes=[cps])
        def _(te, cb=cb):
            return te.transpose(cps[0:8, 0:128], cb[:, :], cs["ident_bf"][:, :])

        cf = cfm[nt % 2]

        @p.step("vector", reads=[cps], writes=[cf])
        def _(v, cf=cf):
            return v.tensor_copy(out=cf[:, :], in_=cps[0:8, 0:128])

        @p.step("tensor", reads=[selb, cf], writes=[p.ps[5]])
        def _(te, cf=cf):
            return te.matmul(p.ps[5][0:8, 0:128], selb[:, :], cf[:, :],
                             start=True, stop=True)
        cl = cfl[nt % 2]

        @p.step("vector", reads=[p.ps[5]], writes=[cl])
        def _(v, cl=cl):
            return v.tensor_copy(out=cl[:, :], in_=p.ps[5][0:8, 0:128])

        @p.step("gpsimd", reads=[cl], dma=True)
        def _(g, cl=cl, nt=nt):
            return g.dma_start(out=comb_out[:, nt * 128:(nt + 1) * 128], in_=cl[:, :])

    return p.done() if standalone else p


def make_inputs_B(inp, consts, core, hs1, x1=None):
    b, j = core // 2, core % 2
    qw = np.zeros((D, 512), np.float32)
    qb = np.zeros((512, 1), np.float32)
    for h in range(8):
        hg = 8 * j + h
        qw[:, 64 * h:64 * h + 32] = inp["L0_ffn_q_w"][:, 64 * hg:64 * hg + 32]
        qb[64 * h:64 * h + 32, 0] = inp["L0_ffn_q_b"][64 * hg:64 * hg + 32]
    kw = np.zeros((D, 512), np.float32)
    kb = np.zeros((512, 1), np.float32)
    for h in range(8):
        hg = 8 * j + h
        kw[:, 64 * h:64 * h + 32] = inp["L0_ffn_k_w"][:, 32 * hg:32 * hg + 32]
        kb[64 * h:64 * h + 32, 0] = inp["L0_ffn_k_b"][32 * hg:32 * hg + 32]
    vw = np.zeros((D, 264), np.float32)
    vb = np.zeros((1, 264), np.float32)
    for h in range(8):
        hg = 8 * j + h
        vw[:, 33 * h:33 * h + 32] = inp["L0_ffn_v_w"][:, 32 * hg:32 * hg + 32]
        vb[0, 33 * h:33 * h + 32] = inp["L0_ffn_v_b"][32 * hg:32 * hg + 32]
        vb[0, 33 * h + 32] = 1.0
    ow = np.ascontiguousarray(inp["L0_ffn_o_w"][256 * j:256 * (j + 1), :])
    ob = inp["L0_ffn_o_b"].reshape(D, 1) if j == 0 else np.zeros((D, 1), np.float32)
    d = {
        "hs1": hs1, "qw": qw, "qb": qb, "kw": kw, "kb": kb, "vw": vw, "vb": vb,
        "ow": ow, "ob": np.ascontiguousarray(ob),
        "rw": inp["L0_ffn_router_w"].astype(np.float32),
        "rb": inp["L0_ffn_router_b"].reshape(1, 8),
        "esel": np.eye(8, dtype=np.float32)[:, list(range(4 * j, 4 * j + 4)) + list(range(0, 4 * j)) + list(range(4 * j + 4, 8))][:, :8][:, :].T.copy() if False else np.ascontiguousarray(np.roll(np.eye(8, dtype=np.float32), -4 * j, axis=0).T),
        "c_ident": consts["ident"], "c_ones_col": consts["ones_col"],
        "c_ones_row": consts["ones_row"],
    }
    return d



def build_B2(debug=False, p=None, pre=None, pfx=""):
    standalone = p is None
    if standalone:
        p = P("B2")
    IN, OUT = _io(p, pre, pfx, standalone, debug)
    mla = IN("mla", [D, S], BF16)
    comb = IN("comb", [8, S], BF16)
    x1 = IN("x1", [D, S], BF16)
    hs1 = IN("hs1", [D, S], BF16)
    w1s = [IN(f"w1_{e}", [D, 2048], F32) for e in range(4)]
    b1s = [IN(f"b1_{e}", [2048, 1], F32) for e in range(4)]
    w2s = [IN(f"w2_{e}", [2048, D], F32) for e in range(4)]
    b2s = [IN(f"b2_{e}", [D, 1], F32) for e in range(4)]
    cs = load_consts(p, ("ident", "ones_col", "ones_row"))

    mk2 = p.outp if debug else p.dram
    h1d = mk2("h1d", [2048, S])
    moe_p = [mk2(f"moe{e}", [D, S]) for e in range(4)]
    cc_in = p.dram("cc_in", [D, S])
    cc_out = p.dram("cc_out", [D, S])
    x2_out = OUT("x2", [D, S])
    hs2_out = OUT("hs2", [D, S])

    crow2 = p.ring("crow2", 2, [1, 512], BF16)

    tf = p.ring("adT", 4, [128, 512], F32)[:2]

    for e in range(4):
        def evac_gelu(p_, ps, o, ft, ci, fw, e=e):
            bb = p_.ring("swB", 2, [128, 1], F32)

            @p_.step("scalar", reads=[ps, bb[ft % 2]], writes=[o])
            def _(sc):
                return sc.activation(out=o[0:fw, :], in_=ps[0:fw, :], func=AF.Gelu,
                                     bias=bb[ft % 2][0:fw, :])
        sweep_proj_fm(p, w1s[e], mla, h1d, 2048, bias_ap=b1s[e], evac=evac_gelu)
        p.barrier()

        def evac_comb(p_, ps, o, ft, ci, fw, e=e):
            bb = p_.ring("swB", 2, [128, 1], F32)
            a = ci * 512
            t = tf[(ft * 8 + ci) % 2]
            ps4 = p_.ps[4]
            cr = crow2[ci % 2]

            @p_.step("sync", writes=[cr], dma=True)
            def _(sy):
                return sy.dma_start(out=cr[:, :], in_=comb[e:e + 1, a:a + 512])

            @p_.step("tensor", reads=[cr, cs["ones_row_bf"]], writes=[ps4])
            def _(te):
                return te.matmul(ps4[:, :], cs["ones_row_bf"][0:1, :],
                                 cr[0:1, :], start=True, stop=True)

            @p_.step("scalar", reads=[ps, bb[ft % 2]], writes=[t])
            def _(sc):
                return sc.activation(out=t[0:fw, :], in_=ps[0:fw, :], func=AF.Identity,
                                     bias=bb[ft % 2][0:fw, :])

            @p_.step("vector", reads=[t, ps4], writes=[o])
            def _(v):
                return v.tensor_mul(out=o[0:fw, :], in0=t[0:fw, :], in1=ps4[0:fw, :])
        sweep_proj_fm(p, w2s[e], h1d, moe_p[e], D, kdim=2048, bias_ap=b2s[e],
                      evac=evac_comb)
        p.barrier()

    at4 = p.ring("swX8", 16, [128, 512], BF16)[:8]
    tt4 = p.ring("adT", 4, [128, 512], F32)
    ob4 = p.ring("m4o", 3, [128, 512], BF16)
    for rt in range(DC):
        for ci in range(8):
            a = ci * 512
            idx = rt * 8 + ci
            avs = [at4[(idx % 2) * 4 + e] for e in range(4)]
            for e in range(4):
                @p.step("gpsimd", writes=[avs[e]], dma=True)
                def _(sy, e=e, rt=rt, a=a, avs=avs):
                    return sy.dma_start(out=avs[e][:, :],
                                        in_=moe_p[e][rt * 128:(rt + 1) * 128, a:a + 512])
            t = tt4[(idx % 2) * 2]
            t2 = tt4[(idx % 2) * 2 + 1]
            o = ob4[idx % 3]

            @p.step("vector", reads=avs[:2], writes=[t])
            def _(v, avs=avs, t=t):
                return v.tensor_add(out=t[:, :], in0=avs[0][:, :], in1=avs[1][:, :])

            @p.step("vector", reads=[t, avs[2]], writes=[t2])
            def _(v, avs=avs, t=t, t2=t2):
                return v.tensor_add(out=t2[:, :], in0=t[:, :], in1=avs[2][:, :])

            @p.step("vector", reads=[t2, avs[3]], writes=[o])
            def _(v, avs=avs, t2=t2, o=o):
                return v.tensor_add(out=o[:, :], in0=t2[:, :], in1=avs[3][:, :])

            @p.step("gpsimd", reads=[o], dma=True)
            def _(g, o=o, rt=rt, a=a):
                return g.dma_start(out=cc_in[rt * 128:(rt + 1) * 128, a:a + 512], in_=o[:, :])
    p.barrier()
    if debug:
        cci_dbg = p.outp("cci_dbg", [D, S])
        cco_dbg = p.outp("cco_dbg", [D, S])

        @p.step("gpsimd", dma=True)
        def _(g):
            return g.dma_start(out=cci_dbg[:, :], in_=cc_in[:, :])
        p.barrier()
    collective_pair(p, cc_in, cc_out)
    wait_cc(p, "sync")
    wait_cc(p, "gpsimd")
    if debug:
        @p.step("gpsimd", dma=True)
        def _(g):
            return g.dma_start(out=cco_dbg[:, :], in_=cc_out[:, :])
        p.barrier()
    sweep_add_fm(p, x1, hs1, x2_out, C_ap=cc_out, scale_c=0.3)
    p.barrier()
    sweep_ln_fm(p, x2_out, hs2_out, cs, kind="ln")
    return p.done() if standalone else p


def make_inputs_B2(inp, consts, core, mla, comb, x1, hs1):
    b, j = core // 2, core % 2
    d = {"mla": mla, "comb": comb,
         "x1": x1, "hs1": hs1,
         "c_ident": consts["ident"], "c_ones_col": consts["ones_col"],
         "c_ones_row": consts["ones_row"]}
    for e in range(4):
        eg = 4 * j + e
        d[f"w1_{e}"] = np.ascontiguousarray(inp["L0_ffn_w1"][eg])
        d[f"b1_{e}"] = np.ascontiguousarray(inp["L0_ffn_b1"][eg].reshape(2048, 1))
        d[f"w2_{e}"] = np.ascontiguousarray(inp["L0_ffn_w2"][eg])
        d[f"b2_{e}"] = np.ascontiguousarray(inp["L0_ffn_b2"][eg].reshape(D, 1))
    return d


C_STAB1 = 12.0
RS64 = 0.125


def build_C(debug=False, p=None, pre=None, pfx=""):
    standalone = p is None
    if standalone:
        p = P("C")
    IN, OUT = _io(p, pre, pfx, standalone, debug)
    hs2 = IN("hs2", [D, S], BF16)
    x2 = IN("x2", [D, S], BF16)
    wq = IN("wq", [D, D], F32)
    bq = IN("bq", [D, 1], F32)
    wk = IN("wk", [D, D], F32)
    bk = IN("bk", [D, 1], F32)
    wv = IN("wv", [D, 520], F32)
    bv = IN("bv", [1, 520], F32)
    wp = IN("wp", [512, D], F32)
    bp_ = IN("bp", [D, 1], F32)
    gam = IN("gam", [D, 1], F32)
    cmask = IN("cmask", [128, 4 * 512], F32)
    cs = load_consts(p, ("ident", "ones_col", "ones_row", "rope_sin", "rope_cos"))

    mk = p.outp if debug else p.dram
    qf = mk("qf", [D, S])
    kf = mk("kf", [D, S])
    vtm = mk("vtm", [S, 520])
    yf = mk("yf", [512, S])
    cc_in = p.dram("cc_in", [D, S])
    cc_out = p.dram("cc_out", [D, S])
    x3_out = OUT("x3", [D, S])
    xn_fm_out = OUT("xn_fm", [D, S])
    xn_tm_out = OUT("xn_tm", [S, D])

    cmk = p.sb("cmk", [128, 4 * 512], BF16)

    @p.step("gpsimd", writes=[cmk], dma=True)
    def _(g):
        return g.dma_start(out=cmk[:, :], in_=cmask[:, :])

    def mk_rope_evac(bias_ring_key):
        def evac(p_, ps, o, ft, ci, fw):
            a = ci * 512
            bb = p_.ring("swB", 2, [128, 1], F32)
            sc1 = [t[0:64, :] for t in p_.ring("adT", 4, [128, 512], F32)]
            raw = p_.ring("ropeR", 2, [64, 512], F32)
            r = raw[ci % 2]
            m1, m2 = sc1[(ci % 2) * 2], sc1[(ci % 2) * 2 + 1]

            @p_.step("scalar", reads=[ps, bb[ft % 2]], writes=[r])
            def _(sc):
                return sc.activation(out=r[:, :], in_=ps[0:64, :], func=AF.Identity,
                                     bias=bb[ft % 2][0:64, :])

            @p_.step("vector", reads=[r, cs["rope_cos_bf"]], writes=[m1])
            def _(v):
                return v.tensor_mul(out=m1[0:32, :], in0=r[0:32, :],
                                    in1=cs["rope_cos_bf"][0:32, a:a + 512])

            @p_.step("vector", reads=[r, cs["rope_sin_bf"]], writes=[m2])
            def _(v):
                return v.tensor_mul(out=m2[0:32, :], in0=r[32:64, :],
                                    in1=cs["rope_sin_bf"][32:64, a:a + 512])

            @p_.step("vector", reads=[m1, m2], writes=[o])
            def _(v):
                return v.tensor_sub(out=o[0:32, :], in0=m1[0:32, :], in1=m2[0:32, :])

            @p_.step("vector", reads=[r, cs["rope_cos_bf"]], writes=[m1])
            def _(v):
                return v.tensor_mul(out=m1[0:32, :], in0=r[32:64, :],
                                    in1=cs["rope_cos_bf"][32:64, a:a + 512])

            @p_.step("vector", reads=[r, cs["rope_sin_bf"]], writes=[m2])
            def _(v):
                return v.tensor_mul(out=m2[0:32, :], in0=r[0:32, :],
                                    in1=cs["rope_sin_bf"][0:32, a:a + 512])

            @p_.step("vector", reads=[m1, m2, o], writes=[o])
            def _(v):
                return v.tensor_add(out=o[32:64, :], in0=m1[0:32, :], in1=m2[0:32, :])
        return evac

    sweep_proj_fm(p, wq, hs2, qf, D, bias_ap=bq, evac=mk_rope_evac("q"))
    p.barrier()
    sweep_proj_fm(p, wk, hs2, kf, D, bias_ap=bk, evac=mk_rope_evac("k"))
    p.barrier()
    sweep_proj_tm(p, hs2, wv, vtm, 260, bias_ap=bv, consts=cs, w_col0=0, o_col0=0)
    p.barrier()
    sweep_proj_tm(p, hs2, wv, vtm, 260, bias_ap=bv, consts=cs, w_col0=260, o_col0=260)
    p.barrier()

    nbias = p.sb("nbias1", [128, 1], F32)

    @p.step("vector", writes=[nbias])
    def _(v):
        return v.memset(nbias[:, :], -C_STAB1)

    qtiles = [t[0:64, :] for t in p.ring("swX8", 16, [128, 512], BF16)[8:16]]
    ktiles = p.ring("caK", 8, [64, 128], BF16)
    vt = p.ring("caV", 2, [128, 520], BF16)
    es = p.ring("swX8", 16, [128, 512], BF16)[0:2]
    drow = p.ring("fbD", 2, [1, 512], F32)
    bcsb = [t[0:64, :] for t in p.ring("adT", 4, [128, 512], F32)[:2]]
    otile = p.ring("swX8", 16, [128, 512], BF16)[2:4]
    psS = [p.ps[0], p.ps[1]]
    psB = p.ps[6]

    for qc in range(8):
        a = qc * 512
        ktmax = 4 * qc + 4
        for hp in range(2):
            qs = [qtiles[(hp % 2) * 4 + g] for g in range(4)]
            for g in range(4):
                h = hp * 4 + g

                @p.step("sync", writes=[qs[g]], dma=True)
                def _(sy, g=g, h=h, a=a, qs=qs):
                    return sy.dma_start(out=qs[g][0:64, :],
                                        in_=qf[128 * h:128 * h + 64, a:a + 512])
            for kt in range(ktmax):
                ks4 = [ktiles[(kt % 2) * 4 + g] for g in range(4)]
                for g in range(4):
                    h = hp * 4 + g

                    @p.step("sync", writes=[ks4[g]], dma=True)
                    def _(sy, g=g, h=h, kt=kt, ks4=ks4):
                        return sy.dma_start(out=ks4[g][:, :],
                                            in_=kf[128 * h:128 * h + 64, kt * 128:(kt + 1) * 128])
                vv = vt[kt % 2]

                @p.step("sync", writes=[vv], dma=True)
                def _(sy, kt=kt, vv=vv):
                    return sy.dma_start(out=vv[:, :], in_=vtm[kt * 128:(kt + 1) * 128, :])

                diag = kt - 4 * qc
                for g in range(4):
                    h = hp * 4 + g
                    ps_ = psS[g % 2]
                    bank = p.ps[2 + g]

                    @p.step("tensor", reads=[ks4[g], qs[g]], writes=[ps_])
                    def _(te, g=g, ps_=ps_, ks4=ks4, qs=qs):
                        return te.matmul(ps_[:, :], ks4[g][0:64, :], qs[g][0:64, :],
                                         start=True, stop=True)

                    ev = es[g % 2]

                    @p.step("scalar", reads=[ps_, nbias], writes=[ev])
                    def _(sc, ps_=ps_, ev=ev):
                        return sc.activation(out=ev[:, :], in_=ps_[:, :], func=AF.Exp,
                                             bias=nbias[:, :], scale=RS64)
                    if diag >= 0:
                        @p.step("vector", reads=[ev, cmk], writes=[ev])
                        def _(v, ev=ev, diag=diag):
                            return v.tensor_mul(out=ev[:, :], in0=ev[:, :],
                                                in1=cmk[:, diag * 512:(diag + 1) * 512])

                    @p.step("tensor", reads=[ev, vv], writes=[bank])
                    def _(te, ev=ev, vv=vv, h=h, kt=kt, bank=bank):
                        return te.matmul(bank[0:65, :], vv[:, 65 * h:65 * h + 65], ev[:, :],
                                         start=(kt == 0), stop=(kt == ktmax - 1))

            for g in range(4):
                h = hp * 4 + g
                bank = p.ps[2 + g]
                dr = drow[g % 2]
                bc = bcsb[g % 2]
                ot = otile[g % 2]

                @p.step("vector", reads=[bank], writes=[dr])
                def _(v, dr=dr, bank=bank):
                    return v.reciprocal(out=dr[:, :], in_=bank[64:65, :])

                @p.step("tensor", reads=[dr, cs["ones_row_f"]], writes=[psB])
                def _(te, dr=dr):
                    return te.matmul(psB[0:64, :], cs["ones_row_f"][0:1, 0:64], dr[:, :],
                                     start=True, stop=True)

                @p.step("vector", reads=[psB], writes=[bc])
                def _(v, bc=bc):
                    return v.tensor_copy(out=bc[:, :], in_=psB[0:64, :])

                @p.step("vector", reads=[bank, bc], writes=[ot])
                def _(v, bc=bc, ot=ot, bank=bank):
                    return v.tensor_mul(out=ot[0:64, :], in0=bank[0:64, :], in1=bc[:, :])

                @p.step("gpsimd", reads=[ot], dma=True)
                def _(g_, ot=ot, h=h, a=a):
                    return g_.dma_start(out=yf[64 * h:64 * h + 64, a:a + 512], in_=ot[0:64, :])

    p.barrier()
    sweep_proj_fm(p, wp, yf, cc_in, D, kdim=512, bias_ap=bp_)
    p.barrier()
    collective_pair(p, cc_in, cc_out)
    sweep_add_fm(p, x2, cc_out, x3_out)
    p.barrier()
    hs3 = p.dram("hs3", [D, S])
    sweep_ln_fm(p, x3_out, hs3, cs, kind="ln")
    p.barrier()
    sweep_ln_fm(p, hs3, xn_fm_out, cs, kind="rms", gamma_ap=gam, scale=32.0,
                also_tm_ap=xn_tm_out)
    return p.done() if standalone else p


def make_inputs_C(inp, consts, core, hs2, x2):
    b, j = core // 2, core % 2
    Wa = inp["L1_attn_W_attn"]
    ba = inp["L1_attn_b_attn"]
    wq = np.zeros((D, D), np.float32)
    bq = np.zeros((D, 1), np.float32)
    wk = np.zeros((D, D), np.float32)
    bk = np.zeros((D, 1), np.float32)
    wv = np.zeros((D, 520), np.float32)
    bv = np.zeros((1, 520), np.float32)
    for h in range(8):
        hg = 8 * j + h
        wq[:, 128 * h:128 * h + 64] = Wa[:, 64 * hg:64 * hg + 64]
        bq[128 * h:128 * h + 64, 0] = ba[64 * hg:64 * hg + 64]
        wk[:, 128 * h:128 * h + 64] = Wa[:, D + 64 * hg:D + 64 * hg + 64]
        bk[128 * h:128 * h + 64, 0] = ba[D + 64 * hg:D + 64 * hg + 64]
        wv[:, 65 * h:65 * h + 64] = Wa[:, 2 * D + 64 * hg:2 * D + 64 * hg + 64]
        bv[0, 65 * h:65 * h + 64] = ba[2 * D + 64 * hg:2 * D + 64 * hg + 64]
        bv[0, 65 * h + 64] = 1.0
    wp = np.ascontiguousarray(inp["L1_attn_W_proj"][512 * j:512 * (j + 1), :])
    bp = inp["L1_attn_b_proj"].reshape(D, 1) if j == 0 else np.zeros((D, 1), np.float32)
    cm = np.zeros((128, 4 * 512), np.float32)
    for m in range(4):
        kr = np.arange(128)[:, None]
        qcc = np.arange(512)[None, :]
        cm[:, m * 512:(m + 1) * 512] = (128 * m + kr <= qcc)
    return {
        "hs2": hs2, "x2": x2, "wq": wq, "bq": bq, "wk": wk, "bk": bk,
        "wv": wv, "bv": bv, "wp": wp, "bp": np.ascontiguousarray(bp),
        "gam": inp["L1_ffn_norm_gamma"].reshape(D, 1).astype(np.float32),
        "cmask": cm,
        "c_ident": consts["ident"], "c_ones_col": consts["ones_col"],
        "c_ones_row": consts["ones_row"], "c_rope_sin": consts["rope_sin"],
        "c_rope_cos": consts["rope_cos"],
    }



def build_D(debug=False, p=None, pre=None, pfx=""):
    standalone = p is None
    if standalone:
        p = P("D")
    IN, OUT = _io(p, pre, pfx, standalone, debug)
    xn_fm = IN("xn_fm", [D, S], BF16)
    xn_tm = IN("xn_tm", [S, D], BF16)
    x3 = IN("x3", [D, S], BF16)
    identf_in = IN("identf_in", [128, 128], F32)
    ses = [IN(f"se_{e}", [512, D], F32) for e in range(4)]
    sgam = IN("sgam", [1, D], F32)
    w1s = [IN(f"w1_{e}", [D, 4096], F32) for e in range(4)]
    b1s = [IN(f"b1_{e}", [4096, 1], F32) for e in range(4)]
    w2s = [IN(f"w2_{e}", [4096, D], F32) for e in range(4)]
    b2s = [IN(f"b2_{e}", [1, D], F32) for e in range(4)]
    cs = load_consts(p, ("ident", "ones_col", "ones_row"))
    identf = p.sb("identf", [128, 128], F32)

    @p.step("sync", writes=[identf], dma=True)
    def _(sy):
        return sy.dma_start(out=identf[:, :], in_=identf_in[:, :])

    mk = p.outp if debug else p.dram
    sn_fm = mk("sn_fm", [D, 2048])
    disp_tm = mk("disp_tm", [S, 2048])
    slots_f = mk("slots_f", [D, 2048])
    h1d = p.dram("h1d", [4096, 512])
    eo_tm = mk("eo_tm", [2048, D])
    cn_d = p.dram("cn_d", [1, S], F32)
    cn_o = p.dram("cn_o", [1, S], F32)
    zn_d = p.dram("zn_d", [1, S], F32)
    zn_o = p.dram("zn_o", [1, S], F32)
    zs_d = p.dram("zs_d", [1, 2048], F32)
    cc_in = p.dram("cc_in", [D, S])
    cc_out = p.dram("cc_out", [D, S])
    out_tm = OUT("out", [S, D], F32) if not standalone else p.outp("out", [S, D], F32)

    gamrow = p.sb("gamrow", [1, D], BF16)

    @p.step("gpsimd", writes=[gamrow], dma=True)
    def _(sy):
        return sy.dma_start(out=gamrow[:, :], in_=sgam[:, :])
    gb = p.sb("gamb", [128, D], BF16)

    @p.step("tensor", reads=[gamrow, cs["ones_row_bf"]], writes=[p.ps[4], p.ps[5]])
    def _(te):
        te.matmul(p.ps[4][:, :], cs["ones_row_bf"][0:1, :], gamrow[0:1, 0:512],
                  start=True, stop=True)
        return te.matmul(p.ps[5][:, :], cs["ones_row_bf"][0:1, :], gamrow[0:1, 512:1024],
                         start=True, stop=True)

    @p.step("vector", reads=[p.ps[4], p.ps[5]], writes=[gb])
    def _(v):
        v.tensor_copy(out=gb[:, 0:512], in_=p.ps[4][:, :])
        return v.tensor_copy(out=gb[:, 512:1024], in_=p.ps[5][:, :])

    fb2 = p.ring("f32buf", 2, [128, D], F32)
    set_ = [fb2[0]]
    sqv = p.ring("d1sq", 2, [128, 1], F32)
    snt = [fb2[1]]
    sno = [p.ring("t2fXb", 2, [128, D], BF16)[0]]
    d1to = [p.ring("t2fXb", 2, [128, D], BF16)[1]]
    for e in range(4):
        for t in range(4):
            mrow = e * 512 + t * 128
            sv, sq, sn, so, to = set_[0], sqv[t % 2], snt[0], sno[0], d1to[0]

            @p.step("sync", writes=[sv], dma=True)
            def _(sy, e=e, t=t, sv=sv):
                return sy.dma_start(out=sv[:, :], in_=ses[e][t * 128:(t + 1) * 128, :])

            @p.step("scalar", reads=[sv], writes=[sn, sq])
            def _(sc, sv=sv, sn=sn, sq=sq):
                return sc.activation(out=sn[:, :], in_=sv[:, :], func=AF.Square,
                                     accum_out=sq[:, :])

            @p.step("scalar", reads=[sq], writes=[sq])
            def _(sc, sq=sq):
                return sc.activation(out=sq[:, :], in_=sq[:, :], func=AF.Sqrt)

            @p.step("vector", reads=[sq], writes=[sq])
            def _(v, sq=sq):
                return v.tensor_scalar(out=sq[:, :], in0=sq[:, :], scalar1=1e-12,
                                       scalar2=None, op0=ALU.max)

            @p.step("vector", reads=[sq], writes=[sq])
            def _(v, sq=sq):
                return v.reciprocal(out=sq[:, :], in_=sq[:, :])

            @p.step("vector", reads=[sv, sq], writes=[sn])
            def _(v, sv=sv, sq=sq, sn=sn):
                return v.tensor_scalar(out=sn[:, :], in0=sv[:, :], scalar1=sq[:, :],
                                       scalar2=32.0, op0=ALU.mult, op1=ALU.mult)

            @p.step("vector", reads=[sn, gb], writes=[so])
            def _(v, sn=sn, so=so):
                return v.tensor_mul(out=so[:, :], in0=sn[:, :], in1=gb[:, :])

            @p.step("tensor", reads=[so, cs["ident_bf"]], writes=[p.psT])
            def _(te, so=so):
                last = None
                for dc in range(DC):
                    last = te.transpose(p.psT[:, dc * 128:(dc + 1) * 128],
                                        so[:, dc * 128:(dc + 1) * 128], cs["ident_bf"][:, :])
                return last

            @p.step("vector", reads=[p.psT], writes=[to])
            def _(v, to=to):
                return v.tensor_copy(out=to[:, :], in_=p.psT[:, :])

            @p.step("gpsimd", reads=[to], dma=True)
            def _(g, to=to, mrow=mrow):
                return [g.dma_start(out=sn_fm[dc * 128:(dc + 1) * 128, mrow:mrow + 128],
                                    in_=to[:, dc * 128:(dc + 1) * 128]) for dc in range(DC)]
    p.barrier()

    snt_l = p.ring("d5h", 32, [128, 128], BF16)[:16]
    xnt_l = p.ring("swX16", 32, [128, 512], BF16)[:DC]
    macc = p.sb("macc", [128, S], BF16)
    csb = p.sb("csb", [128, 16], F32)
    cmx = p.ring("d2c", 2, [128, 1], F32)

    @p.step("vector", writes=[macc])
    def _(v):
        return v.memset(macc[:, :], -1e30)

    @p.step("vector", writes=[csb])
    def _(v):
        return v.memset(csb[:, :], -1e30)

    def logit_block(mt, ci, consumer):
        a = ci * 512
        sns = [snt_l[(mt % 2) * DC + dc] for dc in range(DC)]
        if ci == 0:
            for dc in range(DC):
                @p.step("sync", writes=[sns[dc]], dma=True)
                def _(sy, dc=dc, mt=mt, sns=sns):
                    return sy.dma_start(out=sns[dc][:, :],
                                        in_=sn_fm[dc * 128:(dc + 1) * 128,
                                                  mt * 128:(mt + 1) * 128])
        xns = [xnt_l[dc] for dc in range(DC)]
        for dc in range(DC):
            @p.step("sync", writes=[xns[dc]], dma=True)
            def _(sy, dc=dc, a=a, xns=xns):
                return sy.dma_start(out=xns[dc][:, :],
                                    in_=xn_fm[dc * 128:(dc + 1) * 128, a:a + 512])
        ps_ = p.ps[ci % 2]

        @p.step("tensor", reads=sns + xns, writes=[ps_])
        def _(te, sns=sns, xns=xns, ps_=ps_):
            last = None
            for dc in range(DC):
                last = te.matmul(ps_[:, :], sns[dc][:, :], xns[dc][:, :],
                                 start=(dc == 0), stop=(dc == DC - 1))
            return last
        consumer(ps_, mt, ci)

    def d2_consumer(ps_, mt, ci):
        a = ci * 512
        cm = cmx[ci % 2]

        @p.step("vector", reads=[ps_], writes=[cm])
        def _(v, ps_=ps_, cm=cm):
            return v.reduce_max(out=cm[:, :], in_=ps_[:, :], axis=AX.X)

        @p.step("vector", reads=[cm, csb], writes=[csb])
        def _(v, cm=cm, mt=mt):
            return v.tensor_tensor(out=csb[:, mt:mt + 1], in0=csb[:, mt:mt + 1],
                                   in1=cm[:, :], op=ALU.max)

        @p.step("vector", reads=[ps_, macc], writes=[macc])
        def _(v, ps_=ps_, a=a):
            return v.tensor_tensor(out=macc[:, a:a + 512], in0=macc[:, a:a + 512],
                                   in1=ps_[:, :], op=ALU.max)

    for mt in range(16):
        for ci in range(8):
            logit_block(mt, ci, d2_consumer)

    cnrow = p.sb("cnrow", [128, 1], F32)
    for nt in range(NT):
        @p.step("tensor", reads=[macc, cs["ident_bf"]], writes=[p.psT])
        def _(te, nt=nt):
            return te.transpose(p.psT[:, 128:256], macc[:, nt * 128:(nt + 1) * 128],
                                cs["ident_bf"][:, :])

        @p.step("vector", reads=[p.psT], writes=[cnrow])
        def _(v):
            return v.reduce_max(out=cnrow[:, :], in_=p.psT[:, 128:256], axis=AX.X)

        cnf = p.sb("cnf", [128, 1], BF16) if False else cnrow
        crow_sb = p.ring("crow7", 2, [1, 128], F32)[nt % 2]
        cnbf = p.ring("cnbf7", 2, [128, 1], BF16)[nt % 2]

        @p.step("vector", reads=[cnrow], writes=[cnbf])
        def _(v, cnbf=cnbf):
            return v.tensor_copy(out=cnbf[:, :], in_=cnrow[:, :])

        @p.step("tensor", reads=[cnbf, cs["ident_bf"]], writes=[p.psT])
        def _(te, cnbf=cnbf):
            return te.transpose(p.psT[0:1, 0:128], cnbf[:, :], cs["ident_bf"][:, :])

        @p.step("vector", reads=[p.psT], writes=[crow_sb])
        def _(v, crow_sb=crow_sb):
            return v.tensor_copy(out=crow_sb[:, :], in_=p.psT[0:1, 0:128])

        @p.step("gpsimd", reads=[crow_sb], dma=True)
        def _(g, nt=nt, crow_sb=crow_sb):
            return g.dma_start(out=cn_d[0:1, nt * 128:(nt + 1) * 128], in_=crow_sb[:, :])
    p.barrier()
    collective_pair(p, cn_d, cn_o, op=ALU.max)

    nbias3 = p.ring("d3b", 2, [128, 1], F32)
    zsacc = p.sb("zsacc", [128, 16], F32)

    @p.step("vector", writes=[zsacc])
    def _(v):
        return v.memset(zsacc[:, :], 0.0)
    zstile = p.ring("d3z", 2, [128, 1], F32)
    dtile = p.ring("d3d", 2, [128, 512], BF16)
    ttile = p.ring("d3t", 2, [128, 512], BF16)

    def d3_consumer(ps_, mt, ci):
        a = ci * 512
        nb = nbias3[mt % 2]
        if ci == 0:
            @p.step("vector", reads=[csb], writes=[nb])
            def _(v, nb=nb, mt=mt):
                return v.tensor_scalar(out=nb[:, :], in0=csb[:, mt:mt + 1], scalar1=-1.0,
                                       scalar2=None, op0=ALU.mult)
        dt = dtile[ci % 2]
        zt = zstile[ci % 2]

        @p.step("scalar", reads=[ps_, nb], writes=[dt, zt])
        def _(sc, ps_=ps_, dt=dt, zt=zt, nb=nb):
            return sc.activation(out=dt[:, :], in_=ps_[:, :], func=AF.Exp,
                                 bias=nb[:, :], accum_out=zt[:, :])

        @p.step("vector", reads=[zt, zsacc], writes=[zsacc])
        def _(v, zt=zt, mt=mt):
            return v.tensor_add(out=zsacc[:, mt:mt + 1], in0=zsacc[:, mt:mt + 1],
                                in1=zt[:, :])

        @p.step("tensor", reads=[dt, cs["ident_bf"]], writes=[p.psT])
        def _(te, dt=dt):
            last = None
            for q in range(4):
                last = te.transpose(p.psT[:, q * 128:(q + 1) * 128],
                                    dt[:, q * 128:(q + 1) * 128], cs["ident_bf"][:, :])
            return last
        tt_ = ttile[ci % 2]

        @p.step("vector", reads=[p.psT], writes=[tt_])
        def _(v, tt_=tt_):
            return v.tensor_copy(out=tt_[:, :], in_=p.psT[:, 0:512])

        @p.step("gpsimd", reads=[tt_], dma=True)
        def _(g, tt_=tt_, a=a, mt=mt):
            return [g.dma_start(out=disp_tm[a + q * 128:a + (q + 1) * 128,
                                            mt * 128:(mt + 1) * 128],
                                in_=tt_[:, q * 128:(q + 1) * 128]) for q in range(4)]

    for mt in range(16):
        for ci in range(8):
            logit_block(mt, ci, d3_consumer)

    @p.step("vector", reads=[zsacc], writes=[zsacc])
    def _(v):
        return v.reciprocal(out=zsacc[:, :], in_=zsacc[:, :])
    zso = p.sb("zso", [16, 128], F32)

    @p.step("tensor", reads=[zsacc, identf], writes=[p.ps[5]])
    def _(te):
        return te.transpose(p.ps[5][0:16, 0:128], zsacc[:, 0:16], identf[:, :])

    @p.step("vector", reads=[p.ps[5]], writes=[zso])
    def _(v):
        return v.tensor_copy(out=zso[:, :], in_=p.ps[5][0:16, 0:128])

    @p.step("gpsimd", reads=[zso], dma=True)
    def _(g):
        return g.dma_start(out=zs_d[0:1, :].rearrange("o (a b) -> (o a) b", a=16, b=128),
                           in_=zso[:, :])
    p.barrier()

    xntm_r = p.ring("t2fXb", 2, [128, D], BF16)
    dpt = p.ring("swX8", 16, [128, 512], BF16)[4:6]
    rzrow = p.ring("fbD", 2, [1, 512], F32)
    sbc = p.ring("adT", 4, [128, 512], F32)[2]
    sout = p.ring("swX8", 16, [128, 512], BF16)[13:16]
    for esc in range(4):
        rz = rzrow[esc % 2]

        @p.step("sync", writes=[rz], dma=True)
        def _(sy, esc=esc, rz=rz):
            return sy.dma_start(out=rz[:, :], in_=zs_d[0:1, esc * 512:(esc + 1) * 512])
        for dh in range(2):
            for nt in range(NT):
                dp = dpt[nt % 2]
                xv4 = xntm_r[nt % 2]

                @p.step("sync", writes=[dp], dma=True)
                def _(sy, nt=nt, esc=esc, dp=dp):
                    return sy.dma_start(out=dp[:, :],
                                        in_=disp_tm[nt * 128:(nt + 1) * 128,
                                                    esc * 512:(esc + 1) * 512])

                @p.step("sync", writes=[xv4], dma=True)
                def _(sy, nt=nt, xv4=xv4, dh=dh):
                    return sy.dma_start(out=xv4[:, :],
                                        in_=xn_tm[nt * 128:(nt + 1) * 128,
                                                  dh * 512:(dh + 1) * 512]) if False else \
                        sy.dma_start(out=xv4[:, 0:512],
                                     in_=xn_tm[nt * 128:(nt + 1) * 128, dh * 512:(dh + 1) * 512])

                @p.step("tensor", reads=[dp, xv4], writes=[p.ps[0], p.ps[1], p.ps[2], p.ps[3]])
                def _(te, nt=nt, dh=dh, dp=dp, xv4=xv4):
                    last = None
                    for dt_ in range(4):
                        last = te.matmul(p.ps[dt_][:, :],
                                         xv4[:, dt_ * 128:(dt_ + 1) * 128],
                                         dp[:, :], start=(nt == 0), stop=(nt == NT - 1))
                    return last
            @p.step("tensor", reads=[rz, cs["ones_row_f"]], writes=[p.ps[4]])
            def _(te, rz=rz):
                return te.matmul(p.ps[4][:, :], cs["ones_row_f"][0:1, :], rz[:, :],
                                 start=True, stop=True)

            @p.step("vector", reads=[p.ps[4]], writes=[sbc])
            def _(v):
                return v.tensor_copy(out=sbc[:, :], in_=p.ps[4][:, :])
            for dt_ in range(4):
                dtile_ = dh * 4 + dt_
                so = sout[dt_ % 3]

                @p.step("vector", reads=[p.ps[dt_], sbc], writes=[so])
                def _(v, dt_=dt_, so=so):
                    return v.tensor_mul(out=so[:, :], in0=p.ps[dt_][:, :], in1=sbc[:, :])

                @p.step("gpsimd", reads=[so], dma=True)
                def _(g, so=so, dtile_=dtile_, esc=esc):
                    return g.dma_start(out=slots_f[dtile_ * 128:(dtile_ + 1) * 128,
                                                   esc * 512:(esc + 1) * 512], in_=so[:, :])
            p.barrier()

    for e in range(4):
        def evac_gelu(p_, ps, o, ft, ci, fw, e=e):
            bb = p_.ring("swB", 2, [128, 1], F32)

            @p_.step("scalar", reads=[ps, bb[ft % 2]], writes=[o])
            def _(sc):
                return sc.activation(out=o[0:fw, :], in_=ps[0:fw, :], func=AF.Gelu,
                                     bias=bb[ft % 2][0:fw, :])
        sweep_proj_fm(p, w1s[e], slots_f, h1d, 4096, bias_ap=b1s[e], evac=evac_gelu,
                      n0=0, n1=512, x_col0=e * 512)
        p.barrier()
        w2t = p.ring("swX16", 32, [128, 512], BF16)
        b2row = p.ring("b2r", 1, [1, D], BF16)[0]

        @p.step("gpsimd", writes=[b2row], dma=True)
        def _(g, e=e):
            return g.dma_start(out=b2row[:, :], in_=b2s[e][:, :])
        ht = p.ring("d5h", 32, [128, 128], BF16)
        eo_ring = p.ring("d5eo", 3, [128, 512], BF16)
        for doc in range(2):
            for hc in range(32):
                @p.step("gpsimd", writes=[w2t[hc]], dma=True)
                def _(g, e=e, doc=doc, hc=hc):
                    return g.dma_start(out=w2t[hc][:, :],
                                       in_=w2s[e][hc * 128:(hc + 1) * 128,
                                                  doc * 512:(doc + 1) * 512])
            for est in range(4):
                hs_ = [ht[hc] for hc in range(32)]
                for hc in range(32):
                    @p.step("sync", writes=[hs_[hc]], dma=True)
                    def _(sy, hc=hc, est=est, hs_=hs_):
                        return sy.dma_start(out=hs_[hc][:, :],
                                            in_=h1d[hc * 128:(hc + 1) * 128,
                                                    est * 128:(est + 1) * 128])
                ps_ = p.ps[est % 4]

                @p.step("tensor", reads=hs_ + list(w2t) + [b2row], writes=[ps_])
                def _(te, hs_=hs_, ps_=ps_, doc=doc):
                    for hc in range(32):
                        te.matmul(ps_[:, :], hs_[hc][:, :], w2t[hc][:, :],
                                  start=(hc == 0), stop=False)
                    return te.matmul(ps_[:, :], cs["ones_row_bf"][0:1, :],
                                     b2row[0:1, doc * 512:(doc + 1) * 512],
                                     start=False, stop=True)
                eo_o = eo_ring[est % 3]

                @p.step("scalar", reads=[ps_], writes=[eo_o])
                def _(sc, ps_=ps_, eo_o=eo_o):
                    return sc.activation(out=eo_o[:, :], in_=ps_[:, :], func=AF.Copy)

                @p.step("gpsimd", reads=[eo_o], dma=True)
                def _(g, eo_o=eo_o, e=e, est=est, doc=doc):
                    return g.dma_start(out=eo_tm[e * 512 + est * 128:e * 512 + (est + 1) * 128,
                                                 doc * 512:(doc + 1) * 512], in_=eo_o[:, :])
        p.barrier()

    cnb = p.sb("cnb", [1, 512], F32)
    combt = p.ring("swX8", 16, [128, 512], BF16)
    eot = p.ring("d5h", 32, [128, 128], BF16)[16:]
    oout = p.ring("swX16", 32, [128, 512], BF16)[21:24]
    for ci in range(8):
        a = ci * 512

        @p.step("sync", writes=[cnb], dma=True)
        def _(sy, a=a):
            return sy.dma_start(out=cnb[:, :], in_=cn_o[0:1, a:a + 512])
        cnbbf = p.sb("cnbbf", [1, 512], BF16) if "cnbbf" not in p._rcache else p._rcache["cnbbf"]
        p._rcache["cnbbf"] = cnbbf

        @p.step("vector", reads=[cnb], writes=[cnbbf])
        def _(v, cnbbf=cnbbf):
            return v.tensor_scalar(out=cnbbf[:, :], in0=cnb[:, :], scalar1=-1.0,
                                   scalar2=None, op0=ALU.mult)
        for mt in range(16):
            sns = [snt_l[(mt % 2) * DC + dc] for dc in range(DC)]
            for dc in range(DC):
                @p.step("sync", writes=[sns[dc]], dma=True)
                def _(sy, dc=dc, mt=mt, sns=sns):
                    return sy.dma_start(out=sns[dc][:, :],
                                        in_=sn_fm[dc * 128:(dc + 1) * 128,
                                                  mt * 128:(mt + 1) * 128])
            xns = p.ring("swX16", 32, [128, 512], BF16)[24:32]
            if mt == 0:
                for dc in range(DC):
                    @p.step("sync", writes=[xns[dc]], dma=True)
                    def _(sy, dc=dc, a=a, xns=xns):
                        return sy.dma_start(out=xns[dc][:, :],
                                            in_=xn_fm[dc * 128:(dc + 1) * 128, a:a + 512])
            ps_ = p.ps[mt % 2]

            @p.step("tensor", reads=sns + xns + [cnbbf, cs["ones_row_bf"]], writes=[ps_])
            def _(te, sns=sns, xns=xns, ps_=ps_, cnbbf=cnbbf):
                for dc in range(DC):
                    te.matmul(ps_[:, :], sns[dc][:, :], xns[dc][:, :],
                              start=(dc == 0), stop=False)
                return te.matmul(ps_[:, :], cs["ones_row_bf"][0:1, :], cnbbf[0:1, :],
                                 start=False, stop=True)
            cb = combt[mt]

            @p.step("scalar", reads=[ps_], writes=[cb])
            def _(sc, ps_=ps_, cb=cb):
                return sc.activation(out=cb[:, :], in_=ps_[:, :], func=AF.Exp)

            @p.step("tensor", reads=[cb, cs["ones_col_bf"]], writes=[p.ps[5]])
            def _(te, cb=cb, mt=mt):
                return te.matmul(p.ps[5][0:1, :], cs["ones_col_bf"][:, :], cb[:, :],
                                 start=(mt == 0), stop=(mt == 15))
        znr = p.ring("lnrows", 5, [1, 512], F32)[0]

        @p.step("vector", reads=[p.ps[5]], writes=[znr])
        def _(v, znr=znr):
            return v.tensor_copy(out=znr[:, :], in_=p.ps[5][0:1, :])

        @p.step("gpsimd", reads=[znr], dma=True)
        def _(g, znr=znr, a=a):
            return g.dma_start(out=zn_d[0:1, a:a + 512], in_=znr[:, :])
        for dh in range(2):
            for mt in range(16):
                eos = [eot[(mt % 2) * DC + dc] for dc in range(4)]
                for dt_ in range(4):
                    dtile_ = dh * 4 + dt_

                    @p.step("sync", writes=[eos[dt_]], dma=True)
                    def _(sy, mt=mt, dtile_=dtile_, eos=eos, dt_=dt_):
                        return sy.dma_start(out=eos[dt_][:, :],
                                            in_=eo_tm[mt * 128:(mt + 1) * 128,
                                                      dtile_ * 128:(dtile_ + 1) * 128])

                @p.step("tensor", reads=eos + [combt[mt]], writes=[p.ps[0], p.ps[1], p.ps[2], p.ps[3]])
                def _(te, mt=mt, eos=eos):
                    last = None
                    for dt_ in range(4):
                        last = te.matmul(p.ps[dt_][:, :], eos[dt_][:, :], combt[mt][:, :],
                                         start=(mt == 0), stop=(mt == 15))
                    return last
            for dt_ in range(4):
                dtile_ = dh * 4 + dt_
                oo = oout[dt_ % 3]

                @p.step("scalar", reads=[p.ps[dt_]], writes=[oo])
                def _(sc, dt_=dt_, oo=oo):
                    return sc.activation(out=oo[:, :], in_=p.ps[dt_][:, :], func=AF.Copy)

                @p.step("gpsimd", reads=[oo], dma=True)
                def _(g, oo=oo, dtile_=dtile_, a=a):
                    return g.dma_start(out=cc_in[dtile_ * 128:(dtile_ + 1) * 128, a:a + 512],
                                       in_=oo[:, :])
            p.barrier()
    p.barrier()
    collective_pair(p, cc_in, cc_out)
    collective_pair(p, zn_d, zn_o)

    znrow = p.ring("lnrows", 5, [1, 512], F32)[1:3]
    zbc = p.ring("adT", 4, [128, 512], F32)[0]
    xt7 = p.ring("swX16", 32, [128, 512], BF16)[0:2]
    ct7 = p.ring("swX16", 32, [128, 512], BF16)[2:4]
    f7 = p.ring("adT", 4, [128, 512], F32)[1:3]
    t7 = [p.ring("adT", 4, [128, 512], F32)[3], p.ring("f32buf", 2, [128, D], F32)[0][:, 0:512]]
    for ci in range(8):
        a = ci * 512
        rz = znrow[ci % 2]

        @p.step("sync", writes=[rz], dma=True)
        def _(sy, a=a, rz=rz):
            return sy.dma_start(out=rz[:, :], in_=zn_o[0:1, a:a + 512])

        @p.step("vector", reads=[rz], writes=[rz])
        def _(v, rz=rz):
            return v.reciprocal(out=rz[:, :], in_=rz[:, :])

        @p.step("tensor", reads=[rz, cs["ones_row_f"]], writes=[p.ps[4]])
        def _(te, rz=rz):
            return te.matmul(p.ps[4][:, :], cs["ones_row_f"][0:1, :], rz[:, :],
                             start=True, stop=True)

        @p.step("vector", reads=[p.ps[4]], writes=[zbc])
        def _(v):
            return v.tensor_copy(out=zbc[:, :], in_=p.ps[4][:, :])
        for dc in range(DC):
            xv, cv, fv, tv = xt7[dc % 2], ct7[dc % 2], f7[dc % 2], t7[dc % 2]

            @p.step("sync", writes=[xv], dma=True)
            def _(sy, dc=dc, a=a, xv=xv):
                return sy.dma_start(out=xv[:, :], in_=x3[dc * 128:(dc + 1) * 128, a:a + 512])

            @p.step("sync", writes=[cv], dma=True)
            def _(sy, dc=dc, a=a, cv=cv):
                return sy.dma_start(out=cv[:, :], in_=cc_out[dc * 128:(dc + 1) * 128, a:a + 512])

            @p.step("vector", reads=[cv, zbc], writes=[tv])
            def _(v, cv=cv, tv=tv):
                return v.tensor_mul(out=tv[:, :], in0=cv[:, :], in1=zbc[:, :])

            @p.step("vector", reads=[tv, xv], writes=[fv])
            def _(v, tv=tv, xv=xv, fv=fv):
                return v.tensor_add(out=fv[:, :], in0=tv[:, :], in1=xv[:, :])
            @p.step("tensor", reads=[fv, identf], writes=[p.ps[5], p.ps[6]])
            def _(te, fv=fv):
                last = None
                for q in range(4):
                    tgt = p.ps[5] if q < 2 else p.ps[6]
                    last = te.transpose(tgt[:, (q % 2) * 128:(q % 2) * 128 + 128],
                                        fv[:, q * 128:(q + 1) * 128], identf[:, :])
                return last
            ov = [p.ring("f32buf", 2, [128, D], F32)[0][:, 512:1024],
                  p.ring("f32buf", 2, [128, D], F32)[1][:, 0:512]][dc % 2]

            @p.step("vector", reads=[p.ps[5], p.ps[6]], writes=[ov])
            def _(v, ov=ov):
                v.tensor_copy(out=ov[:, 0:256], in_=p.ps[5][:, 0:256])
                return v.tensor_copy(out=ov[:, 256:512], in_=p.ps[6][:, 0:256])
            @p.step("gpsimd", reads=[ov], dma=True)
            def _(g, ov=ov, dc=dc, a=a):
                return [g.dma_start(out=out_tm[a + q * 128:a + (q + 1) * 128,
                                               dc * 128:(dc + 1) * 128],
                                    in_=ov[:, q * 128:(q + 1) * 128]) for q in range(4)]
    return p.done() if standalone else p


def make_inputs_D(inp, consts, core, xn_fm, xn_tm, x3):
    b, j = core // 2, core % 2
    d = {"xn_fm": xn_fm, "xn_tm": xn_tm, "x3": x3,
         "identf_in": consts["ident"],
         "sgam": inp["L1_ffn_slot_norm_gamma"].reshape(1, D).astype(np.float32),
         "c_ident": consts["ident"], "c_ones_col": consts["ones_col"],
         "c_ones_row": consts["ones_row"]}
    for e in range(4):
        eg = 4 * j + e
        d[f"se_{e}"] = np.ascontiguousarray(inp["L1_ffn_slot_embeds"][eg])
        d[f"w1_{e}"] = np.ascontiguousarray(inp["L1_ffn_w1"][eg])
        d[f"b1_{e}"] = np.ascontiguousarray(inp["L1_ffn_b1"][eg].reshape(4096, 1))
        d[f"w2_{e}"] = np.ascontiguousarray(inp["L1_ffn_w2"][eg])
        d[f"b2_{e}"] = np.ascontiguousarray(inp["L1_ffn_b2"][eg].reshape(1, D))
    return d



def build_fused():
    p = P("F")
    x1 = p.dram("s_x1", [D, S])
    hs1 = p.dram("s_hs1", [D, S])
    mla = p.dram("s_mla", [D, S])
    comb = p.dram("s_comb", [8, S])
    x2 = p.dram("s_x2", [D, S])
    hs2 = p.dram("s_hs2", [D, S])
    x3 = p.dram("s_x3", [D, S])
    xnf = p.dram("s_xnf", [D, S])
    xnt = p.dram("s_xnt", [S, D])
    out = p.outp("out", [S, D], F32)
    build_A(p=p, pre={"x1": x1, "hs1": hs1}, pfx="A_")
    p.barrier()
    build_B(p=p, pre={"hs1": hs1, "mla": mla, "comb": comb}, pfx="B_")
    p.barrier()
    build_B2(p=p, pre={"mla": mla, "comb": comb, "x1": x1, "hs1": hs1,
                       "x2": x2, "hs2": hs2}, pfx="B2_")
    p.barrier()
    build_C(p=p, pre={"hs2": hs2, "x2": x2, "x3": x3, "xn_fm": xnf,
                      "xn_tm": xnt}, pfx="C_")
    p.barrier()
    build_D(p=p, pre={"xn_fm": xnf, "xn_tm": xnt, "x3": x3, "out": out}, pfx="D_")
    return p.done()


def make_inputs_fused(inp, consts, core):
    d = {}
    for pfx, fn, extra in (("A_", make_inputs_A, ()),
                           ("B_", make_inputs_B, (None,)),
                           ("B2_", make_inputs_B2, (None, None, None, None)),
                           ("C_", make_inputs_C, (None, None)),
                           ("D_", make_inputs_D, (None, None, None))):
        sub = fn(inp, consts, core, *extra)
        for k, v in sub.items():
            if v is None:
                continue
            if k.startswith("c_"):
                d[k] = v
            else:
                d[pfx + k] = v
    return d


from concourse.bass_utils import run_bass_kernel_spmd

_BUILD_CACHE = {}


def _get(name, fn):
    if name not in _BUILD_CACHE:
        _BUILD_CACHE[name] = fn()
    return _BUILD_CACHE[name]


def kernel(**inputs):
    consts = host_constants()
    cores = list(range(8))
    nc = _get("F", build_fused)
    im = [make_inputs_fused(inputs, consts, c) for c in cores]
    r = run_bass_kernel_spmd(nc, im, core_ids=cores).results
    out = np.stack([np.asarray(r[2 * b]["out"], dtype=np.float32) for b in range(4)])
    return out


# version 07
# speedup vs baseline: 1.4581x; 1.4581x over previous
import numpy as np
from contextlib import ExitStack
import concourse.bass as bass
import concourse.mybir as mybir

F32 = mybir.dt.float32
BF16 = mybir.dt.bfloat16
AF = mybir.ActivationFunctionType
ALU = mybir.AluOpType
AX = mybir.AxisListType

D = 1024
S = 4096
B = 4
DC = D // 128
NT = S // 128
EPS_LN = 1e-5

ENGINES = ("tensor", "vector", "scalar", "gpsimd", "sync")


def cdiv(a, b):
    return (a + b - 1) // b


class P:

    def __init__(self, name):
        self.nc = bass.Bass("TRN2", target_bir_lowering=False)
        self.ctx = ExitStack()
        self.block = self.ctx.enter_context(self.nc.Block())
        self.name = name
        self._uid = 0
        self.esem = {e: self.ctx.enter_context(self.nc.semaphore("es_" + e))
                     for e in ENGINES + ("cc",)}
        self.ecnt = {e: 0 for e in ENGINES + ("cc",)}
        self.res = {}
        self.fences = {"gpsimd": [0], "sync": [0]}
        self.waited = {e: {} for e in ENGINES}
        self.safe = {e: 0 for e in ENGINES}
        self._eng_obj = {}
        self._open_section = None
        self.ps = [self.ctx.enter_context(self.nc.psum_tensor(f"pool_ps{i}", [128, 512], F32))
                   for i in range(7)]
        self.psT = self.ctx.enter_context(self.nc.psum_tensor("pool_psT", [128, 1024], BF16))

    def uid(self):
        self._uid += 1
        return self._uid

    def dram(self, name, shape, dtype=BF16, kind="Internal"):
        if kind == "Internal":
            name = f"{name}_{self.uid()}"
        return self.nc.dram_tensor(name, list(shape), dtype, kind=kind).ap()

    def inp(self, name, shape, dtype=F32):
        return self.dram(name, shape, dtype, kind="ExternalInput")

    def outp(self, name, shape, dtype=BF16):
        return self.dram(name, shape, dtype, kind="ExternalOutput")

    def sb(self, name, shape, dtype=BF16):
        return self.ctx.enter_context(self.nc.sbuf_tensor(f"{name}_{self.uid()}", list(shape), dtype))

    def psum(self, name, shape, dtype=F32):
        return self.ctx.enter_context(self.nc.psum_tensor(f"{name}_{self.uid()}", list(shape), dtype))

    def ring(self, name, n, shape, dtype=BF16):
        if name == "swX8":
            return self.ring("swX16", 32, shape, dtype)[:n]
        k = (name, n, tuple(shape), dtype)
        if not hasattr(self, "_rcache"):
            self._rcache = {}
        if k not in self._rcache:
            self._rcache[k] = [self.sb(f"{name}{i}", shape, dtype) for i in range(n)]
        return self._rcache[k]

    def _engine(self, ename):
        if self._open_section == ename:
            return self._eng_obj[ename]
        holder = {}

        def body(eng):
            holder["eng"] = eng
        getattr(self.block, ename)(body)
        raise RuntimeError("engine sections must be driven via step()")

    def step(self, ename, reads=(), writes=(), dma=False, n_dma=1):
        def run(fn):
            deps = {}

            def need(e, c):
                if e == ename and ename == "tensor":
                    return
                if e == ename and c <= self.safe[ename]:
                    return
                if e in self.fences and e != ename:
                    fs = self.fences[e]
                    f = next((x for x in fs if x >= c), None)
                    if f is None:
                        f = self.ecnt[e]
                        assert f >= c

                        def fbody(eng, e=e, f=f):
                            eng.wait_ge(self.esem[e], f)
                        getattr(self.block, e)(fbody)
                        fs.append(f)
                    c = f
                deps[e] = max(deps.get(e, 0), c)

            for r in reads:
                st = self.res.get(id(r) if not isinstance(r, (str, tuple)) else r)
                if st and st["w"]:
                    need(*st["w"])
            for w in writes:
                key = id(w) if not isinstance(w, (str, tuple)) else w
                st = self.res.get(key)
                if st:
                    if st["w"]:
                        need(*st["w"])
                    for rr in st["r"]:
                        need(*rr)

            emitted = {}

            wm = self.waited[ename]
            for e in list(deps):
                if deps[e] <= wm.get(e, 0):
                    del deps[e]
                else:
                    wm[e] = deps[e]
            deps_log = dict(deps)

            def body(eng):
                for e, c in deps.items():
                    if c > 0:
                        eng.wait_ge(self.esem[e], c)
                ret = fn(eng)
                instrs = ret if isinstance(ret, (list, tuple)) else ([ret] if ret is not None else [])
                if dma:
                    for ins in instrs:
                        ins.then_inc(self.esem[ename], 16)
                    emitted["inc"] = 16 * len(instrs)
                else:
                    if instrs:
                        instrs[-1].then_inc(self.esem[ename], 1)
                        emitted["inc"] = 1
                    else:
                        emitted["inc"] = 0

            getattr(self.block, ename)(body)
            self.ecnt[ename] += emitted.get("inc", 0)
            if getattr(self, "log_steps", False):
                self.step_log.append((ename, dict(deps_log), emitted.get("inc", 0),
                                      self.ecnt[ename]))
            cnt = (ename, self.ecnt[ename])
            for w in writes:
                key = id(w) if not isinstance(w, (str, tuple)) else w
                self.res[key] = {"w": cnt, "r": []}
            for r in reads:
                key = id(r) if not isinstance(r, (str, tuple)) else r
                st = self.res.setdefault(key, {"w": None, "r": []})
                st["r"].append(cnt)
            return None
        return run

    def barrier(self):
        final = dict(self.ecnt)
        for ename in ENGINES:
            def body(eng, ename=ename):
                for e in ENGINES + ("cc",):
                    if final[e] > 0 and final[e] > self.waited[ename].get(e, 0):
                        eng.wait_ge(self.esem[e], final[e])
            getattr(self.block, ename)(body)
            for e in ENGINES + ("cc",):
                if final[e] > 0:
                    self.waited[ename][e] = max(self.waited[ename].get(e, 0), final[e])
        for e in ENGINES:
            self.safe[e] = self.ecnt[e]
        self.res.clear()

    def done(self):
        self.barrier()
        self.ctx.close()
        return self.nc



def host_constants():
    c = {}
    c["ident"] = np.eye(128, dtype=np.float32)
    c["ones_col"] = np.ones((128, 1), dtype=np.float32)
    c["ones_row"] = np.ones((1, 128), dtype=np.float32)
    kk, qq = np.meshgrid(np.arange(128), np.arange(128), indexing="ij")
    c["tri_kq"] = (kk <= qq).astype(np.float32)
    hd = 64
    fraction = 2.0 * np.arange(0, hd // 2) / hd
    timescale = 10000.0 ** fraction
    pos = np.arange(S, dtype=np.float64)
    ang = pos[None, :] / timescale[:, None]
    c["rope_sin"] = np.concatenate([np.sin(ang), np.sin(ang)], 0).astype(np.float32)
    c["rope_cos"] = np.concatenate([np.cos(ang), np.cos(ang)], 0).astype(np.float32)
    return c


def load_consts(p, want=("ident", "ones_col", "ones_row", "tri_kq")):
    shapes = {"ident": (128, 128), "ones_col": (128, 1), "ones_row": (1, 128),
              "tri_kq": (128, 128), "rope_sin": (64, S), "rope_cos": (64, S)}
    if not hasattr(p, "_consts"):
        p._consts = {}
    cs = p._consts
    want = [nm for nm in want if nm + "_bf" not in cs]
    for nm in want:
        ap = p.inp("c_" + nm, shapes[nm], F32)
        bf = p.sb("c_" + nm + "_bf", shapes[nm], BF16)
        cs[nm + "_bf"] = bf

        @p.step("gpsimd", writes=[bf], dma=True)
        def _(g, ap=ap, bf=bf):
            return g.dma_start(out=bf[:, :], in_=ap[:, :])
        if nm in ("ones_row", "ones_col"):
            ff = p.sb("c_" + nm + "_f", shapes[nm], F32)
            cs[nm + "_f"] = ff

            @p.step("sync", writes=[ff], dma=True)
            def _(sy, ap=ap, ff=ff):
                return sy.dma_start(out=ff[:, :], in_=ap[:, :])
    return cs



def xpool(p):
    return p.ring("XPOOL", 64, [128, 512], BF16)


def pool_load(p, tag, X_ap, KC, NCH, n0=0, x_row0=0):
    tiles = xpool(p)
    key = (tag, KC, NCH, n0, x_row0)
    if getattr(p, "_pool_tag", None) == key:
        return (tiles, KC, NCH, n0)
    p._pool_tag = key
    for kc in range(KC):
        for ci in range(NCH):
            t = tiles[kc * NCH + ci]
            a = n0 + ci * 512
            eng = "sync" if (kc + ci) % 2 == 0 else "gpsimd"

            @p.step(eng, writes=[t], dma=True)
            def _(q, t=t, kc=kc, a=a):
                return q.dma_start(out=t[:, :],
                                   in_=X_ap[x_row0 + kc * 128:x_row0 + (kc + 1) * 128,
                                            a:a + 512])
    return (tiles, KC, NCH, n0)


def pool_use(p, tag):
    key = getattr(p, "_pool_tag", None)
    assert key is not None and key[0] == tag, f"pool tag {key} != {tag}"
    return (xpool(p), key[1], key[2], key[3])



def sweep_proj_fm(p, W_ap, X_ap, O_ap, F_out, kdim=D, n0=0, n1=S,
                  bias_ap=None, evac=None, out_dt=BF16, o_row0=0, x_row0=0,
                  w_col0=0, nbank=4, acc_scale=None, x_col0=0, x_pool=None):
    KC = kdim // 128
    FT = cdiv(F_out, 128)
    NCH = cdiv(n1 - n0, 512)
    if KC > 8:
        assert x_pool is not None
        _wt16 = p.ring("swX16", 32, [128, 512], BF16)[24:32]
        wbufs = [_wt16[0:4], _wt16[4:8]]

        def wtile(buf, kc):
            return wbufs[buf][kc // 4], (kc % 4) * 128
    else:
        wbig = p.ring(f"swW{KC}", 2, [128, KC * 128], BF16)

        def wtile(buf, kc):
            return wbig[buf], kc * 128
    if x_pool is not None:
        ptiles, pKC, pNCH, pn0 = x_pool
        assert pKC == KC
    elif KC <= 8:
        nxb = 2 * KC
        xt = p.ring("swX8", 16, [128, 512], BF16)[:nxb]
    else:
        nxb = KC
        xt = p.ring("swX16", 32, [128, 512], BF16)[:nxb]
    ob = p.ring("swO" + str(out_dt), 2, [128, 512], out_dt)
    bb = p.ring("swB", 2, [128, 1], F32)
    pss = p.ps[:nbank]

    for ft in range(FT):
        fw = min(128, F_out - ft * 128)
        wwr = list(wbufs[ft % 2]) if KC > 8 else [wbig[ft % 2]]

        @p.step("gpsimd", writes=wwr, dma=True)
        def _(g, ft=ft, fw=fw):
            outs = []
            for kc in range(KC):
                wt_, off = wtile(ft % 2, kc)
                outs.append(g.dma_start(out=wt_[:, off:off + fw],
                                        in_=W_ap[kc * 128:(kc + 1) * 128,
                                                 w_col0 + ft * 128:w_col0 + ft * 128 + fw]))
            return outs
        if bias_ap is not None:
            @p.step("sync", writes=[bb[ft % 2]], dma=True)
            def _(sy, ft=ft, fw=fw):
                return sy.dma_start(out=bb[ft % 2][0:fw, :],
                                    in_=bias_ap[w_col0 + ft * 128:w_col0 + ft * 128 + fw, :])

        for ci in range(NCH):
            a = n0 + ci * 512
            if x_pool is not None:
                xs = [ptiles[kc * pNCH + (a - pn0) // 512] for kc in range(KC)]
            else:
                xs = [xt[(ci % (nxb // KC)) * KC + kc] for kc in range(KC)]
                for kc in range(KC):
                    @p.step("sync", writes=[xs[kc]], dma=True)
                    def _(sy, kc=kc, a=a, xs=xs):
                        return sy.dma_start(out=xs[kc][:, :],
                                            in_=X_ap[x_row0 + kc * 128:x_row0 + (kc + 1) * 128,
                                                     x_col0 + a:x_col0 + a + 512])
            idx = ft * NCH + ci
            ps = pss[idx % nbank]

            @p.step("tensor", reads=wwr + xs, writes=[ps])
            def _(te, ft=ft, fw=fw, ps=ps, xs=xs):
                last = None
                for kc in range(KC):
                    wt_, off = wtile(ft % 2, kc)
                    last = te.matmul(ps[0:fw, :], wt_[:, off:off + fw],
                                     xs[kc][:, :], start=(kc == 0), stop=(kc == KC - 1))
                return last

            o = ob[idx % 2]
            if evac is not None:
                evac(p, ps, o, ft, ci, fw)
            else:
                rd = [ps] + ([bb[ft % 2]] if bias_ap is not None else [])

                @p.step("scalar", reads=rd, writes=[o])
                def _(sc, ps=ps, o=o, ft=ft, fw=fw):
                    kw = {}
                    fn = AF.Copy
                    if bias_ap is not None:
                        kw["bias"] = bb[ft % 2][0:fw, :]
                        fn = AF.Identity
                    if acc_scale is not None:
                        kw["scale"] = acc_scale
                        fn = AF.Identity
                    return sc.activation(out=o[0:fw, :], in_=ps[0:fw, :], func=fn, **kw)

            @p.step("gpsimd", reads=[o], dma=True)
            def _(g, o=o, ft=ft, fw=fw, a=a):
                return g.dma_start(out=O_ap[o_row0 + ft * 128:o_row0 + ft * 128 + fw, a:a + 512],
                                   in_=o[0:fw, :])


def sweep_proj_tm(p, X_ap, W_ap, O_ap, F_out, kdim=D, n0=0, n1=S, x_row0=0,
                  bias_ap=None, w_col0=0, out_dt=BF16, evac=None, consts=None,
                  o_col0=0, nbank=4, x_pool=None):
    KC = kdim // 128
    FCH = cdiv(F_out, 512)
    nt0, nt1 = n0 // 128, n1 // 128
    wt = p.ring("swX16", 32, [128, 512], BF16)[16:16 + KC]
    assert F_out <= 512, "sweep_proj_tm assumes F_out<=512"
    xt = p.ring(f"stX{KC}", 2, [128, KC * 128], BF16) if x_pool is None else None
    ob = p.ring("stO" + str(out_dt), 2, [128, 512], out_dt)
    pss = p.ps[:nbank]
    brow = p.sb("stBr", [1, 512], BF16) if bias_ap is not None else None

    for kc in range(KC):
        @p.step("gpsimd", writes=[wt[kc]], dma=True)
        def _(g, kc=kc):
            return g.dma_start(out=wt[kc][:, 0:F_out],
                               in_=W_ap[kc * 128:(kc + 1) * 128, w_col0:w_col0 + F_out])
    if bias_ap is not None:
        @p.step("gpsimd", writes=[brow], dma=True)
        def _(g):
            return g.dma_start(out=brow[0:1, 0:F_out],
                               in_=bias_ap[0:1, w_col0:w_col0 + F_out])

    if x_pool is not None:
        ptiles, pKC, pNCH, pn0 = x_pool
        assert pKC == KC and x_row0 == 0
    for nt in range(nt0, nt1):
        if x_pool is not None:
            pci = (nt * 128 - pn0) // 512
            off = (nt * 128 - pn0) % 512
            xsl = [ptiles[kc * pNCH + pci] for kc in range(KC)]

            def xop(kc):
                return xsl[kc][:, off:off + 128]
            xreads = xsl
        else:
            xv = xt[nt % 2]

            @p.step("sync", writes=[xv], dma=True)
            def _(sy, nt=nt, xv=xv):
                return [sy.dma_start(out=xv[:, kc * 128:(kc + 1) * 128],
                                     in_=X_ap[x_row0 + kc * 128:x_row0 + (kc + 1) * 128,
                                              nt * 128:(nt + 1) * 128])
                        for kc in range(KC)]

            def xop(kc, xv=xv):
                return xv[:, kc * 128:(kc + 1) * 128]
            xreads = [xv]
        idx = nt - nt0
        ps = pss[idx % nbank]
        rd = xreads + list(wt[:KC]) + ([brow] if bias_ap is not None else [])

        @p.step("tensor", reads=rd, writes=[ps])
        def _(te, xop=xop, ps=ps):
            last = None
            for kc in range(KC):
                last = te.matmul(ps[:, 0:F_out], xop(kc),
                                 wt[kc][:, 0:F_out], start=(kc == 0),
                                 stop=(bias_ap is None and kc == KC - 1))
            if bias_ap is not None:
                last = te.matmul(ps[:, 0:F_out], consts["ones_row_bf"][0:1, :],
                                 brow[0:1, 0:F_out], start=False, stop=True)
            return last
        o = ob[idx % 2]
        if evac is not None:
            evac(p, ps, o, nt)
        else:
            @p.step("scalar", reads=[ps], writes=[o])
            def _(sc, ps=ps, o=o):
                return sc.activation(out=o[:, 0:F_out], in_=ps[:, 0:F_out], func=AF.Copy)

        @p.step("gpsimd", reads=[o], dma=True)
        def _(g, o=o, nt=nt):
            return g.dma_start(out=O_ap[nt * 128:(nt + 1) * 128, o_col0:o_col0 + F_out],
                               in_=o[:, 0:F_out])


def sweep_ln_fm(p, X_ap, O_ap, consts, n0=0, n1=S, kind="ln", gamma_ap=None,
                scale=1.0, out_dt=BF16, also_tm_ap=None, out_pool_tag=None,
                out_dram=True):
    NCH = (n1 - n0) // 512
    xt = p.ring("swX8", 16, [128, 512], BF16)
    sq = p.ring("lnSq", 2, [128, 512], BF16)
    ob = sq
    assert out_dt == BF16
    ptiles = None
    if out_pool_tag is not None:
        assert out_dt == BF16
        ptiles = xpool(p)
        p._pool_tag = (out_pool_tag, DC, NCH, n0, 0)
    stats = p.ps[4]
    _lnrows = p.ring("lnrows", 5, [1, 512], F32)
    mu_r, var_r, std_r, r_r, mr_r = _lnrows
    bc_r, bc_mr = p.ps[5], p.ps[6]
    bcs = p.ring("f32buf", 2, [128, 1024], F32)[0]
    gam = None
    if gamma_ap is not None:
        gam = [p.sb("lnG", [128, 1], F32) for _ in range(DC)]
        for dc in range(DC):
            @p.step("sync", writes=[gam[dc]], dma=True)
            def _(sy, dc=dc):
                return sy.dma_start(out=gam[dc][:, :],
                                    in_=gamma_ap[dc * 128:(dc + 1) * 128, :])
    tmp = p.ring("lnT", 2, [128, 512], BF16)
    tout = p.ring("t2fXb", 2, [128, D], BF16) if also_tm_ap is not None else None

    for ci in range(NCH):
        a = n0 + ci * 512
        xs = [xt[(ci % 2) * DC + dc] for dc in range(DC)]
        for dc in range(DC):
            @p.step("gpsimd", writes=[xs[dc]], dma=True)
            def _(sy, dc=dc, a=a):
                return sy.dma_start(out=xs[dc][:, :], in_=X_ap[dc * 128:(dc + 1) * 128, a:a + 512])
        for dc in range(DC):
            sqb = sq[dc % 2]

            @p.step("vector", reads=[xs[dc]], writes=[sqb])
            def _(v, dc=dc, sqb=sqb):
                return v.tensor_mul(out=sqb[:, :], in0=xs[dc][:, :], in1=xs[dc][:, :])

            @p.step("tensor", reads=[xs[dc], sqb, consts["ones_col_bf"]],
                    writes=[stats])
            def _(te, dc=dc, sqb=sqb):
                te.matmul(stats[0:1, :], consts["ones_col_bf"][:, :], xs[dc][:, :],
                          start=(dc == 0), stop=(dc == DC - 1))
                return te.matmul(stats[64:65, :], consts["ones_col_bf"][:, :], sqb[:, :],
                                 start=(dc == 0), stop=(dc == DC - 1))
        if kind == "ln":
            @p.step("vector", reads=[stats], writes=[mu_r])
            def _(v):
                return v.tensor_scalar(out=mu_r[:, :], in0=stats[0:1, :], scalar1=1.0 / D,
                                       scalar2=None, op0=ALU.mult)

            @p.step("scalar", reads=[mu_r], writes=[var_r])
            def _(sc):
                return sc.activation(out=var_r[:, :], in_=mu_r[:, :], func=AF.Square)

            @p.step("vector", reads=[stats, var_r], writes=[var_r])
            def _(v):
                return v.scalar_tensor_tensor(out=var_r[:, :], in0=stats[64:65, :],
                                              scalar=1.0 / D, in1=var_r[:, :],
                                              op0=ALU.mult, op1=ALU.subtract)

            @p.step("vector", reads=[var_r], writes=[var_r])
            def _(v):
                return v.tensor_scalar(out=var_r[:, :], in0=var_r[:, :], scalar1=EPS_LN,
                                       scalar2=None, op0=ALU.add)

            @p.step("scalar", reads=[var_r], writes=[std_r])
            def _(sc):
                return sc.activation(out=std_r[:, :], in_=var_r[:, :], func=AF.Sqrt)

            @p.step("vector", reads=[std_r], writes=[r_r])
            def _(v):
                return v.reciprocal(out=r_r[:, :], in_=std_r[:, :])

            @p.step("vector", reads=[mu_r, r_r], writes=[mr_r])
            def _(v):
                return v.scalar_tensor_tensor(out=mr_r[:, :], in0=mu_r[:, :], scalar=-1.0,
                                              in1=r_r[:, :], op0=ALU.mult, op1=ALU.mult)
        else:
            @p.step("scalar", reads=[stats], writes=[std_r])
            def _(sc):
                return sc.activation(out=std_r[:, :], in_=stats[64:65, :], func=AF.Sqrt)

            @p.step("vector", reads=[std_r], writes=[std_r])
            def _(v):
                return v.tensor_scalar(out=std_r[:, :], in0=std_r[:, :], scalar1=1e-12,
                                       scalar2=None, op0=ALU.max)

            @p.step("vector", reads=[std_r], writes=[r_r])
            def _(v):
                return v.reciprocal(out=r_r[:, :], in_=std_r[:, :])

            @p.step("vector", reads=[r_r], writes=[r_r])
            def _(v):
                return v.tensor_scalar(out=r_r[:, :], in0=r_r[:, :], scalar1=float(scale),
                                       scalar2=None, op0=ALU.mult)

            @p.step("vector", writes=[mr_r])
            def _(v):
                return v.memset(mr_r[:, :], 0.0)

        @p.step("tensor", reads=[r_r, mr_r, consts["ones_row_f"]], writes=[bc_r, bc_mr])
        def _(te):
            te.matmul(bc_r[:, :], consts["ones_row_f"][0:1, :], r_r[:, :],
                      start=True, stop=True)
            return te.matmul(bc_mr[:, :], consts["ones_row_f"][0:1, :], mr_r[:, :],
                             start=True, stop=True)

        @p.step("vector", reads=[bc_r, bc_mr], writes=[bcs])
        def _(v):
            v.tensor_copy(out=bcs[:, 0:512], in_=bc_r[:, :])
            return v.tensor_copy(out=bcs[:, 512:1024], in_=bc_mr[:, :])

        for dc in range(DC):
            t = tmp[dc % 2]
            o = ptiles[dc * NCH + ci] if ptiles is not None else ob[dc % 2]

            @p.step("vector", reads=[xs[dc], bcs], writes=[t])
            def _(v, dc=dc, t=t):
                return v.tensor_mul(out=t[:, :], in0=xs[dc][:, :], in1=bcs[:, 0:512])

            if kind == "ln":
                @p.step("vector", reads=[t, bcs], writes=[o])
                def _(v, t=t, o=o):
                    return v.tensor_add(out=o[:, :], in0=t[:, :], in1=bcs[:, 512:1024])
            else:
                @p.step("vector", reads=[t] + ([gam[dc]] if gam else []), writes=[o])
                def _(v, t=t, o=o, dc=dc):
                    if gam:
                        return v.tensor_scalar(out=o[:, :], in0=t[:, :],
                                               scalar1=gam[dc][:, :], scalar2=None,
                                               op0=ALU.mult)
                    return v.tensor_copy(out=o[:, :], in_=t[:, :])

            if out_dram:
                @p.step("gpsimd", reads=[o], dma=True)
                def _(g, o=o, dc=dc, a=a):
                    return g.dma_start(out=O_ap[dc * 128:(dc + 1) * 128, a:a + 512], in_=o[:, :])

            if also_tm_ap is not None:
                tp = p.psT
                to = tout[dc % 2]

                @p.step("tensor", reads=[o, consts["ident_bf"]], writes=[tp])
                def _(te, o=o, tp=tp):
                    last = None
                    for q in range(4):
                        last = te.transpose(tp[:, q * 128 + 0:q * 128 + 128],
                                            o[:, q * 128:(q + 1) * 128],
                                            consts["ident_bf"][:, :])
                    return last

                @p.step("vector", reads=[tp], writes=[to])
                def _(v, tp=tp, to=to):
                    return v.tensor_copy(out=to[:, :], in_=tp[:, :])

                @p.step("gpsimd", reads=[to], dma=True)
                def _(g, to=to, dc=dc, a=a):
                    return [g.dma_start(
                        out=also_tm_ap[a + q * 128:a + (q + 1) * 128,
                                       dc * 128:(dc + 1) * 128],
                        in_=to[:, q * 128:(q + 1) * 128]) for q in range(4)]


def sweep_add_fm(p, A_ap, B_ap, O_ap, n0=0, n1=S, scale_b=None, C_ap=None,
                 scale_c=None, out_dt=BF16, rows=D):
    RT = rows // 128
    RN = 2
    at = p.ring("adA", RN, [128, 512], BF16)
    bt = p.ring("adB", RN, [128, 512], BF16)
    ct = p.ring("adC", RN, [128, 512], BF16) if C_ap is not None else None
    tt = p.ring("adT", 4, [128, 512], F32)[:RN]
    ob = p.ring("adO", RN, [128, 512], out_dt)
    NCH = (n1 - n0) // 512
    for rt in range(RT):
        for ci in range(NCH):
            a = n0 + ci * 512
            idx = rt * NCH + ci
            av, bv = at[idx % RN], bt[idx % RN]
            o = ob[idx % RN]

            @p.step("gpsimd", writes=[av], dma=True)
            def _(sy, rt=rt, a=a, av=av):
                return sy.dma_start(out=av[:, :], in_=A_ap[rt * 128:(rt + 1) * 128, a:a + 512])

            @p.step("gpsimd", writes=[bv], dma=True)
            def _(sy, rt=rt, a=a, bv=bv):
                return sy.dma_start(out=bv[:, :], in_=B_ap[rt * 128:(rt + 1) * 128, a:a + 512])
            if C_ap is not None:
                cv = ct[idx % RN]

                @p.step("gpsimd", writes=[cv], dma=True)
                def _(sy, rt=rt, a=a, cv=cv):
                    return sy.dma_start(out=cv[:, :], in_=C_ap[rt * 128:(rt + 1) * 128, a:a + 512])
            t = tt[idx % RN]

            @p.step("vector", reads=[av, bv], writes=[t])
            def _(v, av=av, bv=bv, t=t):
                if scale_b is None:
                    return v.tensor_add(out=t[:, :], in0=av[:, :], in1=bv[:, :])
                return v.scalar_tensor_tensor(out=t[:, :], in0=bv[:, :], scalar=float(scale_b),
                                              in1=av[:, :], op0=ALU.mult, op1=ALU.add)
            if C_ap is not None:
                cv = ct[idx % RN]

                @p.step("vector", reads=[t, cv], writes=[o])
                def _(v, t=t, cv=cv, o=o):
                    return v.scalar_tensor_tensor(out=o[:, :], in0=cv[:, :], scalar=float(scale_c),
                                                  in1=t[:, :], op0=ALU.mult, op1=ALU.add)
            else:
                @p.step("vector", reads=[t], writes=[o])
                def _(v, t=t, o=o):
                    return v.tensor_copy(out=o[:, :], in_=t[:, :])

            @p.step("gpsimd", reads=[o], dma=True)
            def _(g, o=o, rt=rt, a=a):
                return g.dma_start(out=O_ap[rt * 128:(rt + 1) * 128, a:a + 512], in_=o[:, :])


def collective_pair(p, in_ap, out_ap, op=ALU.add, groups=None):
    if groups is None:
        groups = [[0, 1], [2, 3], [4, 5], [6, 7]]
    final = dict(p.ecnt)

    def body(g):
        for e in ENGINES:
            if final[e] > 0:
                g.wait_ge(p.esem[e], final[e])
        ins = g.collective_compute(
            "AllReduce", op,
            replica_groups=groups,
            ins=[in_ap.tensor.ap().opt()],
            outs=[out_ap.tensor.ap().opt()],
        )
        ins.then_inc(p.esem["cc"])
    p.block.gpsimd(body)
    p.ecnt["cc"] += 1
    fence = p.ring("ccfence", 1, [128, 128], BF16)[0]
    rr = min(128, out_ap.shape[0])
    cc2 = min(128, out_ap.shape[1])

    @p.step("gpsimd", writes=[fence], dma=True)
    def _(g):
        g.wait_ge(p.esem["cc"], p.ecnt["cc"])
        return g.dma_start(out=fence[0:rr, 0:cc2], in_=out_ap[0:rr, 0:cc2])
    p.barrier()


def wait_cc(p, ename, n=None):
    pass


"""Launch builders. Each build_X() returns (nc, in_names) and a matching
host-side input-prep function make_inputs_X(inputs, consts, core) -> dict.

Core roles: core c -> batch b=c//2, half j=c%2; heads 8j..8j+8, experts 4j..4j+4.
"""

HD = 64
NH_LOC = 8
CHUNK = 128



def phase_tm2fm(p, X_tm_ap, O_fm_ap, consts, in_dt=F32, rows=S, cols=D):
    RT = rows // 128
    CC = cols // 128
    assert cols == D
    xf = p.ring("f32buf", 2, [128, cols], F32)
    x8 = p.ring("swX8", 16, [128, 512], BF16)
    for t in range(RT):
        a = xf[t % 2]
        xb2 = [x8[(t % 2) * 2], x8[(t % 2) * 2 + 1]]
        ob2 = [x8[4 + (t % 2) * 2], x8[4 + (t % 2) * 2 + 1]]

        @p.step("sync", writes=[a], dma=True)
        def _(sy, t=t, a=a):
            return sy.dma_start(out=a[:, :], in_=X_tm_ap[t * 128:(t + 1) * 128, :])

        @p.step("vector", reads=[a], writes=xb2)
        def _(v, a=a, xb2=xb2):
            v.tensor_copy(out=xb2[0][:, :], in_=a[:, 0:512])
            return v.tensor_copy(out=xb2[1][:, :], in_=a[:, 512:1024])

        tpv = p.psT

        @p.step("tensor", reads=xb2 + [consts["ident_bf"]], writes=[tpv])
        def _(te, xb2=xb2, tpv=tpv):
            last = None
            for dc in range(CC):
                last = te.transpose(tpv[:, dc * 128:(dc + 1) * 128],
                                    xb2[dc // 4][:, (dc % 4) * 128:(dc % 4) * 128 + 128],
                                    consts["ident_bf"][:, :])
            return last

        @p.step("vector", reads=[tpv], writes=ob2)
        def _(v, ob2=ob2, tpv=tpv):
            v.tensor_copy(out=ob2[0][:, :], in_=tpv[:, 0:512])
            return v.tensor_copy(out=ob2[1][:, :], in_=tpv[:, 512:1024])

        @p.step("gpsimd", reads=ob2, dma=True)
        def _(g, ob2=ob2, t=t):
            return [g.dma_start(out=O_fm_ap[dc * 128:(dc + 1) * 128, t * 128:(t + 1) * 128],
                                in_=ob2[dc // 4][:, (dc % 4) * 128:(dc % 4) * 128 + 128])
                    for dc in range(CC)]


def evac_elu1(p, ps, o, ft, ci, fw=128):
    adt = p.ring("adT", 4, [128, 512], F32)
    s1 = adt[ci % 2]
    s2 = adt[2 + ci % 2]

    @p.step("vector", reads=[ps], writes=[s1])
    def _(v):
        return v.tensor_scalar(out=s1[0:fw, :], in0=ps[0:fw, :], scalar1=0.0,
                               scalar2=None, op0=ALU.min)

    @p.step("scalar", reads=[s1], writes=[s2])
    def _(sc):
        return sc.activation(out=s2[0:fw, :], in_=s1[0:fw, :], func=AF.Exp)

    @p.step("vector", reads=[ps], writes=[s1])
    def _(v):
        return v.tensor_relu(out=s1[0:fw, :], in_=ps[0:fw, :])

    @p.step("vector", reads=[s1, s2], writes=[o])
    def _(v):
        return v.tensor_add(out=o[0:fw, :], in0=s1[0:fw, :], in1=s2[0:fw, :])



def _io(p, pre, pfx, standalone, debug):
    pre = pre or {}

    def IN(n, sh, dt=F32):
        return pre[n] if n in pre else p.inp(pfx + n, sh, dt)

    def OUT(n, sh, dt=BF16):
        if n in pre:
            return pre[n]
        return p.outp(pfx + n, sh, dt) if standalone else p.dram(pfx + n, sh, dt)
    return IN, OUT



def build_A(debug=False, p=None, pre=None, pfx=""):
    standalone = p is None
    if standalone:
        p = P("A")
    IN, OUT = _io(p, pre, pfx, standalone, debug)
    x_in = IN("x_in", [S, D], F32)
    wq = IN("wq", [D, 512], F32)
    wk = IN("wk", [D, 512], F32)
    wv = IN("wv", [D, 512], F32)
    wo = IN("wo", [512, D], F32)
    consts = {}
    cs = load_consts(p, ("ident", "ones_col", "ones_row", "tri_kq"))
    consts.update(cs)

    mk = p.outp if debug else p.dram
    xfm = mk("xfm", [D, S])
    hs = mk("hs", [D, S])
    qf = mk("qf", [512, S])
    kf = mk("kf", [512, S])
    ktm = mk("ktm", [S, 512])
    vtm = mk("vtm", [S, 512])
    of = mk("of", [512, S])
    cc_in = p.dram("cc_in", [D, S])
    cc_out = p.dram("cc_out", [D, S])
    x1_out = OUT("x1", [D, S])
    hs1_out = OUT("hs1", [D, S])

    phase_tm2fm(p, x_in, xfm, consts)
    p.barrier()
    sweep_ln_fm(p, xfm, hs, consts, kind="ln", out_pool_tag="hs", out_dram=False)
    p.barrier()
    hs_pool = pool_use(p, "hs")
    sweep_proj_fm(p, wq, hs, qf, 512, evac=evac_elu1, x_pool=hs_pool)
    p.barrier()
    sweep_proj_fm(p, wk, hs, kf, 512, evac=evac_elu1, x_pool=hs_pool)
    p.barrier()

    def evac_elu1_tm(p_, ps, o, nt):
        evac_elu1(p_, ps, o, 0, nt, fw=128)
    sweep_proj_tm(p, hs, wk, ktm, 512, evac=evac_elu1_tm, consts=consts,
                  x_pool=hs_pool)
    p.barrier()
    sweep_proj_tm(p, hs, wv, vtm, 512, consts=consts, x_pool=hs_pool)
    p.barrier()

    NCH = S // CHUNK
    _sw8 = p.ring("swX8", 16, [128, 512], BF16)
    kfc = _sw8[0:2]
    qfc = _sw8[2:4]
    ktc = _sw8[4:6]
    vtc = _sw8[6:8]
    psA = [p.ps[4], p.ps[5]]
    psO = [p.ps[0], p.ps[1]]
    psB = [p.ps[2], p.ps[2]]
    psS = [p.ps[3], p.ps[3]]
    amask = p.ring("laM", 2, [128, 128], BF16)
    denr = p.ring("laD", 2, [1, 128], F32)
    bcs = p.ring("laBc", 2, [64, 128], F32)
    otile = p.ring("laOt", 2, [128, 128], BF16)
    _adt = p.ring("adT", 4, [128, 512], F32)
    Sf = [_adt[2 + h // 4][0:64, 65 * (h % 4):65 * (h % 4) + 65] for h in range(NH_LOC)]
    Sb = [p.sb(f"laSb{h}", [128, 65], BF16) for h in range(NH_LOC)]

    for h in range(NH_LOC):
        @p.step("vector", writes=[Sf[h]])
        def _(v, h=h):
            return v.memset(Sf[h][:, :], 0.0)

        @p.step("vector", reads=[Sf[h]], writes=[Sb[h]])
        def _(v, h=h):
            v.tensor_copy(out=Sb[h][0:64, :], in_=Sf[h][:, :])
            return v.tensor_copy(out=Sb[h][64:128, :], in_=Sf[h][:, :])

    for c in range(NCH):
        a = c * CHUNK
        kfv, qfv, ktv, vtv = kfc[c % 2], qfc[c % 2], ktc[c % 2], vtc[c % 2]

        @p.step("sync", writes=[kfv], dma=True)
        def _(sy, a=a, kfv=kfv):
            return [sy.dma_start(out=kfv[:, r * 128:(r + 1) * 128],
                                 in_=kf[r * 128:(r + 1) * 128, a:a + 128])
                    for r in range(4)]

        @p.step("sync", writes=[qfv], dma=True)
        def _(sy, a=a, qfv=qfv):
            return [sy.dma_start(out=qfv[:, r * 128:(r + 1) * 128],
                                 in_=qf[r * 128:(r + 1) * 128, a:a + 128])
                    for r in range(4)]

        @p.step("sync", writes=[ktv], dma=True)
        def _(sy, a=a, ktv=ktv):
            return sy.dma_start(out=ktv[:, :], in_=ktm[a:a + 128, :])

        @p.step("sync", writes=[vtv], dma=True)
        def _(sy, a=a, vtv=vtv):
            return sy.dma_start(out=vtv[:, :], in_=vtm[a:a + 128, :])

        for h in range(NH_LOC):
            r, base = h // 2, 64 * (h % 2)
            kslice = kfv[:, r * 128:(r + 1) * 128]
            idx = (c * NH_LOC + h)
            pa = psA[idx % 2]
            po = psO[idx % 2]
            pb = psB[idx % 2]
            pS = psS[idx % 2]
            am = amask[idx % 2]
            dr = denr[idx % 2]
            bc = bcs[idx % 2]
            ot = otile[idx % 2]

            @p.step("tensor", reads=[kfv, qfv], writes=[pa])
            def _(te, r=r, base=base, kfv=kfv, qfv=qfv, pa=pa):
                return te.matmul(pa[:, 0:128],
                                 kfv[base:base + 64, r * 128:(r + 1) * 128],
                                 qfv[base:base + 64, r * 128:(r + 1) * 128],
                                 start=True, stop=True)

            @p.step("vector", reads=[pa, consts["tri_kq_bf"]], writes=[am])
            def _(v, pa=pa, am=am):
                return v.tensor_mul(out=am[:, :], in0=pa[:, 0:128], in1=consts["tri_kq_bf"][:, :])

            @p.step("tensor", reads=[am, vtv, Sb[h], consts["ones_col_bf"], qfv], writes=[po])
            def _(te, h=h, r=r, base=base, am=am, vtv=vtv, qfv=qfv, po=po):
                hl = 64 * h
                te.matmul(po[0:64, 0:128], vtv[:, hl:hl + 64], am[:, :], start=True, stop=False)
                te.matmul(po[0:64, 0:128], Sb[h][base:base + 64, 0:64],
                          qfv[base:base + 64, r * 128:(r + 1) * 128], start=False, stop=True)
                te.matmul(po[64:65, 0:128], consts["ones_col_bf"][:, :], am[:, :],
                          start=True, stop=False)
                return te.matmul(po[64:65, 0:128], Sb[h][base:base + 64, 64:65],
                                 qfv[base:base + 64, r * 128:(r + 1) * 128],
                                 start=False, stop=True)

            @p.step("vector", reads=[po], writes=[dr])
            def _(v, po=po, dr=dr):
                return v.tensor_scalar(out=dr[:, :], in0=po[64:65, 0:128], scalar1=1e-6,
                                       scalar2=None, op0=ALU.add)

            @p.step("vector", reads=[dr], writes=[dr])
            def _(v, dr=dr):
                return v.reciprocal(out=dr[:, :], in_=dr[:, :])

            @p.step("tensor", reads=[dr, consts["ones_row_f"]], writes=[pb])
            def _(te, dr=dr, pb=pb):
                return te.matmul(pb[0:64, 0:128], consts["ones_row_f"][0:1, 0:64], dr[:, :],
                                 start=True, stop=True)

            @p.step("vector", reads=[pb], writes=[bc])
            def _(v, pb=pb, bc=bc):
                return v.tensor_copy(out=bc[:, :], in_=pb[0:64, 0:128])

            @p.step("vector", reads=[po, bc], writes=[ot])
            def _(v, po=po, bc=bc, ot=ot):
                return v.tensor_mul(out=ot[0:64, :], in0=po[0:64, 0:128], in1=bc[:, :])

            @p.step("gpsimd", reads=[ot], dma=True)
            def _(g, ot=ot, h=h, a=a):
                return g.dma_start(out=of[h * 64:(h + 1) * 64, a:a + 128], in_=ot[0:64, :])

            @p.step("tensor", reads=[ktv, vtv, consts["ones_col_bf"]], writes=[pS])
            def _(te, h=h, ktv=ktv, vtv=vtv, pS=pS):
                hl = 64 * h
                te.matmul(pS[0:64, 0:64], ktv[:, hl:hl + 64], vtv[:, hl:hl + 64],
                          start=True, stop=True)
                return te.matmul(pS[0:64, 64:65], ktv[:, hl:hl + 64],
                                 consts["ones_col_bf"][:, :], start=True, stop=True)

            @p.step("vector", reads=[pS, Sf[h]], writes=[Sf[h]])
            def _(v, h=h, pS=pS):
                return v.tensor_add(out=Sf[h][:, :], in0=Sf[h][:, :], in1=pS[0:64, 0:65])

            @p.step("vector", reads=[Sf[h]], writes=[Sb[h]])
            def _(v, h=h):
                v.tensor_copy(out=Sb[h][0:64, :], in_=Sf[h][:, :])
                return v.tensor_copy(out=Sb[h][64:128, :], in_=Sf[h][:, :])

    p.barrier()
    sweep_proj_fm(p, wo, of, cc_in, D, kdim=512)
    p.barrier()
    if debug:
        cc_dbg = p.outp("cc_dbg", [D, S])

        @p.step("gpsimd", dma=True)
        def _(g):
            return g.dma_start(out=cc_dbg[:, :], in_=cc_in[:, :])
        p.barrier()
    collective_pair(p, cc_in, cc_out)
    wait_cc(p, "sync")
    wait_cc(p, "gpsimd")
    sweep_add_fm(p, xfm, cc_out, x1_out)
    p.barrier()
    sweep_ln_fm(p, x1_out, hs1_out, consts, kind="ln", out_pool_tag="hs1")
    return p.done() if standalone else p


def make_inputs_A(inp, consts, core):
    b, j = core // 2, core % 2
    x = np.ascontiguousarray(inp["L0_attn_x"][b])
    d = {
        "x_in": x,
        "wq": np.ascontiguousarray(inp["L0_attn_wq"][:, 512 * j:512 * (j + 1)]),
        "wk": np.ascontiguousarray(inp["L0_attn_wk"][:, 512 * j:512 * (j + 1)]),
        "wv": np.ascontiguousarray(inp["L0_attn_wv"][:, 512 * j:512 * (j + 1)]),
        "wo": np.ascontiguousarray(inp["L0_attn_wo"][512 * j:512 * (j + 1), :]),
        "c_ident": consts["ident"], "c_ones_col": consts["ones_col"],
        "c_ones_row": consts["ones_row"], "c_tri_kq": consts["tri_kq"],
    }
    return d


C_STAB0 = 12.0
RS32 = 0.17677669529663687


def build_B(debug=False, p=None, pre=None, pfx=""):
    standalone = p is None
    if standalone:
        p = P("B")
    IN, OUT = _io(p, pre, pfx, standalone, debug)
    hs1 = IN("hs1", [D, S], BF16)
    qw = IN("qw", [D, 512], F32)
    qb = IN("qb", [512, 1], F32)
    kw = IN("kw", [D, 512], F32)
    kb = IN("kb", [512, 1], F32)
    vw = IN("vw", [D, 264], F32)
    vb = IN("vb", [1, 264], F32)
    ow = IN("ow", [256, D], F32)
    ob_ = IN("ob", [D, 1], F32)
    rw = IN("rw", [D, 8], F32)
    rb = IN("rb", [1, 8], F32)
    esel = IN("esel", [8, 8], F32)
    cs = load_consts(p, ("ident", "ones_col", "ones_row"))

    mk = p.outp if debug else p.dram
    qf = mk("qf", [512, S])
    kf = mk("kf", [512, S])
    vtm = mk("vtm", [S, 264])
    aof = mk("aof", [256, S])
    cc_in = p.dram("cc_in", [D, S])
    cc_out = p.dram("cc_out", [D, S])
    mla_out = OUT("mla", [D, S])
    comb_out = OUT("comb", [8, S])

    hs1_pool = pool_use(p, "hs1")
    sweep_proj_fm(p, qw, hs1, qf, 512, bias_ap=qb, x_pool=hs1_pool)
    p.barrier()
    sweep_proj_fm(p, kw, hs1, kf, 512, bias_ap=kb, x_pool=hs1_pool)
    p.barrier()
    sweep_proj_tm(p, hs1, vw, vtm, 264, bias_ap=vb, consts=cs, x_pool=hs1_pool)
    p.barrier()

    nbias = p.sb("nbias", [128, 1], F32)

    @p.step("vector", writes=[nbias])
    def _(v):
        return v.memset(nbias[:, :], -C_STAB0)

    qtiles = p.ring("swX8", 16, [128, 512], BF16)[:8]
    vt = p.ring("fbVt", 2, [128, 264], BF16)
    es = p.ring("swX8", 16, [128, 512], BF16)[12:16]
    drow = p.ring("fbD", 2, [1, 512], F32)
    bcsb = [t[0:32, :] for t in p.ring("adT", 4, [128, 512], F32)[:2]]
    otile = p.ring("swX8", 16, [128, 512], BF16)[10:12]
    psS = [p.ps[0], p.ps[1]]
    psB = p.ps[6]

    kf_tiles, _, _, _ = pool_load(p, "Bkf", kf, 4, 8)

    def kslice(r, kt):
        t = kf_tiles[r * 8 + kt // 4]
        off = (kt % 4) * 128
        return t, t[:, off:off + 128]

    for qc in range(8):
        a = qc * 512
        qs = [qtiles[(qc % 2) * 4 + r] for r in range(4)]
        for r in range(4):
            @p.step("sync", writes=[qs[r]], dma=True)
            def _(sy, r=r, a=a, qs=qs):
                return sy.dma_start(out=qs[r][:, :], in_=qf[r * 128:(r + 1) * 128, a:a + 512])
        for hp in range(2):
            for kt in range(NT):
                vv = vt[kt % 2]

                @p.step("sync", writes=[vv], dma=True)
                def _(sy, kt=kt, vv=vv):
                    return sy.dma_start(out=vv[:, :], in_=vtm[kt * 128:(kt + 1) * 128, :])

                for pair in range(2):
                    for g in (2 * pair, 2 * pair + 1):
                        h = hp * 4 + g
                        rq, bq = h // 2, 64 * (h % 2)
                        ps_ = psS[g % 2]
                        ktile, kap = kslice(rq, kt)

                        @p.step("tensor", reads=[ktile, qs[rq]], writes=[ps_])
                        def _(te, rq=rq, bq=bq, ps_=ps_, kap=kap, qs=qs):
                            return te.matmul(ps_[:, :], kap[bq:bq + 32, :],
                                             qs[rq][bq:bq + 32, :], start=True, stop=True)

                    @p.step("scalar", reads=[psS[0], psS[1], nbias],
                            writes=[es[2 * pair], es[2 * pair + 1]])
                    def _(sc, pair=pair):
                        sc.activation(out=es[2 * pair][:, :], in_=psS[0][:, :],
                                      func=AF.Exp, bias=nbias[:, :], scale=RS32)
                        return sc.activation(out=es[2 * pair + 1][:, :], in_=psS[1][:, :],
                                             func=AF.Exp, bias=nbias[:, :], scale=RS32)

                    for g in (2 * pair, 2 * pair + 1):
                        h = hp * 4 + g
                        bank = p.ps[2 + g]
                        ev = es[g]

                        @p.step("tensor", reads=[ev, vv], writes=[bank])
                        def _(te, ev=ev, vv=vv, h=h, kt=kt, bank=bank):
                            return te.matmul(bank[0:33, :], vv[:, 33 * h:33 * h + 33],
                                             ev[:, :], start=(kt == 0), stop=(kt == NT - 1))

            for g in range(4):
                h = hp * 4 + g
                bank = p.ps[2 + g]
                dr = drow[g % 2]
                bc = bcsb[g % 2]
                ot = otile[g % 2]

                @p.step("vector", reads=[bank], writes=[dr])
                def _(v, dr=dr, bank=bank):
                    return v.reciprocal(out=dr[:, :], in_=bank[32:33, :])

                @p.step("tensor", reads=[dr, cs["ones_row_f"]], writes=[psB])
                def _(te, dr=dr):
                    return te.matmul(psB[0:32, :], cs["ones_row_f"][0:1, 0:32], dr[:, :],
                                     start=True, stop=True)

                @p.step("vector", reads=[psB], writes=[bc])
                def _(v, bc=bc):
                    return v.tensor_copy(out=bc[:, :], in_=psB[0:32, :])

                @p.step("vector", reads=[bank, bc], writes=[ot])
                def _(v, bc=bc, ot=ot, bank=bank):
                    return v.tensor_mul(out=ot[0:32, :], in0=bank[0:32, :], in1=bc[:, :])

                @p.step("gpsimd", reads=[ot], dma=True)
                def _(g_, ot=ot, h=h, a=a):
                    return g_.dma_start(out=aof[32 * h:32 * h + 32, a:a + 512], in_=ot[0:32, :])

    p.barrier()
    sweep_proj_fm(p, ow, aof, cc_in, D, kdim=256, bias_ap=ob_)
    p.barrier()
    collective_pair(p, cc_in, cc_out)
    wait_cc(p, "sync")
    wait_cc(p, "gpsimd")
    @p.step("gpsimd", dma=True)
    def _(g):
        return g.dma_start(out=mla_out[:, :], in_=cc_out[:, :])
    p.barrier()

    rwt = [p.sb(f"rwt{kc}", [128, 8], BF16) for kc in range(DC)]
    rbt = p.sb("rbt", [1, 8], BF16)
    for kc in range(DC):
        @p.step("gpsimd", writes=[rwt[kc]], dma=True)
        def _(g, kc=kc):
            return g.dma_start(out=rwt[kc][:, :], in_=rw[kc * 128:(kc + 1) * 128, :])

    @p.step("gpsimd", writes=[rbt], dma=True)
    def _(g):
        return g.dma_start(out=rbt[:, :], in_=rb[:, :])

    mlt = p.ring("t2fXb", 2, [128, D], BF16)
    lg = p.ring("rtL", 2, [128, 8], F32)
    m1 = p.ring("rtm1", 2, [128, 1], F32)
    m2 = p.ring("rtm2", 2, [128, 1], F32)
    den = p.ring("rtden", 2, [128, 1], F32)
    prb = p.ring("rtP", 2, [128, 8], F32)
    sel = p.ring("rtS", 2, [128, 8], F32)
    cmb = p.ring("rtC", 2, [128, 8], BF16)
    cps = p.psT
    cfm = p.ring("rtCf", 2, [8, 128], BF16)
    cfl = p.ring("rtCl", 2, [8, 128], BF16)
    selb = p.sb("selb", [8, 8], BF16)

    @p.step("gpsimd", writes=[selb], dma=True)
    def _(g):
        return g.dma_start(out=selb[:, :], in_=esel[:, :])

    for nt in range(NT):
        ml = mlt[nt % 2]

        @p.step("sync", writes=[ml], dma=True)
        def _(sy, nt=nt, ml=ml):
            return [sy.dma_start(out=ml[:, kc * 128:(kc + 1) * 128],
                                 in_=mla_out[kc * 128:(kc + 1) * 128, nt * 128:(nt + 1) * 128])
                    for kc in range(DC)]
        ps_ = p.ps[4]

        @p.step("tensor", reads=[ml, rbt] + rwt, writes=[ps_])
        def _(te, ml=ml, ps_=ps_):
            for kc in range(DC):
                te.matmul(ps_[:, 0:8], ml[:, kc * 128:(kc + 1) * 128], rwt[kc][:, :],
                          start=(kc == 0), stop=False)
            return te.matmul(ps_[:, 0:8], cs["ones_row_bf"][0:1, :], rbt[0:1, :],
                             start=False, stop=True)
        l, mm1, mm2, dd, pb, se, cb = (lg[nt % 2], m1[nt % 2], m2[nt % 2],
                                       den[nt % 2], prb[nt % 2], sel[nt % 2], cmb[nt % 2])

        @p.step("vector", reads=[ps_], writes=[l])
        def _(v, l=l, ps_=ps_):
            return v.tensor_copy(out=l[:, :], in_=ps_[:, 0:8])

        @p.step("vector", reads=[l], writes=[mm1])
        def _(v, l=l, mm1=mm1):
            return v.reduce_max(out=mm1[:, :], in_=l[:, :], axis=AX.X)

        @p.step("vector", reads=[mm1], writes=[mm1])
        def _(v, mm1=mm1):
            return v.tensor_scalar(out=mm1[:, :], in0=mm1[:, :], scalar1=-1.0,
                                   scalar2=None, op0=ALU.mult)

        @p.step("scalar", reads=[l, mm1], writes=[pb, dd])
        def _(sc, l=l, mm1=mm1, pb=pb, dd=dd):
            return sc.activation(out=pb[:, :], in_=l[:, :], func=AF.Exp,
                                 bias=mm1[:, :], accum_out=dd[:, :])

        @p.step("vector", reads=[dd], writes=[dd])
        def _(v, dd=dd):
            return v.reciprocal(out=dd[:, :], in_=dd[:, :])

        @p.step("vector", reads=[pb, dd], writes=[pb])
        def _(v, pb=pb, dd=dd):
            return v.tensor_scalar(out=pb[:, :], in0=pb[:, :], scalar1=dd[:, :],
                                   scalar2=None, op0=ALU.mult)
        @p.step("vector", reads=[pb], writes=[mm1])
        def _(v, pb=pb, mm1=mm1):
            return v.reduce_max(out=mm1[:, :], in_=pb[:, :], axis=AX.X)

        @p.step("vector", reads=[pb, mm1], writes=[se])
        def _(v, pb=pb, mm1=mm1, se=se):
            return v.tensor_scalar(out=se[:, :], in0=pb[:, :], scalar1=mm1[:, :],
                                   scalar2=None, op0=ALU.is_ge)

        @p.step("vector", reads=[se, pb], writes=[se])
        def _(v, se=se, pb=pb):
            return v.scalar_tensor_tensor(out=se[:, :], in0=se[:, :], scalar=-1e30,
                                          in1=pb[:, :], op0=ALU.mult, op1=ALU.add)

        @p.step("vector", reads=[se], writes=[mm2])
        def _(v, se=se, mm2=mm2):
            return v.reduce_max(out=mm2[:, :], in_=se[:, :], axis=AX.X)

        @p.step("vector", reads=[pb, mm2], writes=[se])
        def _(v, pb=pb, mm2=mm2, se=se):
            return v.tensor_scalar(out=se[:, :], in0=pb[:, :], scalar1=mm2[:, :],
                                   scalar2=None, op0=ALU.is_ge)

        @p.step("vector", reads=[se, pb], writes=[se])
        def _(v, se=se, pb=pb):
            return v.tensor_mul(out=se[:, :], in0=se[:, :], in1=pb[:, :])

        @p.step("vector", reads=[se], writes=[dd])
        def _(v, se=se, dd=dd):
            return v.reduce_sum(out=dd[:, :], in_=se[:, :], axis=AX.X)

        @p.step("vector", reads=[dd], writes=[dd])
        def _(v, dd=dd):
            return v.tensor_scalar(out=dd[:, :], in0=dd[:, :], scalar1=1e-8,
                                   scalar2=None, op0=ALU.add)

        @p.step("vector", reads=[dd], writes=[dd])
        def _(v, dd=dd):
            return v.reciprocal(out=dd[:, :], in_=dd[:, :])

        @p.step("vector", reads=[se, dd], writes=[cb])
        def _(v, se=se, dd=dd, cb=cb):
            return v.tensor_scalar(out=cb[:, :], in0=se[:, :], scalar1=dd[:, :],
                                   scalar2=None, op0=ALU.mult)

        @p.step("tensor", reads=[cb, cs["ident_bf"]], writes=[cps])
        def _(te, cb=cb):
            return te.transpose(cps[0:8, 0:128], cb[:, :], cs["ident_bf"][:, :])

        cf = cfm[nt % 2]

        @p.step("vector", reads=[cps], writes=[cf])
        def _(v, cf=cf):
            return v.tensor_copy(out=cf[:, :], in_=cps[0:8, 0:128])

        @p.step("tensor", reads=[selb, cf], writes=[p.ps[5]])
        def _(te, cf=cf):
            return te.matmul(p.ps[5][0:8, 0:128], selb[:, :], cf[:, :],
                             start=True, stop=True)
        cl = cfl[nt % 2]

        @p.step("vector", reads=[p.ps[5]], writes=[cl])
        def _(v, cl=cl):
            return v.tensor_copy(out=cl[:, :], in_=p.ps[5][0:8, 0:128])

        @p.step("gpsimd", reads=[cl], dma=True)
        def _(g, cl=cl, nt=nt):
            return g.dma_start(out=comb_out[:, nt * 128:(nt + 1) * 128], in_=cl[:, :])

    return p.done() if standalone else p


def make_inputs_B(inp, consts, core, hs1, x1=None):
    b, j = core // 2, core % 2
    qw = np.zeros((D, 512), np.float32)
    qb = np.zeros((512, 1), np.float32)
    for h in range(8):
        hg = 8 * j + h
        qw[:, 64 * h:64 * h + 32] = inp["L0_ffn_q_w"][:, 64 * hg:64 * hg + 32]
        qb[64 * h:64 * h + 32, 0] = inp["L0_ffn_q_b"][64 * hg:64 * hg + 32]
    kw = np.zeros((D, 512), np.float32)
    kb = np.zeros((512, 1), np.float32)
    for h in range(8):
        hg = 8 * j + h
        kw[:, 64 * h:64 * h + 32] = inp["L0_ffn_k_w"][:, 32 * hg:32 * hg + 32]
        kb[64 * h:64 * h + 32, 0] = inp["L0_ffn_k_b"][32 * hg:32 * hg + 32]
    vw = np.zeros((D, 264), np.float32)
    vb = np.zeros((1, 264), np.float32)
    for h in range(8):
        hg = 8 * j + h
        vw[:, 33 * h:33 * h + 32] = inp["L0_ffn_v_w"][:, 32 * hg:32 * hg + 32]
        vb[0, 33 * h:33 * h + 32] = inp["L0_ffn_v_b"][32 * hg:32 * hg + 32]
        vb[0, 33 * h + 32] = 1.0
    ow = np.ascontiguousarray(inp["L0_ffn_o_w"][256 * j:256 * (j + 1), :])
    ob = inp["L0_ffn_o_b"].reshape(D, 1) if j == 0 else np.zeros((D, 1), np.float32)
    d = {
        "hs1": hs1, "qw": qw, "qb": qb, "kw": kw, "kb": kb, "vw": vw, "vb": vb,
        "ow": ow, "ob": np.ascontiguousarray(ob),
        "rw": inp["L0_ffn_router_w"].astype(np.float32),
        "rb": inp["L0_ffn_router_b"].reshape(1, 8),
        "esel": np.eye(8, dtype=np.float32)[:, list(range(4 * j, 4 * j + 4)) + list(range(0, 4 * j)) + list(range(4 * j + 4, 8))][:, :8][:, :].T.copy() if False else np.ascontiguousarray(np.roll(np.eye(8, dtype=np.float32), -4 * j, axis=0).T),
        "c_ident": consts["ident"], "c_ones_col": consts["ones_col"],
        "c_ones_row": consts["ones_row"],
    }
    return d



def build_B2(debug=False, p=None, pre=None, pfx=""):
    standalone = p is None
    if standalone:
        p = P("B2")
    IN, OUT = _io(p, pre, pfx, standalone, debug)
    mla = IN("mla", [D, S], BF16)
    comb = IN("comb", [8, S], BF16)
    x1 = IN("x1", [D, S], BF16)
    hs1 = IN("hs1", [D, S], BF16)
    w1s = [IN(f"w1_{e}", [D, 2048], F32) for e in range(4)]
    b1s = [IN(f"b1_{e}", [2048, 1], F32) for e in range(4)]
    w2s = [IN(f"w2_{e}", [2048, D], F32) for e in range(4)]
    b2s = [IN(f"b2_{e}", [D, 1], F32) for e in range(4)]
    cs = load_consts(p, ("ident", "ones_col", "ones_row"))

    mk2 = p.outp if debug else p.dram
    h1d = mk2("h1d", [2048, S])
    moe_p = [mk2(f"moe{e}", [D, S]) for e in range(4)]
    cc_in = p.dram("cc_in", [D, S])
    cc_out = p.dram("cc_out", [D, S])
    x2_out = OUT("x2", [D, S])
    hs2_out = OUT("hs2", [D, S])

    crow2 = p.ring("crow2", 2, [1, 512], BF16)

    tf = p.ring("adT", 4, [128, 512], F32)[:2]

    for e in range(4):
        mla_pool = pool_load(p, "mla", mla, 8, 8)

        def evac_gelu(p_, ps, o, ft, ci, fw, e=e):
            bb = p_.ring("swB", 2, [128, 1], F32)

            @p_.step("scalar", reads=[ps, bb[ft % 2]], writes=[o])
            def _(sc):
                return sc.activation(out=o[0:fw, :], in_=ps[0:fw, :], func=AF.Gelu,
                                     bias=bb[ft % 2][0:fw, :])
        sweep_proj_fm(p, w1s[e], mla, h1d, 2048, bias_ap=b1s[e], evac=evac_gelu,
                      x_pool=mla_pool)
        p.barrier()

        def mk_evac_comb(base):
            def evac_comb(p_, ps, o, ft, ci, fw, e=e):
                bb = p_.ring("swB", 2, [128, 1], F32)
                a = base + ci * 512
                t = tf[(ft * 8 + ci) % 2]
                ps4 = p_.ps[4]
                cr = crow2[ci % 2]

                @p_.step("sync", writes=[cr], dma=True)
                def _(sy):
                    return sy.dma_start(out=cr[:, :], in_=comb[e:e + 1, a:a + 512])

                @p_.step("tensor", reads=[cr, cs["ones_row_bf"]], writes=[ps4])
                def _(te):
                    return te.matmul(ps4[:, :], cs["ones_row_bf"][0:1, :],
                                     cr[0:1, :], start=True, stop=True)

                @p_.step("vector", reads=[ps, bb[ft % 2]], writes=[t])
                def _(v):
                    return v.tensor_scalar(out=t[0:fw, :], in0=ps[0:fw, :],
                                           scalar1=bb[ft % 2][0:fw, :], scalar2=None,
                                           op0=ALU.add)

                @p_.step("vector", reads=[t, ps4], writes=[o])
                def _(v):
                    return v.tensor_mul(out=o[0:fw, :], in0=t[0:fw, :], in1=ps4[0:fw, :])
            return evac_comb
        for half in range(2):
            hp_pool = pool_load(p, f"h1_{e}_{half}", h1d, 16, 4, n0=half * 2048)
            sweep_proj_fm(p, w2s[e], h1d, moe_p[e], D, kdim=2048, bias_ap=b2s[e],
                          evac=mk_evac_comb(half * 2048), n0=half * 2048,
                          n1=(half + 1) * 2048, x_pool=hp_pool)
            p.barrier()

    at4 = p.ring("swX8", 16, [128, 512], BF16)[:8]
    tt4 = p.ring("adT", 4, [128, 512], F32)
    ob4 = p.ring("m4o", 2, [128, 512], BF16)
    for rt in range(DC):
        for ci in range(8):
            a = ci * 512
            idx = rt * 8 + ci
            avs = [at4[(idx % 2) * 4 + e] for e in range(4)]
            for e in range(4):
                @p.step("gpsimd", writes=[avs[e]], dma=True)
                def _(sy, e=e, rt=rt, a=a, avs=avs):
                    return sy.dma_start(out=avs[e][:, :],
                                        in_=moe_p[e][rt * 128:(rt + 1) * 128, a:a + 512])
            t = tt4[(idx % 2) * 2]
            t2 = tt4[(idx % 2) * 2 + 1]
            o = ob4[idx % 2]

            @p.step("vector", reads=avs[:2], writes=[t])
            def _(v, avs=avs, t=t):
                return v.tensor_add(out=t[:, :], in0=avs[0][:, :], in1=avs[1][:, :])

            @p.step("vector", reads=[t, avs[2]], writes=[t2])
            def _(v, avs=avs, t=t, t2=t2):
                return v.tensor_add(out=t2[:, :], in0=t[:, :], in1=avs[2][:, :])

            @p.step("vector", reads=[t2, avs[3]], writes=[o])
            def _(v, avs=avs, t2=t2, o=o):
                return v.tensor_add(out=o[:, :], in0=t2[:, :], in1=avs[3][:, :])

            @p.step("gpsimd", reads=[o], dma=True)
            def _(g, o=o, rt=rt, a=a):
                return g.dma_start(out=cc_in[rt * 128:(rt + 1) * 128, a:a + 512], in_=o[:, :])
    p.barrier()
    if debug:
        cci_dbg = p.outp("cci_dbg", [D, S])
        cco_dbg = p.outp("cco_dbg", [D, S])

        @p.step("gpsimd", dma=True)
        def _(g):
            return g.dma_start(out=cci_dbg[:, :], in_=cc_in[:, :])
        p.barrier()
    collective_pair(p, cc_in, cc_out)
    wait_cc(p, "sync")
    wait_cc(p, "gpsimd")
    if debug:
        @p.step("gpsimd", dma=True)
        def _(g):
            return g.dma_start(out=cco_dbg[:, :], in_=cc_out[:, :])
        p.barrier()
    sweep_add_fm(p, x1, hs1, x2_out, C_ap=cc_out, scale_c=0.3)
    p.barrier()
    sweep_ln_fm(p, x2_out, hs2_out, cs, kind="ln", out_pool_tag="hs2")
    return p.done() if standalone else p


def make_inputs_B2(inp, consts, core, mla, comb, x1, hs1):
    b, j = core // 2, core % 2
    d = {"mla": mla, "comb": comb,
         "x1": x1, "hs1": hs1,
         "c_ident": consts["ident"], "c_ones_col": consts["ones_col"],
         "c_ones_row": consts["ones_row"]}
    for e in range(4):
        eg = 4 * j + e
        d[f"w1_{e}"] = np.ascontiguousarray(inp["L0_ffn_w1"][eg])
        d[f"b1_{e}"] = np.ascontiguousarray(inp["L0_ffn_b1"][eg].reshape(2048, 1))
        d[f"w2_{e}"] = np.ascontiguousarray(inp["L0_ffn_w2"][eg])
        d[f"b2_{e}"] = np.ascontiguousarray(inp["L0_ffn_b2"][eg].reshape(D, 1))
    return d


C_STAB1 = 12.0
RS64 = 0.125


def build_C(debug=False, p=None, pre=None, pfx=""):
    standalone = p is None
    if standalone:
        p = P("C")
    IN, OUT = _io(p, pre, pfx, standalone, debug)
    hs2 = IN("hs2", [D, S], BF16)
    x2 = IN("x2", [D, S], BF16)
    wq = IN("wq", [D, D], F32)
    bq = IN("bq", [D, 1], F32)
    wk = IN("wk", [D, D], F32)
    bk = IN("bk", [D, 1], F32)
    wv = IN("wv", [D, 520], F32)
    bv = IN("bv", [1, 520], F32)
    wp = IN("wp", [512, D], F32)
    bp_ = IN("bp", [D, 1], F32)
    gam = IN("gam", [D, 1], F32)
    cmask = IN("cmask", [128, 4 * 512], F32)
    cs = load_consts(p, ("ident", "ones_col", "ones_row", "rope_sin", "rope_cos"))

    mk = p.outp if debug else p.dram
    qf = mk("qf", [D, S])
    kf = mk("kf", [D, S])
    vtm = mk("vtm", [S, 520])
    yf = mk("yf", [512, S])
    cc_in = p.dram("cc_in", [D, S])
    cc_out = p.dram("cc_out", [D, S])
    x3_out = OUT("x3", [D, S])
    xn_fm_out = OUT("xn_fm", [D, S])
    xn_tm_out = OUT("xn_tm", [S, D])

    cmk_t = p.ring("swX16", 32, [128, 512], BF16)[16:20]

    def mk_rope_evac(bias_ring_key):
        def evac(p_, ps, o, ft, ci, fw):
            a = ci * 512
            rb = 0
            ac = a
            bb = p_.ring("swB", 2, [128, 1], F32)
            sc1 = [t[0:64, :] for t in p_.ring("adT", 4, [128, 512], F32)]
            raw = p_.ring("ropeR", 2, [64, 512], F32)
            r = raw[ci % 2]
            m1, m2 = sc1[(ci % 2) * 2], sc1[(ci % 2) * 2 + 1]

            @p_.step("scalar", reads=[ps, bb[ft % 2]], writes=[r])
            def _(sc):
                return sc.activation(out=r[:, :], in_=ps[0:64, :], func=AF.Identity,
                                     bias=bb[ft % 2][0:64, :])

            @p_.step("vector", reads=[r, cs["rope_cos_bf"]], writes=[m1])
            def _(v):
                return v.tensor_mul(out=m1[0:32, :], in0=r[0:32, :],
                                    in1=cs["rope_cos_bf"][rb + 0:rb + 32, ac:ac + 512])

            @p_.step("vector", reads=[r, cs["rope_sin_bf"]], writes=[m2])
            def _(v):
                return v.tensor_mul(out=m2[0:32, :], in0=r[32:64, :],
                                    in1=cs["rope_sin_bf"][rb + 32:rb + 64, ac:ac + 512])

            @p_.step("vector", reads=[m1, m2], writes=[o])
            def _(v):
                return v.tensor_sub(out=o[0:32, :], in0=m1[0:32, :], in1=m2[0:32, :])

            @p_.step("vector", reads=[r, cs["rope_cos_bf"]], writes=[m1])
            def _(v):
                return v.tensor_mul(out=m1[0:32, :], in0=r[32:64, :],
                                    in1=cs["rope_cos_bf"][rb + 32:rb + 64, ac:ac + 512])

            @p_.step("vector", reads=[r, cs["rope_sin_bf"]], writes=[m2])
            def _(v):
                return v.tensor_mul(out=m2[0:32, :], in0=r[0:32, :],
                                    in1=cs["rope_sin_bf"][rb + 0:rb + 32, ac:ac + 512])

            @p_.step("vector", reads=[m1, m2, o], writes=[o])
            def _(v):
                return v.tensor_add(out=o[32:64, :], in0=m1[0:32, :], in1=m2[0:32, :])
        return evac

    hs2_pool = pool_use(p, "hs2")
    sweep_proj_fm(p, wq, hs2, qf, D, bias_ap=bq, evac=mk_rope_evac("q"),
                  x_pool=hs2_pool)
    p.barrier()
    sweep_proj_fm(p, wk, hs2, kf, D, bias_ap=bk, evac=mk_rope_evac("k"),
                  x_pool=hs2_pool)
    p.barrier()
    sweep_proj_tm(p, hs2, wv, vtm, 260, bias_ap=bv, consts=cs, w_col0=0, o_col0=0,
                  x_pool=hs2_pool)
    p.barrier()
    sweep_proj_tm(p, hs2, wv, vtm, 260, bias_ap=bv, consts=cs, w_col0=260, o_col0=260,
                  x_pool=hs2_pool)
    p.barrier()

    nbias = p.sb("nbias1", [128, 1], F32)

    @p.step("vector", writes=[nbias])
    def _(v):
        return v.memset(nbias[:, :], -C_STAB1)
    for mi in range(4):
        @p.step("gpsimd", writes=[cmk_t[mi]], dma=True)
        def _(g, mi=mi):
            return g.dma_start(out=cmk_t[mi][:, :], in_=cmask[:, mi * 512:(mi + 1) * 512])

    qtiles = [t[0:64, :] for t in p.ring("swX8", 16, [128, 512], BF16)[8:16]]
    ktiles = p.ring("caK", 8, [64, 128], BF16)
    vt = p.ring("caV", 2, [128, 520], BF16)
    es = p.ring("swX8", 16, [128, 512], BF16)[4:8]
    drow = p.ring("fbD", 2, [1, 512], F32)
    bcsb = [t[0:64, :] for t in p.ring("adT", 4, [128, 512], F32)[:2]]
    otile = p.ring("swX8", 16, [128, 512], BF16)[2:4]
    psS = [p.ps[0], p.ps[1]]
    psB = p.ps[6]

    for qc in range(8):
        a = qc * 512
        ktmax = 4 * qc + 4
        for hp in range(2):
            qs = [qtiles[(hp % 2) * 4 + g] for g in range(4)]
            for g in range(4):
                h = hp * 4 + g

                @p.step("sync", writes=[qs[g]], dma=True)
                def _(sy, g=g, h=h, a=a, qs=qs):
                    return sy.dma_start(out=qs[g][0:64, :],
                                        in_=qf[128 * h:128 * h + 64, a:a + 512])
            for kt in range(ktmax):
                ks4 = [ktiles[(kt % 2) * 4 + g] for g in range(4)]
                for g in range(4):
                    h = hp * 4 + g

                    @p.step("sync", writes=[ks4[g]], dma=True)
                    def _(sy, g=g, h=h, kt=kt, ks4=ks4):
                        return sy.dma_start(out=ks4[g][:, :],
                                            in_=kf[128 * h:128 * h + 64, kt * 128:(kt + 1) * 128])
                vv = vt[kt % 2]

                @p.step("sync", writes=[vv], dma=True)
                def _(sy, kt=kt, vv=vv):
                    return sy.dma_start(out=vv[:, :], in_=vtm[kt * 128:(kt + 1) * 128, :])

                diag = kt - 4 * qc
                for pair in range(2):
                    for g in (2 * pair, 2 * pair + 1):
                        ps_ = psS[g % 2]

                        @p.step("tensor", reads=[ks4[g], qs[g]], writes=[ps_])
                        def _(te, g=g, ps_=ps_, ks4=ks4, qs=qs):
                            return te.matmul(ps_[:, :], ks4[g][0:64, :], qs[g][0:64, :],
                                             start=True, stop=True)

                    @p.step("scalar", reads=[psS[0], psS[1], nbias],
                            writes=[es[2 * pair], es[2 * pair + 1]])
                    def _(sc, pair=pair):
                        sc.activation(out=es[2 * pair][:, :], in_=psS[0][:, :],
                                      func=AF.Exp, bias=nbias[:, :], scale=RS64)
                        return sc.activation(out=es[2 * pair + 1][:, :], in_=psS[1][:, :],
                                             func=AF.Exp, bias=nbias[:, :], scale=RS64)

                    for g in (2 * pair, 2 * pair + 1):
                        h = hp * 4 + g
                        bank = p.ps[2 + g]
                        ev = es[g]
                        if diag >= 0:
                            @p.step("vector", reads=[ev, cmk_t[diag]], writes=[ev])
                            def _(v, ev=ev, diag=diag):
                                return v.tensor_mul(out=ev[:, :], in0=ev[:, :],
                                                    in1=cmk_t[diag][:, :])

                        @p.step("tensor", reads=[ev, vv], writes=[bank])
                        def _(te, ev=ev, vv=vv, h=h, kt=kt, bank=bank):
                            return te.matmul(bank[0:65, :], vv[:, 65 * h:65 * h + 65],
                                             ev[:, :], start=(kt == 0),
                                             stop=(kt == ktmax - 1))

            for g in range(4):
                h = hp * 4 + g
                bank = p.ps[2 + g]
                dr = drow[g % 2]
                bc = bcsb[g % 2]
                ot = otile[g % 2]

                @p.step("vector", reads=[bank], writes=[dr])
                def _(v, dr=dr, bank=bank):
                    return v.reciprocal(out=dr[:, :], in_=bank[64:65, :])

                @p.step("tensor", reads=[dr, cs["ones_row_f"]], writes=[psB])
                def _(te, dr=dr):
                    return te.matmul(psB[0:64, :], cs["ones_row_f"][0:1, 0:64], dr[:, :],
                                     start=True, stop=True)

                @p.step("vector", reads=[psB], writes=[bc])
                def _(v, bc=bc):
                    return v.tensor_copy(out=bc[:, :], in_=psB[0:64, :])

                @p.step("vector", reads=[bank, bc], writes=[ot])
                def _(v, bc=bc, ot=ot, bank=bank):
                    return v.tensor_mul(out=ot[0:64, :], in0=bank[0:64, :], in1=bc[:, :])

                @p.step("gpsimd", reads=[ot], dma=True)
                def _(g_, ot=ot, h=h, a=a):
                    return g_.dma_start(out=yf[64 * h:64 * h + 64, a:a + 512], in_=ot[0:64, :])

    p.barrier()
    sweep_proj_fm(p, wp, yf, cc_in, D, kdim=512, bias_ap=bp_)
    p.barrier()
    collective_pair(p, cc_in, cc_out)
    sweep_add_fm(p, x2, cc_out, x3_out)
    p.barrier()
    hs3 = p.dram("hs3", [D, S])
    sweep_ln_fm(p, x3_out, hs3, cs, kind="ln")
    p.barrier()
    sweep_ln_fm(p, hs3, xn_fm_out, cs, kind="rms", gamma_ap=gam, scale=32.0,
                also_tm_ap=xn_tm_out, out_pool_tag="xn")
    return p.done() if standalone else p


def make_inputs_C(inp, consts, core, hs2, x2):
    b, j = core // 2, core % 2
    Wa = inp["L1_attn_W_attn"]
    ba = inp["L1_attn_b_attn"]
    wq = np.zeros((D, D), np.float32)
    bq = np.zeros((D, 1), np.float32)
    wk = np.zeros((D, D), np.float32)
    bk = np.zeros((D, 1), np.float32)
    wv = np.zeros((D, 520), np.float32)
    bv = np.zeros((1, 520), np.float32)
    for h in range(8):
        hg = 8 * j + h
        wq[:, 128 * h:128 * h + 64] = Wa[:, 64 * hg:64 * hg + 64]
        bq[128 * h:128 * h + 64, 0] = ba[64 * hg:64 * hg + 64]
        wk[:, 128 * h:128 * h + 64] = Wa[:, D + 64 * hg:D + 64 * hg + 64]
        bk[128 * h:128 * h + 64, 0] = ba[D + 64 * hg:D + 64 * hg + 64]
        wv[:, 65 * h:65 * h + 64] = Wa[:, 2 * D + 64 * hg:2 * D + 64 * hg + 64]
        bv[0, 65 * h:65 * h + 64] = ba[2 * D + 64 * hg:2 * D + 64 * hg + 64]
        bv[0, 65 * h + 64] = 1.0
    wp = np.ascontiguousarray(inp["L1_attn_W_proj"][512 * j:512 * (j + 1), :])
    bp = inp["L1_attn_b_proj"].reshape(D, 1) if j == 0 else np.zeros((D, 1), np.float32)
    cm = np.zeros((128, 4 * 512), np.float32)
    for m in range(4):
        kr = np.arange(128)[:, None]
        qcc = np.arange(512)[None, :]
        cm[:, m * 512:(m + 1) * 512] = (128 * m + kr <= qcc)
    return {
        "hs2": hs2, "x2": x2, "wq": wq, "bq": bq, "wk": wk, "bk": bk,
        "wv": wv, "bv": bv, "wp": wp, "bp": np.ascontiguousarray(bp),
        "gam": inp["L1_ffn_norm_gamma"].reshape(D, 1).astype(np.float32),
        "cmask": cm,
        "c_ident": consts["ident"], "c_ones_col": consts["ones_col"],
        "c_ones_row": consts["ones_row"], "c_rope_sin": consts["rope_sin"],
        "c_rope_cos": consts["rope_cos"],
    }



def build_D(debug=False, p=None, pre=None, pfx=""):
    standalone = p is None
    if standalone:
        p = P("D")
    IN, OUT = _io(p, pre, pfx, standalone, debug)
    xn_fm = IN("xn_fm", [D, S], BF16)
    xn_tm = IN("xn_tm", [S, D], BF16)
    x3 = IN("x3", [D, S], BF16)
    identf_in = IN("identf_in", [128, 128], F32)
    ses = [IN(f"se_{e}", [512, D], F32) for e in range(4)]
    sgam = IN("sgam", [1, D], F32)
    w1s = [IN(f"w1_{e}", [D, 4096], F32) for e in range(4)]
    b1s = [IN(f"b1_{e}", [4096, 1], F32) for e in range(4)]
    w2s = [IN(f"w2_{e}", [4096, D], F32) for e in range(4)]
    b2s = [IN(f"b2_{e}", [1, D], F32) for e in range(4)]
    cs = load_consts(p, ("ident", "ones_col", "ones_row"))
    identf = p.sb("identf", [128, 128], F32)

    @p.step("sync", writes=[identf], dma=True)
    def _(sy):
        return sy.dma_start(out=identf[:, :], in_=identf_in[:, :])

    mk = p.outp if debug else p.dram
    sn_fm = mk("sn_fm", [D, 2048])
    disp_tm = mk("disp_tm", [S, 2048])
    slots_f = mk("slots_f", [D, 2048])
    h1d = p.dram("h1d", [4096, 512])
    eo_tm = mk("eo_tm", [2048, D])
    cn_d = p.dram("cn_d", [1, S], F32)
    cn_o = p.dram("cn_o", [1, S], F32)
    zn_d = p.dram("zn_d", [1, S], F32)
    zn_o = p.dram("zn_o", [1, S], F32)
    zs_d = p.dram("zs_d", [1, 2048], F32)
    cc_in = p.dram("cc_in", [D, S])
    cc_out = p.dram("cc_out", [D, S])
    out_tm = OUT("out", [S, D], F32) if not standalone else p.outp("out", [S, D], F32)
    p._d_dump = {"sn_fm": sn_fm, "disp_tm": disp_tm, "slots_f": slots_f,
                 "eo_tm": eo_tm, "cn_o": cn_o, "zn_o": zn_o, "zs_d": zs_d,
                 "cc_out": cc_out, "h1d": h1d, "cc_in": cc_in, "zn_d": zn_d}

    gamrow = p.sb("gamrow", [1, D], BF16)

    @p.step("gpsimd", writes=[gamrow], dma=True)
    def _(sy):
        return sy.dma_start(out=gamrow[:, :], in_=sgam[:, :])
    gb = p.sb("gamb", [128, D], BF16)

    @p.step("tensor", reads=[gamrow, cs["ones_row_bf"]], writes=[p.ps[4], p.ps[5]])
    def _(te):
        te.matmul(p.ps[4][:, :], cs["ones_row_bf"][0:1, :], gamrow[0:1, 0:512],
                  start=True, stop=True)
        return te.matmul(p.ps[5][:, :], cs["ones_row_bf"][0:1, :], gamrow[0:1, 512:1024],
                         start=True, stop=True)

    @p.step("vector", reads=[p.ps[4], p.ps[5]], writes=[gb])
    def _(v):
        v.tensor_copy(out=gb[:, 0:512], in_=p.ps[4][:, :])
        return v.tensor_copy(out=gb[:, 512:1024], in_=p.ps[5][:, :])

    fb2 = p.ring("f32buf", 2, [128, D], F32)
    set_ = [fb2[0]]
    sqv = p.ring("d1sq", 2, [128, 1], F32)
    snt = [fb2[1]]
    sno = [p.ring("t2fXb", 2, [128, D], BF16)[0]]
    d1to = [p.ring("t2fXb", 2, [128, D], BF16)[1]]
    for e in range(4):
        for t in range(4):
            mrow = e * 512 + t * 128
            sv, sq, sn, so, to = set_[0], sqv[t % 2], snt[0], sno[0], d1to[0]

            @p.step("sync", writes=[sv], dma=True)
            def _(sy, e=e, t=t, sv=sv):
                return sy.dma_start(out=sv[:, :], in_=ses[e][t * 128:(t + 1) * 128, :])

            @p.step("vector", reads=[sv], writes=[sn, sq])
            def _(v, sv=sv, sn=sn, sq=sq):
                v.tensor_mul(out=sn[:, :], in0=sv[:, :], in1=sv[:, :])
                return v.reduce_sum(out=sq[:, :], in_=sn[:, :], axis=AX.X)

            @p.step("scalar", reads=[sq], writes=[sq])
            def _(sc, sq=sq):
                return sc.activation(out=sq[:, :], in_=sq[:, :], func=AF.Sqrt)

            @p.step("vector", reads=[sq], writes=[sq])
            def _(v, sq=sq):
                return v.tensor_scalar(out=sq[:, :], in0=sq[:, :], scalar1=1e-12,
                                       scalar2=None, op0=ALU.max)

            @p.step("vector", reads=[sq], writes=[sq])
            def _(v, sq=sq):
                return v.reciprocal(out=sq[:, :], in_=sq[:, :])

            @p.step("vector", reads=[sv, sq], writes=[sn])
            def _(v, sv=sv, sq=sq, sn=sn):
                return v.tensor_scalar(out=sn[:, :], in0=sv[:, :], scalar1=sq[:, :],
                                       scalar2=32.0, op0=ALU.mult, op1=ALU.mult)

            @p.step("vector", reads=[sn, gb], writes=[so])
            def _(v, sn=sn, so=so):
                return v.tensor_mul(out=so[:, :], in0=sn[:, :], in1=gb[:, :])

            @p.step("tensor", reads=[so, cs["ident_bf"]], writes=[p.psT])
            def _(te, so=so):
                last = None
                for dc in range(DC):
                    last = te.transpose(p.psT[:, dc * 128:(dc + 1) * 128],
                                        so[:, dc * 128:(dc + 1) * 128], cs["ident_bf"][:, :])
                return last

            @p.step("vector", reads=[p.psT], writes=[to])
            def _(v, to=to):
                return v.tensor_copy(out=to[:, :], in_=p.psT[:, :])

            @p.step("gpsimd", reads=[to], dma=True)
            def _(g, to=to, mrow=mrow):
                return [g.dma_start(out=sn_fm[dc * 128:(dc + 1) * 128, mrow:mrow + 128],
                                    in_=to[:, dc * 128:(dc + 1) * 128]) for dc in range(DC)]
    p.barrier()

    snt_l = p.ring("d5h", 16, [128, 128], BF16)
    macct = p.ring("swX16", 32, [128, 512], BF16)[16:24]
    csb = p.sb("csb", [128, 16], F32)
    cmx = p.ring("d2c", 2, [128, 1], F32)

    @p.step("vector", writes=list(macct))
    def _(v):
        last = None
        for mi in range(8):
            last = v.memset(macct[mi][:, :], -1e30)
        return last

    @p.step("vector", writes=[csb])
    def _(v):
        return v.memset(csb[:, :], -1e30)

    def logit_block(mt, ci, consumer):
        a = ci * 512
        sns = [snt_l[(mt % 2) * DC + dc] for dc in range(DC)]
        if ci == 0:
            for dc in range(DC):
                @p.step("sync", writes=[sns[dc]], dma=True)
                def _(sy, dc=dc, mt=mt, sns=sns):
                    return sy.dma_start(out=sns[dc][:, :],
                                        in_=sn_fm[dc * 128:(dc + 1) * 128,
                                                  mt * 128:(mt + 1) * 128])
        xnp = pool_use(p, "xn")[0]
        xns = [xnp[dc * 8 + ci] for dc in range(DC)]
        ps_ = p.ps[ci % 2]

        @p.step("tensor", reads=sns + xns, writes=[ps_])
        def _(te, sns=sns, xns=xns, ps_=ps_):
            last = None
            for dc in range(DC):
                last = te.matmul(ps_[:, :], sns[dc][:, :], xns[dc][:, :],
                                 start=(dc == 0), stop=(dc == DC - 1))
            return last
        consumer(ps_, mt, ci)

    def d2_consumer(ps_, mt, ci):
        a = ci * 512
        cm = cmx[ci % 2]

        @p.step("vector", reads=[ps_], writes=[cm])
        def _(v, ps_=ps_, cm=cm):
            return v.reduce_max(out=cm[:, :], in_=ps_[:, :], axis=AX.X)

        @p.step("vector", reads=[cm, csb], writes=[csb])
        def _(v, cm=cm, mt=mt):
            return v.tensor_tensor(out=csb[:, mt:mt + 1], in0=csb[:, mt:mt + 1],
                                   in1=cm[:, :], op=ALU.max)

        @p.step("vector", reads=[ps_, macct[ci]], writes=[macct[ci]])
        def _(v, ps_=ps_, ci=ci):
            return v.tensor_tensor(out=macct[ci][:, :], in0=macct[ci][:, :],
                                   in1=ps_[:, :], op=ALU.max)

    for mt in range(16):
        for ci in range(8):
            logit_block(mt, ci, d2_consumer)

    cnrow = p.sb("cnrow", [128, 1], F32)
    for nt in range(NT):
        @p.step("tensor", reads=[macct[nt // 4], cs["ident_bf"]], writes=[p.psT])
        def _(te, nt=nt):
            return te.transpose(p.psT[:, 128:256],
                                macct[nt // 4][:, (nt % 4) * 128:(nt % 4) * 128 + 128],
                                cs["ident_bf"][:, :])

        @p.step("vector", reads=[p.psT], writes=[cnrow])
        def _(v):
            return v.reduce_max(out=cnrow[:, :], in_=p.psT[:, 128:256], axis=AX.X)

        cnf = p.sb("cnf", [128, 1], BF16) if False else cnrow
        if not hasattr(p, "_crow7"):
            _lr = p.ring("lnrows", 5, [1, 512], F32)[3]
            p._crow7 = [_lr[0:1, 0:128], _lr[0:1, 128:256]]
        crow_sb = p._crow7[nt % 2]
        cnbf = p.ring("cnbf7", 2, [128, 1], BF16)[nt % 2]

        @p.step("vector", reads=[cnrow], writes=[cnbf])
        def _(v, cnbf=cnbf):
            return v.tensor_copy(out=cnbf[:, :], in_=cnrow[:, :])

        @p.step("tensor", reads=[cnbf, cs["ident_bf"]], writes=[p.psT])
        def _(te, cnbf=cnbf):
            return te.transpose(p.psT[0:1, 0:128], cnbf[:, :], cs["ident_bf"][:, :])

        @p.step("vector", reads=[p.psT], writes=[crow_sb])
        def _(v, crow_sb=crow_sb):
            return v.tensor_copy(out=crow_sb[:, :], in_=p.psT[0:1, 0:128])

        @p.step("gpsimd", reads=[crow_sb], dma=True)
        def _(g, nt=nt, crow_sb=crow_sb):
            return g.dma_start(out=cn_d[0:1, nt * 128:(nt + 1) * 128], in_=crow_sb[:, :])
    p.barrier()
    collective_pair(p, cn_d, cn_o, op=ALU.max)

    nbias3 = p.ring("d1sq", 2, [128, 1], F32)
    zsacc = p.sb("zsacc", [128, 16], F32)

    @p.step("vector", writes=[zsacc])
    def _(v):
        return v.memset(zsacc[:, :], 0.0)
    zstile = p.ring("d2c", 2, [128, 1], F32)
    dtile = p.ring("swX8", 16, [128, 512], BF16)[0:2]
    ttile = p.ring("swX8", 16, [128, 512], BF16)[2:4]

    def d3_consumer(ps_, mt, ci):
        a = ci * 512
        nb = nbias3[mt % 2]
        if ci == 0:
            @p.step("vector", reads=[csb], writes=[nb])
            def _(v, nb=nb, mt=mt):
                return v.tensor_scalar(out=nb[:, :], in0=csb[:, mt:mt + 1], scalar1=-1.0,
                                       scalar2=None, op0=ALU.mult)
        dt = dtile[ci % 2]
        zt = zstile[ci % 2]

        @p.step("scalar", reads=[ps_, nb], writes=[dt, zt])
        def _(sc, ps_=ps_, dt=dt, zt=zt, nb=nb):
            return sc.activation(out=dt[:, :], in_=ps_[:, :], func=AF.Exp,
                                 bias=nb[:, :], accum_out=zt[:, :])

        @p.step("vector", reads=[zt, zsacc], writes=[zsacc])
        def _(v, zt=zt, mt=mt):
            return v.tensor_add(out=zsacc[:, mt:mt + 1], in0=zsacc[:, mt:mt + 1],
                                in1=zt[:, :])

        @p.step("tensor", reads=[dt, cs["ident_bf"]], writes=[p.psT])
        def _(te, dt=dt):
            last = None
            for q in range(4):
                last = te.transpose(p.psT[:, q * 128:(q + 1) * 128],
                                    dt[:, q * 128:(q + 1) * 128], cs["ident_bf"][:, :])
            return last
        tt_ = ttile[ci % 2]

        @p.step("vector", reads=[p.psT], writes=[tt_])
        def _(v, tt_=tt_):
            return v.tensor_copy(out=tt_[:, :], in_=p.psT[:, 0:512])

        @p.step("gpsimd", reads=[tt_], dma=True)
        def _(g, tt_=tt_, a=a, mt=mt):
            return [g.dma_start(out=disp_tm[a + q * 128:a + (q + 1) * 128,
                                            mt * 128:(mt + 1) * 128],
                                in_=tt_[:, q * 128:(q + 1) * 128]) for q in range(4)]

    for mt in range(16):
        for ci in range(8):
            logit_block(mt, ci, d3_consumer)

    @p.step("vector", reads=[zsacc], writes=[zsacc])
    def _(v):
        return v.reciprocal(out=zsacc[:, :], in_=zsacc[:, :])
    zso = p.sb("zso", [16, 128], F32)

    @p.step("tensor", reads=[zsacc, identf], writes=[p.ps[5]])
    def _(te):
        return te.transpose(p.ps[5][0:16, 0:128], zsacc[:, 0:16], identf[:, :])

    @p.step("vector", reads=[p.ps[5]], writes=[zso])
    def _(v):
        return v.tensor_copy(out=zso[:, :], in_=p.ps[5][0:16, 0:128])

    @p.step("gpsimd", reads=[zso], dma=True)
    def _(g):
        return g.dma_start(out=zs_d[0:1, :].rearrange("o (a b) -> (o a) b", a=16, b=128),
                           in_=zso[:, :])
    p.barrier()

    dpt = p.ring("swX8", 16, [128, 512], BF16)[4:6]
    rzrow = p.ring("fbD", 2, [1, 512], F32)
    sbc = p.ring("adT", 4, [128, 512], F32)[2]
    sout = p.ring("swX8", 16, [128, 512], BF16)[13:16]
    xntm_pool = pool_load(p, "xn_tm", xn_tm, 32, 2)[0]
    for esc in range(4):
        rz = rzrow[esc % 2]

        @p.step("sync", writes=[rz], dma=True)
        def _(sy, esc=esc, rz=rz):
            return sy.dma_start(out=rz[:, :], in_=zs_d[0:1, esc * 512:(esc + 1) * 512])
        for dh in range(2):
            for nt in range(NT):
                dp = dpt[nt % 2]
                xv4 = xntm_pool[nt * 2 + dh]

                @p.step("sync", writes=[dp], dma=True)
                def _(sy, nt=nt, esc=esc, dp=dp):
                    return sy.dma_start(out=dp[:, :],
                                        in_=disp_tm[nt * 128:(nt + 1) * 128,
                                                    esc * 512:(esc + 1) * 512])

                @p.step("tensor", reads=[dp, xv4], writes=[p.ps[0], p.ps[1], p.ps[2], p.ps[3]])
                def _(te, nt=nt, dh=dh, dp=dp, xv4=xv4):
                    last = None
                    for dt_ in range(4):
                        last = te.matmul(p.ps[dt_][:, :],
                                         xv4[:, dt_ * 128:(dt_ + 1) * 128],
                                         dp[:, :], start=(nt == 0), stop=(nt == NT - 1))
                    return last
            @p.step("tensor", reads=[rz, cs["ones_row_f"]], writes=[p.ps[4]])
            def _(te, rz=rz):
                return te.matmul(p.ps[4][:, :], cs["ones_row_f"][0:1, :], rz[:, :],
                                 start=True, stop=True)

            @p.step("vector", reads=[p.ps[4]], writes=[sbc])
            def _(v):
                return v.tensor_copy(out=sbc[:, :], in_=p.ps[4][:, :])
            for dt_ in range(4):
                dtile_ = dh * 4 + dt_
                so = sout[dt_ % 3]

                @p.step("vector", reads=[p.ps[dt_], sbc], writes=[so])
                def _(v, dt_=dt_, so=so):
                    return v.tensor_mul(out=so[:, :], in0=p.ps[dt_][:, :], in1=sbc[:, :])

                @p.step("gpsimd", reads=[so], dma=True)
                def _(g, so=so, dtile_=dtile_, esc=esc):
                    return g.dma_start(out=slots_f[dtile_ * 128:(dtile_ + 1) * 128,
                                                   esc * 512:(esc + 1) * 512], in_=so[:, :])
            p.barrier()

    for e in range(4):
        def evac_gelu(p_, ps, o, ft, ci, fw, e=e):
            bb = p_.ring("swB", 2, [128, 1], F32)

            @p_.step("scalar", reads=[ps, bb[ft % 2]], writes=[o])
            def _(sc):
                return sc.activation(out=o[0:fw, :], in_=ps[0:fw, :], func=AF.Gelu,
                                     bias=bb[ft % 2][0:fw, :])
        sweep_proj_fm(p, w1s[e], slots_f, h1d, 4096, bias_ap=b1s[e], evac=evac_gelu,
                      n0=0, n1=512, x_col0=e * 512)
        p.barrier()
        w2t = p.ring("swX16", 32, [128, 512], BF16)
        b2row = gamrow

        @p.step("gpsimd", writes=[b2row], dma=True)
        def _(g, e=e):
            return g.dma_start(out=b2row[:, :], in_=b2s[e][:, :])
        p._pool_tag = ("d5scratch", 0, 0, 0, 0)
        htp = xpool(p)
        eo_ring = p.ring("lnT", 2, [128, 512], BF16)
        for doc in range(2):
            for hc in range(32):
                @p.step("gpsimd", writes=[w2t[hc]], dma=True)
                def _(g, e=e, doc=doc, hc=hc):
                    return g.dma_start(out=w2t[hc][:, :],
                                       in_=w2s[e][hc * 128:(hc + 1) * 128,
                                                  doc * 512:(doc + 1) * 512])
            for est in range(4):
                hs_ = [htp[(est % 2) * 32 + hc] for hc in range(32)]
                for hc in range(32):
                    @p.step("sync", writes=[hs_[hc]], dma=True)
                    def _(sy, hc=hc, est=est, hs_=hs_):
                        return sy.dma_start(out=hs_[hc][:, 0:128],
                                            in_=h1d[hc * 128:(hc + 1) * 128,
                                                    est * 128:(est + 1) * 128])
                ps_ = p.ps[est % 4]

                @p.step("tensor", reads=hs_ + list(w2t) + [b2row], writes=[ps_])
                def _(te, hs_=hs_, ps_=ps_, doc=doc):
                    for hc in range(32):
                        te.matmul(ps_[:, :], hs_[hc][:, 0:128], w2t[hc][:, :],
                                  start=(hc == 0), stop=False)
                    return te.matmul(ps_[:, :], cs["ones_row_bf"][0:1, :],
                                     b2row[0:1, doc * 512:(doc + 1) * 512],
                                     start=False, stop=True)
                eo_o = eo_ring[est % 2]

                @p.step("scalar", reads=[ps_], writes=[eo_o])
                def _(sc, ps_=ps_, eo_o=eo_o):
                    return sc.activation(out=eo_o[:, :], in_=ps_[:, :], func=AF.Copy)

                @p.step("gpsimd", reads=[eo_o], dma=True)
                def _(g, eo_o=eo_o, e=e, est=est, doc=doc):
                    return g.dma_start(out=eo_tm[e * 512 + est * 128:e * 512 + (est + 1) * 128,
                                                 doc * 512:(doc + 1) * 512], in_=eo_o[:, :])
        p.barrier()

    cnb = p.ring("lnrows", 5, [1, 512], F32)[4]
    combt = p.ring("swX8", 16, [128, 512], BF16)
    _eo16 = p.ring("swX16", 32, [128, 512], BF16)[24:32]
    eot = [_eo16[i // 4][:, (i % 4) * 128:(i % 4) * 128 + 128] for i in range(16)]
    oout = p.ring("swX16", 32, [128, 512], BF16)[21:24]
    xnp6 = pool_load(p, "xn", xn_fm, 8, 8)[0]
    for ci in range(8):
        a = ci * 512

        @p.step("sync", writes=[cnb], dma=True)
        def _(sy, a=a):
            return sy.dma_start(out=cnb[:, :], in_=cn_o[0:1, a:a + 512])
        cnbbf = p.sb("cnbbf", [1, 512], BF16) if "cnbbf" not in p._rcache else p._rcache["cnbbf"]
        p._rcache["cnbbf"] = cnbbf

        @p.step("vector", reads=[cnb], writes=[cnbbf])
        def _(v, cnbbf=cnbbf):
            return v.tensor_scalar(out=cnbbf[:, :], in0=cnb[:, :], scalar1=-1.0,
                                   scalar2=None, op0=ALU.mult)
        xns = [xnp6[dc * 8 + ci] for dc in range(DC)]
        for mt in range(16):
            sns = [snt_l[(mt % 2) * DC + dc] for dc in range(DC)]
            for dc in range(DC):
                @p.step("sync", writes=[sns[dc]], dma=True)
                def _(sy, dc=dc, mt=mt, sns=sns):
                    return sy.dma_start(out=sns[dc][:, :],
                                        in_=sn_fm[dc * 128:(dc + 1) * 128,
                                                  mt * 128:(mt + 1) * 128])
            ps_ = p.ps[mt % 2]

            @p.step("tensor", reads=sns + xns + [cnbbf, cs["ones_row_bf"]], writes=[ps_])
            def _(te, sns=sns, xns=xns, ps_=ps_, cnbbf=cnbbf):
                for dc in range(DC):
                    te.matmul(ps_[:, :], sns[dc][:, :], xns[dc][:, :],
                              start=(dc == 0), stop=False)
                return te.matmul(ps_[:, :], cs["ones_row_bf"][0:1, :], cnbbf[0:1, :],
                                 start=False, stop=True)
            if mt % 2 == 1:
                @p.step("scalar", reads=[p.ps[0], p.ps[1]],
                        writes=[combt[mt - 1], combt[mt]])
                def _(sc, mt=mt):
                    sc.activation(out=combt[mt - 1][:, :], in_=p.ps[0][:, :], func=AF.Exp)
                    return sc.activation(out=combt[mt][:, :], in_=p.ps[1][:, :], func=AF.Exp)

                @p.step("tensor", reads=[combt[mt - 1], combt[mt], cs["ones_col_bf"]],
                        writes=[p.ps[5]])
                def _(te, mt=mt):
                    te.matmul(p.ps[5][0:1, :], cs["ones_col_bf"][:, :], combt[mt - 1][:, :],
                              start=(mt == 1), stop=False)
                    return te.matmul(p.ps[5][0:1, :], cs["ones_col_bf"][:, :], combt[mt][:, :],
                                     start=False, stop=(mt == 15))
        znr = p.ring("lnrows", 5, [1, 512], F32)[0]

        @p.step("vector", reads=[p.ps[5]], writes=[znr])
        def _(v, znr=znr):
            return v.tensor_copy(out=znr[:, :], in_=p.ps[5][0:1, :])

        @p.step("gpsimd", reads=[znr], dma=True)
        def _(g, znr=znr, a=a):
            return g.dma_start(out=zn_d[0:1, a:a + 512], in_=znr[:, :])
        for dh in range(2):
            for mt in range(16):
                eos = [eot[(mt % 2) * DC + dc] for dc in range(4)]
                for dt_ in range(4):
                    dtile_ = dh * 4 + dt_

                    @p.step("sync", writes=[eos[dt_]], dma=True)
                    def _(sy, mt=mt, dtile_=dtile_, eos=eos, dt_=dt_):
                        return sy.dma_start(out=eos[dt_][:, :],
                                            in_=eo_tm[mt * 128:(mt + 1) * 128,
                                                      dtile_ * 128:(dtile_ + 1) * 128])

                @p.step("tensor", reads=eos + [combt[mt]], writes=[p.ps[0], p.ps[1], p.ps[2], p.ps[3]])
                def _(te, mt=mt, eos=eos):
                    last = None
                    for dt_ in range(4):
                        last = te.matmul(p.ps[dt_][:, :], eos[dt_][:, :], combt[mt][:, :],
                                         start=(mt == 0), stop=(mt == 15))
                    return last
            for dt_ in range(4):
                dtile_ = dh * 4 + dt_
                oo = oout[dt_ % 3]

                @p.step("scalar", reads=[p.ps[dt_]], writes=[oo])
                def _(sc, dt_=dt_, oo=oo):
                    return sc.activation(out=oo[:, :], in_=p.ps[dt_][:, :], func=AF.Copy)

                @p.step("gpsimd", reads=[oo], dma=True)
                def _(g, oo=oo, dtile_=dtile_, a=a):
                    return g.dma_start(out=cc_in[dtile_ * 128:(dtile_ + 1) * 128, a:a + 512],
                                       in_=oo[:, :])
            p.barrier()
    p.barrier()
    collective_pair(p, cc_in, cc_out)
    collective_pair(p, zn_d, zn_o)

    znrow = p.ring("lnrows", 5, [1, 512], F32)[1:3]
    zbc = p.ring("adT", 4, [128, 512], F32)[0]
    xt7 = p.ring("swX16", 32, [128, 512], BF16)[0:2]
    ct7 = p.ring("swX16", 32, [128, 512], BF16)[2:4]
    f7 = p.ring("adT", 4, [128, 512], F32)[1:3]
    t7 = [p.ring("adT", 4, [128, 512], F32)[3], p.ring("f32buf", 2, [128, D], F32)[0][:, 0:512]]
    for ci in range(8):
        a = ci * 512
        rz = znrow[ci % 2]

        @p.step("sync", writes=[rz], dma=True)
        def _(sy, a=a, rz=rz):
            return sy.dma_start(out=rz[:, :], in_=zn_o[0:1, a:a + 512])

        @p.step("vector", reads=[rz], writes=[rz])
        def _(v, rz=rz):
            return v.reciprocal(out=rz[:, :], in_=rz[:, :])

        @p.step("tensor", reads=[rz, cs["ones_row_f"]], writes=[p.ps[4]])
        def _(te, rz=rz):
            return te.matmul(p.ps[4][:, :], cs["ones_row_f"][0:1, :], rz[:, :],
                             start=True, stop=True)

        @p.step("vector", reads=[p.ps[4]], writes=[zbc])
        def _(v):
            return v.tensor_copy(out=zbc[:, :], in_=p.ps[4][:, :])
        for dc in range(DC):
            xv, cv, fv, tv = xt7[dc % 2], ct7[dc % 2], f7[dc % 2], t7[dc % 2]

            @p.step("sync", writes=[xv], dma=True)
            def _(sy, dc=dc, a=a, xv=xv):
                return sy.dma_start(out=xv[:, :], in_=x3[dc * 128:(dc + 1) * 128, a:a + 512])

            @p.step("sync", writes=[cv], dma=True)
            def _(sy, dc=dc, a=a, cv=cv):
                return sy.dma_start(out=cv[:, :], in_=cc_out[dc * 128:(dc + 1) * 128, a:a + 512])

            @p.step("vector", reads=[cv, zbc], writes=[tv])
            def _(v, cv=cv, tv=tv):
                return v.tensor_mul(out=tv[:, :], in0=cv[:, :], in1=zbc[:, :])

            @p.step("vector", reads=[tv, xv], writes=[fv])
            def _(v, tv=tv, xv=xv, fv=fv):
                return v.tensor_add(out=fv[:, :], in0=tv[:, :], in1=xv[:, :])
            @p.step("tensor", reads=[fv, identf], writes=[p.ps[5], p.ps[6]])
            def _(te, fv=fv):
                last = None
                for q in range(4):
                    tgt = p.ps[5] if q < 2 else p.ps[6]
                    last = te.transpose(tgt[:, (q % 2) * 128:(q % 2) * 128 + 128],
                                        fv[:, q * 128:(q + 1) * 128], identf[:, :])
                return last
            ov = [p.ring("f32buf", 2, [128, D], F32)[0][:, 512:1024],
                  p.ring("f32buf", 2, [128, D], F32)[1][:, 0:512]][dc % 2]

            @p.step("vector", reads=[p.ps[5], p.ps[6]], writes=[ov])
            def _(v, ov=ov):
                v.tensor_copy(out=ov[:, 0:256], in_=p.ps[5][:, 0:256])
                return v.tensor_copy(out=ov[:, 256:512], in_=p.ps[6][:, 0:256])
            @p.step("gpsimd", reads=[ov], dma=True)
            def _(g, ov=ov, dc=dc, a=a):
                return [g.dma_start(out=out_tm[a + q * 128:a + (q + 1) * 128,
                                               dc * 128:(dc + 1) * 128],
                                    in_=ov[:, q * 128:(q + 1) * 128]) for q in range(4)]
    return p.done() if standalone else p


def make_inputs_D(inp, consts, core, xn_fm, xn_tm, x3):
    b, j = core // 2, core % 2
    d = {"xn_fm": xn_fm, "xn_tm": xn_tm, "x3": x3,
         "identf_in": consts["ident"],
         "sgam": inp["L1_ffn_slot_norm_gamma"].reshape(1, D).astype(np.float32),
         "c_ident": consts["ident"], "c_ones_col": consts["ones_col"],
         "c_ones_row": consts["ones_row"]}
    for e in range(4):
        eg = 4 * j + e
        d[f"se_{e}"] = np.ascontiguousarray(inp["L1_ffn_slot_embeds"][eg])
        d[f"w1_{e}"] = np.ascontiguousarray(inp["L1_ffn_w1"][eg])
        d[f"b1_{e}"] = np.ascontiguousarray(inp["L1_ffn_b1"][eg].reshape(4096, 1))
        d[f"w2_{e}"] = np.ascontiguousarray(inp["L1_ffn_w2"][eg])
        d[f"b2_{e}"] = np.ascontiguousarray(inp["L1_ffn_b2"][eg].reshape(1, D))
    return d



def build_fused(debug=False):
    p = P("F")
    x1 = p.dram("s_x1", [D, S])
    hs1 = p.dram("s_hs1", [D, S])
    mla = p.dram("s_mla", [D, S])
    comb = p.dram("s_comb", [8, S])
    x2 = p.dram("s_x2", [D, S])
    hs2 = p.dram("s_hs2", [D, S])
    x3 = p.dram("s_x3", [D, S])
    xnf = p.dram("s_xnf", [D, S])
    xnt = p.dram("s_xnt", [S, D])
    out = p.outp("out", [S, D], F32)
    build_A(p=p, pre={"x1": x1, "hs1": hs1}, pfx="A_")
    p.barrier()
    build_B(p=p, pre={"hs1": hs1, "mla": mla, "comb": comb}, pfx="B_")
    p.barrier()
    build_B2(p=p, pre={"mla": mla, "comb": comb, "x1": x1, "hs1": hs1,
                       "x2": x2, "hs2": hs2}, pfx="B2_")
    p.barrier()
    build_C(p=p, pre={"hs2": hs2, "x2": x2, "x3": x3, "xn_fm": xnf,
                      "xn_tm": xnt}, pfx="C_")
    p.barrier()
    build_D(p=p, pre={"xn_fm": xnf, "xn_tm": xnt, "x3": x3, "out": out}, pfx="D_")
    if debug:
        p.barrier()
        for nm, t in (("d_x1", x1), ("d_hs1", hs1), ("d_mla", mla),
                      ("d_comb", comb), ("d_x2", x2), ("d_hs2", hs2),
                      ("d_x3", x3), ("d_xnf", xnf), ("d_xnt", xnt)) + tuple(
                          ("d_" + k, v) for k, v in p._d_dump.items()):
            dbg = p.outp(nm, list(t.shape), BF16)

            @p.step("gpsimd", dma=True)
            def _(g, dbg=dbg, t=t):
                return g.dma_start(out=dbg[:, :], in_=t[:, :])
        p.barrier()
    return p.done()


def make_inputs_fused(inp, consts, core):
    d = {}
    for pfx, fn, extra in (("A_", make_inputs_A, ()),
                           ("B_", make_inputs_B, (None,)),
                           ("B2_", make_inputs_B2, (None, None, None, None)),
                           ("C_", make_inputs_C, (None, None)),
                           ("D_", make_inputs_D, (None, None, None))):
        sub = fn(inp, consts, core, *extra)
        for k, v in sub.items():
            if v is None:
                continue
            if k.startswith("c_"):
                d[k] = v
            else:
                d[pfx + k] = v
    return d


from concourse.bass_utils import run_bass_kernel_spmd

_BUILD_CACHE = {}


def _get(name, fn):
    if name not in _BUILD_CACHE:
        _BUILD_CACHE[name] = fn()
    return _BUILD_CACHE[name]


def kernel(**inputs):
    consts = host_constants()
    cores = list(range(8))
    nc = _get("F", build_fused)
    im = [make_inputs_fused(inputs, consts, c) for c in cores]
    r = run_bass_kernel_spmd(nc, im, core_ids=cores).results
    out = np.stack([np.asarray(r[2 * b]["out"], dtype=np.float32) for b in range(4)])
    return out


# version 08
# speedup vs baseline: 1.4893x; 1.0214x over previous
import numpy as np
from contextlib import ExitStack
import concourse.bass as bass
import concourse.mybir as mybir

F32 = mybir.dt.float32
BF16 = mybir.dt.bfloat16
AF = mybir.ActivationFunctionType
ALU = mybir.AluOpType
AX = mybir.AxisListType

D = 1024
S = 4096
B = 4
DC = D // 128
NT = S // 128
EPS_LN = 1e-5

ENGINES = ("tensor", "vector", "scalar", "gpsimd", "sync")


def cdiv(a, b):
    return (a + b - 1) // b


class P:

    def __init__(self, name):
        self.nc = bass.Bass("TRN2", target_bir_lowering=False)
        self.ctx = ExitStack()
        self.block = self.ctx.enter_context(self.nc.Block())
        self.name = name
        self._uid = 0
        self.esem = {e: self.ctx.enter_context(self.nc.semaphore("es_" + e))
                     for e in ENGINES + ("cc",)}
        self.ecnt = {e: 0 for e in ENGINES + ("cc",)}
        self.res = {}
        self.fences = {"gpsimd": [0], "sync": [0]}
        self.waited = {e: {} for e in ENGINES}
        self.safe = {e: 0 for e in ENGINES}
        self._eng_obj = {}
        self._open_section = None
        self.ps = [self.ctx.enter_context(self.nc.psum_tensor(f"pool_ps{i}", [128, 512], F32))
                   for i in range(7)]
        self.psT = self.ctx.enter_context(self.nc.psum_tensor("pool_psT", [128, 1024], BF16))

    def uid(self):
        self._uid += 1
        return self._uid

    def dram(self, name, shape, dtype=BF16, kind="Internal"):
        if kind == "Internal":
            name = f"{name}_{self.uid()}"
        return self.nc.dram_tensor(name, list(shape), dtype, kind=kind).ap()

    def inp(self, name, shape, dtype=F32):
        return self.dram(name, shape, dtype, kind="ExternalInput")

    def outp(self, name, shape, dtype=BF16):
        return self.dram(name, shape, dtype, kind="ExternalOutput")

    def sb(self, name, shape, dtype=BF16):
        return self.ctx.enter_context(self.nc.sbuf_tensor(f"{name}_{self.uid()}", list(shape), dtype))

    def psum(self, name, shape, dtype=F32):
        return self.ctx.enter_context(self.nc.psum_tensor(f"{name}_{self.uid()}", list(shape), dtype))

    def ring(self, name, n, shape, dtype=BF16):
        if name == "swX8":
            return self.ring("swX16", 32, shape, dtype)[:n]
        k = (name, n, tuple(shape), dtype)
        if not hasattr(self, "_rcache"):
            self._rcache = {}
        if k not in self._rcache:
            self._rcache[k] = [self.sb(f"{name}{i}", shape, dtype) for i in range(n)]
        return self._rcache[k]

    def _engine(self, ename):
        if self._open_section == ename:
            return self._eng_obj[ename]
        holder = {}

        def body(eng):
            holder["eng"] = eng
        getattr(self.block, ename)(body)
        raise RuntimeError("engine sections must be driven via step()")

    def step(self, ename, reads=(), writes=(), dma=False, n_dma=1):
        def run(fn):
            deps = {}

            def need(e, c):
                if e == ename and ename == "tensor":
                    return
                if e == ename and c <= self.safe[ename]:
                    return
                if e in self.fences and e != ename:
                    fs = self.fences[e]
                    f = next((x for x in fs if x >= c), None)
                    if f is None:
                        f = self.ecnt[e]
                        assert f >= c

                        def fbody(eng, e=e, f=f):
                            eng.wait_ge(self.esem[e], f)
                        getattr(self.block, e)(fbody)
                        fs.append(f)
                    c = f
                deps[e] = max(deps.get(e, 0), c)

            for r in reads:
                st = self.res.get(id(r) if not isinstance(r, (str, tuple)) else r)
                if st and st["w"]:
                    need(*st["w"])
            for w in writes:
                key = id(w) if not isinstance(w, (str, tuple)) else w
                st = self.res.get(key)
                if st:
                    if st["w"]:
                        need(*st["w"])
                    for rr in st["r"]:
                        need(*rr)

            emitted = {}

            wm = self.waited[ename]
            for e in list(deps):
                if deps[e] <= wm.get(e, 0):
                    del deps[e]
                else:
                    wm[e] = deps[e]
            deps_log = dict(deps)

            def body(eng):
                for e, c in deps.items():
                    if c > 0:
                        eng.wait_ge(self.esem[e], c)
                ret = fn(eng)
                instrs = ret if isinstance(ret, (list, tuple)) else ([ret] if ret is not None else [])
                if dma:
                    for ins in instrs:
                        ins.then_inc(self.esem[ename], 16)
                    emitted["inc"] = 16 * len(instrs)
                else:
                    if instrs:
                        instrs[-1].then_inc(self.esem[ename], 1)
                        emitted["inc"] = 1
                    else:
                        emitted["inc"] = 0

            getattr(self.block, ename)(body)
            self.ecnt[ename] += emitted.get("inc", 0)
            if getattr(self, "log_steps", False):
                self.step_log.append((ename, dict(deps_log), emitted.get("inc", 0),
                                      self.ecnt[ename]))
            cnt = (ename, self.ecnt[ename])
            for w in writes:
                key = id(w) if not isinstance(w, (str, tuple)) else w
                self.res[key] = {"w": cnt, "r": []}
            for r in reads:
                key = id(r) if not isinstance(r, (str, tuple)) else r
                st = self.res.setdefault(key, {"w": None, "r": []})
                st["r"].append(cnt)
            return None
        return run

    def barrier(self):
        final = dict(self.ecnt)
        for ename in ENGINES:
            def body(eng, ename=ename):
                for e in ENGINES + ("cc",):
                    if final[e] > 0 and final[e] > self.waited[ename].get(e, 0):
                        eng.wait_ge(self.esem[e], final[e])
            getattr(self.block, ename)(body)
            for e in ENGINES + ("cc",):
                if final[e] > 0:
                    self.waited[ename][e] = max(self.waited[ename].get(e, 0), final[e])
        for e in ENGINES:
            self.safe[e] = self.ecnt[e]
        self.res.clear()

    def done(self):
        self.barrier()
        self.ctx.close()
        return self.nc



def host_constants():
    c = {}
    c["ident"] = np.eye(128, dtype=np.float32)
    c["ones_col"] = np.ones((128, 1), dtype=np.float32)
    c["ones_row"] = np.ones((1, 128), dtype=np.float32)
    kk, qq = np.meshgrid(np.arange(128), np.arange(128), indexing="ij")
    c["tri_kq"] = (kk <= qq).astype(np.float32)
    hd = 64
    fraction = 2.0 * np.arange(0, hd // 2) / hd
    timescale = 10000.0 ** fraction
    pos = np.arange(S, dtype=np.float64)
    ang = pos[None, :] / timescale[:, None]
    c["rope_sin"] = np.concatenate([np.sin(ang), np.sin(ang)], 0).astype(np.float32)
    c["rope_cos"] = np.concatenate([np.cos(ang), np.cos(ang)], 0).astype(np.float32)
    return c


def load_consts(p, want=("ident", "ones_col", "ones_row", "tri_kq")):
    shapes = {"ident": (128, 128), "ones_col": (128, 1), "ones_row": (1, 128),
              "tri_kq": (128, 128), "rope_sin": (64, S), "rope_cos": (64, S)}
    if not hasattr(p, "_consts"):
        p._consts = {}
    cs = p._consts
    want = [nm for nm in want if nm + "_bf" not in cs]
    for nm in want:
        ap = p.inp("c_" + nm, shapes[nm], F32)
        bf = p.sb("c_" + nm + "_bf", shapes[nm], BF16)
        cs[nm + "_bf"] = bf

        @p.step("gpsimd", writes=[bf], dma=True)
        def _(g, ap=ap, bf=bf):
            return g.dma_start(out=bf[:, :], in_=ap[:, :])
        if nm in ("ones_row", "ones_col"):
            ff = p.sb("c_" + nm + "_f", shapes[nm], F32)
            cs[nm + "_f"] = ff

            @p.step("sync", writes=[ff], dma=True)
            def _(sy, ap=ap, ff=ff):
                return sy.dma_start(out=ff[:, :], in_=ap[:, :])
    return cs



def xpool(p):
    return p.ring("XPOOL", 64, [128, 512], BF16)


def pool_load(p, tag, X_ap, KC, NCH, n0=0, x_row0=0):
    tiles = xpool(p)
    key = (tag, KC, NCH, n0, x_row0)
    if getattr(p, "_pool_tag", None) == key:
        return (tiles, KC, NCH, n0)
    p._pool_tag = key
    for kc in range(KC):
        for ci in range(NCH):
            t = tiles[kc * NCH + ci]
            a = n0 + ci * 512
            eng = "sync" if (kc + ci) % 2 == 0 else "gpsimd"

            @p.step(eng, writes=[t], dma=True)
            def _(q, t=t, kc=kc, a=a):
                return q.dma_start(out=t[:, :],
                                   in_=X_ap[x_row0 + kc * 128:x_row0 + (kc + 1) * 128,
                                            a:a + 512])
    return (tiles, KC, NCH, n0)


def pool_use(p, tag):
    key = getattr(p, "_pool_tag", None)
    assert key is not None and key[0] == tag, f"pool tag {key} != {tag}"
    return (xpool(p), key[1], key[2], key[3])



def sweep_proj_fm(p, W_ap, X_ap, O_ap, F_out, kdim=D, n0=0, n1=S,
                  bias_ap=None, evac=None, out_dt=BF16, o_row0=0, x_row0=0,
                  w_col0=0, nbank=4, acc_scale=None, x_col0=0, x_pool=None,
                  evac2=None):
    KC = kdim // 128
    FT = cdiv(F_out, 128)
    NCH = cdiv(n1 - n0, 512)
    if KC > 8:
        assert x_pool is not None
        _wt16 = p.ring("swX16", 32, [128, 512], BF16)[24:32]
        wbufs = [_wt16[0:4], _wt16[4:8]]

        def wtile(buf, kc):
            return wbufs[buf][kc // 4], (kc % 4) * 128
    else:
        wbig = p.ring(f"swW{KC}", 2, [128, KC * 128], BF16)

        def wtile(buf, kc):
            return wbig[buf], kc * 128
    if x_pool is not None:
        ptiles, pKC, pNCH, pn0 = x_pool
        assert pKC == KC
    elif KC <= 8:
        nxb = 2 * KC
        xt = p.ring("swX8", 16, [128, 512], BF16)[:nxb]
    else:
        nxb = KC
        xt = p.ring("swX16", 32, [128, 512], BF16)[:nxb]
    ob = p.ring("swO" + str(out_dt), 2, [128, 512], out_dt)
    bb = p.ring("swB", 2, [128, 1], F32)
    pss = p.ps[:nbank]
    pend = []

    def _flush2():
        if not pend:
            return
        items = list(pend)
        pend.clear()
        evac2(p, items)
        for (ps_, o_, b_, ft_, fw_, a_) in items:
            @p.step("gpsimd", reads=[o_], dma=True)
            def _(g, o_=o_, ft_=ft_, fw_=fw_, a_=a_):
                return g.dma_start(
                    out=O_ap[o_row0 + ft_ * 128:o_row0 + ft_ * 128 + fw_, a_:a_ + 512],
                    in_=o_[0:fw_, :])

    for ft in range(FT):
        fw = min(128, F_out - ft * 128)
        wwr = list(wbufs[ft % 2]) if KC > 8 else [wbig[ft % 2]]

        @p.step("gpsimd", writes=wwr, dma=True)
        def _(g, ft=ft, fw=fw):
            outs = []
            for kc in range(KC):
                wt_, off = wtile(ft % 2, kc)
                outs.append(g.dma_start(out=wt_[:, off:off + fw],
                                        in_=W_ap[kc * 128:(kc + 1) * 128,
                                                 w_col0 + ft * 128:w_col0 + ft * 128 + fw]))
            return outs
        if bias_ap is not None:
            @p.step("sync", writes=[bb[ft % 2]], dma=True)
            def _(sy, ft=ft, fw=fw):
                return sy.dma_start(out=bb[ft % 2][0:fw, :],
                                    in_=bias_ap[w_col0 + ft * 128:w_col0 + ft * 128 + fw, :])

        for ci in range(NCH):
            a = n0 + ci * 512
            if x_pool is not None:
                xs = [ptiles[kc * pNCH + (a - pn0) // 512] for kc in range(KC)]
            else:
                xs = [xt[(ci % (nxb // KC)) * KC + kc] for kc in range(KC)]
                for kc in range(KC):
                    @p.step("sync", writes=[xs[kc]], dma=True)
                    def _(sy, kc=kc, a=a, xs=xs):
                        return sy.dma_start(out=xs[kc][:, :],
                                            in_=X_ap[x_row0 + kc * 128:x_row0 + (kc + 1) * 128,
                                                     x_col0 + a:x_col0 + a + 512])
            idx = ft * NCH + ci
            ps = pss[idx % nbank]

            @p.step("tensor", reads=wwr + xs, writes=[ps])
            def _(te, ft=ft, fw=fw, ps=ps, xs=xs):
                last = None
                for kc in range(KC):
                    wt_, off = wtile(ft % 2, kc)
                    last = te.matmul(ps[0:fw, :], wt_[:, off:off + fw],
                                     xs[kc][:, :], start=(kc == 0), stop=(kc == KC - 1))
                return last

            o = ob[idx % 2]
            if evac2 is not None:
                pend.append((ps, o, bb[ft % 2], ft, fw, a))
                if len(pend) == 2:
                    _flush2()
                continue
            if evac is not None:
                evac(p, ps, o, ft, ci, fw)
            else:
                rd = [ps] + ([bb[ft % 2]] if bias_ap is not None else [])

                @p.step("scalar", reads=rd, writes=[o])
                def _(sc, ps=ps, o=o, ft=ft, fw=fw):
                    kw = {}
                    fn = AF.Copy
                    if bias_ap is not None:
                        kw["bias"] = bb[ft % 2][0:fw, :]
                        fn = AF.Identity
                    if acc_scale is not None:
                        kw["scale"] = acc_scale
                        fn = AF.Identity
                    return sc.activation(out=o[0:fw, :], in_=ps[0:fw, :], func=fn, **kw)

            @p.step("gpsimd", reads=[o], dma=True)
            def _(g, o=o, ft=ft, fw=fw, a=a):
                return g.dma_start(out=O_ap[o_row0 + ft * 128:o_row0 + ft * 128 + fw, a:a + 512],
                                   in_=o[0:fw, :])
    if evac2 is not None:
        _flush2()


def sweep_proj_tm(p, X_ap, W_ap, O_ap, F_out, kdim=D, n0=0, n1=S, x_row0=0,
                  bias_ap=None, w_col0=0, out_dt=BF16, evac=None, consts=None,
                  o_col0=0, nbank=4, x_pool=None):
    KC = kdim // 128
    FCH = cdiv(F_out, 512)
    nt0, nt1 = n0 // 128, n1 // 128
    wt = p.ring("swX16", 32, [128, 512], BF16)[16:16 + KC]
    assert F_out <= 512, "sweep_proj_tm assumes F_out<=512"
    xt = p.ring(f"stX{KC}", 2, [128, KC * 128], BF16) if x_pool is None else None
    ob = p.ring("stO" + str(out_dt), 2, [128, 512], out_dt)
    pss = p.ps[:nbank]
    brow = p.sb("stBr", [1, 512], BF16) if bias_ap is not None else None

    for kc in range(KC):
        @p.step("gpsimd", writes=[wt[kc]], dma=True)
        def _(g, kc=kc):
            return g.dma_start(out=wt[kc][:, 0:F_out],
                               in_=W_ap[kc * 128:(kc + 1) * 128, w_col0:w_col0 + F_out])
    if bias_ap is not None:
        @p.step("gpsimd", writes=[brow], dma=True)
        def _(g):
            return g.dma_start(out=brow[0:1, 0:F_out],
                               in_=bias_ap[0:1, w_col0:w_col0 + F_out])

    if x_pool is not None:
        ptiles, pKC, pNCH, pn0 = x_pool
        assert pKC == KC and x_row0 == 0
    for nt in range(nt0, nt1):
        if x_pool is not None:
            pci = (nt * 128 - pn0) // 512
            off = (nt * 128 - pn0) % 512
            xsl = [ptiles[kc * pNCH + pci] for kc in range(KC)]

            def xop(kc):
                return xsl[kc][:, off:off + 128]
            xreads = xsl
        else:
            xv = xt[nt % 2]

            @p.step("sync", writes=[xv], dma=True)
            def _(sy, nt=nt, xv=xv):
                return [sy.dma_start(out=xv[:, kc * 128:(kc + 1) * 128],
                                     in_=X_ap[x_row0 + kc * 128:x_row0 + (kc + 1) * 128,
                                              nt * 128:(nt + 1) * 128])
                        for kc in range(KC)]

            def xop(kc, xv=xv):
                return xv[:, kc * 128:(kc + 1) * 128]
            xreads = [xv]
        idx = nt - nt0
        ps = pss[idx % nbank]
        rd = xreads + list(wt[:KC]) + ([brow] if bias_ap is not None else [])

        @p.step("tensor", reads=rd, writes=[ps])
        def _(te, xop=xop, ps=ps):
            last = None
            for kc in range(KC):
                last = te.matmul(ps[:, 0:F_out], xop(kc),
                                 wt[kc][:, 0:F_out], start=(kc == 0),
                                 stop=(bias_ap is None and kc == KC - 1))
            if bias_ap is not None:
                last = te.matmul(ps[:, 0:F_out], consts["ones_row_bf"][0:1, :],
                                 brow[0:1, 0:F_out], start=False, stop=True)
            return last
        o = ob[idx % 2]
        if evac is not None:
            evac(p, ps, o, nt)
        else:
            @p.step("scalar", reads=[ps], writes=[o])
            def _(sc, ps=ps, o=o):
                return sc.activation(out=o[:, 0:F_out], in_=ps[:, 0:F_out], func=AF.Copy)

        @p.step("gpsimd", reads=[o], dma=True)
        def _(g, o=o, nt=nt):
            return g.dma_start(out=O_ap[nt * 128:(nt + 1) * 128, o_col0:o_col0 + F_out],
                               in_=o[:, 0:F_out])


def sweep_ln_fm(p, X_ap, O_ap, consts, n0=0, n1=S, kind="ln", gamma_ap=None,
                scale=1.0, out_dt=BF16, also_tm_ap=None, out_pool_tag=None,
                out_dram=True):
    NCH = (n1 - n0) // 512
    xt = p.ring("swX8", 16, [128, 512], BF16)
    sq = p.ring("lnSq", 2, [128, 512], BF16)
    ob = sq
    assert out_dt == BF16
    ptiles = None
    if out_pool_tag is not None:
        assert out_dt == BF16
        ptiles = xpool(p)
        p._pool_tag = (out_pool_tag, DC, NCH, n0, 0)
    stats = p.ps[4]
    _lnrows = p.ring("lnrows", 5, [1, 512], F32)
    mu_r, var_r, std_r, r_r, mr_r = _lnrows
    bc_r, bc_mr = p.ps[5], p.ps[6]
    bcs = p.ring("f32buf", 2, [128, 1024], F32)[0]
    gam = None
    if gamma_ap is not None:
        gam = [p.sb("lnG", [128, 1], F32) for _ in range(DC)]
        for dc in range(DC):
            @p.step("sync", writes=[gam[dc]], dma=True)
            def _(sy, dc=dc):
                return sy.dma_start(out=gam[dc][:, :],
                                    in_=gamma_ap[dc * 128:(dc + 1) * 128, :])
    tmp = p.ring("lnT", 2, [128, 512], BF16)
    tout = p.ring("t2fXb", 2, [128, D], BF16) if also_tm_ap is not None else None

    for ci in range(NCH):
        a = n0 + ci * 512
        xs = [xt[(ci % 2) * DC + dc] for dc in range(DC)]
        for dc in range(DC):
            @p.step("gpsimd", writes=[xs[dc]], dma=True)
            def _(sy, dc=dc, a=a):
                return sy.dma_start(out=xs[dc][:, :], in_=X_ap[dc * 128:(dc + 1) * 128, a:a + 512])
        for dc in range(DC):
            sqb = sq[dc % 2]

            @p.step("vector", reads=[xs[dc]], writes=[sqb])
            def _(v, dc=dc, sqb=sqb):
                return v.tensor_mul(out=sqb[:, :], in0=xs[dc][:, :], in1=xs[dc][:, :])

            @p.step("tensor", reads=[xs[dc], sqb, consts["ones_col_bf"]],
                    writes=[stats])
            def _(te, dc=dc, sqb=sqb):
                te.matmul(stats[0:1, :], consts["ones_col_bf"][:, :], xs[dc][:, :],
                          start=(dc == 0), stop=(dc == DC - 1))
                return te.matmul(stats[64:65, :], consts["ones_col_bf"][:, :], sqb[:, :],
                                 start=(dc == 0), stop=(dc == DC - 1))
        if kind == "ln":
            @p.step("vector", reads=[stats], writes=[mu_r])
            def _(v):
                return v.tensor_scalar(out=mu_r[:, :], in0=stats[0:1, :], scalar1=1.0 / D,
                                       scalar2=None, op0=ALU.mult)

            @p.step("scalar", reads=[mu_r], writes=[var_r])
            def _(sc):
                return sc.activation(out=var_r[:, :], in_=mu_r[:, :], func=AF.Square)

            @p.step("vector", reads=[stats, var_r], writes=[var_r])
            def _(v):
                return v.scalar_tensor_tensor(out=var_r[:, :], in0=stats[64:65, :],
                                              scalar=1.0 / D, in1=var_r[:, :],
                                              op0=ALU.mult, op1=ALU.subtract)

            @p.step("vector", reads=[var_r], writes=[var_r])
            def _(v):
                return v.tensor_scalar(out=var_r[:, :], in0=var_r[:, :], scalar1=EPS_LN,
                                       scalar2=None, op0=ALU.add)

            @p.step("scalar", reads=[var_r], writes=[std_r])
            def _(sc):
                return sc.activation(out=std_r[:, :], in_=var_r[:, :], func=AF.Sqrt)

            @p.step("vector", reads=[std_r], writes=[r_r])
            def _(v):
                return v.reciprocal(out=r_r[:, :], in_=std_r[:, :])

            @p.step("vector", reads=[mu_r, r_r], writes=[mr_r])
            def _(v):
                return v.scalar_tensor_tensor(out=mr_r[:, :], in0=mu_r[:, :], scalar=-1.0,
                                              in1=r_r[:, :], op0=ALU.mult, op1=ALU.mult)
        else:
            @p.step("scalar", reads=[stats], writes=[std_r])
            def _(sc):
                return sc.activation(out=std_r[:, :], in_=stats[64:65, :], func=AF.Sqrt)

            @p.step("vector", reads=[std_r], writes=[std_r])
            def _(v):
                return v.tensor_scalar(out=std_r[:, :], in0=std_r[:, :], scalar1=1e-12,
                                       scalar2=None, op0=ALU.max)

            @p.step("vector", reads=[std_r], writes=[r_r])
            def _(v):
                return v.reciprocal(out=r_r[:, :], in_=std_r[:, :])

            @p.step("vector", reads=[r_r], writes=[r_r])
            def _(v):
                return v.tensor_scalar(out=r_r[:, :], in0=r_r[:, :], scalar1=float(scale),
                                       scalar2=None, op0=ALU.mult)

            @p.step("vector", writes=[mr_r])
            def _(v):
                return v.memset(mr_r[:, :], 0.0)

        @p.step("tensor", reads=[r_r, mr_r, consts["ones_row_f"]], writes=[bc_r, bc_mr])
        def _(te):
            te.matmul(bc_r[:, :], consts["ones_row_f"][0:1, :], r_r[:, :],
                      start=True, stop=True)
            return te.matmul(bc_mr[:, :], consts["ones_row_f"][0:1, :], mr_r[:, :],
                             start=True, stop=True)

        @p.step("vector", reads=[bc_r, bc_mr], writes=[bcs])
        def _(v):
            v.tensor_copy(out=bcs[:, 0:512], in_=bc_r[:, :])
            return v.tensor_copy(out=bcs[:, 512:1024], in_=bc_mr[:, :])

        for dc in range(DC):
            t = tmp[dc % 2]
            o = ptiles[dc * NCH + ci] if ptiles is not None else ob[dc % 2]

            @p.step("vector", reads=[xs[dc], bcs], writes=[t])
            def _(v, dc=dc, t=t):
                return v.tensor_mul(out=t[:, :], in0=xs[dc][:, :], in1=bcs[:, 0:512])

            if kind == "ln":
                @p.step("vector", reads=[t, bcs], writes=[o])
                def _(v, t=t, o=o):
                    return v.tensor_add(out=o[:, :], in0=t[:, :], in1=bcs[:, 512:1024])
            else:
                @p.step("vector", reads=[t] + ([gam[dc]] if gam else []), writes=[o])
                def _(v, t=t, o=o, dc=dc):
                    if gam:
                        return v.tensor_scalar(out=o[:, :], in0=t[:, :],
                                               scalar1=gam[dc][:, :], scalar2=None,
                                               op0=ALU.mult)
                    return v.tensor_copy(out=o[:, :], in_=t[:, :])

            if out_dram:
                @p.step("gpsimd", reads=[o], dma=True)
                def _(g, o=o, dc=dc, a=a):
                    return g.dma_start(out=O_ap[dc * 128:(dc + 1) * 128, a:a + 512], in_=o[:, :])

            if also_tm_ap is not None:
                tp = p.psT
                to = tout[dc % 2]

                @p.step("tensor", reads=[o, consts["ident_bf"]], writes=[tp])
                def _(te, o=o, tp=tp):
                    last = None
                    for q in range(4):
                        last = te.transpose(tp[:, q * 128 + 0:q * 128 + 128],
                                            o[:, q * 128:(q + 1) * 128],
                                            consts["ident_bf"][:, :])
                    return last

                @p.step("vector", reads=[tp], writes=[to])
                def _(v, tp=tp, to=to):
                    return v.tensor_copy(out=to[:, :], in_=tp[:, :])

                @p.step("gpsimd", reads=[to], dma=True)
                def _(g, to=to, dc=dc, a=a):
                    return [g.dma_start(
                        out=also_tm_ap[a + q * 128:a + (q + 1) * 128,
                                       dc * 128:(dc + 1) * 128],
                        in_=to[:, q * 128:(q + 1) * 128]) for q in range(4)]


def sweep_add_fm(p, A_ap, B_ap, O_ap, n0=0, n1=S, scale_b=None, C_ap=None,
                 scale_c=None, out_dt=BF16, rows=D):
    RT = rows // 128
    RN = 2
    at = p.ring("adA", RN, [128, 512], BF16)
    bt = p.ring("adB", RN, [128, 512], BF16)
    ct = p.ring("adC", RN, [128, 512], BF16) if C_ap is not None else None
    tt = p.ring("adT", 4, [128, 512], F32)[:RN]
    ob = p.ring("adO", RN, [128, 512], out_dt)
    NCH = (n1 - n0) // 512
    for rt in range(RT):
        for ci in range(NCH):
            a = n0 + ci * 512
            idx = rt * NCH + ci
            av, bv = at[idx % RN], bt[idx % RN]
            o = ob[idx % RN]

            @p.step("gpsimd", writes=[av], dma=True)
            def _(sy, rt=rt, a=a, av=av):
                return sy.dma_start(out=av[:, :], in_=A_ap[rt * 128:(rt + 1) * 128, a:a + 512])

            @p.step("gpsimd", writes=[bv], dma=True)
            def _(sy, rt=rt, a=a, bv=bv):
                return sy.dma_start(out=bv[:, :], in_=B_ap[rt * 128:(rt + 1) * 128, a:a + 512])
            if C_ap is not None:
                cv = ct[idx % RN]

                @p.step("gpsimd", writes=[cv], dma=True)
                def _(sy, rt=rt, a=a, cv=cv):
                    return sy.dma_start(out=cv[:, :], in_=C_ap[rt * 128:(rt + 1) * 128, a:a + 512])
            t = tt[idx % RN]

            @p.step("vector", reads=[av, bv], writes=[t])
            def _(v, av=av, bv=bv, t=t):
                if scale_b is None:
                    return v.tensor_add(out=t[:, :], in0=av[:, :], in1=bv[:, :])
                return v.scalar_tensor_tensor(out=t[:, :], in0=bv[:, :], scalar=float(scale_b),
                                              in1=av[:, :], op0=ALU.mult, op1=ALU.add)
            if C_ap is not None:
                cv = ct[idx % RN]

                @p.step("vector", reads=[t, cv], writes=[o])
                def _(v, t=t, cv=cv, o=o):
                    return v.scalar_tensor_tensor(out=o[:, :], in0=cv[:, :], scalar=float(scale_c),
                                                  in1=t[:, :], op0=ALU.mult, op1=ALU.add)
            else:
                @p.step("vector", reads=[t], writes=[o])
                def _(v, t=t, o=o):
                    return v.tensor_copy(out=o[:, :], in_=t[:, :])

            @p.step("gpsimd", reads=[o], dma=True)
            def _(g, o=o, rt=rt, a=a):
                return g.dma_start(out=O_ap[rt * 128:(rt + 1) * 128, a:a + 512], in_=o[:, :])


def collective_pair(p, in_ap, out_ap, op=ALU.add, groups=None):
    if groups is None:
        groups = [[0, 1], [2, 3], [4, 5], [6, 7]]
    final = dict(p.ecnt)

    def body(g):
        for e in ENGINES:
            if final[e] > 0:
                g.wait_ge(p.esem[e], final[e])
        ins = g.collective_compute(
            "AllReduce", op,
            replica_groups=groups,
            ins=[in_ap.tensor.ap().opt()],
            outs=[out_ap.tensor.ap().opt()],
        )
        ins.then_inc(p.esem["cc"])
    p.block.gpsimd(body)
    p.ecnt["cc"] += 1
    fence = p.ring("ccfence", 1, [128, 128], BF16)[0]
    rr = min(128, out_ap.shape[0])
    cc2 = min(128, out_ap.shape[1])

    @p.step("gpsimd", writes=[fence], dma=True)
    def _(g):
        g.wait_ge(p.esem["cc"], p.ecnt["cc"])
        return g.dma_start(out=fence[0:rr, 0:cc2], in_=out_ap[0:rr, 0:cc2])
    p.barrier()


def wait_cc(p, ename, n=None):
    pass


"""Launch builders. Each build_X() returns (nc, in_names) and a matching
host-side input-prep function make_inputs_X(inputs, consts, core) -> dict.

Core roles: core c -> batch b=c//2, half j=c%2; heads 8j..8j+8, experts 4j..4j+4.
"""

HD = 64
NH_LOC = 8
CHUNK = 128



def phase_tm2fm(p, X_tm_ap, O_fm_ap, consts, in_dt=F32, rows=S, cols=D):
    RT = rows // 128
    CC = cols // 128
    assert cols == D
    xf = p.ring("f32buf", 2, [128, cols], F32)
    x8 = p.ring("swX8", 16, [128, 512], BF16)
    for t in range(RT):
        a = xf[t % 2]
        xb2 = [x8[(t % 2) * 2], x8[(t % 2) * 2 + 1]]
        ob2 = [x8[4 + (t % 2) * 2], x8[4 + (t % 2) * 2 + 1]]

        @p.step("sync", writes=[a], dma=True)
        def _(sy, t=t, a=a):
            return sy.dma_start(out=a[:, :], in_=X_tm_ap[t * 128:(t + 1) * 128, :])

        @p.step("vector", reads=[a], writes=xb2)
        def _(v, a=a, xb2=xb2):
            v.tensor_copy(out=xb2[0][:, :], in_=a[:, 0:512])
            return v.tensor_copy(out=xb2[1][:, :], in_=a[:, 512:1024])

        tpv = p.psT

        @p.step("tensor", reads=xb2 + [consts["ident_bf"]], writes=[tpv])
        def _(te, xb2=xb2, tpv=tpv):
            last = None
            for dc in range(CC):
                last = te.transpose(tpv[:, dc * 128:(dc + 1) * 128],
                                    xb2[dc // 4][:, (dc % 4) * 128:(dc % 4) * 128 + 128],
                                    consts["ident_bf"][:, :])
            return last

        @p.step("vector", reads=[tpv], writes=ob2)
        def _(v, ob2=ob2, tpv=tpv):
            v.tensor_copy(out=ob2[0][:, :], in_=tpv[:, 0:512])
            return v.tensor_copy(out=ob2[1][:, :], in_=tpv[:, 512:1024])

        @p.step("gpsimd", reads=ob2, dma=True)
        def _(g, ob2=ob2, t=t):
            return [g.dma_start(out=O_fm_ap[dc * 128:(dc + 1) * 128, t * 128:(t + 1) * 128],
                                in_=ob2[dc // 4][:, (dc % 4) * 128:(dc % 4) * 128 + 128])
                    for dc in range(CC)]


def evac2_gelu(p, items):
    rd = []
    wr = []
    for (ps_, o_, b_, ft_, fw_, a_) in items:
        rd += [ps_, b_]
        wr.append(o_)

    @p.step("scalar", reads=rd, writes=wr)
    def _(sc):
        last = None
        for (ps_, o_, b_, ft_, fw_, a_) in items:
            last = sc.activation(out=o_[0:fw_, :], in_=ps_[0:fw_, :], func=AF.Gelu,
                                 bias=b_[0:fw_, :])
        return last


def evac_elu1(p, ps, o, ft, ci, fw=128):
    adt = p.ring("adT", 4, [128, 512], F32)
    s1 = adt[ci % 2]
    s2 = adt[2 + ci % 2]

    @p.step("vector", reads=[ps], writes=[s1])
    def _(v):
        return v.tensor_scalar(out=s1[0:fw, :], in0=ps[0:fw, :], scalar1=0.0,
                               scalar2=None, op0=ALU.min)

    @p.step("scalar", reads=[s1], writes=[s2])
    def _(sc):
        return sc.activation(out=s2[0:fw, :], in_=s1[0:fw, :], func=AF.Exp)

    @p.step("vector", reads=[ps], writes=[s1])
    def _(v):
        return v.tensor_relu(out=s1[0:fw, :], in_=ps[0:fw, :])

    @p.step("vector", reads=[s1, s2], writes=[o])
    def _(v):
        return v.tensor_add(out=o[0:fw, :], in0=s1[0:fw, :], in1=s2[0:fw, :])



def _io(p, pre, pfx, standalone, debug):
    pre = pre or {}

    def IN(n, sh, dt=F32):
        return pre[n] if n in pre else p.inp(pfx + n, sh, dt)

    def OUT(n, sh, dt=BF16):
        if n in pre:
            return pre[n]
        return p.outp(pfx + n, sh, dt) if standalone else p.dram(pfx + n, sh, dt)
    return IN, OUT



def build_A(debug=False, p=None, pre=None, pfx=""):
    standalone = p is None
    if standalone:
        p = P("A")
    IN, OUT = _io(p, pre, pfx, standalone, debug)
    x_in = IN("x_in", [S, D], F32)
    wq = IN("wq", [D, 512], F32)
    wk = IN("wk", [D, 512], F32)
    wv = IN("wv", [D, 512], F32)
    wo = IN("wo", [512, D], F32)
    consts = {}
    cs = load_consts(p, ("ident", "ones_col", "ones_row", "tri_kq"))
    consts.update(cs)

    mk = p.outp if debug else p.dram
    xfm = mk("xfm", [D, S])
    hs = mk("hs", [D, S])
    qf = mk("qf", [512, S])
    kf = mk("kf", [512, S])
    ktm = mk("ktm", [S, 512])
    vtm = mk("vtm", [S, 512])
    of = mk("of", [512, S])
    cc_in = p.dram("cc_in", [D, S])
    cc_out = p.dram("cc_out", [D, S])
    x1_out = OUT("x1", [D, S])
    hs1_out = OUT("hs1", [D, S])

    phase_tm2fm(p, x_in, xfm, consts)
    p.barrier()
    sweep_ln_fm(p, xfm, hs, consts, kind="ln", out_pool_tag="hs", out_dram=False)
    p.barrier()
    hs_pool = pool_use(p, "hs")
    sweep_proj_fm(p, wq, hs, qf, 512, evac=evac_elu1, x_pool=hs_pool)
    p.barrier()
    sweep_proj_fm(p, wk, hs, kf, 512, evac=evac_elu1, x_pool=hs_pool)
    p.barrier()

    def evac_elu1_tm(p_, ps, o, nt):
        evac_elu1(p_, ps, o, 0, nt, fw=128)
    sweep_proj_tm(p, hs, wk, ktm, 512, evac=evac_elu1_tm, consts=consts,
                  x_pool=hs_pool)
    p.barrier()
    sweep_proj_tm(p, hs, wv, vtm, 512, consts=consts, x_pool=hs_pool)
    p.barrier()

    NCH = S // CHUNK
    _sw8 = p.ring("swX8", 16, [128, 512], BF16)
    kfc = _sw8[0:2]
    qfc = _sw8[2:4]
    ktc = _sw8[4:6]
    vtc = _sw8[6:8]
    psA = [p.ps[4], p.ps[5]]
    psO = [p.ps[0], p.ps[1]]
    psB = [p.ps[2], p.ps[2]]
    psS = [p.ps[3], p.ps[3]]
    amask = p.ring("laM", 2, [128, 128], BF16)
    denr = p.ring("laD", 2, [1, 128], F32)
    bcs = p.ring("laBc", 2, [64, 128], F32)
    otile = p.ring("laOt", 2, [128, 128], BF16)
    _adt = p.ring("adT", 4, [128, 512], F32)
    Sf = [_adt[2 + h // 4][0:64, 65 * (h % 4):65 * (h % 4) + 65] for h in range(NH_LOC)]
    Sb = [p.sb(f"laSb{h}", [128, 65], BF16) for h in range(NH_LOC)]

    for h in range(NH_LOC):
        @p.step("vector", writes=[Sf[h]])
        def _(v, h=h):
            return v.memset(Sf[h][:, :], 0.0)

        @p.step("vector", reads=[Sf[h]], writes=[Sb[h]])
        def _(v, h=h):
            v.tensor_copy(out=Sb[h][0:64, :], in_=Sf[h][:, :])
            return v.tensor_copy(out=Sb[h][64:128, :], in_=Sf[h][:, :])

    for c in range(NCH):
        a = c * CHUNK
        kfv, qfv, ktv, vtv = kfc[c % 2], qfc[c % 2], ktc[c % 2], vtc[c % 2]

        @p.step("sync", writes=[kfv], dma=True)
        def _(sy, a=a, kfv=kfv):
            return [sy.dma_start(out=kfv[:, r * 128:(r + 1) * 128],
                                 in_=kf[r * 128:(r + 1) * 128, a:a + 128])
                    for r in range(4)]

        @p.step("sync", writes=[qfv], dma=True)
        def _(sy, a=a, qfv=qfv):
            return [sy.dma_start(out=qfv[:, r * 128:(r + 1) * 128],
                                 in_=qf[r * 128:(r + 1) * 128, a:a + 128])
                    for r in range(4)]

        @p.step("sync", writes=[ktv], dma=True)
        def _(sy, a=a, ktv=ktv):
            return sy.dma_start(out=ktv[:, :], in_=ktm[a:a + 128, :])

        @p.step("sync", writes=[vtv], dma=True)
        def _(sy, a=a, vtv=vtv):
            return sy.dma_start(out=vtv[:, :], in_=vtm[a:a + 128, :])

        for h in range(NH_LOC):
            r, base = h // 2, 64 * (h % 2)
            kslice = kfv[:, r * 128:(r + 1) * 128]
            idx = (c * NH_LOC + h)
            pa = psA[idx % 2]
            po = psO[idx % 2]
            pb = psB[idx % 2]
            pS = psS[idx % 2]
            am = amask[idx % 2]
            dr = denr[idx % 2]
            bc = bcs[idx % 2]
            ot = otile[idx % 2]

            @p.step("tensor", reads=[kfv, qfv], writes=[pa])
            def _(te, r=r, base=base, kfv=kfv, qfv=qfv, pa=pa):
                return te.matmul(pa[:, 0:128],
                                 kfv[base:base + 64, r * 128:(r + 1) * 128],
                                 qfv[base:base + 64, r * 128:(r + 1) * 128],
                                 start=True, stop=True)

            @p.step("vector", reads=[pa, consts["tri_kq_bf"]], writes=[am])
            def _(v, pa=pa, am=am):
                return v.tensor_mul(out=am[:, :], in0=pa[:, 0:128], in1=consts["tri_kq_bf"][:, :])

            @p.step("tensor", reads=[am, vtv, Sb[h], consts["ones_col_bf"], qfv], writes=[po])
            def _(te, h=h, r=r, base=base, am=am, vtv=vtv, qfv=qfv, po=po):
                hl = 64 * h
                te.matmul(po[0:64, 0:128], vtv[:, hl:hl + 64], am[:, :], start=True, stop=False)
                te.matmul(po[0:64, 0:128], Sb[h][base:base + 64, 0:64],
                          qfv[base:base + 64, r * 128:(r + 1) * 128], start=False, stop=True)
                te.matmul(po[64:65, 0:128], consts["ones_col_bf"][:, :], am[:, :],
                          start=True, stop=False)
                return te.matmul(po[64:65, 0:128], Sb[h][base:base + 64, 64:65],
                                 qfv[base:base + 64, r * 128:(r + 1) * 128],
                                 start=False, stop=True)

            @p.step("vector", reads=[po], writes=[dr])
            def _(v, po=po, dr=dr):
                return v.tensor_scalar(out=dr[:, :], in0=po[64:65, 0:128], scalar1=1e-6,
                                       scalar2=None, op0=ALU.add)

            @p.step("vector", reads=[dr], writes=[dr])
            def _(v, dr=dr):
                return v.reciprocal(out=dr[:, :], in_=dr[:, :])

            @p.step("tensor", reads=[dr, consts["ones_row_f"]], writes=[pb])
            def _(te, dr=dr, pb=pb):
                return te.matmul(pb[0:64, 0:128], consts["ones_row_f"][0:1, 0:64], dr[:, :],
                                 start=True, stop=True)

            @p.step("vector", reads=[pb], writes=[bc])
            def _(v, pb=pb, bc=bc):
                return v.tensor_copy(out=bc[:, :], in_=pb[0:64, 0:128])

            @p.step("vector", reads=[po, bc], writes=[ot])
            def _(v, po=po, bc=bc, ot=ot):
                return v.tensor_mul(out=ot[0:64, :], in0=po[0:64, 0:128], in1=bc[:, :])

            @p.step("gpsimd", reads=[ot], dma=True)
            def _(g, ot=ot, h=h, a=a):
                return g.dma_start(out=of[h * 64:(h + 1) * 64, a:a + 128], in_=ot[0:64, :])

            @p.step("tensor", reads=[ktv, vtv, consts["ones_col_bf"]], writes=[pS])
            def _(te, h=h, ktv=ktv, vtv=vtv, pS=pS):
                hl = 64 * h
                te.matmul(pS[0:64, 0:64], ktv[:, hl:hl + 64], vtv[:, hl:hl + 64],
                          start=True, stop=True)
                return te.matmul(pS[0:64, 64:65], ktv[:, hl:hl + 64],
                                 consts["ones_col_bf"][:, :], start=True, stop=True)

            @p.step("vector", reads=[pS, Sf[h]], writes=[Sf[h]])
            def _(v, h=h, pS=pS):
                return v.tensor_add(out=Sf[h][:, :], in0=Sf[h][:, :], in1=pS[0:64, 0:65])

            @p.step("vector", reads=[Sf[h]], writes=[Sb[h]])
            def _(v, h=h):
                v.tensor_copy(out=Sb[h][0:64, :], in_=Sf[h][:, :])
                return v.tensor_copy(out=Sb[h][64:128, :], in_=Sf[h][:, :])

    p.barrier()
    sweep_proj_fm(p, wo, of, cc_in, D, kdim=512)
    p.barrier()
    if debug:
        cc_dbg = p.outp("cc_dbg", [D, S])

        @p.step("gpsimd", dma=True)
        def _(g):
            return g.dma_start(out=cc_dbg[:, :], in_=cc_in[:, :])
        p.barrier()
    collective_pair(p, cc_in, cc_out)
    wait_cc(p, "sync")
    wait_cc(p, "gpsimd")
    sweep_add_fm(p, xfm, cc_out, x1_out)
    p.barrier()
    sweep_ln_fm(p, x1_out, hs1_out, consts, kind="ln", out_pool_tag="hs1")
    return p.done() if standalone else p


def make_inputs_A(inp, consts, core):
    b, j = core // 2, core % 2
    x = np.ascontiguousarray(inp["L0_attn_x"][b])
    d = {
        "x_in": x,
        "wq": np.ascontiguousarray(inp["L0_attn_wq"][:, 512 * j:512 * (j + 1)]),
        "wk": np.ascontiguousarray(inp["L0_attn_wk"][:, 512 * j:512 * (j + 1)]),
        "wv": np.ascontiguousarray(inp["L0_attn_wv"][:, 512 * j:512 * (j + 1)]),
        "wo": np.ascontiguousarray(inp["L0_attn_wo"][512 * j:512 * (j + 1), :]),
        "c_ident": consts["ident"], "c_ones_col": consts["ones_col"],
        "c_ones_row": consts["ones_row"], "c_tri_kq": consts["tri_kq"],
    }
    return d


C_STAB0 = 12.0
RS32 = 0.17677669529663687


def build_B(debug=False, p=None, pre=None, pfx=""):
    standalone = p is None
    if standalone:
        p = P("B")
    IN, OUT = _io(p, pre, pfx, standalone, debug)
    hs1 = IN("hs1", [D, S], BF16)
    qw = IN("qw", [D, 512], F32)
    qb = IN("qb", [512, 1], F32)
    kw = IN("kw", [D, 512], F32)
    kb = IN("kb", [512, 1], F32)
    vw = IN("vw", [D, 264], F32)
    vb = IN("vb", [1, 264], F32)
    ow = IN("ow", [256, D], F32)
    ob_ = IN("ob", [D, 1], F32)
    rw = IN("rw", [D, 8], F32)
    rb = IN("rb", [1, 8], F32)
    esel = IN("esel", [8, 8], F32)
    cs = load_consts(p, ("ident", "ones_col", "ones_row"))

    mk = p.outp if debug else p.dram
    qf = mk("qf", [512, S])
    kf = mk("kf", [512, S])
    vtm = mk("vtm", [S, 264])
    aof = mk("aof", [256, S])
    cc_in = p.dram("cc_in", [D, S])
    cc_out = p.dram("cc_out", [D, S])
    mla_out = OUT("mla", [D, S])
    comb_out = OUT("comb", [8, S])

    hs1_pool = pool_use(p, "hs1")
    sweep_proj_fm(p, qw, hs1, qf, 512, bias_ap=qb, x_pool=hs1_pool)
    p.barrier()
    sweep_proj_fm(p, kw, hs1, kf, 512, bias_ap=kb, x_pool=hs1_pool)
    p.barrier()
    sweep_proj_tm(p, hs1, vw, vtm, 264, bias_ap=vb, consts=cs, x_pool=hs1_pool)
    p.barrier()

    nbias = p.sb("nbias", [128, 1], F32)

    @p.step("vector", writes=[nbias])
    def _(v):
        return v.memset(nbias[:, :], -C_STAB0)

    qtiles = p.ring("swX8", 16, [128, 512], BF16)[:8]
    vt = p.ring("fbVt", 2, [128, 264], BF16)
    es = p.ring("swX8", 16, [128, 512], BF16)[12:16]
    drow = p.ring("fbD", 2, [1, 512], F32)
    bcsb = [t[0:32, :] for t in p.ring("adT", 4, [128, 512], F32)[:2]]
    otile = p.ring("swX8", 16, [128, 512], BF16)[10:12]
    psS = [p.ps[0], p.ps[1]]
    psB = p.ps[6]

    kf_tiles, _, _, _ = pool_load(p, "Bkf", kf, 4, 8)

    def kslice(r, kt):
        t = kf_tiles[r * 8 + kt // 4]
        off = (kt % 4) * 128
        return t, t[:, off:off + 128]

    for qc in range(8):
        a = qc * 512
        qs = [qtiles[(qc % 2) * 4 + r] for r in range(4)]
        for r in range(4):
            @p.step("sync", writes=[qs[r]], dma=True)
            def _(sy, r=r, a=a, qs=qs):
                return sy.dma_start(out=qs[r][:, :], in_=qf[r * 128:(r + 1) * 128, a:a + 512])
        for hp in range(2):
            for kt in range(NT):
                vv = vt[kt % 2]

                @p.step("sync", writes=[vv], dma=True)
                def _(sy, kt=kt, vv=vv):
                    return sy.dma_start(out=vv[:, :], in_=vtm[kt * 128:(kt + 1) * 128, :])

                for pair in range(2):
                    for g in (2 * pair, 2 * pair + 1):
                        h = hp * 4 + g
                        rq, bq = h // 2, 64 * (h % 2)
                        ps_ = psS[g % 2]
                        ktile, kap = kslice(rq, kt)

                        @p.step("tensor", reads=[ktile, qs[rq]], writes=[ps_])
                        def _(te, rq=rq, bq=bq, ps_=ps_, kap=kap, qs=qs):
                            return te.matmul(ps_[:, :], kap[bq:bq + 32, :],
                                             qs[rq][bq:bq + 32, :], start=True, stop=True)

                    @p.step("scalar", reads=[psS[0], psS[1], nbias],
                            writes=[es[2 * pair], es[2 * pair + 1]])
                    def _(sc, pair=pair):
                        sc.activation(out=es[2 * pair][:, :], in_=psS[0][:, :],
                                      func=AF.Exp, bias=nbias[:, :], scale=RS32)
                        return sc.activation(out=es[2 * pair + 1][:, :], in_=psS[1][:, :],
                                             func=AF.Exp, bias=nbias[:, :], scale=RS32)

                    for g in (2 * pair, 2 * pair + 1):
                        h = hp * 4 + g
                        bank = p.ps[2 + g]
                        ev = es[g]

                        @p.step("tensor", reads=[ev, vv], writes=[bank])
                        def _(te, ev=ev, vv=vv, h=h, kt=kt, bank=bank):
                            return te.matmul(bank[0:33, :], vv[:, 33 * h:33 * h + 33],
                                             ev[:, :], start=(kt == 0), stop=(kt == NT - 1))

            for g in range(4):
                h = hp * 4 + g
                bank = p.ps[2 + g]
                dr = drow[g % 2]
                bc = bcsb[g % 2]
                ot = otile[g % 2]

                @p.step("vector", reads=[bank], writes=[dr])
                def _(v, dr=dr, bank=bank):
                    return v.reciprocal(out=dr[:, :], in_=bank[32:33, :])

                @p.step("tensor", reads=[dr, cs["ones_row_f"]], writes=[psB])
                def _(te, dr=dr):
                    return te.matmul(psB[0:32, :], cs["ones_row_f"][0:1, 0:32], dr[:, :],
                                     start=True, stop=True)

                @p.step("vector", reads=[psB], writes=[bc])
                def _(v, bc=bc):
                    return v.tensor_copy(out=bc[:, :], in_=psB[0:32, :])

                @p.step("vector", reads=[bank, bc], writes=[ot])
                def _(v, bc=bc, ot=ot, bank=bank):
                    return v.tensor_mul(out=ot[0:32, :], in0=bank[0:32, :], in1=bc[:, :])

                @p.step("gpsimd", reads=[ot], dma=True)
                def _(g_, ot=ot, h=h, a=a):
                    return g_.dma_start(out=aof[32 * h:32 * h + 32, a:a + 512], in_=ot[0:32, :])

    p.barrier()
    sweep_proj_fm(p, ow, aof, cc_in, D, kdim=256, bias_ap=ob_)
    p.barrier()
    collective_pair(p, cc_in, cc_out)
    wait_cc(p, "sync")
    wait_cc(p, "gpsimd")
    @p.step("gpsimd", dma=True)
    def _(g):
        return g.dma_start(out=mla_out[:, :], in_=cc_out[:, :])
    p.barrier()

    rwt = [p.sb(f"rwt{kc}", [128, 8], BF16) for kc in range(DC)]
    rbt = p.sb("rbt", [1, 8], BF16)
    for kc in range(DC):
        @p.step("gpsimd", writes=[rwt[kc]], dma=True)
        def _(g, kc=kc):
            return g.dma_start(out=rwt[kc][:, :], in_=rw[kc * 128:(kc + 1) * 128, :])

    @p.step("gpsimd", writes=[rbt], dma=True)
    def _(g):
        return g.dma_start(out=rbt[:, :], in_=rb[:, :])

    mlt = p.ring("t2fXb", 2, [128, D], BF16)
    lg = p.ring("rtL", 2, [128, 8], F32)
    m1 = p.ring("rtm1", 2, [128, 1], F32)
    m2 = p.ring("rtm2", 2, [128, 1], F32)
    den = p.ring("rtden", 2, [128, 1], F32)
    prb = p.ring("rtP", 2, [128, 8], F32)
    sel = p.ring("rtS", 2, [128, 8], F32)
    cmb = p.ring("rtC", 2, [128, 8], BF16)
    cps = p.psT
    cfm = p.ring("rtCf", 2, [8, 128], BF16)
    cfl = p.ring("rtCl", 2, [8, 128], BF16)
    selb = p.sb("selb", [8, 8], BF16)

    @p.step("gpsimd", writes=[selb], dma=True)
    def _(g):
        return g.dma_start(out=selb[:, :], in_=esel[:, :])

    for nt in range(NT):
        ml = mlt[nt % 2]

        @p.step("sync", writes=[ml], dma=True)
        def _(sy, nt=nt, ml=ml):
            return [sy.dma_start(out=ml[:, kc * 128:(kc + 1) * 128],
                                 in_=mla_out[kc * 128:(kc + 1) * 128, nt * 128:(nt + 1) * 128])
                    for kc in range(DC)]
        ps_ = p.ps[4]

        @p.step("tensor", reads=[ml, rbt] + rwt, writes=[ps_])
        def _(te, ml=ml, ps_=ps_):
            for kc in range(DC):
                te.matmul(ps_[:, 0:8], ml[:, kc * 128:(kc + 1) * 128], rwt[kc][:, :],
                          start=(kc == 0), stop=False)
            return te.matmul(ps_[:, 0:8], cs["ones_row_bf"][0:1, :], rbt[0:1, :],
                             start=False, stop=True)
        l, mm1, mm2, dd, pb, se, cb = (lg[nt % 2], m1[nt % 2], m2[nt % 2],
                                       den[nt % 2], prb[nt % 2], sel[nt % 2], cmb[nt % 2])

        @p.step("vector", reads=[ps_], writes=[l])
        def _(v, l=l, ps_=ps_):
            return v.tensor_copy(out=l[:, :], in_=ps_[:, 0:8])

        @p.step("vector", reads=[l], writes=[mm1])
        def _(v, l=l, mm1=mm1):
            return v.reduce_max(out=mm1[:, :], in_=l[:, :], axis=AX.X)

        @p.step("vector", reads=[mm1], writes=[mm1])
        def _(v, mm1=mm1):
            return v.tensor_scalar(out=mm1[:, :], in0=mm1[:, :], scalar1=-1.0,
                                   scalar2=None, op0=ALU.mult)

        @p.step("scalar", reads=[l, mm1], writes=[pb, dd])
        def _(sc, l=l, mm1=mm1, pb=pb, dd=dd):
            return sc.activation(out=pb[:, :], in_=l[:, :], func=AF.Exp,
                                 bias=mm1[:, :], accum_out=dd[:, :])

        @p.step("vector", reads=[dd], writes=[dd])
        def _(v, dd=dd):
            return v.reciprocal(out=dd[:, :], in_=dd[:, :])

        @p.step("vector", reads=[pb, dd], writes=[pb])
        def _(v, pb=pb, dd=dd):
            return v.tensor_scalar(out=pb[:, :], in0=pb[:, :], scalar1=dd[:, :],
                                   scalar2=None, op0=ALU.mult)
        @p.step("vector", reads=[pb], writes=[mm1])
        def _(v, pb=pb, mm1=mm1):
            return v.reduce_max(out=mm1[:, :], in_=pb[:, :], axis=AX.X)

        @p.step("vector", reads=[pb, mm1], writes=[se])
        def _(v, pb=pb, mm1=mm1, se=se):
            return v.tensor_scalar(out=se[:, :], in0=pb[:, :], scalar1=mm1[:, :],
                                   scalar2=None, op0=ALU.is_ge)

        @p.step("vector", reads=[se, pb], writes=[se])
        def _(v, se=se, pb=pb):
            return v.scalar_tensor_tensor(out=se[:, :], in0=se[:, :], scalar=-1e30,
                                          in1=pb[:, :], op0=ALU.mult, op1=ALU.add)

        @p.step("vector", reads=[se], writes=[mm2])
        def _(v, se=se, mm2=mm2):
            return v.reduce_max(out=mm2[:, :], in_=se[:, :], axis=AX.X)

        @p.step("vector", reads=[pb, mm2], writes=[se])
        def _(v, pb=pb, mm2=mm2, se=se):
            return v.tensor_scalar(out=se[:, :], in0=pb[:, :], scalar1=mm2[:, :],
                                   scalar2=None, op0=ALU.is_ge)

        @p.step("vector", reads=[se, pb], writes=[se])
        def _(v, se=se, pb=pb):
            return v.tensor_mul(out=se[:, :], in0=se[:, :], in1=pb[:, :])

        @p.step("vector", reads=[se], writes=[dd])
        def _(v, se=se, dd=dd):
            return v.reduce_sum(out=dd[:, :], in_=se[:, :], axis=AX.X)

        @p.step("vector", reads=[dd], writes=[dd])
        def _(v, dd=dd):
            return v.tensor_scalar(out=dd[:, :], in0=dd[:, :], scalar1=1e-8,
                                   scalar2=None, op0=ALU.add)

        @p.step("vector", reads=[dd], writes=[dd])
        def _(v, dd=dd):
            return v.reciprocal(out=dd[:, :], in_=dd[:, :])

        @p.step("vector", reads=[se, dd], writes=[cb])
        def _(v, se=se, dd=dd, cb=cb):
            return v.tensor_scalar(out=cb[:, :], in0=se[:, :], scalar1=dd[:, :],
                                   scalar2=None, op0=ALU.mult)

        @p.step("tensor", reads=[cb, cs["ident_bf"]], writes=[cps])
        def _(te, cb=cb):
            return te.transpose(cps[0:8, 0:128], cb[:, :], cs["ident_bf"][:, :])

        cf = cfm[nt % 2]

        @p.step("vector", reads=[cps], writes=[cf])
        def _(v, cf=cf):
            return v.tensor_copy(out=cf[:, :], in_=cps[0:8, 0:128])

        @p.step("tensor", reads=[selb, cf], writes=[p.ps[5]])
        def _(te, cf=cf):
            return te.matmul(p.ps[5][0:8, 0:128], selb[:, :], cf[:, :],
                             start=True, stop=True)
        cl = cfl[nt % 2]

        @p.step("vector", reads=[p.ps[5]], writes=[cl])
        def _(v, cl=cl):
            return v.tensor_copy(out=cl[:, :], in_=p.ps[5][0:8, 0:128])

        @p.step("gpsimd", reads=[cl], dma=True)
        def _(g, cl=cl, nt=nt):
            return g.dma_start(out=comb_out[:, nt * 128:(nt + 1) * 128], in_=cl[:, :])

    return p.done() if standalone else p


def make_inputs_B(inp, consts, core, hs1, x1=None):
    b, j = core // 2, core % 2
    qw = np.zeros((D, 512), np.float32)
    qb = np.zeros((512, 1), np.float32)
    for h in range(8):
        hg = 8 * j + h
        qw[:, 64 * h:64 * h + 32] = inp["L0_ffn_q_w"][:, 64 * hg:64 * hg + 32]
        qb[64 * h:64 * h + 32, 0] = inp["L0_ffn_q_b"][64 * hg:64 * hg + 32]
    kw = np.zeros((D, 512), np.float32)
    kb = np.zeros((512, 1), np.float32)
    for h in range(8):
        hg = 8 * j + h
        kw[:, 64 * h:64 * h + 32] = inp["L0_ffn_k_w"][:, 32 * hg:32 * hg + 32]
        kb[64 * h:64 * h + 32, 0] = inp["L0_ffn_k_b"][32 * hg:32 * hg + 32]
    vw = np.zeros((D, 264), np.float32)
    vb = np.zeros((1, 264), np.float32)
    for h in range(8):
        hg = 8 * j + h
        vw[:, 33 * h:33 * h + 32] = inp["L0_ffn_v_w"][:, 32 * hg:32 * hg + 32]
        vb[0, 33 * h:33 * h + 32] = inp["L0_ffn_v_b"][32 * hg:32 * hg + 32]
        vb[0, 33 * h + 32] = 1.0
    ow = np.ascontiguousarray(inp["L0_ffn_o_w"][256 * j:256 * (j + 1), :])
    ob = inp["L0_ffn_o_b"].reshape(D, 1) if j == 0 else np.zeros((D, 1), np.float32)
    d = {
        "hs1": hs1, "qw": qw, "qb": qb, "kw": kw, "kb": kb, "vw": vw, "vb": vb,
        "ow": ow, "ob": np.ascontiguousarray(ob),
        "rw": inp["L0_ffn_router_w"].astype(np.float32),
        "rb": inp["L0_ffn_router_b"].reshape(1, 8),
        "esel": np.eye(8, dtype=np.float32)[:, list(range(4 * j, 4 * j + 4)) + list(range(0, 4 * j)) + list(range(4 * j + 4, 8))][:, :8][:, :].T.copy() if False else np.ascontiguousarray(np.roll(np.eye(8, dtype=np.float32), -4 * j, axis=0).T),
        "c_ident": consts["ident"], "c_ones_col": consts["ones_col"],
        "c_ones_row": consts["ones_row"],
    }
    return d



def build_B2(debug=False, p=None, pre=None, pfx=""):
    standalone = p is None
    if standalone:
        p = P("B2")
    IN, OUT = _io(p, pre, pfx, standalone, debug)
    mla = IN("mla", [D, S], BF16)
    comb = IN("comb", [8, S], BF16)
    x1 = IN("x1", [D, S], BF16)
    hs1 = IN("hs1", [D, S], BF16)
    w1s = [IN(f"w1_{e}", [D, 2048], F32) for e in range(4)]
    b1s = [IN(f"b1_{e}", [2048, 1], F32) for e in range(4)]
    w2s = [IN(f"w2_{e}", [2048, D], F32) for e in range(4)]
    b2s = [IN(f"b2_{e}", [D, 1], F32) for e in range(4)]
    cs = load_consts(p, ("ident", "ones_col", "ones_row"))

    mk2 = p.outp if debug else p.dram
    h1d = mk2("h1d", [2048, S])
    moe_p = [mk2(f"moe{e}", [D, S]) for e in range(4)]
    cc_in = p.dram("cc_in", [D, S])
    cc_out = p.dram("cc_out", [D, S])
    x2_out = OUT("x2", [D, S])
    hs2_out = OUT("hs2", [D, S])

    crow2 = p.ring("crow2", 2, [1, 512], BF16)

    tf = p.ring("adT", 4, [128, 512], F32)[:2]

    for e in range(4):
        mla_pool = pool_load(p, "mla", mla, 8, 8)
        sweep_proj_fm(p, w1s[e], mla, h1d, 2048, bias_ap=b1s[e], evac2=evac2_gelu,
                      x_pool=mla_pool)
        p.barrier()

        def mk_evac_comb(base):
            def evac_comb(p_, ps, o, ft, ci, fw, e=e):
                bb = p_.ring("swB", 2, [128, 1], F32)
                a = base + ci * 512
                t = tf[(ft * 8 + ci) % 2]
                ps4 = p_.ps[4]
                cr = crow2[ci % 2]

                @p_.step("sync", writes=[cr], dma=True)
                def _(sy):
                    return sy.dma_start(out=cr[:, :], in_=comb[e:e + 1, a:a + 512])

                @p_.step("tensor", reads=[cr, cs["ones_row_bf"]], writes=[ps4])
                def _(te):
                    return te.matmul(ps4[:, :], cs["ones_row_bf"][0:1, :],
                                     cr[0:1, :], start=True, stop=True)

                @p_.step("vector", reads=[ps, bb[ft % 2]], writes=[t])
                def _(v):
                    return v.tensor_scalar(out=t[0:fw, :], in0=ps[0:fw, :],
                                           scalar1=bb[ft % 2][0:fw, :], scalar2=None,
                                           op0=ALU.add)

                @p_.step("vector", reads=[t, ps4], writes=[o])
                def _(v):
                    return v.tensor_mul(out=o[0:fw, :], in0=t[0:fw, :], in1=ps4[0:fw, :])
            return evac_comb
        for half in range(2):
            hp_pool = pool_load(p, f"h1_{e}_{half}", h1d, 16, 4, n0=half * 2048)
            sweep_proj_fm(p, w2s[e], h1d, moe_p[e], D, kdim=2048, bias_ap=b2s[e],
                          evac=mk_evac_comb(half * 2048), n0=half * 2048,
                          n1=(half + 1) * 2048, x_pool=hp_pool)
            p.barrier()

    at4 = p.ring("swX8", 16, [128, 512], BF16)[:8]
    tt4 = p.ring("adT", 4, [128, 512], F32)
    ob4 = p.ring("m4o", 2, [128, 512], BF16)
    for rt in range(DC):
        for ci in range(8):
            a = ci * 512
            idx = rt * 8 + ci
            avs = [at4[(idx % 2) * 4 + e] for e in range(4)]
            for e in range(4):
                @p.step("gpsimd", writes=[avs[e]], dma=True)
                def _(sy, e=e, rt=rt, a=a, avs=avs):
                    return sy.dma_start(out=avs[e][:, :],
                                        in_=moe_p[e][rt * 128:(rt + 1) * 128, a:a + 512])
            t = tt4[(idx % 2) * 2]
            t2 = tt4[(idx % 2) * 2 + 1]
            o = ob4[idx % 2]

            @p.step("vector", reads=avs[:2], writes=[t])
            def _(v, avs=avs, t=t):
                return v.tensor_add(out=t[:, :], in0=avs[0][:, :], in1=avs[1][:, :])

            @p.step("vector", reads=[t, avs[2]], writes=[t2])
            def _(v, avs=avs, t=t, t2=t2):
                return v.tensor_add(out=t2[:, :], in0=t[:, :], in1=avs[2][:, :])

            @p.step("vector", reads=[t2, avs[3]], writes=[o])
            def _(v, avs=avs, t2=t2, o=o):
                return v.tensor_add(out=o[:, :], in0=t2[:, :], in1=avs[3][:, :])

            @p.step("gpsimd", reads=[o], dma=True)
            def _(g, o=o, rt=rt, a=a):
                return g.dma_start(out=cc_in[rt * 128:(rt + 1) * 128, a:a + 512], in_=o[:, :])
    p.barrier()
    if debug:
        cci_dbg = p.outp("cci_dbg", [D, S])
        cco_dbg = p.outp("cco_dbg", [D, S])

        @p.step("gpsimd", dma=True)
        def _(g):
            return g.dma_start(out=cci_dbg[:, :], in_=cc_in[:, :])
        p.barrier()
    collective_pair(p, cc_in, cc_out)
    wait_cc(p, "sync")
    wait_cc(p, "gpsimd")
    if debug:
        @p.step("gpsimd", dma=True)
        def _(g):
            return g.dma_start(out=cco_dbg[:, :], in_=cc_out[:, :])
        p.barrier()
    sweep_add_fm(p, x1, hs1, x2_out, C_ap=cc_out, scale_c=0.3)
    p.barrier()
    sweep_ln_fm(p, x2_out, hs2_out, cs, kind="ln", out_pool_tag="hs2")
    return p.done() if standalone else p


def make_inputs_B2(inp, consts, core, mla, comb, x1, hs1):
    b, j = core // 2, core % 2
    d = {"mla": mla, "comb": comb,
         "x1": x1, "hs1": hs1,
         "c_ident": consts["ident"], "c_ones_col": consts["ones_col"],
         "c_ones_row": consts["ones_row"]}
    for e in range(4):
        eg = 4 * j + e
        d[f"w1_{e}"] = np.ascontiguousarray(inp["L0_ffn_w1"][eg])
        d[f"b1_{e}"] = np.ascontiguousarray(inp["L0_ffn_b1"][eg].reshape(2048, 1))
        d[f"w2_{e}"] = np.ascontiguousarray(inp["L0_ffn_w2"][eg])
        d[f"b2_{e}"] = np.ascontiguousarray(inp["L0_ffn_b2"][eg].reshape(D, 1))
    return d


C_STAB1 = 12.0
RS64 = 0.125


def build_C(debug=False, p=None, pre=None, pfx=""):
    standalone = p is None
    if standalone:
        p = P("C")
    IN, OUT = _io(p, pre, pfx, standalone, debug)
    hs2 = IN("hs2", [D, S], BF16)
    x2 = IN("x2", [D, S], BF16)
    wq = IN("wq", [D, D], F32)
    bq = IN("bq", [D, 1], F32)
    wk = IN("wk", [D, D], F32)
    bk = IN("bk", [D, 1], F32)
    wv = IN("wv", [D, 520], F32)
    bv = IN("bv", [1, 520], F32)
    wp = IN("wp", [512, D], F32)
    bp_ = IN("bp", [D, 1], F32)
    gam = IN("gam", [D, 1], F32)
    cmask = IN("cmask", [128, 4 * 512], F32)
    cs = load_consts(p, ("ident", "ones_col", "ones_row", "rope_sin", "rope_cos"))

    mk = p.outp if debug else p.dram
    qf = mk("qf", [D, S])
    kf = mk("kf", [D, S])
    vtm = mk("vtm", [S, 520])
    yf = mk("yf", [512, S])
    cc_in = p.dram("cc_in", [D, S])
    cc_out = p.dram("cc_out", [D, S])
    x3_out = OUT("x3", [D, S])
    xn_fm_out = OUT("xn_fm", [D, S])
    xn_tm_out = OUT("xn_tm", [S, D])

    cmk_t = p.ring("swX16", 32, [128, 512], BF16)[16:20]

    def mk_rope_evac(bias_ring_key):
        def evac(p_, ps, o, ft, ci, fw):
            a = ci * 512
            rb = 0
            ac = a
            bb = p_.ring("swB", 2, [128, 1], F32)
            sc1 = [t[0:64, :] for t in p_.ring("adT", 4, [128, 512], F32)]
            raw = p_.ring("ropeR", 2, [64, 512], F32)
            r = raw[ci % 2]
            m1, m2 = sc1[(ci % 2) * 2], sc1[(ci % 2) * 2 + 1]

            @p_.step("scalar", reads=[ps, bb[ft % 2]], writes=[r])
            def _(sc):
                return sc.activation(out=r[:, :], in_=ps[0:64, :], func=AF.Identity,
                                     bias=bb[ft % 2][0:64, :])

            @p_.step("vector", reads=[r, cs["rope_cos_bf"]], writes=[m1])
            def _(v):
                return v.tensor_mul(out=m1[0:32, :], in0=r[0:32, :],
                                    in1=cs["rope_cos_bf"][rb + 0:rb + 32, ac:ac + 512])

            @p_.step("vector", reads=[r, cs["rope_sin_bf"]], writes=[m2])
            def _(v):
                return v.tensor_mul(out=m2[0:32, :], in0=r[32:64, :],
                                    in1=cs["rope_sin_bf"][rb + 32:rb + 64, ac:ac + 512])

            @p_.step("vector", reads=[m1, m2], writes=[o])
            def _(v):
                return v.tensor_sub(out=o[0:32, :], in0=m1[0:32, :], in1=m2[0:32, :])

            @p_.step("vector", reads=[r, cs["rope_cos_bf"]], writes=[m1])
            def _(v):
                return v.tensor_mul(out=m1[0:32, :], in0=r[32:64, :],
                                    in1=cs["rope_cos_bf"][rb + 32:rb + 64, ac:ac + 512])

            @p_.step("vector", reads=[r, cs["rope_sin_bf"]], writes=[m2])
            def _(v):
                return v.tensor_mul(out=m2[0:32, :], in0=r[0:32, :],
                                    in1=cs["rope_sin_bf"][rb + 0:rb + 32, ac:ac + 512])

            @p_.step("vector", reads=[m1, m2, o], writes=[o])
            def _(v):
                return v.tensor_add(out=o[32:64, :], in0=m1[0:32, :], in1=m2[0:32, :])
        return evac

    hs2_pool = pool_use(p, "hs2")
    sweep_proj_fm(p, wq, hs2, qf, D, bias_ap=bq, evac=mk_rope_evac("q"),
                  x_pool=hs2_pool)
    p.barrier()
    sweep_proj_fm(p, wk, hs2, kf, D, bias_ap=bk, evac=mk_rope_evac("k"),
                  x_pool=hs2_pool)
    p.barrier()
    sweep_proj_tm(p, hs2, wv, vtm, 260, bias_ap=bv, consts=cs, w_col0=0, o_col0=0,
                  x_pool=hs2_pool)
    p.barrier()
    sweep_proj_tm(p, hs2, wv, vtm, 260, bias_ap=bv, consts=cs, w_col0=260, o_col0=260,
                  x_pool=hs2_pool)
    p.barrier()

    nbias = p.sb("nbias1", [128, 1], F32)

    @p.step("vector", writes=[nbias])
    def _(v):
        return v.memset(nbias[:, :], -C_STAB1)
    for mi in range(4):
        @p.step("gpsimd", writes=[cmk_t[mi]], dma=True)
        def _(g, mi=mi):
            return g.dma_start(out=cmk_t[mi][:, :], in_=cmask[:, mi * 512:(mi + 1) * 512])

    qtiles = [t[0:64, :] for t in p.ring("swX8", 16, [128, 512], BF16)[8:16]]
    ktiles = p.ring("caK", 8, [64, 128], BF16)
    vt = p.ring("caV", 2, [128, 520], BF16)
    es = p.ring("swX8", 16, [128, 512], BF16)[4:8]
    drow = p.ring("fbD", 2, [1, 512], F32)
    bcsb = [t[0:64, :] for t in p.ring("adT", 4, [128, 512], F32)[:2]]
    otile = p.ring("swX8", 16, [128, 512], BF16)[2:4]
    psS = [p.ps[0], p.ps[1]]
    psB = p.ps[6]

    for qc in range(8):
        a = qc * 512
        ktmax = 4 * qc + 4
        for hp in range(2):
            qs = [qtiles[(hp % 2) * 4 + g] for g in range(4)]
            for g in range(4):
                h = hp * 4 + g

                @p.step("sync", writes=[qs[g]], dma=True)
                def _(sy, g=g, h=h, a=a, qs=qs):
                    return sy.dma_start(out=qs[g][0:64, :],
                                        in_=qf[128 * h:128 * h + 64, a:a + 512])
            for kt in range(ktmax):
                ks4 = [ktiles[(kt % 2) * 4 + g] for g in range(4)]
                for g in range(4):
                    h = hp * 4 + g

                    @p.step("sync", writes=[ks4[g]], dma=True)
                    def _(sy, g=g, h=h, kt=kt, ks4=ks4):
                        return sy.dma_start(out=ks4[g][:, :],
                                            in_=kf[128 * h:128 * h + 64, kt * 128:(kt + 1) * 128])
                vv = vt[kt % 2]

                @p.step("sync", writes=[vv], dma=True)
                def _(sy, kt=kt, vv=vv):
                    return sy.dma_start(out=vv[:, :], in_=vtm[kt * 128:(kt + 1) * 128, :])

                diag = kt - 4 * qc
                for pair in range(2):
                    for g in (2 * pair, 2 * pair + 1):
                        ps_ = psS[g % 2]

                        @p.step("tensor", reads=[ks4[g], qs[g]], writes=[ps_])
                        def _(te, g=g, ps_=ps_, ks4=ks4, qs=qs):
                            return te.matmul(ps_[:, :], ks4[g][0:64, :], qs[g][0:64, :],
                                             start=True, stop=True)

                    @p.step("scalar", reads=[psS[0], psS[1], nbias],
                            writes=[es[2 * pair], es[2 * pair + 1]])
                    def _(sc, pair=pair):
                        sc.activation(out=es[2 * pair][:, :], in_=psS[0][:, :],
                                      func=AF.Exp, bias=nbias[:, :], scale=RS64)
                        return sc.activation(out=es[2 * pair + 1][:, :], in_=psS[1][:, :],
                                             func=AF.Exp, bias=nbias[:, :], scale=RS64)

                    for g in (2 * pair, 2 * pair + 1):
                        h = hp * 4 + g
                        bank = p.ps[2 + g]
                        ev = es[g]
                        if diag >= 0:
                            @p.step("vector", reads=[ev, cmk_t[diag]], writes=[ev])
                            def _(v, ev=ev, diag=diag):
                                return v.tensor_mul(out=ev[:, :], in0=ev[:, :],
                                                    in1=cmk_t[diag][:, :])

                        @p.step("tensor", reads=[ev, vv], writes=[bank])
                        def _(te, ev=ev, vv=vv, h=h, kt=kt, bank=bank):
                            return te.matmul(bank[0:65, :], vv[:, 65 * h:65 * h + 65],
                                             ev[:, :], start=(kt == 0),
                                             stop=(kt == ktmax - 1))

            for g in range(4):
                h = hp * 4 + g
                bank = p.ps[2 + g]
                dr = drow[g % 2]
                bc = bcsb[g % 2]
                ot = otile[g % 2]

                @p.step("vector", reads=[bank], writes=[dr])
                def _(v, dr=dr, bank=bank):
                    return v.reciprocal(out=dr[:, :], in_=bank[64:65, :])

                @p.step("tensor", reads=[dr, cs["ones_row_f"]], writes=[psB])
                def _(te, dr=dr):
                    return te.matmul(psB[0:64, :], cs["ones_row_f"][0:1, 0:64], dr[:, :],
                                     start=True, stop=True)

                @p.step("vector", reads=[psB], writes=[bc])
                def _(v, bc=bc):
                    return v.tensor_copy(out=bc[:, :], in_=psB[0:64, :])

                @p.step("vector", reads=[bank, bc], writes=[ot])
                def _(v, bc=bc, ot=ot, bank=bank):
                    return v.tensor_mul(out=ot[0:64, :], in0=bank[0:64, :], in1=bc[:, :])

                @p.step("gpsimd", reads=[ot], dma=True)
                def _(g_, ot=ot, h=h, a=a):
                    return g_.dma_start(out=yf[64 * h:64 * h + 64, a:a + 512], in_=ot[0:64, :])

    p.barrier()
    sweep_proj_fm(p, wp, yf, cc_in, D, kdim=512, bias_ap=bp_)
    p.barrier()
    collective_pair(p, cc_in, cc_out)
    sweep_add_fm(p, x2, cc_out, x3_out)
    p.barrier()
    hs3 = p.dram("hs3", [D, S])
    sweep_ln_fm(p, x3_out, hs3, cs, kind="ln")
    p.barrier()
    sweep_ln_fm(p, hs3, xn_fm_out, cs, kind="rms", gamma_ap=gam, scale=32.0,
                also_tm_ap=xn_tm_out, out_pool_tag="xn")
    return p.done() if standalone else p


def make_inputs_C(inp, consts, core, hs2, x2):
    b, j = core // 2, core % 2
    Wa = inp["L1_attn_W_attn"]
    ba = inp["L1_attn_b_attn"]
    wq = np.zeros((D, D), np.float32)
    bq = np.zeros((D, 1), np.float32)
    wk = np.zeros((D, D), np.float32)
    bk = np.zeros((D, 1), np.float32)
    wv = np.zeros((D, 520), np.float32)
    bv = np.zeros((1, 520), np.float32)
    for h in range(8):
        hg = 8 * j + h
        wq[:, 128 * h:128 * h + 64] = Wa[:, 64 * hg:64 * hg + 64]
        bq[128 * h:128 * h + 64, 0] = ba[64 * hg:64 * hg + 64]
        wk[:, 128 * h:128 * h + 64] = Wa[:, D + 64 * hg:D + 64 * hg + 64]
        bk[128 * h:128 * h + 64, 0] = ba[D + 64 * hg:D + 64 * hg + 64]
        wv[:, 65 * h:65 * h + 64] = Wa[:, 2 * D + 64 * hg:2 * D + 64 * hg + 64]
        bv[0, 65 * h:65 * h + 64] = ba[2 * D + 64 * hg:2 * D + 64 * hg + 64]
        bv[0, 65 * h + 64] = 1.0
    wp = np.ascontiguousarray(inp["L1_attn_W_proj"][512 * j:512 * (j + 1), :])
    bp = inp["L1_attn_b_proj"].reshape(D, 1) if j == 0 else np.zeros((D, 1), np.float32)
    cm = np.zeros((128, 4 * 512), np.float32)
    for m in range(4):
        kr = np.arange(128)[:, None]
        qcc = np.arange(512)[None, :]
        cm[:, m * 512:(m + 1) * 512] = (128 * m + kr <= qcc)
    return {
        "hs2": hs2, "x2": x2, "wq": wq, "bq": bq, "wk": wk, "bk": bk,
        "wv": wv, "bv": bv, "wp": wp, "bp": np.ascontiguousarray(bp),
        "gam": inp["L1_ffn_norm_gamma"].reshape(D, 1).astype(np.float32),
        "cmask": cm,
        "c_ident": consts["ident"], "c_ones_col": consts["ones_col"],
        "c_ones_row": consts["ones_row"], "c_rope_sin": consts["rope_sin"],
        "c_rope_cos": consts["rope_cos"],
    }



def build_D(debug=False, p=None, pre=None, pfx=""):
    standalone = p is None
    if standalone:
        p = P("D")
    IN, OUT = _io(p, pre, pfx, standalone, debug)
    xn_fm = IN("xn_fm", [D, S], BF16)
    xn_tm = IN("xn_tm", [S, D], BF16)
    x3 = IN("x3", [D, S], BF16)
    identf_in = IN("identf_in", [128, 128], F32)
    ses = [IN(f"se_{e}", [512, D], F32) for e in range(4)]
    sgam = IN("sgam", [1, D], F32)
    w1s = [IN(f"w1_{e}", [D, 4096], F32) for e in range(4)]
    b1s = [IN(f"b1_{e}", [4096, 1], F32) for e in range(4)]
    w2s = [IN(f"w2_{e}", [4096, D], F32) for e in range(4)]
    b2s = [IN(f"b2_{e}", [1, D], F32) for e in range(4)]
    cs = load_consts(p, ("ident", "ones_col", "ones_row"))
    identf = p.sb("identf", [128, 128], F32)

    @p.step("sync", writes=[identf], dma=True)
    def _(sy):
        return sy.dma_start(out=identf[:, :], in_=identf_in[:, :])

    mk = p.outp if debug else p.dram
    sn_fm = mk("sn_fm", [D, 2048])
    disp_tm = mk("disp_tm", [S, 2048])
    slots_f = mk("slots_f", [D, 2048])
    h1d = p.dram("h1d", [4096, 512])
    eo_tm = mk("eo_tm", [2048, D])
    cn_d = p.dram("cn_d", [1, S], F32)
    cn_o = p.dram("cn_o", [1, S], F32)
    zn_d = p.dram("zn_d", [1, S], F32)
    zn_o = p.dram("zn_o", [1, S], F32)
    zs_d = p.dram("zs_d", [1, 2048], F32)
    cc_in = p.dram("cc_in", [D, S])
    cc_out = p.dram("cc_out", [D, S])
    out_tm = OUT("out", [S, D], F32) if not standalone else p.outp("out", [S, D], F32)
    p._d_dump = {"sn_fm": sn_fm, "disp_tm": disp_tm, "slots_f": slots_f,
                 "eo_tm": eo_tm, "cn_o": cn_o, "zn_o": zn_o, "zs_d": zs_d,
                 "cc_out": cc_out, "h1d": h1d, "cc_in": cc_in, "zn_d": zn_d}

    gamrow = p.sb("gamrow", [1, D], BF16)

    @p.step("gpsimd", writes=[gamrow], dma=True)
    def _(sy):
        return sy.dma_start(out=gamrow[:, :], in_=sgam[:, :])
    gb = p.sb("gamb", [128, D], BF16)

    @p.step("tensor", reads=[gamrow, cs["ones_row_bf"]], writes=[p.ps[4], p.ps[5]])
    def _(te):
        te.matmul(p.ps[4][:, :], cs["ones_row_bf"][0:1, :], gamrow[0:1, 0:512],
                  start=True, stop=True)
        return te.matmul(p.ps[5][:, :], cs["ones_row_bf"][0:1, :], gamrow[0:1, 512:1024],
                         start=True, stop=True)

    @p.step("vector", reads=[p.ps[4], p.ps[5]], writes=[gb])
    def _(v):
        v.tensor_copy(out=gb[:, 0:512], in_=p.ps[4][:, :])
        return v.tensor_copy(out=gb[:, 512:1024], in_=p.ps[5][:, :])

    fb2 = p.ring("f32buf", 2, [128, D], F32)
    set_ = [fb2[0]]
    sqv = p.ring("d1sq", 2, [128, 1], F32)
    snt = [fb2[1]]
    sno = [p.ring("t2fXb", 2, [128, D], BF16)[0]]
    d1to = [p.ring("t2fXb", 2, [128, D], BF16)[1]]
    for e in range(4):
        for t in range(4):
            mrow = e * 512 + t * 128
            sv, sq, sn, so, to = set_[0], sqv[t % 2], snt[0], sno[0], d1to[0]

            @p.step("sync", writes=[sv], dma=True)
            def _(sy, e=e, t=t, sv=sv):
                return sy.dma_start(out=sv[:, :], in_=ses[e][t * 128:(t + 1) * 128, :])

            @p.step("vector", reads=[sv], writes=[sn, sq])
            def _(v, sv=sv, sn=sn, sq=sq):
                v.tensor_mul(out=sn[:, :], in0=sv[:, :], in1=sv[:, :])
                return v.reduce_sum(out=sq[:, :], in_=sn[:, :], axis=AX.X)

            @p.step("scalar", reads=[sq], writes=[sq])
            def _(sc, sq=sq):
                return sc.activation(out=sq[:, :], in_=sq[:, :], func=AF.Sqrt)

            @p.step("vector", reads=[sq], writes=[sq])
            def _(v, sq=sq):
                return v.tensor_scalar(out=sq[:, :], in0=sq[:, :], scalar1=1e-12,
                                       scalar2=None, op0=ALU.max)

            @p.step("vector", reads=[sq], writes=[sq])
            def _(v, sq=sq):
                return v.reciprocal(out=sq[:, :], in_=sq[:, :])

            @p.step("vector", reads=[sv, sq], writes=[sn])
            def _(v, sv=sv, sq=sq, sn=sn):
                return v.tensor_scalar(out=sn[:, :], in0=sv[:, :], scalar1=sq[:, :],
                                       scalar2=32.0, op0=ALU.mult, op1=ALU.mult)

            @p.step("vector", reads=[sn, gb], writes=[so])
            def _(v, sn=sn, so=so):
                return v.tensor_mul(out=so[:, :], in0=sn[:, :], in1=gb[:, :])

            @p.step("tensor", reads=[so, cs["ident_bf"]], writes=[p.psT])
            def _(te, so=so):
                last = None
                for dc in range(DC):
                    last = te.transpose(p.psT[:, dc * 128:(dc + 1) * 128],
                                        so[:, dc * 128:(dc + 1) * 128], cs["ident_bf"][:, :])
                return last

            @p.step("vector", reads=[p.psT], writes=[to])
            def _(v, to=to):
                return v.tensor_copy(out=to[:, :], in_=p.psT[:, :])

            @p.step("gpsimd", reads=[to], dma=True)
            def _(g, to=to, mrow=mrow):
                return [g.dma_start(out=sn_fm[dc * 128:(dc + 1) * 128, mrow:mrow + 128],
                                    in_=to[:, dc * 128:(dc + 1) * 128]) for dc in range(DC)]
    p.barrier()

    snt_l = p.ring("d5h", 16, [128, 128], BF16)
    macct = p.ring("swX16", 32, [128, 512], BF16)[16:24]
    csb = p.sb("csb", [128, 16], F32)
    cmx = p.ring("d2c", 2, [128, 1], F32)

    @p.step("vector", writes=list(macct))
    def _(v):
        last = None
        for mi in range(8):
            last = v.memset(macct[mi][:, :], -1e30)
        return last

    @p.step("vector", writes=[csb])
    def _(v):
        return v.memset(csb[:, :], -1e30)

    def logit_block(mt, ci, consumer):
        a = ci * 512
        sns = [snt_l[(mt % 2) * DC + dc] for dc in range(DC)]
        if ci == 0:
            for dc in range(DC):
                @p.step("sync", writes=[sns[dc]], dma=True)
                def _(sy, dc=dc, mt=mt, sns=sns):
                    return sy.dma_start(out=sns[dc][:, :],
                                        in_=sn_fm[dc * 128:(dc + 1) * 128,
                                                  mt * 128:(mt + 1) * 128])
        xnp = pool_use(p, "xn")[0]
        xns = [xnp[dc * 8 + ci] for dc in range(DC)]
        ps_ = p.ps[ci % 2]

        @p.step("tensor", reads=sns + xns, writes=[ps_])
        def _(te, sns=sns, xns=xns, ps_=ps_):
            last = None
            for dc in range(DC):
                last = te.matmul(ps_[:, :], sns[dc][:, :], xns[dc][:, :],
                                 start=(dc == 0), stop=(dc == DC - 1))
            return last
        consumer(ps_, mt, ci)

    def d2_consumer(ps_, mt, ci):
        a = ci * 512
        cm = cmx[ci % 2]

        @p.step("vector", reads=[ps_], writes=[cm])
        def _(v, ps_=ps_, cm=cm):
            return v.reduce_max(out=cm[:, :], in_=ps_[:, :], axis=AX.X)

        @p.step("vector", reads=[cm, csb], writes=[csb])
        def _(v, cm=cm, mt=mt):
            return v.tensor_tensor(out=csb[:, mt:mt + 1], in0=csb[:, mt:mt + 1],
                                   in1=cm[:, :], op=ALU.max)

        @p.step("vector", reads=[ps_, macct[ci]], writes=[macct[ci]])
        def _(v, ps_=ps_, ci=ci):
            return v.tensor_tensor(out=macct[ci][:, :], in0=macct[ci][:, :],
                                   in1=ps_[:, :], op=ALU.max)

    for mt in range(16):
        for ci in range(8):
            logit_block(mt, ci, d2_consumer)

    cnrow = p.sb("cnrow", [128, 1], F32)
    for nt in range(NT):
        @p.step("tensor", reads=[macct[nt // 4], cs["ident_bf"]], writes=[p.psT])
        def _(te, nt=nt):
            return te.transpose(p.psT[:, 128:256],
                                macct[nt // 4][:, (nt % 4) * 128:(nt % 4) * 128 + 128],
                                cs["ident_bf"][:, :])

        @p.step("vector", reads=[p.psT], writes=[cnrow])
        def _(v):
            return v.reduce_max(out=cnrow[:, :], in_=p.psT[:, 128:256], axis=AX.X)

        cnf = p.sb("cnf", [128, 1], BF16) if False else cnrow
        if not hasattr(p, "_crow7"):
            _lr = p.ring("lnrows", 5, [1, 512], F32)[3]
            p._crow7 = [_lr[0:1, 0:128], _lr[0:1, 128:256]]
        crow_sb = p._crow7[nt % 2]
        cnbf = p.ring("cnbf7", 2, [128, 1], BF16)[nt % 2]

        @p.step("vector", reads=[cnrow], writes=[cnbf])
        def _(v, cnbf=cnbf):
            return v.tensor_copy(out=cnbf[:, :], in_=cnrow[:, :])

        @p.step("tensor", reads=[cnbf, cs["ident_bf"]], writes=[p.psT])
        def _(te, cnbf=cnbf):
            return te.transpose(p.psT[0:1, 0:128], cnbf[:, :], cs["ident_bf"][:, :])

        @p.step("vector", reads=[p.psT], writes=[crow_sb])
        def _(v, crow_sb=crow_sb):
            return v.tensor_copy(out=crow_sb[:, :], in_=p.psT[0:1, 0:128])

        @p.step("gpsimd", reads=[crow_sb], dma=True)
        def _(g, nt=nt, crow_sb=crow_sb):
            return g.dma_start(out=cn_d[0:1, nt * 128:(nt + 1) * 128], in_=crow_sb[:, :])
    p.barrier()
    collective_pair(p, cn_d, cn_o, op=ALU.max)

    nbias3 = p.ring("d1sq", 2, [128, 1], F32)
    zsacc = p.sb("zsacc", [128, 16], F32)

    @p.step("vector", writes=[zsacc])
    def _(v):
        return v.memset(zsacc[:, :], 0.0)
    zstile = p.ring("d2c", 2, [128, 1], F32)
    dtile = p.ring("swX8", 16, [128, 512], BF16)[0:2]
    ttile = p.ring("swX8", 16, [128, 512], BF16)[2:4]

    def d3_consumer(ps_, mt, ci):
        a = ci * 512
        nb = nbias3[mt % 2]
        if ci == 0:
            @p.step("vector", reads=[csb], writes=[nb])
            def _(v, nb=nb, mt=mt):
                return v.tensor_scalar(out=nb[:, :], in0=csb[:, mt:mt + 1], scalar1=-1.0,
                                       scalar2=None, op0=ALU.mult)
        dt = dtile[ci % 2]
        zt = zstile[ci % 2]

        @p.step("scalar", reads=[ps_, nb], writes=[dt, zt])
        def _(sc, ps_=ps_, dt=dt, zt=zt, nb=nb):
            return sc.activation(out=dt[:, :], in_=ps_[:, :], func=AF.Exp,
                                 bias=nb[:, :], accum_out=zt[:, :])

        @p.step("vector", reads=[zt, zsacc], writes=[zsacc])
        def _(v, zt=zt, mt=mt):
            return v.tensor_add(out=zsacc[:, mt:mt + 1], in0=zsacc[:, mt:mt + 1],
                                in1=zt[:, :])

        @p.step("tensor", reads=[dt, cs["ident_bf"]], writes=[p.psT])
        def _(te, dt=dt):
            last = None
            for q in range(4):
                last = te.transpose(p.psT[:, q * 128:(q + 1) * 128],
                                    dt[:, q * 128:(q + 1) * 128], cs["ident_bf"][:, :])
            return last
        tt_ = ttile[ci % 2]

        @p.step("vector", reads=[p.psT], writes=[tt_])
        def _(v, tt_=tt_):
            return v.tensor_copy(out=tt_[:, :], in_=p.psT[:, 0:512])

        @p.step("gpsimd", reads=[tt_], dma=True)
        def _(g, tt_=tt_, a=a, mt=mt):
            return [g.dma_start(out=disp_tm[a + q * 128:a + (q + 1) * 128,
                                            mt * 128:(mt + 1) * 128],
                                in_=tt_[:, q * 128:(q + 1) * 128]) for q in range(4)]

    for mt in range(16):
        for ci in range(8):
            logit_block(mt, ci, d3_consumer)

    @p.step("vector", reads=[zsacc], writes=[zsacc])
    def _(v):
        return v.reciprocal(out=zsacc[:, :], in_=zsacc[:, :])
    zso = p.sb("zso", [16, 128], F32)

    @p.step("tensor", reads=[zsacc, identf], writes=[p.ps[5]])
    def _(te):
        return te.transpose(p.ps[5][0:16, 0:128], zsacc[:, 0:16], identf[:, :])

    @p.step("vector", reads=[p.ps[5]], writes=[zso])
    def _(v):
        return v.tensor_copy(out=zso[:, :], in_=p.ps[5][0:16, 0:128])

    @p.step("gpsimd", reads=[zso], dma=True)
    def _(g):
        return g.dma_start(out=zs_d[0:1, :].rearrange("o (a b) -> (o a) b", a=16, b=128),
                           in_=zso[:, :])
    p.barrier()

    dpt = p.ring("swX8", 16, [128, 512], BF16)[4:6]
    rzrow = p.ring("fbD", 2, [1, 512], F32)
    sbc = p.ring("adT", 4, [128, 512], F32)[2]
    sout = p.ring("swX8", 16, [128, 512], BF16)[13:16]
    xntm_pool = pool_load(p, "xn_tm", xn_tm, 32, 2)[0]
    for esc in range(4):
        rz = rzrow[esc % 2]

        @p.step("sync", writes=[rz], dma=True)
        def _(sy, esc=esc, rz=rz):
            return sy.dma_start(out=rz[:, :], in_=zs_d[0:1, esc * 512:(esc + 1) * 512])
        for dh in range(2):
            for nt in range(NT):
                dp = dpt[nt % 2]
                xv4 = xntm_pool[nt * 2 + dh]

                @p.step("sync", writes=[dp], dma=True)
                def _(sy, nt=nt, esc=esc, dp=dp):
                    return sy.dma_start(out=dp[:, :],
                                        in_=disp_tm[nt * 128:(nt + 1) * 128,
                                                    esc * 512:(esc + 1) * 512])

                @p.step("tensor", reads=[dp, xv4], writes=[p.ps[0], p.ps[1], p.ps[2], p.ps[3]])
                def _(te, nt=nt, dh=dh, dp=dp, xv4=xv4):
                    last = None
                    for dt_ in range(4):
                        last = te.matmul(p.ps[dt_][:, :],
                                         xv4[:, dt_ * 128:(dt_ + 1) * 128],
                                         dp[:, :], start=(nt == 0), stop=(nt == NT - 1))
                    return last
            @p.step("tensor", reads=[rz, cs["ones_row_f"]], writes=[p.ps[4]])
            def _(te, rz=rz):
                return te.matmul(p.ps[4][:, :], cs["ones_row_f"][0:1, :], rz[:, :],
                                 start=True, stop=True)

            @p.step("vector", reads=[p.ps[4]], writes=[sbc])
            def _(v):
                return v.tensor_copy(out=sbc[:, :], in_=p.ps[4][:, :])
            for dt_ in range(4):
                dtile_ = dh * 4 + dt_
                so = sout[dt_ % 3]

                @p.step("vector", reads=[p.ps[dt_], sbc], writes=[so])
                def _(v, dt_=dt_, so=so):
                    return v.tensor_mul(out=so[:, :], in0=p.ps[dt_][:, :], in1=sbc[:, :])

                @p.step("gpsimd", reads=[so], dma=True)
                def _(g, so=so, dtile_=dtile_, esc=esc):
                    return g.dma_start(out=slots_f[dtile_ * 128:(dtile_ + 1) * 128,
                                                   esc * 512:(esc + 1) * 512], in_=so[:, :])
            p.barrier()

    for e in range(4):
        sweep_proj_fm(p, w1s[e], slots_f, h1d, 4096, bias_ap=b1s[e],
                      evac2=evac2_gelu, n0=0, n1=512, x_col0=e * 512)
        p.barrier()
        w2t = p.ring("swX16", 32, [128, 512], BF16)
        b2row = gamrow

        @p.step("gpsimd", writes=[b2row], dma=True)
        def _(g, e=e):
            return g.dma_start(out=b2row[:, :], in_=b2s[e][:, :])
        p._pool_tag = ("d5scratch", 0, 0, 0, 0)
        htp = xpool(p)
        eo_ring = p.ring("lnT", 2, [128, 512], BF16)
        for doc in range(2):
            for hc in range(32):
                @p.step("gpsimd", writes=[w2t[hc]], dma=True)
                def _(g, e=e, doc=doc, hc=hc):
                    return g.dma_start(out=w2t[hc][:, :],
                                       in_=w2s[e][hc * 128:(hc + 1) * 128,
                                                  doc * 512:(doc + 1) * 512])
            for est in range(4):
                hs_ = [htp[(est % 2) * 32 + hc] for hc in range(32)]
                for hc in range(32):
                    @p.step("sync", writes=[hs_[hc]], dma=True)
                    def _(sy, hc=hc, est=est, hs_=hs_):
                        return sy.dma_start(out=hs_[hc][:, 0:128],
                                            in_=h1d[hc * 128:(hc + 1) * 128,
                                                    est * 128:(est + 1) * 128])
                ps_ = p.ps[est % 4]

                @p.step("tensor", reads=hs_ + list(w2t) + [b2row], writes=[ps_])
                def _(te, hs_=hs_, ps_=ps_, doc=doc):
                    for hc in range(32):
                        te.matmul(ps_[:, :], hs_[hc][:, 0:128], w2t[hc][:, :],
                                  start=(hc == 0), stop=False)
                    return te.matmul(ps_[:, :], cs["ones_row_bf"][0:1, :],
                                     b2row[0:1, doc * 512:(doc + 1) * 512],
                                     start=False, stop=True)
                eo_o = eo_ring[est % 2]

                @p.step("scalar", reads=[ps_], writes=[eo_o])
                def _(sc, ps_=ps_, eo_o=eo_o):
                    return sc.activation(out=eo_o[:, :], in_=ps_[:, :], func=AF.Copy)

                @p.step("gpsimd", reads=[eo_o], dma=True)
                def _(g, eo_o=eo_o, e=e, est=est, doc=doc):
                    return g.dma_start(out=eo_tm[e * 512 + est * 128:e * 512 + (est + 1) * 128,
                                                 doc * 512:(doc + 1) * 512], in_=eo_o[:, :])
        p.barrier()

    cnb = p.ring("lnrows", 5, [1, 512], F32)[4]
    combt = p.ring("swX8", 16, [128, 512], BF16)
    _eo16 = p.ring("swX16", 32, [128, 512], BF16)[24:32]
    eot = [_eo16[i // 4][:, (i % 4) * 128:(i % 4) * 128 + 128] for i in range(16)]
    oout = p.ring("swX16", 32, [128, 512], BF16)[21:24]
    snp = pool_load(p, "sn6", sn_fm, 8, 4)[0]
    sns_ap = [[snp[dc * 4 + mt // 4][:, (mt % 4) * 128:(mt % 4) * 128 + 128]
               for dc in range(DC)] for mt in range(16)]
    sns_par = [[snp[dc * 4 + mt // 4] for dc in range(DC)] for mt in range(16)]
    xstr = xpool(p)[32:48]
    for ci in range(8):
        a = ci * 512

        @p.step("sync", writes=[cnb], dma=True)
        def _(sy, a=a):
            return sy.dma_start(out=cnb[:, :], in_=cn_o[0:1, a:a + 512])
        cnbbf = p.sb("cnbbf", [1, 512], BF16) if "cnbbf" not in p._rcache else p._rcache["cnbbf"]
        p._rcache["cnbbf"] = cnbbf

        @p.step("vector", reads=[cnb], writes=[cnbbf])
        def _(v, cnbbf=cnbbf):
            return v.tensor_scalar(out=cnbbf[:, :], in0=cnb[:, :], scalar1=-1.0,
                                   scalar2=None, op0=ALU.mult)
        xns = [xstr[(ci % 2) * 8 + dc] for dc in range(DC)]
        for dc in range(DC):
            @p.step("sync", writes=[xns[dc]], dma=True)
            def _(sy, dc=dc, a=a, xns=xns):
                return sy.dma_start(out=xns[dc][:, :],
                                    in_=xn_fm[dc * 128:(dc + 1) * 128, a:a + 512])
        for mt in range(16):
            ps_ = p.ps[mt % 2]

            @p.step("tensor", reads=sns_par[mt] + xns + [cnbbf, cs["ones_row_bf"]],
                    writes=[ps_])
            def _(te, mt=mt, xns=xns, ps_=ps_, cnbbf=cnbbf):
                for dc in range(DC):
                    te.matmul(ps_[:, :], sns_ap[mt][dc], xns[dc][:, :],
                              start=(dc == 0), stop=False)
                return te.matmul(ps_[:, :], cs["ones_row_bf"][0:1, :], cnbbf[0:1, :],
                                 start=False, stop=True)
            if mt % 2 == 1:
                @p.step("scalar", reads=[p.ps[0], p.ps[1]],
                        writes=[combt[mt - 1], combt[mt]])
                def _(sc, mt=mt):
                    sc.activation(out=combt[mt - 1][:, :], in_=p.ps[0][:, :], func=AF.Exp)
                    return sc.activation(out=combt[mt][:, :], in_=p.ps[1][:, :], func=AF.Exp)

                @p.step("tensor", reads=[combt[mt - 1], combt[mt], cs["ones_col_bf"]],
                        writes=[p.ps[5]])
                def _(te, mt=mt):
                    te.matmul(p.ps[5][0:1, :], cs["ones_col_bf"][:, :], combt[mt - 1][:, :],
                              start=(mt == 1), stop=False)
                    return te.matmul(p.ps[5][0:1, :], cs["ones_col_bf"][:, :], combt[mt][:, :],
                                     start=False, stop=(mt == 15))
        znr = p.ring("lnrows", 5, [1, 512], F32)[0]

        @p.step("vector", reads=[p.ps[5]], writes=[znr])
        def _(v, znr=znr):
            return v.tensor_copy(out=znr[:, :], in_=p.ps[5][0:1, :])

        @p.step("gpsimd", reads=[znr], dma=True)
        def _(g, znr=znr, a=a):
            return g.dma_start(out=zn_d[0:1, a:a + 512], in_=znr[:, :])
        for dh in range(2):
            for mt in range(16):
                eos = [eot[(mt % 2) * DC + dc] for dc in range(4)]
                for dt_ in range(4):
                    dtile_ = dh * 4 + dt_

                    @p.step("sync", writes=[eos[dt_]], dma=True)
                    def _(sy, mt=mt, dtile_=dtile_, eos=eos, dt_=dt_):
                        return sy.dma_start(out=eos[dt_][:, :],
                                            in_=eo_tm[mt * 128:(mt + 1) * 128,
                                                      dtile_ * 128:(dtile_ + 1) * 128])

                @p.step("tensor", reads=eos + [combt[mt]], writes=[p.ps[0], p.ps[1], p.ps[2], p.ps[3]])
                def _(te, mt=mt, eos=eos):
                    last = None
                    for dt_ in range(4):
                        last = te.matmul(p.ps[dt_][:, :], eos[dt_][:, :], combt[mt][:, :],
                                         start=(mt == 0), stop=(mt == 15))
                    return last
            for dt_ in range(4):
                dtile_ = dh * 4 + dt_
                oo = oout[dt_ % 3]

                @p.step("scalar", reads=[p.ps[dt_]], writes=[oo])
                def _(sc, dt_=dt_, oo=oo):
                    return sc.activation(out=oo[:, :], in_=p.ps[dt_][:, :], func=AF.Copy)

                @p.step("gpsimd", reads=[oo], dma=True)
                def _(g, oo=oo, dtile_=dtile_, a=a):
                    return g.dma_start(out=cc_in[dtile_ * 128:(dtile_ + 1) * 128, a:a + 512],
                                       in_=oo[:, :])
            p.barrier()
    p.barrier()
    collective_pair(p, cc_in, cc_out)
    collective_pair(p, zn_d, zn_o)

    znrow = p.ring("lnrows", 5, [1, 512], F32)[1:3]
    zbc = p.ring("adT", 4, [128, 512], F32)[0]
    xt7 = p.ring("swX16", 32, [128, 512], BF16)[0:2]
    ct7 = p.ring("swX16", 32, [128, 512], BF16)[2:4]
    f7 = p.ring("adT", 4, [128, 512], F32)[1:3]
    t7 = [p.ring("adT", 4, [128, 512], F32)[3], p.ring("f32buf", 2, [128, D], F32)[0][:, 0:512]]
    for ci in range(8):
        a = ci * 512
        rz = znrow[ci % 2]

        @p.step("sync", writes=[rz], dma=True)
        def _(sy, a=a, rz=rz):
            return sy.dma_start(out=rz[:, :], in_=zn_o[0:1, a:a + 512])

        @p.step("vector", reads=[rz], writes=[rz])
        def _(v, rz=rz):
            return v.reciprocal(out=rz[:, :], in_=rz[:, :])

        @p.step("tensor", reads=[rz, cs["ones_row_f"]], writes=[p.ps[4]])
        def _(te, rz=rz):
            return te.matmul(p.ps[4][:, :], cs["ones_row_f"][0:1, :], rz[:, :],
                             start=True, stop=True)

        @p.step("vector", reads=[p.ps[4]], writes=[zbc])
        def _(v):
            return v.tensor_copy(out=zbc[:, :], in_=p.ps[4][:, :])
        for dc in range(DC):
            xv, cv, fv, tv = xt7[dc % 2], ct7[dc % 2], f7[dc % 2], t7[dc % 2]

            @p.step("sync", writes=[xv], dma=True)
            def _(sy, dc=dc, a=a, xv=xv):
                return sy.dma_start(out=xv[:, :], in_=x3[dc * 128:(dc + 1) * 128, a:a + 512])

            @p.step("sync", writes=[cv], dma=True)
            def _(sy, dc=dc, a=a, cv=cv):
                return sy.dma_start(out=cv[:, :], in_=cc_out[dc * 128:(dc + 1) * 128, a:a + 512])

            @p.step("vector", reads=[cv, zbc], writes=[tv])
            def _(v, cv=cv, tv=tv):
                return v.tensor_mul(out=tv[:, :], in0=cv[:, :], in1=zbc[:, :])

            @p.step("vector", reads=[tv, xv], writes=[fv])
            def _(v, tv=tv, xv=xv, fv=fv):
                return v.tensor_add(out=fv[:, :], in0=tv[:, :], in1=xv[:, :])
            @p.step("tensor", reads=[fv, identf], writes=[p.ps[5], p.ps[6]])
            def _(te, fv=fv):
                last = None
                for q in range(4):
                    tgt = p.ps[5] if q < 2 else p.ps[6]
                    last = te.transpose(tgt[:, (q % 2) * 128:(q % 2) * 128 + 128],
                                        fv[:, q * 128:(q + 1) * 128], identf[:, :])
                return last
            ov = [p.ring("f32buf", 2, [128, D], F32)[0][:, 512:1024],
                  p.ring("f32buf", 2, [128, D], F32)[1][:, 0:512]][dc % 2]

            @p.step("vector", reads=[p.ps[5], p.ps[6]], writes=[ov])
            def _(v, ov=ov):
                v.tensor_copy(out=ov[:, 0:256], in_=p.ps[5][:, 0:256])
                return v.tensor_copy(out=ov[:, 256:512], in_=p.ps[6][:, 0:256])
            @p.step("gpsimd", reads=[ov], dma=True)
            def _(g, ov=ov, dc=dc, a=a):
                return [g.dma_start(out=out_tm[a + q * 128:a + (q + 1) * 128,
                                               dc * 128:(dc + 1) * 128],
                                    in_=ov[:, q * 128:(q + 1) * 128]) for q in range(4)]
    return p.done() if standalone else p


def make_inputs_D(inp, consts, core, xn_fm, xn_tm, x3):
    b, j = core // 2, core % 2
    d = {"xn_fm": xn_fm, "xn_tm": xn_tm, "x3": x3,
         "identf_in": consts["ident"],
         "sgam": inp["L1_ffn_slot_norm_gamma"].reshape(1, D).astype(np.float32),
         "c_ident": consts["ident"], "c_ones_col": consts["ones_col"],
         "c_ones_row": consts["ones_row"]}
    for e in range(4):
        eg = 4 * j + e
        d[f"se_{e}"] = np.ascontiguousarray(inp["L1_ffn_slot_embeds"][eg])
        d[f"w1_{e}"] = np.ascontiguousarray(inp["L1_ffn_w1"][eg])
        d[f"b1_{e}"] = np.ascontiguousarray(inp["L1_ffn_b1"][eg].reshape(4096, 1))
        d[f"w2_{e}"] = np.ascontiguousarray(inp["L1_ffn_w2"][eg])
        d[f"b2_{e}"] = np.ascontiguousarray(inp["L1_ffn_b2"][eg].reshape(1, D))
    return d



def build_fused(debug=False):
    p = P("F")
    x1 = p.dram("s_x1", [D, S])
    hs1 = p.dram("s_hs1", [D, S])
    mla = p.dram("s_mla", [D, S])
    comb = p.dram("s_comb", [8, S])
    x2 = p.dram("s_x2", [D, S])
    hs2 = p.dram("s_hs2", [D, S])
    x3 = p.dram("s_x3", [D, S])
    xnf = p.dram("s_xnf", [D, S])
    xnt = p.dram("s_xnt", [S, D])
    out = p.outp("out", [S, D], F32)
    build_A(p=p, pre={"x1": x1, "hs1": hs1}, pfx="A_")
    p.barrier()
    build_B(p=p, pre={"hs1": hs1, "mla": mla, "comb": comb}, pfx="B_")
    p.barrier()
    build_B2(p=p, pre={"mla": mla, "comb": comb, "x1": x1, "hs1": hs1,
                       "x2": x2, "hs2": hs2}, pfx="B2_")
    p.barrier()
    build_C(p=p, pre={"hs2": hs2, "x2": x2, "x3": x3, "xn_fm": xnf,
                      "xn_tm": xnt}, pfx="C_")
    p.barrier()
    build_D(p=p, pre={"xn_fm": xnf, "xn_tm": xnt, "x3": x3, "out": out}, pfx="D_")
    if debug:
        p.barrier()
        for nm, t in (("d_x1", x1), ("d_hs1", hs1), ("d_mla", mla),
                      ("d_comb", comb), ("d_x2", x2), ("d_hs2", hs2),
                      ("d_x3", x3), ("d_xnf", xnf), ("d_xnt", xnt)) + tuple(
                          ("d_" + k, v) for k, v in p._d_dump.items()):
            dbg = p.outp(nm, list(t.shape), BF16)

            @p.step("gpsimd", dma=True)
            def _(g, dbg=dbg, t=t):
                return g.dma_start(out=dbg[:, :], in_=t[:, :])
        p.barrier()
    return p.done()


def make_inputs_fused(inp, consts, core):
    d = {}
    for pfx, fn, extra in (("A_", make_inputs_A, ()),
                           ("B_", make_inputs_B, (None,)),
                           ("B2_", make_inputs_B2, (None, None, None, None)),
                           ("C_", make_inputs_C, (None, None)),
                           ("D_", make_inputs_D, (None, None, None))):
        sub = fn(inp, consts, core, *extra)
        for k, v in sub.items():
            if v is None:
                continue
            if k.startswith("c_"):
                d[k] = v
            else:
                d[pfx + k] = v
    return d


from concourse.bass_utils import run_bass_kernel_spmd

_BUILD_CACHE = {}


def _get(name, fn):
    if name not in _BUILD_CACHE:
        _BUILD_CACHE[name] = fn()
    return _BUILD_CACHE[name]


def kernel(**inputs):
    consts = host_constants()
    cores = list(range(8))
    nc = _get("F", build_fused)
    im = [make_inputs_fused(inputs, consts, c) for c in cores]
    r = run_bass_kernel_spmd(nc, im, core_ids=cores).results
    out = np.stack([np.asarray(r[2 * b]["out"], dtype=np.float32) for b in range(4)])
    return out


# version 09
# speedup vs baseline: 1.4924x; 1.0021x over previous
import numpy as np
from contextlib import ExitStack
import concourse.bass as bass
import concourse.mybir as mybir

F32 = mybir.dt.float32
BF16 = mybir.dt.bfloat16
AF = mybir.ActivationFunctionType
ALU = mybir.AluOpType
AX = mybir.AxisListType

D = 1024
S = 4096
B = 4
DC = D // 128
NT = S // 128
EPS_LN = 1e-5

ENGINES = ("tensor", "vector", "scalar", "gpsimd", "sync")


def cdiv(a, b):
    return (a + b - 1) // b


class P:

    def __init__(self, name):
        self.nc = bass.Bass("TRN2", target_bir_lowering=False)
        self.ctx = ExitStack()
        self.block = self.ctx.enter_context(self.nc.Block())
        self.name = name
        self._uid = 0
        self.esem = {e: self.ctx.enter_context(self.nc.semaphore("es_" + e))
                     for e in ENGINES + ("cc",)}
        self.ecnt = {e: 0 for e in ENGINES + ("cc",)}
        self.res = {}
        self.fences = {"gpsimd": [0], "sync": [0]}
        self.waited = {e: {} for e in ENGINES}
        self.safe = {e: 0 for e in ENGINES}
        self._eng_obj = {}
        self._open_section = None
        self.ps = [self.ctx.enter_context(self.nc.psum_tensor(f"pool_ps{i}", [128, 512], F32))
                   for i in range(7)]
        self.psT = self.ctx.enter_context(self.nc.psum_tensor("pool_psT", [128, 1024], BF16))

    def uid(self):
        self._uid += 1
        return self._uid

    def dram(self, name, shape, dtype=BF16, kind="Internal"):
        if kind == "Internal":
            name = f"{name}_{self.uid()}"
        return self.nc.dram_tensor(name, list(shape), dtype, kind=kind).ap()

    def inp(self, name, shape, dtype=F32):
        return self.dram(name, shape, dtype, kind="ExternalInput")

    def outp(self, name, shape, dtype=BF16):
        return self.dram(name, shape, dtype, kind="ExternalOutput")

    def sb(self, name, shape, dtype=BF16):
        return self.ctx.enter_context(self.nc.sbuf_tensor(f"{name}_{self.uid()}", list(shape), dtype))

    def psum(self, name, shape, dtype=F32):
        return self.ctx.enter_context(self.nc.psum_tensor(f"{name}_{self.uid()}", list(shape), dtype))

    def ring(self, name, n, shape, dtype=BF16):
        if name == "swX8":
            return self.ring("swX16", 32, shape, dtype)[:n]
        k = (name, n, tuple(shape), dtype)
        if not hasattr(self, "_rcache"):
            self._rcache = {}
        if k not in self._rcache:
            self._rcache[k] = [self.sb(f"{name}{i}", shape, dtype) for i in range(n)]
        return self._rcache[k]

    def _engine(self, ename):
        if self._open_section == ename:
            return self._eng_obj[ename]
        holder = {}

        def body(eng):
            holder["eng"] = eng
        getattr(self.block, ename)(body)
        raise RuntimeError("engine sections must be driven via step()")

    def step(self, ename, reads=(), writes=(), dma=False, n_dma=1):
        def run(fn):
            deps = {}

            def need(e, c):
                if e == ename and ename == "tensor":
                    return
                if e == ename and c <= self.safe[ename]:
                    return
                if e in self.fences and e != ename:
                    fs = self.fences[e]
                    f = next((x for x in fs if x >= c), None)
                    if f is None:
                        f = self.ecnt[e]
                        assert f >= c

                        def fbody(eng, e=e, f=f):
                            eng.wait_ge(self.esem[e], f)
                        getattr(self.block, e)(fbody)
                        fs.append(f)
                    c = f
                deps[e] = max(deps.get(e, 0), c)

            for r in reads:
                st = self.res.get(id(r) if not isinstance(r, (str, tuple)) else r)
                if st and st["w"]:
                    need(*st["w"])
            for w in writes:
                key = id(w) if not isinstance(w, (str, tuple)) else w
                st = self.res.get(key)
                if st:
                    if st["w"]:
                        need(*st["w"])
                    for rr in st["r"]:
                        need(*rr)

            emitted = {}

            wm = self.waited[ename]
            for e in list(deps):
                if deps[e] <= wm.get(e, 0):
                    del deps[e]
                else:
                    wm[e] = deps[e]
            deps_log = dict(deps)

            def body(eng):
                for e, c in deps.items():
                    if c > 0:
                        eng.wait_ge(self.esem[e], c)
                ret = fn(eng)
                instrs = ret if isinstance(ret, (list, tuple)) else ([ret] if ret is not None else [])
                if dma:
                    for ins in instrs:
                        ins.then_inc(self.esem[ename], 16)
                    emitted["inc"] = 16 * len(instrs)
                else:
                    if instrs:
                        instrs[-1].then_inc(self.esem[ename], 1)
                        emitted["inc"] = 1
                    else:
                        emitted["inc"] = 0

            getattr(self.block, ename)(body)
            self.ecnt[ename] += emitted.get("inc", 0)
            if getattr(self, "log_steps", False):
                self.step_log.append((ename, dict(deps_log), emitted.get("inc", 0),
                                      self.ecnt[ename]))
            cnt = (ename, self.ecnt[ename])
            for w in writes:
                key = id(w) if not isinstance(w, (str, tuple)) else w
                self.res[key] = {"w": cnt, "r": []}
            for r in reads:
                key = id(r) if not isinstance(r, (str, tuple)) else r
                st = self.res.setdefault(key, {"w": None, "r": []})
                st["r"].append(cnt)
            return None
        return run

    def barrier(self):
        final = dict(self.ecnt)
        for ename in ENGINES:
            def body(eng, ename=ename):
                for e in ENGINES + ("cc",):
                    if final[e] > 0 and final[e] > self.waited[ename].get(e, 0):
                        eng.wait_ge(self.esem[e], final[e])
            getattr(self.block, ename)(body)
            for e in ENGINES + ("cc",):
                if final[e] > 0:
                    self.waited[ename][e] = max(self.waited[ename].get(e, 0), final[e])
        for e in ENGINES:
            self.safe[e] = self.ecnt[e]
        self.res.clear()

    def done(self):
        self.barrier()
        self.ctx.close()
        return self.nc



def host_constants():
    c = {}
    c["ident"] = np.eye(128, dtype=np.float32)
    c["ones_col"] = np.ones((128, 1), dtype=np.float32)
    c["ones_row"] = np.ones((1, 128), dtype=np.float32)
    kk, qq = np.meshgrid(np.arange(128), np.arange(128), indexing="ij")
    c["tri_kq"] = (kk <= qq).astype(np.float32)
    hd = 64
    fraction = 2.0 * np.arange(0, hd // 2) / hd
    timescale = 10000.0 ** fraction
    pos = np.arange(S, dtype=np.float64)
    ang = pos[None, :] / timescale[:, None]
    c["rope_sin"] = np.concatenate([np.sin(ang), np.sin(ang)], 0).astype(np.float32)
    c["rope_cos"] = np.concatenate([np.cos(ang), np.cos(ang)], 0).astype(np.float32)
    return c


def load_consts(p, want=("ident", "ones_col", "ones_row", "tri_kq")):
    shapes = {"ident": (128, 128), "ones_col": (128, 1), "ones_row": (1, 128),
              "tri_kq": (128, 128), "rope_sin": (64, S), "rope_cos": (64, S)}
    if not hasattr(p, "_consts"):
        p._consts = {}
    cs = p._consts
    want = [nm for nm in want if nm + "_bf" not in cs]
    for nm in want:
        ap = p.inp("c_" + nm, shapes[nm], F32)
        bf = p.sb("c_" + nm + "_bf", shapes[nm], BF16)
        cs[nm + "_bf"] = bf

        @p.step("gpsimd", writes=[bf], dma=True)
        def _(g, ap=ap, bf=bf):
            return g.dma_start(out=bf[:, :], in_=ap[:, :])
        if nm in ("ones_row", "ones_col"):
            ff = p.sb("c_" + nm + "_f", shapes[nm], F32)
            cs[nm + "_f"] = ff

            @p.step("sync", writes=[ff], dma=True)
            def _(sy, ap=ap, ff=ff):
                return sy.dma_start(out=ff[:, :], in_=ap[:, :])
    return cs



def xpool(p):
    return p.ring("XPOOL", 64, [128, 512], BF16)


def pool_load(p, tag, X_ap, KC, NCH, n0=0, x_row0=0):
    tiles = xpool(p)
    key = (tag, KC, NCH, n0, x_row0)
    if getattr(p, "_pool_tag", None) == key:
        return (tiles, KC, NCH, n0)
    p._pool_tag = key
    for kc in range(KC):
        for ci in range(NCH):
            t = tiles[kc * NCH + ci]
            a = n0 + ci * 512
            eng = "sync" if (kc + ci) % 2 == 0 else "gpsimd"

            @p.step(eng, writes=[t], dma=True)
            def _(q, t=t, kc=kc, a=a):
                return q.dma_start(out=t[:, :],
                                   in_=X_ap[x_row0 + kc * 128:x_row0 + (kc + 1) * 128,
                                            a:a + 512])
    return (tiles, KC, NCH, n0)


def pool_use(p, tag):
    key = getattr(p, "_pool_tag", None)
    assert key is not None and key[0] == tag, f"pool tag {key} != {tag}"
    return (xpool(p), key[1], key[2], key[3])



def sweep_proj_fm(p, W_ap, X_ap, O_ap, F_out, kdim=D, n0=0, n1=S,
                  bias_ap=None, evac=None, out_dt=BF16, o_row0=0, x_row0=0,
                  w_col0=0, nbank=4, acc_scale=None, x_col0=0, x_pool=None,
                  evac2=None):
    KC = kdim // 128
    FT = cdiv(F_out, 128)
    NCH = cdiv(n1 - n0, 512)
    if KC > 8:
        assert x_pool is not None
        _wt16 = p.ring("swX16", 32, [128, 512], BF16)[24:32]
        wbufs = [_wt16[0:4], _wt16[4:8]]

        def wtile(buf, kc):
            return wbufs[buf][kc // 4], (kc % 4) * 128
    else:
        wbig = p.ring(f"swW{KC}", 2, [128, KC * 128], BF16)

        def wtile(buf, kc):
            return wbig[buf], kc * 128
    if x_pool is not None:
        ptiles, pKC, pNCH, pn0 = x_pool
        assert pKC == KC
    elif KC <= 8:
        nxb = 2 * KC
        xt = p.ring("swX8", 16, [128, 512], BF16)[:nxb]
    else:
        nxb = KC
        xt = p.ring("swX16", 32, [128, 512], BF16)[:nxb]
    ob = p.ring("swO" + str(out_dt), 2, [128, 512], out_dt)
    bb = p.ring("swB", 2, [128, 1], F32)
    pss = p.ps[:nbank]
    pend = []

    def _flush2():
        if not pend:
            return
        items = list(pend)
        pend.clear()
        evac2(p, items)
        for (ps_, o_, b_, ft_, fw_, a_) in items:
            @p.step("gpsimd", reads=[o_], dma=True)
            def _(g, o_=o_, ft_=ft_, fw_=fw_, a_=a_):
                return g.dma_start(
                    out=O_ap[o_row0 + ft_ * 128:o_row0 + ft_ * 128 + fw_, a_:a_ + 512],
                    in_=o_[0:fw_, :])

    for ft in range(FT):
        fw = min(128, F_out - ft * 128)
        wwr = list(wbufs[ft % 2]) if KC > 8 else [wbig[ft % 2]]

        @p.step("gpsimd", writes=wwr, dma=True)
        def _(g, ft=ft, fw=fw):
            outs = []
            for kc in range(KC):
                wt_, off = wtile(ft % 2, kc)
                outs.append(g.dma_start(out=wt_[:, off:off + fw],
                                        in_=W_ap[kc * 128:(kc + 1) * 128,
                                                 w_col0 + ft * 128:w_col0 + ft * 128 + fw]))
            return outs
        if bias_ap is not None:
            @p.step("sync", writes=[bb[ft % 2]], dma=True)
            def _(sy, ft=ft, fw=fw):
                return sy.dma_start(out=bb[ft % 2][0:fw, :],
                                    in_=bias_ap[w_col0 + ft * 128:w_col0 + ft * 128 + fw, :])

        for ci in range(NCH):
            a = n0 + ci * 512
            if x_pool is not None:
                xs = [ptiles[kc * pNCH + (a - pn0) // 512] for kc in range(KC)]
            else:
                xs = [xt[(ci % (nxb // KC)) * KC + kc] for kc in range(KC)]
                for kc in range(KC):
                    @p.step("sync", writes=[xs[kc]], dma=True)
                    def _(sy, kc=kc, a=a, xs=xs):
                        return sy.dma_start(out=xs[kc][:, :],
                                            in_=X_ap[x_row0 + kc * 128:x_row0 + (kc + 1) * 128,
                                                     x_col0 + a:x_col0 + a + 512])
            idx = ft * NCH + ci
            ps = pss[idx % nbank]

            @p.step("tensor", reads=wwr + xs, writes=[ps])
            def _(te, ft=ft, fw=fw, ps=ps, xs=xs):
                last = None
                for kc in range(KC):
                    wt_, off = wtile(ft % 2, kc)
                    last = te.matmul(ps[0:fw, :], wt_[:, off:off + fw],
                                     xs[kc][:, :], start=(kc == 0), stop=(kc == KC - 1))
                return last

            o = ob[idx % 2]
            if evac2 is not None:
                pend.append((ps, o, bb[ft % 2], ft, fw, a))
                if len(pend) == 2:
                    _flush2()
                continue
            if evac is not None:
                evac(p, ps, o, ft, ci, fw)
            else:
                rd = [ps] + ([bb[ft % 2]] if bias_ap is not None else [])

                @p.step("scalar", reads=rd, writes=[o])
                def _(sc, ps=ps, o=o, ft=ft, fw=fw):
                    kw = {}
                    fn = AF.Copy
                    if bias_ap is not None:
                        kw["bias"] = bb[ft % 2][0:fw, :]
                        fn = AF.Identity
                    if acc_scale is not None:
                        kw["scale"] = acc_scale
                        fn = AF.Identity
                    return sc.activation(out=o[0:fw, :], in_=ps[0:fw, :], func=fn, **kw)

            @p.step("gpsimd", reads=[o], dma=True)
            def _(g, o=o, ft=ft, fw=fw, a=a):
                return g.dma_start(out=O_ap[o_row0 + ft * 128:o_row0 + ft * 128 + fw, a:a + 512],
                                   in_=o[0:fw, :])
    if evac2 is not None:
        _flush2()


def sweep_proj_tm(p, X_ap, W_ap, O_ap, F_out, kdim=D, n0=0, n1=S, x_row0=0,
                  bias_ap=None, w_col0=0, out_dt=BF16, evac=None, consts=None,
                  o_col0=0, nbank=4, x_pool=None):
    KC = kdim // 128
    FCH = cdiv(F_out, 512)
    nt0, nt1 = n0 // 128, n1 // 128
    wt = p.ring("swX16", 32, [128, 512], BF16)[16:16 + KC]
    assert F_out <= 512, "sweep_proj_tm assumes F_out<=512"
    xt = p.ring(f"stX{KC}", 2, [128, KC * 128], BF16) if x_pool is None else None
    ob = p.ring("stO" + str(out_dt), 2, [128, 512], out_dt)
    pss = p.ps[:nbank]
    brow = p.sb("stBr", [1, 512], BF16) if bias_ap is not None else None

    for kc in range(KC):
        @p.step("gpsimd", writes=[wt[kc]], dma=True)
        def _(g, kc=kc):
            return g.dma_start(out=wt[kc][:, 0:F_out],
                               in_=W_ap[kc * 128:(kc + 1) * 128, w_col0:w_col0 + F_out])
    if bias_ap is not None:
        @p.step("gpsimd", writes=[brow], dma=True)
        def _(g):
            return g.dma_start(out=brow[0:1, 0:F_out],
                               in_=bias_ap[0:1, w_col0:w_col0 + F_out])

    if x_pool is not None:
        ptiles, pKC, pNCH, pn0 = x_pool
        assert pKC == KC and x_row0 == 0
    for nt in range(nt0, nt1):
        if x_pool is not None:
            pci = (nt * 128 - pn0) // 512
            off = (nt * 128 - pn0) % 512
            xsl = [ptiles[kc * pNCH + pci] for kc in range(KC)]

            def xop(kc):
                return xsl[kc][:, off:off + 128]
            xreads = xsl
        else:
            xv = xt[nt % 2]

            @p.step("sync", writes=[xv], dma=True)
            def _(sy, nt=nt, xv=xv):
                return [sy.dma_start(out=xv[:, kc * 128:(kc + 1) * 128],
                                     in_=X_ap[x_row0 + kc * 128:x_row0 + (kc + 1) * 128,
                                              nt * 128:(nt + 1) * 128])
                        for kc in range(KC)]

            def xop(kc, xv=xv):
                return xv[:, kc * 128:(kc + 1) * 128]
            xreads = [xv]
        idx = nt - nt0
        ps = pss[idx % nbank]
        rd = xreads + list(wt[:KC]) + ([brow] if bias_ap is not None else [])

        @p.step("tensor", reads=rd, writes=[ps])
        def _(te, xop=xop, ps=ps):
            last = None
            for kc in range(KC):
                last = te.matmul(ps[:, 0:F_out], xop(kc),
                                 wt[kc][:, 0:F_out], start=(kc == 0),
                                 stop=(bias_ap is None and kc == KC - 1))
            if bias_ap is not None:
                last = te.matmul(ps[:, 0:F_out], consts["ones_row_bf"][0:1, :],
                                 brow[0:1, 0:F_out], start=False, stop=True)
            return last
        o = ob[idx % 2]
        if evac is not None:
            evac(p, ps, o, nt)
        else:
            @p.step("scalar", reads=[ps], writes=[o])
            def _(sc, ps=ps, o=o):
                return sc.activation(out=o[:, 0:F_out], in_=ps[:, 0:F_out], func=AF.Copy)

        @p.step("gpsimd", reads=[o], dma=True)
        def _(g, o=o, nt=nt):
            return g.dma_start(out=O_ap[nt * 128:(nt + 1) * 128, o_col0:o_col0 + F_out],
                               in_=o[:, 0:F_out])


def sweep_ln_fm(p, X_ap, O_ap, consts, n0=0, n1=S, kind="ln", gamma_ap=None,
                scale=1.0, out_dt=BF16, also_tm_ap=None, out_pool_tag=None,
                out_dram=True):
    NCH = (n1 - n0) // 512
    xt = p.ring("swX8", 16, [128, 512], BF16)
    sq = p.ring("lnSq", 2, [128, 512], BF16)
    ob = sq
    assert out_dt == BF16
    ptiles = None
    if out_pool_tag is not None:
        assert out_dt == BF16
        ptiles = xpool(p)
        p._pool_tag = (out_pool_tag, DC, NCH, n0, 0)
    stats = p.ps[4]
    _lnrows = p.ring("lnrows", 5, [1, 512], F32)
    mu_r, var_r, std_r, r_r, mr_r = _lnrows
    bc_r, bc_mr = p.ps[5], p.ps[6]
    bcs = p.ring("f32buf", 2, [128, 1024], F32)[0]
    gam = None
    if gamma_ap is not None:
        gam = [p.sb("lnG", [128, 1], F32) for _ in range(DC)]
        for dc in range(DC):
            @p.step("sync", writes=[gam[dc]], dma=True)
            def _(sy, dc=dc):
                return sy.dma_start(out=gam[dc][:, :],
                                    in_=gamma_ap[dc * 128:(dc + 1) * 128, :])
    tmp = p.ring("lnT", 2, [128, 512], BF16)
    tout = p.ring("t2fXb", 2, [128, D], BF16) if also_tm_ap is not None else None

    for ci in range(NCH):
        a = n0 + ci * 512
        xs = [xt[(ci % 2) * DC + dc] for dc in range(DC)]
        for dc in range(DC):
            @p.step("gpsimd", writes=[xs[dc]], dma=True)
            def _(sy, dc=dc, a=a):
                return sy.dma_start(out=xs[dc][:, :], in_=X_ap[dc * 128:(dc + 1) * 128, a:a + 512])
        for dc in range(DC):
            sqb = sq[dc % 2]

            @p.step("vector", reads=[xs[dc]], writes=[sqb])
            def _(v, dc=dc, sqb=sqb):
                return v.tensor_mul(out=sqb[:, :], in0=xs[dc][:, :], in1=xs[dc][:, :])

            @p.step("tensor", reads=[xs[dc], sqb, consts["ones_col_bf"]],
                    writes=[stats])
            def _(te, dc=dc, sqb=sqb):
                te.matmul(stats[0:1, :], consts["ones_col_bf"][:, :], xs[dc][:, :],
                          start=(dc == 0), stop=(dc == DC - 1))
                return te.matmul(stats[64:65, :], consts["ones_col_bf"][:, :], sqb[:, :],
                                 start=(dc == 0), stop=(dc == DC - 1))
        if kind == "ln":
            @p.step("vector", reads=[stats], writes=[mu_r])
            def _(v):
                return v.tensor_scalar(out=mu_r[:, :], in0=stats[0:1, :], scalar1=1.0 / D,
                                       scalar2=None, op0=ALU.mult)

            @p.step("scalar", reads=[mu_r], writes=[var_r])
            def _(sc):
                return sc.activation(out=var_r[:, :], in_=mu_r[:, :], func=AF.Square)

            @p.step("vector", reads=[stats, var_r], writes=[var_r])
            def _(v):
                return v.scalar_tensor_tensor(out=var_r[:, :], in0=stats[64:65, :],
                                              scalar=1.0 / D, in1=var_r[:, :],
                                              op0=ALU.mult, op1=ALU.subtract)

            @p.step("vector", reads=[var_r], writes=[var_r])
            def _(v):
                return v.tensor_scalar(out=var_r[:, :], in0=var_r[:, :], scalar1=EPS_LN,
                                       scalar2=None, op0=ALU.add)

            @p.step("scalar", reads=[var_r], writes=[std_r])
            def _(sc):
                return sc.activation(out=std_r[:, :], in_=var_r[:, :], func=AF.Sqrt)

            @p.step("vector", reads=[std_r], writes=[r_r])
            def _(v):
                return v.reciprocal(out=r_r[:, :], in_=std_r[:, :])

            @p.step("vector", reads=[mu_r, r_r], writes=[mr_r])
            def _(v):
                return v.scalar_tensor_tensor(out=mr_r[:, :], in0=mu_r[:, :], scalar=-1.0,
                                              in1=r_r[:, :], op0=ALU.mult, op1=ALU.mult)
        else:
            @p.step("scalar", reads=[stats], writes=[std_r])
            def _(sc):
                return sc.activation(out=std_r[:, :], in_=stats[64:65, :], func=AF.Sqrt)

            @p.step("vector", reads=[std_r], writes=[std_r])
            def _(v):
                return v.tensor_scalar(out=std_r[:, :], in0=std_r[:, :], scalar1=1e-12,
                                       scalar2=None, op0=ALU.max)

            @p.step("vector", reads=[std_r], writes=[r_r])
            def _(v):
                return v.reciprocal(out=r_r[:, :], in_=std_r[:, :])

            @p.step("vector", reads=[r_r], writes=[r_r])
            def _(v):
                return v.tensor_scalar(out=r_r[:, :], in0=r_r[:, :], scalar1=float(scale),
                                       scalar2=None, op0=ALU.mult)

            @p.step("vector", writes=[mr_r])
            def _(v):
                return v.memset(mr_r[:, :], 0.0)

        @p.step("tensor", reads=[r_r, mr_r, consts["ones_row_f"]], writes=[bc_r, bc_mr])
        def _(te):
            te.matmul(bc_r[:, :], consts["ones_row_f"][0:1, :], r_r[:, :],
                      start=True, stop=True)
            return te.matmul(bc_mr[:, :], consts["ones_row_f"][0:1, :], mr_r[:, :],
                             start=True, stop=True)

        @p.step("vector", reads=[bc_r, bc_mr], writes=[bcs])
        def _(v):
            v.tensor_copy(out=bcs[:, 0:512], in_=bc_r[:, :])
            return v.tensor_copy(out=bcs[:, 512:1024], in_=bc_mr[:, :])

        for dc in range(DC):
            t = tmp[dc % 2]
            o = ptiles[dc * NCH + ci] if ptiles is not None else ob[dc % 2]

            @p.step("vector", reads=[xs[dc], bcs], writes=[t])
            def _(v, dc=dc, t=t):
                return v.tensor_mul(out=t[:, :], in0=xs[dc][:, :], in1=bcs[:, 0:512])

            if kind == "ln":
                @p.step("vector", reads=[t, bcs], writes=[o])
                def _(v, t=t, o=o):
                    return v.tensor_add(out=o[:, :], in0=t[:, :], in1=bcs[:, 512:1024])
            else:
                @p.step("vector", reads=[t] + ([gam[dc]] if gam else []), writes=[o])
                def _(v, t=t, o=o, dc=dc):
                    if gam:
                        return v.tensor_scalar(out=o[:, :], in0=t[:, :],
                                               scalar1=gam[dc][:, :], scalar2=None,
                                               op0=ALU.mult)
                    return v.tensor_copy(out=o[:, :], in_=t[:, :])

            if out_dram:
                @p.step("gpsimd", reads=[o], dma=True)
                def _(g, o=o, dc=dc, a=a):
                    return g.dma_start(out=O_ap[dc * 128:(dc + 1) * 128, a:a + 512], in_=o[:, :])

            if also_tm_ap is not None:
                tp = p.psT
                to = tout[dc % 2]

                @p.step("tensor", reads=[o, consts["ident_bf"]], writes=[tp])
                def _(te, o=o, tp=tp):
                    last = None
                    for q in range(4):
                        last = te.transpose(tp[:, q * 128 + 0:q * 128 + 128],
                                            o[:, q * 128:(q + 1) * 128],
                                            consts["ident_bf"][:, :])
                    return last

                @p.step("vector", reads=[tp], writes=[to])
                def _(v, tp=tp, to=to):
                    return v.tensor_copy(out=to[:, :], in_=tp[:, :])

                @p.step("gpsimd", reads=[to], dma=True)
                def _(g, to=to, dc=dc, a=a):
                    return [g.dma_start(
                        out=also_tm_ap[a + q * 128:a + (q + 1) * 128,
                                       dc * 128:(dc + 1) * 128],
                        in_=to[:, q * 128:(q + 1) * 128]) for q in range(4)]


def sweep_add_fm(p, A_ap, B_ap, O_ap, n0=0, n1=S, scale_b=None, C_ap=None,
                 scale_c=None, out_dt=BF16, rows=D):
    RT = rows // 128
    RN = 2
    at = p.ring("adA", RN, [128, 512], BF16)
    bt = p.ring("adB", RN, [128, 512], BF16)
    ct = p.ring("adC", RN, [128, 512], BF16) if C_ap is not None else None
    tt = p.ring("adT", 4, [128, 512], F32)[:RN]
    ob = p.ring("adO", RN, [128, 512], out_dt)
    NCH = (n1 - n0) // 512
    for rt in range(RT):
        for ci in range(NCH):
            a = n0 + ci * 512
            idx = rt * NCH + ci
            av, bv = at[idx % RN], bt[idx % RN]
            o = ob[idx % RN]

            @p.step("gpsimd", writes=[av], dma=True)
            def _(sy, rt=rt, a=a, av=av):
                return sy.dma_start(out=av[:, :], in_=A_ap[rt * 128:(rt + 1) * 128, a:a + 512])

            @p.step("gpsimd", writes=[bv], dma=True)
            def _(sy, rt=rt, a=a, bv=bv):
                return sy.dma_start(out=bv[:, :], in_=B_ap[rt * 128:(rt + 1) * 128, a:a + 512])
            if C_ap is not None:
                cv = ct[idx % RN]

                @p.step("gpsimd", writes=[cv], dma=True)
                def _(sy, rt=rt, a=a, cv=cv):
                    return sy.dma_start(out=cv[:, :], in_=C_ap[rt * 128:(rt + 1) * 128, a:a + 512])
            t = tt[idx % RN]

            @p.step("vector", reads=[av, bv], writes=[t])
            def _(v, av=av, bv=bv, t=t):
                if scale_b is None:
                    return v.tensor_add(out=t[:, :], in0=av[:, :], in1=bv[:, :])
                return v.scalar_tensor_tensor(out=t[:, :], in0=bv[:, :], scalar=float(scale_b),
                                              in1=av[:, :], op0=ALU.mult, op1=ALU.add)
            if C_ap is not None:
                cv = ct[idx % RN]

                @p.step("vector", reads=[t, cv], writes=[o])
                def _(v, t=t, cv=cv, o=o):
                    return v.scalar_tensor_tensor(out=o[:, :], in0=cv[:, :], scalar=float(scale_c),
                                                  in1=t[:, :], op0=ALU.mult, op1=ALU.add)
            else:
                @p.step("vector", reads=[t], writes=[o])
                def _(v, t=t, o=o):
                    return v.tensor_copy(out=o[:, :], in_=t[:, :])

            @p.step("gpsimd", reads=[o], dma=True)
            def _(g, o=o, rt=rt, a=a):
                return g.dma_start(out=O_ap[rt * 128:(rt + 1) * 128, a:a + 512], in_=o[:, :])


def collective_pair(p, in_ap, out_ap, op=ALU.add, groups=None):
    if groups is None:
        groups = [[0, 1], [2, 3], [4, 5], [6, 7]]
    final = dict(p.ecnt)

    def body(g):
        for e in ENGINES:
            if final[e] > 0:
                g.wait_ge(p.esem[e], final[e])
        ins = g.collective_compute(
            "AllReduce", op,
            replica_groups=groups,
            ins=[in_ap.tensor.ap().opt()],
            outs=[out_ap.tensor.ap().opt()],
        )
        ins.then_inc(p.esem["cc"])
    p.block.gpsimd(body)
    p.ecnt["cc"] += 1
    fence = p.ring("ccfence", 1, [128, 128], BF16)[0]
    rr = min(128, out_ap.shape[0])
    cc2 = min(128, out_ap.shape[1])

    @p.step("gpsimd", writes=[fence], dma=True)
    def _(g):
        g.wait_ge(p.esem["cc"], p.ecnt["cc"])
        return g.dma_start(out=fence[0:rr, 0:cc2], in_=out_ap[0:rr, 0:cc2])
    p.barrier()


def wait_cc(p, ename, n=None):
    pass


"""Launch builders. Each build_X() returns (nc, in_names) and a matching
host-side input-prep function make_inputs_X(inputs, consts, core) -> dict.

Core roles: core c -> batch b=c//2, half j=c%2; heads 8j..8j+8, experts 4j..4j+4.
"""

HD = 64
NH_LOC = 8
CHUNK = 128



def phase_tm2fm(p, X_tm_ap, O_fm_ap, consts, in_dt=F32, rows=S, cols=D):
    RT = rows // 128
    CC = cols // 128
    assert cols == D
    xf = p.ring("f32buf", 2, [128, cols], F32)
    x8 = p.ring("swX8", 16, [128, 512], BF16)
    for t in range(RT):
        a = xf[t % 2]
        xb2 = [x8[(t % 2) * 2], x8[(t % 2) * 2 + 1]]
        ob2 = [x8[4 + (t % 2) * 2], x8[4 + (t % 2) * 2 + 1]]

        @p.step("sync", writes=[a], dma=True)
        def _(sy, t=t, a=a):
            return sy.dma_start(out=a[:, :], in_=X_tm_ap[t * 128:(t + 1) * 128, :])

        @p.step("vector", reads=[a], writes=xb2)
        def _(v, a=a, xb2=xb2):
            v.tensor_copy(out=xb2[0][:, :], in_=a[:, 0:512])
            return v.tensor_copy(out=xb2[1][:, :], in_=a[:, 512:1024])

        tpv = p.psT

        @p.step("tensor", reads=xb2 + [consts["ident_bf"]], writes=[tpv])
        def _(te, xb2=xb2, tpv=tpv):
            last = None
            for dc in range(CC):
                last = te.transpose(tpv[:, dc * 128:(dc + 1) * 128],
                                    xb2[dc // 4][:, (dc % 4) * 128:(dc % 4) * 128 + 128],
                                    consts["ident_bf"][:, :])
            return last

        @p.step("vector", reads=[tpv], writes=ob2)
        def _(v, ob2=ob2, tpv=tpv):
            v.tensor_copy(out=ob2[0][:, :], in_=tpv[:, 0:512])
            return v.tensor_copy(out=ob2[1][:, :], in_=tpv[:, 512:1024])

        @p.step("gpsimd", reads=ob2, dma=True)
        def _(g, ob2=ob2, t=t):
            return [g.dma_start(out=O_fm_ap[dc * 128:(dc + 1) * 128, t * 128:(t + 1) * 128],
                                in_=ob2[dc // 4][:, (dc % 4) * 128:(dc % 4) * 128 + 128])
                    for dc in range(CC)]


def evac2_gelu(p, items):
    rd = []
    wr = []
    for (ps_, o_, b_, ft_, fw_, a_) in items:
        rd += [ps_, b_]
        wr.append(o_)

    @p.step("scalar", reads=rd, writes=wr)
    def _(sc):
        last = None
        for (ps_, o_, b_, ft_, fw_, a_) in items:
            last = sc.activation(out=o_[0:fw_, :], in_=ps_[0:fw_, :], func=AF.Gelu,
                                 bias=b_[0:fw_, :])
        return last


def evac2_elu1(p, items):
    adt = p.ring("adT", 4, [128, 512], F32)
    tmp = [(adt[n % 2], adt[2 + n % 2]) for n in range(len(items))]

    @p.step("vector", reads=[it[0] for it in items], writes=[t[0] for t in tmp])
    def _(v):
        last = None
        for (ps_, o_, b_, ft_, fw_, a_), (s1, s2) in zip(items, tmp):
            last = v.tensor_scalar(out=s1[0:fw_, :], in0=ps_[0:fw_, :], scalar1=0.0,
                                   scalar2=None, op0=ALU.min)
        return last

    @p.step("scalar", reads=[t[0] for t in tmp], writes=[t[1] for t in tmp])
    def _(sc):
        last = None
        for (ps_, o_, b_, ft_, fw_, a_), (s1, s2) in zip(items, tmp):
            last = sc.activation(out=s2[0:fw_, :], in_=s1[0:fw_, :], func=AF.Exp)
        return last

    @p.step("vector", reads=[it[0] for it in items] + [t[1] for t in tmp],
            writes=[t[0] for t in tmp] + [it[1] for it in items])
    def _(v):
        last = None
        for (ps_, o_, b_, ft_, fw_, a_), (s1, s2) in zip(items, tmp):
            v.tensor_relu(out=s1[0:fw_, :], in_=ps_[0:fw_, :])
            last = v.tensor_add(out=o_[0:fw_, :], in0=s1[0:fw_, :], in1=s2[0:fw_, :])
        return last


def evac_elu1(p, ps, o, ft, ci, fw=128):
    adt = p.ring("adT", 4, [128, 512], F32)
    s1 = adt[ci % 2]
    s2 = adt[2 + ci % 2]

    @p.step("vector", reads=[ps], writes=[s1])
    def _(v):
        return v.tensor_scalar(out=s1[0:fw, :], in0=ps[0:fw, :], scalar1=0.0,
                               scalar2=None, op0=ALU.min)

    @p.step("scalar", reads=[s1], writes=[s2])
    def _(sc):
        return sc.activation(out=s2[0:fw, :], in_=s1[0:fw, :], func=AF.Exp)

    @p.step("vector", reads=[ps], writes=[s1])
    def _(v):
        return v.tensor_relu(out=s1[0:fw, :], in_=ps[0:fw, :])

    @p.step("vector", reads=[s1, s2], writes=[o])
    def _(v):
        return v.tensor_add(out=o[0:fw, :], in0=s1[0:fw, :], in1=s2[0:fw, :])



def _io(p, pre, pfx, standalone, debug):
    pre = pre or {}

    def IN(n, sh, dt=F32):
        return pre[n] if n in pre else p.inp(pfx + n, sh, dt)

    def OUT(n, sh, dt=BF16):
        if n in pre:
            return pre[n]
        return p.outp(pfx + n, sh, dt) if standalone else p.dram(pfx + n, sh, dt)
    return IN, OUT



def build_A(debug=False, p=None, pre=None, pfx=""):
    standalone = p is None
    if standalone:
        p = P("A")
    IN, OUT = _io(p, pre, pfx, standalone, debug)
    x_in = IN("x_in", [S, D], F32)
    wq = IN("wq", [D, 512], F32)
    wk = IN("wk", [D, 512], F32)
    wv = IN("wv", [D, 512], F32)
    wo = IN("wo", [512, D], F32)
    consts = {}
    cs = load_consts(p, ("ident", "ones_col", "ones_row", "tri_kq"))
    consts.update(cs)

    mk = p.outp if debug else p.dram
    xfm = mk("xfm", [D, S])
    hs = mk("hs", [D, S])
    qf = mk("qf", [512, S])
    kf = mk("kf", [512, S])
    ktm = mk("ktm", [S, 512])
    vtm = mk("vtm", [S, 512])
    of = mk("of", [512, S])
    cc_in = p.dram("cc_in", [D, S])
    cc_out = p.dram("cc_out", [D, S])
    x1_out = OUT("x1", [D, S])
    hs1_out = OUT("hs1", [D, S])

    phase_tm2fm(p, x_in, xfm, consts)
    p.barrier()
    sweep_ln_fm(p, xfm, hs, consts, kind="ln", out_pool_tag="hs", out_dram=False)
    p.barrier()
    hs_pool = pool_use(p, "hs")
    sweep_proj_fm(p, wq, hs, qf, 512, evac2=evac2_elu1, x_pool=hs_pool)
    p.barrier()
    sweep_proj_fm(p, wk, hs, kf, 512, evac2=evac2_elu1, x_pool=hs_pool)
    p.barrier()

    def evac_elu1_tm(p_, ps, o, nt):
        evac_elu1(p_, ps, o, 0, nt, fw=128)
    sweep_proj_tm(p, hs, wk, ktm, 512, evac=evac_elu1_tm, consts=consts,
                  x_pool=hs_pool)
    p.barrier()
    sweep_proj_tm(p, hs, wv, vtm, 512, consts=consts, x_pool=hs_pool)
    p.barrier()

    NCH = S // CHUNK
    _sw8 = p.ring("swX8", 16, [128, 512], BF16)
    kfc = _sw8[0:2]
    qfc = _sw8[2:4]
    ktc = _sw8[4:6]
    vtc = _sw8[6:8]
    psA = [p.ps[4], p.ps[5]]
    psO = [p.ps[0], p.ps[1]]
    psB = [p.ps[2], p.ps[2]]
    psS = [p.ps[3], p.ps[3]]
    amask = p.ring("laM", 2, [128, 128], BF16)
    denr = p.ring("laD", 2, [1, 128], F32)
    bcs = p.ring("laBc", 2, [64, 128], F32)
    otile = p.ring("laOt", 2, [128, 128], BF16)
    _adt = p.ring("adT", 4, [128, 512], F32)
    Sf = [_adt[2 + h // 4][0:64, 65 * (h % 4):65 * (h % 4) + 65] for h in range(NH_LOC)]
    Sb = [p.sb(f"laSb{h}", [128, 65], BF16) for h in range(NH_LOC)]

    for h in range(NH_LOC):
        @p.step("vector", writes=[Sf[h]])
        def _(v, h=h):
            return v.memset(Sf[h][:, :], 0.0)

        @p.step("vector", reads=[Sf[h]], writes=[Sb[h]])
        def _(v, h=h):
            v.tensor_copy(out=Sb[h][0:64, :], in_=Sf[h][:, :])
            return v.tensor_copy(out=Sb[h][64:128, :], in_=Sf[h][:, :])

    for c in range(NCH):
        a = c * CHUNK
        kfv, qfv, ktv, vtv = kfc[c % 2], qfc[c % 2], ktc[c % 2], vtc[c % 2]

        @p.step("sync", writes=[kfv], dma=True)
        def _(sy, a=a, kfv=kfv):
            return [sy.dma_start(out=kfv[:, r * 128:(r + 1) * 128],
                                 in_=kf[r * 128:(r + 1) * 128, a:a + 128])
                    for r in range(4)]

        @p.step("sync", writes=[qfv], dma=True)
        def _(sy, a=a, qfv=qfv):
            return [sy.dma_start(out=qfv[:, r * 128:(r + 1) * 128],
                                 in_=qf[r * 128:(r + 1) * 128, a:a + 128])
                    for r in range(4)]

        @p.step("sync", writes=[ktv], dma=True)
        def _(sy, a=a, ktv=ktv):
            return sy.dma_start(out=ktv[:, :], in_=ktm[a:a + 128, :])

        @p.step("sync", writes=[vtv], dma=True)
        def _(sy, a=a, vtv=vtv):
            return sy.dma_start(out=vtv[:, :], in_=vtm[a:a + 128, :])

        for h in range(NH_LOC):
            r, base = h // 2, 64 * (h % 2)
            kslice = kfv[:, r * 128:(r + 1) * 128]
            idx = (c * NH_LOC + h)
            pa = psA[idx % 2]
            po = psO[idx % 2]
            pb = psB[idx % 2]
            pS = psS[idx % 2]
            am = amask[idx % 2]
            dr = denr[idx % 2]
            bc = bcs[idx % 2]
            ot = otile[idx % 2]

            @p.step("tensor", reads=[kfv, qfv], writes=[pa])
            def _(te, r=r, base=base, kfv=kfv, qfv=qfv, pa=pa):
                return te.matmul(pa[:, 0:128],
                                 kfv[base:base + 64, r * 128:(r + 1) * 128],
                                 qfv[base:base + 64, r * 128:(r + 1) * 128],
                                 start=True, stop=True)

            @p.step("vector", reads=[pa, consts["tri_kq_bf"]], writes=[am])
            def _(v, pa=pa, am=am):
                return v.tensor_mul(out=am[:, :], in0=pa[:, 0:128], in1=consts["tri_kq_bf"][:, :])

            @p.step("tensor", reads=[am, vtv, Sb[h], consts["ones_col_bf"], qfv], writes=[po])
            def _(te, h=h, r=r, base=base, am=am, vtv=vtv, qfv=qfv, po=po):
                hl = 64 * h
                te.matmul(po[0:64, 0:128], vtv[:, hl:hl + 64], am[:, :], start=True, stop=False)
                te.matmul(po[0:64, 0:128], Sb[h][base:base + 64, 0:64],
                          qfv[base:base + 64, r * 128:(r + 1) * 128], start=False, stop=True)
                te.matmul(po[64:65, 0:128], consts["ones_col_bf"][:, :], am[:, :],
                          start=True, stop=False)
                return te.matmul(po[64:65, 0:128], Sb[h][base:base + 64, 64:65],
                                 qfv[base:base + 64, r * 128:(r + 1) * 128],
                                 start=False, stop=True)

            @p.step("vector", reads=[po], writes=[dr])
            def _(v, po=po, dr=dr):
                return v.tensor_scalar(out=dr[:, :], in0=po[64:65, 0:128], scalar1=1e-6,
                                       scalar2=None, op0=ALU.add)

            @p.step("vector", reads=[dr], writes=[dr])
            def _(v, dr=dr):
                return v.reciprocal(out=dr[:, :], in_=dr[:, :])

            @p.step("tensor", reads=[dr, consts["ones_row_f"]], writes=[pb])
            def _(te, dr=dr, pb=pb):
                return te.matmul(pb[0:64, 0:128], consts["ones_row_f"][0:1, 0:64], dr[:, :],
                                 start=True, stop=True)

            @p.step("vector", reads=[pb], writes=[bc])
            def _(v, pb=pb, bc=bc):
                return v.tensor_copy(out=bc[:, :], in_=pb[0:64, 0:128])

            @p.step("vector", reads=[po, bc], writes=[ot])
            def _(v, po=po, bc=bc, ot=ot):
                return v.tensor_mul(out=ot[0:64, :], in0=po[0:64, 0:128], in1=bc[:, :])

            @p.step("gpsimd", reads=[ot], dma=True)
            def _(g, ot=ot, h=h, a=a):
                return g.dma_start(out=of[h * 64:(h + 1) * 64, a:a + 128], in_=ot[0:64, :])

            @p.step("tensor", reads=[ktv, vtv, consts["ones_col_bf"]], writes=[pS])
            def _(te, h=h, ktv=ktv, vtv=vtv, pS=pS):
                hl = 64 * h
                te.matmul(pS[0:64, 0:64], ktv[:, hl:hl + 64], vtv[:, hl:hl + 64],
                          start=True, stop=True)
                return te.matmul(pS[0:64, 64:65], ktv[:, hl:hl + 64],
                                 consts["ones_col_bf"][:, :], start=True, stop=True)

            @p.step("vector", reads=[pS, Sf[h]], writes=[Sf[h]])
            def _(v, h=h, pS=pS):
                return v.tensor_add(out=Sf[h][:, :], in0=Sf[h][:, :], in1=pS[0:64, 0:65])

            @p.step("vector", reads=[Sf[h]], writes=[Sb[h]])
            def _(v, h=h):
                v.tensor_copy(out=Sb[h][0:64, :], in_=Sf[h][:, :])
                return v.tensor_copy(out=Sb[h][64:128, :], in_=Sf[h][:, :])

    p.barrier()
    sweep_proj_fm(p, wo, of, cc_in, D, kdim=512)
    p.barrier()
    if debug:
        cc_dbg = p.outp("cc_dbg", [D, S])

        @p.step("gpsimd", dma=True)
        def _(g):
            return g.dma_start(out=cc_dbg[:, :], in_=cc_in[:, :])
        p.barrier()
    collective_pair(p, cc_in, cc_out)
    wait_cc(p, "sync")
    wait_cc(p, "gpsimd")
    sweep_add_fm(p, xfm, cc_out, x1_out)
    p.barrier()
    sweep_ln_fm(p, x1_out, hs1_out, consts, kind="ln", out_pool_tag="hs1")
    return p.done() if standalone else p


def make_inputs_A(inp, consts, core):
    b, j = core // 2, core % 2
    x = np.ascontiguousarray(inp["L0_attn_x"][b])
    d = {
        "x_in": x,
        "wq": np.ascontiguousarray(inp["L0_attn_wq"][:, 512 * j:512 * (j + 1)]),
        "wk": np.ascontiguousarray(inp["L0_attn_wk"][:, 512 * j:512 * (j + 1)]),
        "wv": np.ascontiguousarray(inp["L0_attn_wv"][:, 512 * j:512 * (j + 1)]),
        "wo": np.ascontiguousarray(inp["L0_attn_wo"][512 * j:512 * (j + 1), :]),
        "c_ident": consts["ident"], "c_ones_col": consts["ones_col"],
        "c_ones_row": consts["ones_row"], "c_tri_kq": consts["tri_kq"],
    }
    return d


C_STAB0 = 12.0
RS32 = 0.17677669529663687


def build_B(debug=False, p=None, pre=None, pfx=""):
    standalone = p is None
    if standalone:
        p = P("B")
    IN, OUT = _io(p, pre, pfx, standalone, debug)
    hs1 = IN("hs1", [D, S], BF16)
    qw = IN("qw", [D, 512], F32)
    qb = IN("qb", [512, 1], F32)
    kw = IN("kw", [D, 512], F32)
    kb = IN("kb", [512, 1], F32)
    vw = IN("vw", [D, 264], F32)
    vb = IN("vb", [1, 264], F32)
    ow = IN("ow", [256, D], F32)
    ob_ = IN("ob", [D, 1], F32)
    rw = IN("rw", [D, 8], F32)
    rb = IN("rb", [1, 8], F32)
    esel = IN("esel", [8, 8], F32)
    cs = load_consts(p, ("ident", "ones_col", "ones_row"))

    mk = p.outp if debug else p.dram
    qf = mk("qf", [512, S])
    kf = mk("kf", [512, S])
    vtm = mk("vtm", [S, 264])
    aof = mk("aof", [256, S])
    cc_in = p.dram("cc_in", [D, S])
    cc_out = p.dram("cc_out", [D, S])
    mla_out = OUT("mla", [D, S])
    comb_out = OUT("comb", [8, S])

    hs1_pool = pool_use(p, "hs1")
    sweep_proj_fm(p, qw, hs1, qf, 512, bias_ap=qb, x_pool=hs1_pool)
    p.barrier()
    sweep_proj_fm(p, kw, hs1, kf, 512, bias_ap=kb, x_pool=hs1_pool)
    p.barrier()
    sweep_proj_tm(p, hs1, vw, vtm, 264, bias_ap=vb, consts=cs, x_pool=hs1_pool)
    p.barrier()

    nbias = p.sb("nbias", [128, 1], F32)

    @p.step("vector", writes=[nbias])
    def _(v):
        return v.memset(nbias[:, :], -C_STAB0)

    qtiles = p.ring("swX8", 16, [128, 512], BF16)[:8]
    vt = p.ring("fbVt", 2, [128, 264], BF16)
    es = p.ring("swX8", 16, [128, 512], BF16)[12:16]
    drow = p.ring("fbD", 2, [1, 512], F32)
    bcsb = [t[0:32, :] for t in p.ring("adT", 4, [128, 512], F32)[:2]]
    otile = p.ring("swX8", 16, [128, 512], BF16)[10:12]
    psS = [p.ps[0], p.ps[1]]
    psB = p.ps[6]

    kf_tiles, _, _, _ = pool_load(p, "Bkf", kf, 4, 8)

    def kslice(r, kt):
        t = kf_tiles[r * 8 + kt // 4]
        off = (kt % 4) * 128
        return t, t[:, off:off + 128]

    for qc in range(8):
        a = qc * 512
        qs = [qtiles[(qc % 2) * 4 + r] for r in range(4)]
        for r in range(4):
            @p.step("sync", writes=[qs[r]], dma=True)
            def _(sy, r=r, a=a, qs=qs):
                return sy.dma_start(out=qs[r][:, :], in_=qf[r * 128:(r + 1) * 128, a:a + 512])
        for hp in range(2):
            for kt in range(NT):
                vv = vt[kt % 2]

                @p.step("sync", writes=[vv], dma=True)
                def _(sy, kt=kt, vv=vv):
                    return sy.dma_start(out=vv[:, :], in_=vtm[kt * 128:(kt + 1) * 128, :])

                for pair in range(2):
                    for g in (2 * pair, 2 * pair + 1):
                        h = hp * 4 + g
                        rq, bq = h // 2, 64 * (h % 2)
                        ps_ = psS[g % 2]
                        ktile, kap = kslice(rq, kt)

                        @p.step("tensor", reads=[ktile, qs[rq]], writes=[ps_])
                        def _(te, rq=rq, bq=bq, ps_=ps_, kap=kap, qs=qs):
                            return te.matmul(ps_[:, :], kap[bq:bq + 32, :],
                                             qs[rq][bq:bq + 32, :], start=True, stop=True)

                    @p.step("scalar", reads=[psS[0], psS[1], nbias],
                            writes=[es[2 * pair], es[2 * pair + 1]])
                    def _(sc, pair=pair):
                        sc.activation(out=es[2 * pair][:, :], in_=psS[0][:, :],
                                      func=AF.Exp, bias=nbias[:, :], scale=RS32)
                        return sc.activation(out=es[2 * pair + 1][:, :], in_=psS[1][:, :],
                                             func=AF.Exp, bias=nbias[:, :], scale=RS32)

                    for g in (2 * pair, 2 * pair + 1):
                        h = hp * 4 + g
                        bank = p.ps[2 + g]
                        ev = es[g]

                        @p.step("tensor", reads=[ev, vv], writes=[bank])
                        def _(te, ev=ev, vv=vv, h=h, kt=kt, bank=bank):
                            return te.matmul(bank[0:33, :], vv[:, 33 * h:33 * h + 33],
                                             ev[:, :], start=(kt == 0), stop=(kt == NT - 1))

            for g in range(4):
                h = hp * 4 + g
                bank = p.ps[2 + g]
                dr = drow[g % 2]
                bc = bcsb[g % 2]
                ot = otile[g % 2]

                @p.step("vector", reads=[bank], writes=[dr])
                def _(v, dr=dr, bank=bank):
                    return v.reciprocal(out=dr[:, :], in_=bank[32:33, :])

                @p.step("tensor", reads=[dr, cs["ones_row_f"]], writes=[psB])
                def _(te, dr=dr):
                    return te.matmul(psB[0:32, :], cs["ones_row_f"][0:1, 0:32], dr[:, :],
                                     start=True, stop=True)

                @p.step("vector", reads=[psB], writes=[bc])
                def _(v, bc=bc):
                    return v.tensor_copy(out=bc[:, :], in_=psB[0:32, :])

                @p.step("vector", reads=[bank, bc], writes=[ot])
                def _(v, bc=bc, ot=ot, bank=bank):
                    return v.tensor_mul(out=ot[0:32, :], in0=bank[0:32, :], in1=bc[:, :])

                @p.step("gpsimd", reads=[ot], dma=True)
                def _(g_, ot=ot, h=h, a=a):
                    return g_.dma_start(out=aof[32 * h:32 * h + 32, a:a + 512], in_=ot[0:32, :])

    p.barrier()
    sweep_proj_fm(p, ow, aof, cc_in, D, kdim=256, bias_ap=ob_)
    p.barrier()
    collective_pair(p, cc_in, cc_out)
    wait_cc(p, "sync")
    wait_cc(p, "gpsimd")
    @p.step("gpsimd", dma=True)
    def _(g):
        return g.dma_start(out=mla_out[:, :], in_=cc_out[:, :])
    p.barrier()

    rwt = [p.sb(f"rwt{kc}", [128, 8], BF16) for kc in range(DC)]
    rbt = p.sb("rbt", [1, 8], BF16)
    for kc in range(DC):
        @p.step("gpsimd", writes=[rwt[kc]], dma=True)
        def _(g, kc=kc):
            return g.dma_start(out=rwt[kc][:, :], in_=rw[kc * 128:(kc + 1) * 128, :])

    @p.step("gpsimd", writes=[rbt], dma=True)
    def _(g):
        return g.dma_start(out=rbt[:, :], in_=rb[:, :])

    mlt = p.ring("t2fXb", 2, [128, D], BF16)
    lg = p.ring("rtL", 2, [128, 8], F32)
    m1 = p.ring("rtm1", 2, [128, 1], F32)
    m2 = p.ring("rtm2", 2, [128, 1], F32)
    den = p.ring("rtden", 2, [128, 1], F32)
    prb = p.ring("rtP", 2, [128, 8], F32)
    sel = p.ring("rtS", 2, [128, 8], F32)
    cmb = p.ring("rtC", 2, [128, 8], BF16)
    cps = p.psT
    cfm = p.ring("rtCf", 2, [8, 128], BF16)
    cfl = p.ring("rtCl", 2, [8, 128], BF16)
    selb = p.sb("selb", [8, 8], BF16)

    @p.step("gpsimd", writes=[selb], dma=True)
    def _(g):
        return g.dma_start(out=selb[:, :], in_=esel[:, :])

    for nt in range(NT):
        ml = mlt[nt % 2]

        @p.step("sync", writes=[ml], dma=True)
        def _(sy, nt=nt, ml=ml):
            return [sy.dma_start(out=ml[:, kc * 128:(kc + 1) * 128],
                                 in_=mla_out[kc * 128:(kc + 1) * 128, nt * 128:(nt + 1) * 128])
                    for kc in range(DC)]
        ps_ = p.ps[4]

        @p.step("tensor", reads=[ml, rbt] + rwt, writes=[ps_])
        def _(te, ml=ml, ps_=ps_):
            for kc in range(DC):
                te.matmul(ps_[:, 0:8], ml[:, kc * 128:(kc + 1) * 128], rwt[kc][:, :],
                          start=(kc == 0), stop=False)
            return te.matmul(ps_[:, 0:8], cs["ones_row_bf"][0:1, :], rbt[0:1, :],
                             start=False, stop=True)
        l, mm1, mm2, dd, pb, se, cb = (lg[nt % 2], m1[nt % 2], m2[nt % 2],
                                       den[nt % 2], prb[nt % 2], sel[nt % 2], cmb[nt % 2])

        @p.step("vector", reads=[ps_], writes=[l])
        def _(v, l=l, ps_=ps_):
            return v.tensor_copy(out=l[:, :], in_=ps_[:, 0:8])

        @p.step("vector", reads=[l], writes=[mm1])
        def _(v, l=l, mm1=mm1):
            return v.reduce_max(out=mm1[:, :], in_=l[:, :], axis=AX.X)

        @p.step("vector", reads=[mm1], writes=[mm1])
        def _(v, mm1=mm1):
            return v.tensor_scalar(out=mm1[:, :], in0=mm1[:, :], scalar1=-1.0,
                                   scalar2=None, op0=ALU.mult)

        @p.step("scalar", reads=[l, mm1], writes=[pb, dd])
        def _(sc, l=l, mm1=mm1, pb=pb, dd=dd):
            return sc.activation(out=pb[:, :], in_=l[:, :], func=AF.Exp,
                                 bias=mm1[:, :], accum_out=dd[:, :])

        @p.step("vector", reads=[dd], writes=[dd])
        def _(v, dd=dd):
            return v.reciprocal(out=dd[:, :], in_=dd[:, :])

        @p.step("vector", reads=[pb, dd], writes=[pb])
        def _(v, pb=pb, dd=dd):
            return v.tensor_scalar(out=pb[:, :], in0=pb[:, :], scalar1=dd[:, :],
                                   scalar2=None, op0=ALU.mult)
        @p.step("vector", reads=[pb], writes=[mm1])
        def _(v, pb=pb, mm1=mm1):
            return v.reduce_max(out=mm1[:, :], in_=pb[:, :], axis=AX.X)

        @p.step("vector", reads=[pb, mm1], writes=[se])
        def _(v, pb=pb, mm1=mm1, se=se):
            return v.tensor_scalar(out=se[:, :], in0=pb[:, :], scalar1=mm1[:, :],
                                   scalar2=None, op0=ALU.is_ge)

        @p.step("vector", reads=[se, pb], writes=[se])
        def _(v, se=se, pb=pb):
            return v.scalar_tensor_tensor(out=se[:, :], in0=se[:, :], scalar=-1e30,
                                          in1=pb[:, :], op0=ALU.mult, op1=ALU.add)

        @p.step("vector", reads=[se], writes=[mm2])
        def _(v, se=se, mm2=mm2):
            return v.reduce_max(out=mm2[:, :], in_=se[:, :], axis=AX.X)

        @p.step("vector", reads=[pb, mm2], writes=[se])
        def _(v, pb=pb, mm2=mm2, se=se):
            return v.tensor_scalar(out=se[:, :], in0=pb[:, :], scalar1=mm2[:, :],
                                   scalar2=None, op0=ALU.is_ge)

        @p.step("vector", reads=[se, pb], writes=[se])
        def _(v, se=se, pb=pb):
            return v.tensor_mul(out=se[:, :], in0=se[:, :], in1=pb[:, :])

        @p.step("vector", reads=[se], writes=[dd])
        def _(v, se=se, dd=dd):
            return v.reduce_sum(out=dd[:, :], in_=se[:, :], axis=AX.X)

        @p.step("vector", reads=[dd], writes=[dd])
        def _(v, dd=dd):
            return v.tensor_scalar(out=dd[:, :], in0=dd[:, :], scalar1=1e-8,
                                   scalar2=None, op0=ALU.add)

        @p.step("vector", reads=[dd], writes=[dd])
        def _(v, dd=dd):
            return v.reciprocal(out=dd[:, :], in_=dd[:, :])

        @p.step("vector", reads=[se, dd], writes=[cb])
        def _(v, se=se, dd=dd, cb=cb):
            return v.tensor_scalar(out=cb[:, :], in0=se[:, :], scalar1=dd[:, :],
                                   scalar2=None, op0=ALU.mult)

        @p.step("tensor", reads=[cb, cs["ident_bf"]], writes=[cps])
        def _(te, cb=cb):
            return te.transpose(cps[0:8, 0:128], cb[:, :], cs["ident_bf"][:, :])

        cf = cfm[nt % 2]

        @p.step("vector", reads=[cps], writes=[cf])
        def _(v, cf=cf):
            return v.tensor_copy(out=cf[:, :], in_=cps[0:8, 0:128])

        @p.step("tensor", reads=[selb, cf], writes=[p.ps[5]])
        def _(te, cf=cf):
            return te.matmul(p.ps[5][0:8, 0:128], selb[:, :], cf[:, :],
                             start=True, stop=True)
        cl = cfl[nt % 2]

        @p.step("vector", reads=[p.ps[5]], writes=[cl])
        def _(v, cl=cl):
            return v.tensor_copy(out=cl[:, :], in_=p.ps[5][0:8, 0:128])

        @p.step("gpsimd", reads=[cl], dma=True)
        def _(g, cl=cl, nt=nt):
            return g.dma_start(out=comb_out[:, nt * 128:(nt + 1) * 128], in_=cl[:, :])

    return p.done() if standalone else p


def make_inputs_B(inp, consts, core, hs1, x1=None):
    b, j = core // 2, core % 2
    qw = np.zeros((D, 512), np.float32)
    qb = np.zeros((512, 1), np.float32)
    for h in range(8):
        hg = 8 * j + h
        qw[:, 64 * h:64 * h + 32] = inp["L0_ffn_q_w"][:, 64 * hg:64 * hg + 32]
        qb[64 * h:64 * h + 32, 0] = inp["L0_ffn_q_b"][64 * hg:64 * hg + 32]
    kw = np.zeros((D, 512), np.float32)
    kb = np.zeros((512, 1), np.float32)
    for h in range(8):
        hg = 8 * j + h
        kw[:, 64 * h:64 * h + 32] = inp["L0_ffn_k_w"][:, 32 * hg:32 * hg + 32]
        kb[64 * h:64 * h + 32, 0] = inp["L0_ffn_k_b"][32 * hg:32 * hg + 32]
    vw = np.zeros((D, 264), np.float32)
    vb = np.zeros((1, 264), np.float32)
    for h in range(8):
        hg = 8 * j + h
        vw[:, 33 * h:33 * h + 32] = inp["L0_ffn_v_w"][:, 32 * hg:32 * hg + 32]
        vb[0, 33 * h:33 * h + 32] = inp["L0_ffn_v_b"][32 * hg:32 * hg + 32]
        vb[0, 33 * h + 32] = 1.0
    ow = np.ascontiguousarray(inp["L0_ffn_o_w"][256 * j:256 * (j + 1), :])
    ob = inp["L0_ffn_o_b"].reshape(D, 1) if j == 0 else np.zeros((D, 1), np.float32)
    d = {
        "hs1": hs1, "qw": qw, "qb": qb, "kw": kw, "kb": kb, "vw": vw, "vb": vb,
        "ow": ow, "ob": np.ascontiguousarray(ob),
        "rw": inp["L0_ffn_router_w"].astype(np.float32),
        "rb": inp["L0_ffn_router_b"].reshape(1, 8),
        "esel": np.eye(8, dtype=np.float32)[:, list(range(4 * j, 4 * j + 4)) + list(range(0, 4 * j)) + list(range(4 * j + 4, 8))][:, :8][:, :].T.copy() if False else np.ascontiguousarray(np.roll(np.eye(8, dtype=np.float32), -4 * j, axis=0).T),
        "c_ident": consts["ident"], "c_ones_col": consts["ones_col"],
        "c_ones_row": consts["ones_row"],
    }
    return d



def build_B2(debug=False, p=None, pre=None, pfx=""):
    standalone = p is None
    if standalone:
        p = P("B2")
    IN, OUT = _io(p, pre, pfx, standalone, debug)
    mla = IN("mla", [D, S], BF16)
    comb = IN("comb", [8, S], BF16)
    x1 = IN("x1", [D, S], BF16)
    hs1 = IN("hs1", [D, S], BF16)
    w1s = [IN(f"w1_{e}", [D, 2048], F32) for e in range(4)]
    b1s = [IN(f"b1_{e}", [2048, 1], F32) for e in range(4)]
    w2s = [IN(f"w2_{e}", [2048, D], F32) for e in range(4)]
    b2s = [IN(f"b2_{e}", [D, 1], F32) for e in range(4)]
    cs = load_consts(p, ("ident", "ones_col", "ones_row"))

    mk2 = p.outp if debug else p.dram
    h1d = mk2("h1d", [2048, S])
    moe_p = [mk2(f"moe{e}", [D, S]) for e in range(4)]
    cc_in = p.dram("cc_in", [D, S])
    cc_out = p.dram("cc_out", [D, S])
    x2_out = OUT("x2", [D, S])
    hs2_out = OUT("hs2", [D, S])

    crow2 = p.ring("crow2", 2, [1, 512], BF16)

    tf = p.ring("adT", 4, [128, 512], F32)[:2]

    for e in range(4):
        mla_pool = pool_load(p, "mla", mla, 8, 8)
        sweep_proj_fm(p, w1s[e], mla, h1d, 2048, bias_ap=b1s[e], evac2=evac2_gelu,
                      x_pool=mla_pool)
        p.barrier()

        def mk_evac_comb(base):
            def evac_comb(p_, ps, o, ft, ci, fw, e=e):
                bb = p_.ring("swB", 2, [128, 1], F32)
                a = base + ci * 512
                t = tf[(ft * 8 + ci) % 2]
                ps4 = p_.ps[4]
                cr = crow2[ci % 2]

                @p_.step("sync", writes=[cr], dma=True)
                def _(sy):
                    return sy.dma_start(out=cr[:, :], in_=comb[e:e + 1, a:a + 512])

                @p_.step("tensor", reads=[cr, cs["ones_row_bf"]], writes=[ps4])
                def _(te):
                    return te.matmul(ps4[:, :], cs["ones_row_bf"][0:1, :],
                                     cr[0:1, :], start=True, stop=True)

                @p_.step("vector", reads=[ps, bb[ft % 2]], writes=[t])
                def _(v):
                    return v.tensor_scalar(out=t[0:fw, :], in0=ps[0:fw, :],
                                           scalar1=bb[ft % 2][0:fw, :], scalar2=None,
                                           op0=ALU.add)

                @p_.step("vector", reads=[t, ps4], writes=[o])
                def _(v):
                    return v.tensor_mul(out=o[0:fw, :], in0=t[0:fw, :], in1=ps4[0:fw, :])
            return evac_comb
        for half in range(2):
            hp_pool = pool_load(p, f"h1_{e}_{half}", h1d, 16, 4, n0=half * 2048)
            sweep_proj_fm(p, w2s[e], h1d, moe_p[e], D, kdim=2048, bias_ap=b2s[e],
                          evac=mk_evac_comb(half * 2048), n0=half * 2048,
                          n1=(half + 1) * 2048, x_pool=hp_pool)
            p.barrier()

    at4 = p.ring("swX8", 16, [128, 512], BF16)[:8]
    tt4 = p.ring("adT", 4, [128, 512], F32)
    ob4 = p.ring("m4o", 2, [128, 512], BF16)
    for rt in range(DC):
        for ci in range(8):
            a = ci * 512
            idx = rt * 8 + ci
            avs = [at4[(idx % 2) * 4 + e] for e in range(4)]
            for e in range(4):
                @p.step("gpsimd", writes=[avs[e]], dma=True)
                def _(sy, e=e, rt=rt, a=a, avs=avs):
                    return sy.dma_start(out=avs[e][:, :],
                                        in_=moe_p[e][rt * 128:(rt + 1) * 128, a:a + 512])
            t = tt4[(idx % 2) * 2]
            t2 = tt4[(idx % 2) * 2 + 1]
            o = ob4[idx % 2]

            @p.step("vector", reads=avs[:2], writes=[t])
            def _(v, avs=avs, t=t):
                return v.tensor_add(out=t[:, :], in0=avs[0][:, :], in1=avs[1][:, :])

            @p.step("vector", reads=[t, avs[2]], writes=[t2])
            def _(v, avs=avs, t=t, t2=t2):
                return v.tensor_add(out=t2[:, :], in0=t[:, :], in1=avs[2][:, :])

            @p.step("vector", reads=[t2, avs[3]], writes=[o])
            def _(v, avs=avs, t2=t2, o=o):
                return v.tensor_add(out=o[:, :], in0=t2[:, :], in1=avs[3][:, :])

            @p.step("gpsimd", reads=[o], dma=True)
            def _(g, o=o, rt=rt, a=a):
                return g.dma_start(out=cc_in[rt * 128:(rt + 1) * 128, a:a + 512], in_=o[:, :])
    p.barrier()
    if debug:
        cci_dbg = p.outp("cci_dbg", [D, S])
        cco_dbg = p.outp("cco_dbg", [D, S])

        @p.step("gpsimd", dma=True)
        def _(g):
            return g.dma_start(out=cci_dbg[:, :], in_=cc_in[:, :])
        p.barrier()
    collective_pair(p, cc_in, cc_out)
    wait_cc(p, "sync")
    wait_cc(p, "gpsimd")
    if debug:
        @p.step("gpsimd", dma=True)
        def _(g):
            return g.dma_start(out=cco_dbg[:, :], in_=cc_out[:, :])
        p.barrier()
    sweep_add_fm(p, x1, hs1, x2_out, C_ap=cc_out, scale_c=0.3)
    p.barrier()
    sweep_ln_fm(p, x2_out, hs2_out, cs, kind="ln", out_pool_tag="hs2")
    return p.done() if standalone else p


def make_inputs_B2(inp, consts, core, mla, comb, x1, hs1):
    b, j = core // 2, core % 2
    d = {"mla": mla, "comb": comb,
         "x1": x1, "hs1": hs1,
         "c_ident": consts["ident"], "c_ones_col": consts["ones_col"],
         "c_ones_row": consts["ones_row"]}
    for e in range(4):
        eg = 4 * j + e
        d[f"w1_{e}"] = np.ascontiguousarray(inp["L0_ffn_w1"][eg])
        d[f"b1_{e}"] = np.ascontiguousarray(inp["L0_ffn_b1"][eg].reshape(2048, 1))
        d[f"w2_{e}"] = np.ascontiguousarray(inp["L0_ffn_w2"][eg])
        d[f"b2_{e}"] = np.ascontiguousarray(inp["L0_ffn_b2"][eg].reshape(D, 1))
    return d


C_STAB1 = 12.0
RS64 = 0.125


def build_C(debug=False, p=None, pre=None, pfx=""):
    standalone = p is None
    if standalone:
        p = P("C")
    IN, OUT = _io(p, pre, pfx, standalone, debug)
    hs2 = IN("hs2", [D, S], BF16)
    x2 = IN("x2", [D, S], BF16)
    wq = IN("wq", [D, D], F32)
    bq = IN("bq", [D, 1], F32)
    wk = IN("wk", [D, D], F32)
    bk = IN("bk", [D, 1], F32)
    wv = IN("wv", [D, 520], F32)
    bv = IN("bv", [1, 520], F32)
    wp = IN("wp", [512, D], F32)
    bp_ = IN("bp", [D, 1], F32)
    gam = IN("gam", [D, 1], F32)
    cmask = IN("cmask", [128, 4 * 512], F32)
    cs = load_consts(p, ("ident", "ones_col", "ones_row", "rope_sin", "rope_cos"))

    mk = p.outp if debug else p.dram
    qf = mk("qf", [D, S])
    kf = mk("kf", [D, S])
    vtm = mk("vtm", [S, 520])
    yf = mk("yf", [512, S])
    cc_in = p.dram("cc_in", [D, S])
    cc_out = p.dram("cc_out", [D, S])
    x3_out = OUT("x3", [D, S])
    xn_fm_out = OUT("xn_fm", [D, S])
    xn_tm_out = OUT("xn_tm", [S, D])

    cmk_t = p.ring("swX16", 32, [128, 512], BF16)[16:20]

    def mk_rope_evac(bias_ring_key):
        def evac(p_, ps, o, ft, ci, fw):
            a = ci * 512
            rb = 0
            ac = a
            bb = p_.ring("swB", 2, [128, 1], F32)
            sc1 = [t[0:64, :] for t in p_.ring("adT", 4, [128, 512], F32)]
            raw = p_.ring("ropeR", 2, [64, 512], F32)
            r = raw[ci % 2]
            m1, m2 = sc1[(ci % 2) * 2], sc1[(ci % 2) * 2 + 1]

            @p_.step("scalar", reads=[ps, bb[ft % 2]], writes=[r])
            def _(sc):
                return sc.activation(out=r[:, :], in_=ps[0:64, :], func=AF.Identity,
                                     bias=bb[ft % 2][0:64, :])

            @p_.step("vector", reads=[r, cs["rope_cos_bf"]], writes=[m1])
            def _(v):
                return v.tensor_mul(out=m1[0:32, :], in0=r[0:32, :],
                                    in1=cs["rope_cos_bf"][rb + 0:rb + 32, ac:ac + 512])

            @p_.step("vector", reads=[r, cs["rope_sin_bf"]], writes=[m2])
            def _(v):
                return v.tensor_mul(out=m2[0:32, :], in0=r[32:64, :],
                                    in1=cs["rope_sin_bf"][rb + 32:rb + 64, ac:ac + 512])

            @p_.step("vector", reads=[m1, m2], writes=[o])
            def _(v):
                return v.tensor_sub(out=o[0:32, :], in0=m1[0:32, :], in1=m2[0:32, :])

            @p_.step("vector", reads=[r, cs["rope_cos_bf"]], writes=[m1])
            def _(v):
                return v.tensor_mul(out=m1[0:32, :], in0=r[32:64, :],
                                    in1=cs["rope_cos_bf"][rb + 32:rb + 64, ac:ac + 512])

            @p_.step("vector", reads=[r, cs["rope_sin_bf"]], writes=[m2])
            def _(v):
                return v.tensor_mul(out=m2[0:32, :], in0=r[0:32, :],
                                    in1=cs["rope_sin_bf"][rb + 0:rb + 32, ac:ac + 512])

            @p_.step("vector", reads=[m1, m2, o], writes=[o])
            def _(v):
                return v.tensor_add(out=o[32:64, :], in0=m1[0:32, :], in1=m2[0:32, :])
        return evac

    hs2_pool = pool_use(p, "hs2")
    sweep_proj_fm(p, wq, hs2, qf, D, bias_ap=bq, evac=mk_rope_evac("q"),
                  x_pool=hs2_pool)
    p.barrier()
    sweep_proj_fm(p, wk, hs2, kf, D, bias_ap=bk, evac=mk_rope_evac("k"),
                  x_pool=hs2_pool)
    p.barrier()
    sweep_proj_tm(p, hs2, wv, vtm, 260, bias_ap=bv, consts=cs, w_col0=0, o_col0=0,
                  x_pool=hs2_pool)
    p.barrier()
    sweep_proj_tm(p, hs2, wv, vtm, 260, bias_ap=bv, consts=cs, w_col0=260, o_col0=260,
                  x_pool=hs2_pool)
    p.barrier()

    nbias = p.sb("nbias1", [128, 1], F32)

    @p.step("vector", writes=[nbias])
    def _(v):
        return v.memset(nbias[:, :], -C_STAB1)
    for mi in range(4):
        @p.step("gpsimd", writes=[cmk_t[mi]], dma=True)
        def _(g, mi=mi):
            return g.dma_start(out=cmk_t[mi][:, :], in_=cmask[:, mi * 512:(mi + 1) * 512])

    qtiles = [t[0:64, :] for t in p.ring("swX8", 16, [128, 512], BF16)[8:16]]
    ktiles = p.ring("caK", 8, [64, 128], BF16)
    vt = p.ring("caV", 2, [128, 520], BF16)
    es = p.ring("swX8", 16, [128, 512], BF16)[4:8]
    drow = p.ring("fbD", 2, [1, 512], F32)
    bcsb = [t[0:64, :] for t in p.ring("adT", 4, [128, 512], F32)[:2]]
    otile = p.ring("swX8", 16, [128, 512], BF16)[2:4]
    psS = [p.ps[0], p.ps[1]]
    psB = p.ps[6]

    for qc in range(8):
        a = qc * 512
        ktmax = 4 * qc + 4
        for hp in range(2):
            qs = [qtiles[(hp % 2) * 4 + g] for g in range(4)]
            for g in range(4):
                h = hp * 4 + g

                @p.step("sync", writes=[qs[g]], dma=True)
                def _(sy, g=g, h=h, a=a, qs=qs):
                    return sy.dma_start(out=qs[g][0:64, :],
                                        in_=qf[128 * h:128 * h + 64, a:a + 512])
            for kt in range(ktmax):
                ks4 = [ktiles[(kt % 2) * 4 + g] for g in range(4)]
                for g in range(4):
                    h = hp * 4 + g

                    @p.step("sync", writes=[ks4[g]], dma=True)
                    def _(sy, g=g, h=h, kt=kt, ks4=ks4):
                        return sy.dma_start(out=ks4[g][:, :],
                                            in_=kf[128 * h:128 * h + 64, kt * 128:(kt + 1) * 128])
                vv = vt[kt % 2]

                @p.step("sync", writes=[vv], dma=True)
                def _(sy, kt=kt, vv=vv):
                    return sy.dma_start(out=vv[:, :], in_=vtm[kt * 128:(kt + 1) * 128, :])

                diag = kt - 4 * qc
                for pair in range(2):
                    for g in (2 * pair, 2 * pair + 1):
                        ps_ = psS[g % 2]

                        @p.step("tensor", reads=[ks4[g], qs[g]], writes=[ps_])
                        def _(te, g=g, ps_=ps_, ks4=ks4, qs=qs):
                            return te.matmul(ps_[:, :], ks4[g][0:64, :], qs[g][0:64, :],
                                             start=True, stop=True)

                    @p.step("scalar", reads=[psS[0], psS[1], nbias],
                            writes=[es[2 * pair], es[2 * pair + 1]])
                    def _(sc, pair=pair):
                        sc.activation(out=es[2 * pair][:, :], in_=psS[0][:, :],
                                      func=AF.Exp, bias=nbias[:, :], scale=RS64)
                        return sc.activation(out=es[2 * pair + 1][:, :], in_=psS[1][:, :],
                                             func=AF.Exp, bias=nbias[:, :], scale=RS64)

                    for g in (2 * pair, 2 * pair + 1):
                        h = hp * 4 + g
                        bank = p.ps[2 + g]
                        ev = es[g]
                        if diag >= 0:
                            @p.step("vector", reads=[ev, cmk_t[diag]], writes=[ev])
                            def _(v, ev=ev, diag=diag):
                                return v.tensor_mul(out=ev[:, :], in0=ev[:, :],
                                                    in1=cmk_t[diag][:, :])

                        @p.step("tensor", reads=[ev, vv], writes=[bank])
                        def _(te, ev=ev, vv=vv, h=h, kt=kt, bank=bank):
                            return te.matmul(bank[0:65, :], vv[:, 65 * h:65 * h + 65],
                                             ev[:, :], start=(kt == 0),
                                             stop=(kt == ktmax - 1))

            for g in range(4):
                h = hp * 4 + g
                bank = p.ps[2 + g]
                dr = drow[g % 2]
                bc = bcsb[g % 2]
                ot = otile[g % 2]

                @p.step("vector", reads=[bank], writes=[dr])
                def _(v, dr=dr, bank=bank):
                    return v.reciprocal(out=dr[:, :], in_=bank[64:65, :])

                @p.step("tensor", reads=[dr, cs["ones_row_f"]], writes=[psB])
                def _(te, dr=dr):
                    return te.matmul(psB[0:64, :], cs["ones_row_f"][0:1, 0:64], dr[:, :],
                                     start=True, stop=True)

                @p.step("vector", reads=[psB], writes=[bc])
                def _(v, bc=bc):
                    return v.tensor_copy(out=bc[:, :], in_=psB[0:64, :])

                @p.step("vector", reads=[bank, bc], writes=[ot])
                def _(v, bc=bc, ot=ot, bank=bank):
                    return v.tensor_mul(out=ot[0:64, :], in0=bank[0:64, :], in1=bc[:, :])

                @p.step("gpsimd", reads=[ot], dma=True)
                def _(g_, ot=ot, h=h, a=a):
                    return g_.dma_start(out=yf[64 * h:64 * h + 64, a:a + 512], in_=ot[0:64, :])

    p.barrier()
    sweep_proj_fm(p, wp, yf, cc_in, D, kdim=512, bias_ap=bp_)
    p.barrier()
    collective_pair(p, cc_in, cc_out)
    sweep_add_fm(p, x2, cc_out, x3_out)
    p.barrier()
    hs3 = p.dram("hs3", [D, S])
    sweep_ln_fm(p, x3_out, hs3, cs, kind="ln")
    p.barrier()
    sweep_ln_fm(p, hs3, xn_fm_out, cs, kind="rms", gamma_ap=gam, scale=32.0,
                also_tm_ap=xn_tm_out, out_pool_tag="xn")
    return p.done() if standalone else p


def make_inputs_C(inp, consts, core, hs2, x2):
    b, j = core // 2, core % 2
    Wa = inp["L1_attn_W_attn"]
    ba = inp["L1_attn_b_attn"]
    wq = np.zeros((D, D), np.float32)
    bq = np.zeros((D, 1), np.float32)
    wk = np.zeros((D, D), np.float32)
    bk = np.zeros((D, 1), np.float32)
    wv = np.zeros((D, 520), np.float32)
    bv = np.zeros((1, 520), np.float32)
    for h in range(8):
        hg = 8 * j + h
        wq[:, 128 * h:128 * h + 64] = Wa[:, 64 * hg:64 * hg + 64]
        bq[128 * h:128 * h + 64, 0] = ba[64 * hg:64 * hg + 64]
        wk[:, 128 * h:128 * h + 64] = Wa[:, D + 64 * hg:D + 64 * hg + 64]
        bk[128 * h:128 * h + 64, 0] = ba[D + 64 * hg:D + 64 * hg + 64]
        wv[:, 65 * h:65 * h + 64] = Wa[:, 2 * D + 64 * hg:2 * D + 64 * hg + 64]
        bv[0, 65 * h:65 * h + 64] = ba[2 * D + 64 * hg:2 * D + 64 * hg + 64]
        bv[0, 65 * h + 64] = 1.0
    wp = np.ascontiguousarray(inp["L1_attn_W_proj"][512 * j:512 * (j + 1), :])
    bp = inp["L1_attn_b_proj"].reshape(D, 1) if j == 0 else np.zeros((D, 1), np.float32)
    cm = np.zeros((128, 4 * 512), np.float32)
    for m in range(4):
        kr = np.arange(128)[:, None]
        qcc = np.arange(512)[None, :]
        cm[:, m * 512:(m + 1) * 512] = (128 * m + kr <= qcc)
    return {
        "hs2": hs2, "x2": x2, "wq": wq, "bq": bq, "wk": wk, "bk": bk,
        "wv": wv, "bv": bv, "wp": wp, "bp": np.ascontiguousarray(bp),
        "gam": inp["L1_ffn_norm_gamma"].reshape(D, 1).astype(np.float32),
        "cmask": cm,
        "c_ident": consts["ident"], "c_ones_col": consts["ones_col"],
        "c_ones_row": consts["ones_row"], "c_rope_sin": consts["rope_sin"],
        "c_rope_cos": consts["rope_cos"],
    }



def build_D(debug=False, p=None, pre=None, pfx=""):
    standalone = p is None
    if standalone:
        p = P("D")
    IN, OUT = _io(p, pre, pfx, standalone, debug)
    xn_fm = IN("xn_fm", [D, S], BF16)
    xn_tm = IN("xn_tm", [S, D], BF16)
    x3 = IN("x3", [D, S], BF16)
    identf_in = IN("identf_in", [128, 128], F32)
    ses = [IN(f"se_{e}", [512, D], F32) for e in range(4)]
    sgam = IN("sgam", [1, D], F32)
    w1s = [IN(f"w1_{e}", [D, 4096], F32) for e in range(4)]
    b1s = [IN(f"b1_{e}", [4096, 1], F32) for e in range(4)]
    w2s = [IN(f"w2_{e}", [4096, D], F32) for e in range(4)]
    b2s = [IN(f"b2_{e}", [1, D], F32) for e in range(4)]
    cs = load_consts(p, ("ident", "ones_col", "ones_row"))
    identf = p.sb("identf", [128, 128], F32)

    @p.step("sync", writes=[identf], dma=True)
    def _(sy):
        return sy.dma_start(out=identf[:, :], in_=identf_in[:, :])

    mk = p.outp if debug else p.dram
    sn_fm = mk("sn_fm", [D, 2048])
    disp_tm = mk("disp_tm", [S, 2048])
    slots_f = mk("slots_f", [D, 2048])
    h1d = p.dram("h1d", [4096, 512])
    eo_tm = mk("eo_tm", [2048, D])
    cn_d = p.dram("cn_d", [1, S], F32)
    cn_o = p.dram("cn_o", [1, S], F32)
    zn_d = p.dram("zn_d", [1, S], F32)
    zn_o = p.dram("zn_o", [1, S], F32)
    zs_d = p.dram("zs_d", [1, 2048], F32)
    cc_in = p.dram("cc_in", [D, S])
    cc_out = p.dram("cc_out", [D, S])
    out_tm = OUT("out", [S, D], F32) if not standalone else p.outp("out", [S, D], F32)
    p._d_dump = {"sn_fm": sn_fm, "disp_tm": disp_tm, "slots_f": slots_f,
                 "eo_tm": eo_tm, "cn_o": cn_o, "zn_o": zn_o, "zs_d": zs_d,
                 "cc_out": cc_out, "h1d": h1d, "cc_in": cc_in, "zn_d": zn_d}

    gamrow = p.sb("gamrow", [1, D], BF16)

    @p.step("gpsimd", writes=[gamrow], dma=True)
    def _(sy):
        return sy.dma_start(out=gamrow[:, :], in_=sgam[:, :])
    gb = p.sb("gamb", [128, D], BF16)

    @p.step("tensor", reads=[gamrow, cs["ones_row_bf"]], writes=[p.ps[4], p.ps[5]])
    def _(te):
        te.matmul(p.ps[4][:, :], cs["ones_row_bf"][0:1, :], gamrow[0:1, 0:512],
                  start=True, stop=True)
        return te.matmul(p.ps[5][:, :], cs["ones_row_bf"][0:1, :], gamrow[0:1, 512:1024],
                         start=True, stop=True)

    @p.step("vector", reads=[p.ps[4], p.ps[5]], writes=[gb])
    def _(v):
        v.tensor_copy(out=gb[:, 0:512], in_=p.ps[4][:, :])
        return v.tensor_copy(out=gb[:, 512:1024], in_=p.ps[5][:, :])

    fb2 = p.ring("f32buf", 2, [128, D], F32)
    set_ = [fb2[0]]
    sqv = p.ring("d1sq", 2, [128, 1], F32)
    snt = [fb2[1]]
    sno = [p.ring("t2fXb", 2, [128, D], BF16)[0]]
    d1to = [p.ring("t2fXb", 2, [128, D], BF16)[1]]
    for e in range(4):
        for t in range(4):
            mrow = e * 512 + t * 128
            sv, sq, sn, so, to = set_[0], sqv[t % 2], snt[0], sno[0], d1to[0]

            @p.step("sync", writes=[sv], dma=True)
            def _(sy, e=e, t=t, sv=sv):
                return sy.dma_start(out=sv[:, :], in_=ses[e][t * 128:(t + 1) * 128, :])

            @p.step("vector", reads=[sv], writes=[sn, sq])
            def _(v, sv=sv, sn=sn, sq=sq):
                v.tensor_mul(out=sn[:, :], in0=sv[:, :], in1=sv[:, :])
                return v.reduce_sum(out=sq[:, :], in_=sn[:, :], axis=AX.X)

            @p.step("scalar", reads=[sq], writes=[sq])
            def _(sc, sq=sq):
                return sc.activation(out=sq[:, :], in_=sq[:, :], func=AF.Sqrt)

            @p.step("vector", reads=[sq], writes=[sq])
            def _(v, sq=sq):
                return v.tensor_scalar(out=sq[:, :], in0=sq[:, :], scalar1=1e-12,
                                       scalar2=None, op0=ALU.max)

            @p.step("vector", reads=[sq], writes=[sq])
            def _(v, sq=sq):
                return v.reciprocal(out=sq[:, :], in_=sq[:, :])

            @p.step("vector", reads=[sv, sq], writes=[sn])
            def _(v, sv=sv, sq=sq, sn=sn):
                return v.tensor_scalar(out=sn[:, :], in0=sv[:, :], scalar1=sq[:, :],
                                       scalar2=32.0, op0=ALU.mult, op1=ALU.mult)

            @p.step("vector", reads=[sn, gb], writes=[so])
            def _(v, sn=sn, so=so):
                return v.tensor_mul(out=so[:, :], in0=sn[:, :], in1=gb[:, :])

            @p.step("tensor", reads=[so, cs["ident_bf"]], writes=[p.psT])
            def _(te, so=so):
                last = None
                for dc in range(DC):
                    last = te.transpose(p.psT[:, dc * 128:(dc + 1) * 128],
                                        so[:, dc * 128:(dc + 1) * 128], cs["ident_bf"][:, :])
                return last

            @p.step("vector", reads=[p.psT], writes=[to])
            def _(v, to=to):
                return v.tensor_copy(out=to[:, :], in_=p.psT[:, :])

            @p.step("gpsimd", reads=[to], dma=True)
            def _(g, to=to, mrow=mrow):
                return [g.dma_start(out=sn_fm[dc * 128:(dc + 1) * 128, mrow:mrow + 128],
                                    in_=to[:, dc * 128:(dc + 1) * 128]) for dc in range(DC)]
    p.barrier()

    snt_l = p.ring("d5h", 16, [128, 128], BF16)
    macct = p.ring("swX16", 32, [128, 512], BF16)[16:24]
    csb = p.sb("csb", [128, 16], F32)
    cmx = p.ring("d2c", 2, [128, 1], F32)

    @p.step("vector", writes=list(macct))
    def _(v):
        last = None
        for mi in range(8):
            last = v.memset(macct[mi][:, :], -1e30)
        return last

    @p.step("vector", writes=[csb])
    def _(v):
        return v.memset(csb[:, :], -1e30)

    def logit_block(mt, ci, consumer):
        a = ci * 512
        sns = [snt_l[(mt % 2) * DC + dc] for dc in range(DC)]
        if ci == 0:
            for dc in range(DC):
                @p.step("sync", writes=[sns[dc]], dma=True)
                def _(sy, dc=dc, mt=mt, sns=sns):
                    return sy.dma_start(out=sns[dc][:, :],
                                        in_=sn_fm[dc * 128:(dc + 1) * 128,
                                                  mt * 128:(mt + 1) * 128])
        xnp = pool_use(p, "xn")[0]
        xns = [xnp[dc * 8 + ci] for dc in range(DC)]
        ps_ = p.ps[ci % 2]

        @p.step("tensor", reads=sns + xns, writes=[ps_])
        def _(te, sns=sns, xns=xns, ps_=ps_):
            last = None
            for dc in range(DC):
                last = te.matmul(ps_[:, :], sns[dc][:, :], xns[dc][:, :],
                                 start=(dc == 0), stop=(dc == DC - 1))
            return last
        consumer(ps_, mt, ci)

    def d2_consumer(ps_, mt, ci):
        a = ci * 512
        cm = cmx[ci % 2]

        @p.step("vector", reads=[ps_], writes=[cm])
        def _(v, ps_=ps_, cm=cm):
            return v.reduce_max(out=cm[:, :], in_=ps_[:, :], axis=AX.X)

        @p.step("vector", reads=[cm, csb], writes=[csb])
        def _(v, cm=cm, mt=mt):
            return v.tensor_tensor(out=csb[:, mt:mt + 1], in0=csb[:, mt:mt + 1],
                                   in1=cm[:, :], op=ALU.max)

        @p.step("vector", reads=[ps_, macct[ci]], writes=[macct[ci]])
        def _(v, ps_=ps_, ci=ci):
            return v.tensor_tensor(out=macct[ci][:, :], in0=macct[ci][:, :],
                                   in1=ps_[:, :], op=ALU.max)

    for mt in range(16):
        for ci in range(8):
            logit_block(mt, ci, d2_consumer)

    cnrow = p.sb("cnrow", [128, 1], F32)
    for nt in range(NT):
        @p.step("tensor", reads=[macct[nt // 4], cs["ident_bf"]], writes=[p.psT])
        def _(te, nt=nt):
            return te.transpose(p.psT[:, 128:256],
                                macct[nt // 4][:, (nt % 4) * 128:(nt % 4) * 128 + 128],
                                cs["ident_bf"][:, :])

        @p.step("vector", reads=[p.psT], writes=[cnrow])
        def _(v):
            return v.reduce_max(out=cnrow[:, :], in_=p.psT[:, 128:256], axis=AX.X)

        cnf = p.sb("cnf", [128, 1], BF16) if False else cnrow
        if not hasattr(p, "_crow7"):
            _lr = p.ring("lnrows", 5, [1, 512], F32)[3]
            p._crow7 = [_lr[0:1, 0:128], _lr[0:1, 128:256]]
        crow_sb = p._crow7[nt % 2]
        cnbf = p.ring("cnbf7", 2, [128, 1], BF16)[nt % 2]

        @p.step("vector", reads=[cnrow], writes=[cnbf])
        def _(v, cnbf=cnbf):
            return v.tensor_copy(out=cnbf[:, :], in_=cnrow[:, :])

        @p.step("tensor", reads=[cnbf, cs["ident_bf"]], writes=[p.psT])
        def _(te, cnbf=cnbf):
            return te.transpose(p.psT[0:1, 0:128], cnbf[:, :], cs["ident_bf"][:, :])

        @p.step("vector", reads=[p.psT], writes=[crow_sb])
        def _(v, crow_sb=crow_sb):
            return v.tensor_copy(out=crow_sb[:, :], in_=p.psT[0:1, 0:128])

        @p.step("gpsimd", reads=[crow_sb], dma=True)
        def _(g, nt=nt, crow_sb=crow_sb):
            return g.dma_start(out=cn_d[0:1, nt * 128:(nt + 1) * 128], in_=crow_sb[:, :])
    p.barrier()
    collective_pair(p, cn_d, cn_o, op=ALU.max)

    nbias3 = p.ring("d1sq", 2, [128, 1], F32)
    zsacc = p.sb("zsacc", [128, 16], F32)

    @p.step("vector", writes=[zsacc])
    def _(v):
        return v.memset(zsacc[:, :], 0.0)
    zstile = p.ring("d2c", 2, [128, 1], F32)
    dtile = p.ring("swX8", 16, [128, 512], BF16)[0:2]
    ttile = p.ring("swX8", 16, [128, 512], BF16)[2:4]

    def d3_consumer(ps_, mt, ci):
        a = ci * 512
        nb = nbias3[mt % 2]
        if ci == 0:
            @p.step("vector", reads=[csb], writes=[nb])
            def _(v, nb=nb, mt=mt):
                return v.tensor_scalar(out=nb[:, :], in0=csb[:, mt:mt + 1], scalar1=-1.0,
                                       scalar2=None, op0=ALU.mult)
        dt = dtile[ci % 2]
        zt = zstile[ci % 2]

        @p.step("scalar", reads=[ps_, nb], writes=[dt, zt])
        def _(sc, ps_=ps_, dt=dt, zt=zt, nb=nb):
            return sc.activation(out=dt[:, :], in_=ps_[:, :], func=AF.Exp,
                                 bias=nb[:, :], accum_out=zt[:, :])

        @p.step("vector", reads=[zt, zsacc], writes=[zsacc])
        def _(v, zt=zt, mt=mt):
            return v.tensor_add(out=zsacc[:, mt:mt + 1], in0=zsacc[:, mt:mt + 1],
                                in1=zt[:, :])

        @p.step("tensor", reads=[dt, cs["ident_bf"]], writes=[p.psT])
        def _(te, dt=dt):
            last = None
            for q in range(4):
                last = te.transpose(p.psT[:, q * 128:(q + 1) * 128],
                                    dt[:, q * 128:(q + 1) * 128], cs["ident_bf"][:, :])
            return last
        tt_ = ttile[ci % 2]

        @p.step("vector", reads=[p.psT], writes=[tt_])
        def _(v, tt_=tt_):
            return v.tensor_copy(out=tt_[:, :], in_=p.psT[:, 0:512])

        @p.step("gpsimd", reads=[tt_], dma=True)
        def _(g, tt_=tt_, a=a, mt=mt):
            return [g.dma_start(out=disp_tm[a + q * 128:a + (q + 1) * 128,
                                            mt * 128:(mt + 1) * 128],
                                in_=tt_[:, q * 128:(q + 1) * 128]) for q in range(4)]

    for mt in range(16):
        for ci in range(8):
            logit_block(mt, ci, d3_consumer)

    @p.step("vector", reads=[zsacc], writes=[zsacc])
    def _(v):
        return v.reciprocal(out=zsacc[:, :], in_=zsacc[:, :])
    zso = p.sb("zso", [16, 128], F32)

    @p.step("tensor", reads=[zsacc, identf], writes=[p.ps[5]])
    def _(te):
        return te.transpose(p.ps[5][0:16, 0:128], zsacc[:, 0:16], identf[:, :])

    @p.step("vector", reads=[p.ps[5]], writes=[zso])
    def _(v):
        return v.tensor_copy(out=zso[:, :], in_=p.ps[5][0:16, 0:128])

    @p.step("gpsimd", reads=[zso], dma=True)
    def _(g):
        return g.dma_start(out=zs_d[0:1, :].rearrange("o (a b) -> (o a) b", a=16, b=128),
                           in_=zso[:, :])
    p.barrier()

    dpt = p.ring("swX8", 16, [128, 512], BF16)[4:6]
    rzrow = p.ring("fbD", 2, [1, 512], F32)
    sbc = p.ring("adT", 4, [128, 512], F32)[2]
    sout = p.ring("swX8", 16, [128, 512], BF16)[13:16]
    xntm_pool = pool_load(p, "xn_tm", xn_tm, 32, 2)[0]
    for esc in range(4):
        rz = rzrow[esc % 2]

        @p.step("sync", writes=[rz], dma=True)
        def _(sy, esc=esc, rz=rz):
            return sy.dma_start(out=rz[:, :], in_=zs_d[0:1, esc * 512:(esc + 1) * 512])
        for dh in range(2):
            for nt in range(NT):
                dp = dpt[nt % 2]
                xv4 = xntm_pool[nt * 2 + dh]

                @p.step("sync", writes=[dp], dma=True)
                def _(sy, nt=nt, esc=esc, dp=dp):
                    return sy.dma_start(out=dp[:, :],
                                        in_=disp_tm[nt * 128:(nt + 1) * 128,
                                                    esc * 512:(esc + 1) * 512])

                @p.step("tensor", reads=[dp, xv4], writes=[p.ps[0], p.ps[1], p.ps[2], p.ps[3]])
                def _(te, nt=nt, dh=dh, dp=dp, xv4=xv4):
                    last = None
                    for dt_ in range(4):
                        last = te.matmul(p.ps[dt_][:, :],
                                         xv4[:, dt_ * 128:(dt_ + 1) * 128],
                                         dp[:, :], start=(nt == 0), stop=(nt == NT - 1))
                    return last
            @p.step("tensor", reads=[rz, cs["ones_row_f"]], writes=[p.ps[4]])
            def _(te, rz=rz):
                return te.matmul(p.ps[4][:, :], cs["ones_row_f"][0:1, :], rz[:, :],
                                 start=True, stop=True)

            @p.step("vector", reads=[p.ps[4]], writes=[sbc])
            def _(v):
                return v.tensor_copy(out=sbc[:, :], in_=p.ps[4][:, :])
            for dt_ in range(4):
                dtile_ = dh * 4 + dt_
                so = sout[dt_ % 3]

                @p.step("vector", reads=[p.ps[dt_], sbc], writes=[so])
                def _(v, dt_=dt_, so=so):
                    return v.tensor_mul(out=so[:, :], in0=p.ps[dt_][:, :], in1=sbc[:, :])

                @p.step("gpsimd", reads=[so], dma=True)
                def _(g, so=so, dtile_=dtile_, esc=esc):
                    return g.dma_start(out=slots_f[dtile_ * 128:(dtile_ + 1) * 128,
                                                   esc * 512:(esc + 1) * 512], in_=so[:, :])
            p.barrier()

    for e in range(4):
        sweep_proj_fm(p, w1s[e], slots_f, h1d, 4096, bias_ap=b1s[e],
                      evac2=evac2_gelu, n0=0, n1=512, x_col0=e * 512)
        p.barrier()
        w2t = p.ring("swX16", 32, [128, 512], BF16)
        b2row = gamrow

        @p.step("gpsimd", writes=[b2row], dma=True)
        def _(g, e=e):
            return g.dma_start(out=b2row[:, :], in_=b2s[e][:, :])
        p._pool_tag = ("d5scratch", 0, 0, 0, 0)
        htp = xpool(p)
        eo_ring = p.ring("lnT", 2, [128, 512], BF16)
        for doc in range(2):
            for hc in range(32):
                @p.step("gpsimd", writes=[w2t[hc]], dma=True)
                def _(g, e=e, doc=doc, hc=hc):
                    return g.dma_start(out=w2t[hc][:, :],
                                       in_=w2s[e][hc * 128:(hc + 1) * 128,
                                                  doc * 512:(doc + 1) * 512])
            for est in range(4):
                hs_ = [htp[(est % 2) * 32 + hc] for hc in range(32)]
                for hc in range(32):
                    @p.step("sync", writes=[hs_[hc]], dma=True)
                    def _(sy, hc=hc, est=est, hs_=hs_):
                        return sy.dma_start(out=hs_[hc][:, 0:128],
                                            in_=h1d[hc * 128:(hc + 1) * 128,
                                                    est * 128:(est + 1) * 128])
                ps_ = p.ps[est % 4]

                @p.step("tensor", reads=hs_ + list(w2t) + [b2row], writes=[ps_])
                def _(te, hs_=hs_, ps_=ps_, doc=doc):
                    for hc in range(32):
                        te.matmul(ps_[:, :], hs_[hc][:, 0:128], w2t[hc][:, :],
                                  start=(hc == 0), stop=False)
                    return te.matmul(ps_[:, :], cs["ones_row_bf"][0:1, :],
                                     b2row[0:1, doc * 512:(doc + 1) * 512],
                                     start=False, stop=True)
                eo_o = eo_ring[est % 2]

                @p.step("scalar", reads=[ps_], writes=[eo_o])
                def _(sc, ps_=ps_, eo_o=eo_o):
                    return sc.activation(out=eo_o[:, :], in_=ps_[:, :], func=AF.Copy)

                @p.step("gpsimd", reads=[eo_o], dma=True)
                def _(g, eo_o=eo_o, e=e, est=est, doc=doc):
                    return g.dma_start(out=eo_tm[e * 512 + est * 128:e * 512 + (est + 1) * 128,
                                                 doc * 512:(doc + 1) * 512], in_=eo_o[:, :])
        p.barrier()

    cnb = p.ring("lnrows", 5, [1, 512], F32)[4]
    combt = p.ring("swX8", 16, [128, 512], BF16)
    _eo16 = p.ring("swX16", 32, [128, 512], BF16)[24:32]
    eot = [_eo16[i // 4][:, (i % 4) * 128:(i % 4) * 128 + 128] for i in range(16)]
    oout = p.ring("swX16", 32, [128, 512], BF16)[21:24]
    snp = pool_load(p, "sn6", sn_fm, 8, 4)[0]
    sns_ap = [[snp[dc * 4 + mt // 4][:, (mt % 4) * 128:(mt % 4) * 128 + 128]
               for dc in range(DC)] for mt in range(16)]
    sns_par = [[snp[dc * 4 + mt // 4] for dc in range(DC)] for mt in range(16)]
    xstr = xpool(p)[32:48]
    for ci in range(8):
        a = ci * 512

        @p.step("sync", writes=[cnb], dma=True)
        def _(sy, a=a):
            return sy.dma_start(out=cnb[:, :], in_=cn_o[0:1, a:a + 512])
        cnbbf = p.sb("cnbbf", [1, 512], BF16) if "cnbbf" not in p._rcache else p._rcache["cnbbf"]
        p._rcache["cnbbf"] = cnbbf

        @p.step("vector", reads=[cnb], writes=[cnbbf])
        def _(v, cnbbf=cnbbf):
            return v.tensor_scalar(out=cnbbf[:, :], in0=cnb[:, :], scalar1=-1.0,
                                   scalar2=None, op0=ALU.mult)
        xns = [xstr[(ci % 2) * 8 + dc] for dc in range(DC)]
        for dc in range(DC):
            @p.step("sync", writes=[xns[dc]], dma=True)
            def _(sy, dc=dc, a=a, xns=xns):
                return sy.dma_start(out=xns[dc][:, :],
                                    in_=xn_fm[dc * 128:(dc + 1) * 128, a:a + 512])
        for mt in range(16):
            ps_ = p.ps[mt % 2]

            @p.step("tensor", reads=sns_par[mt] + xns + [cnbbf, cs["ones_row_bf"]],
                    writes=[ps_])
            def _(te, mt=mt, xns=xns, ps_=ps_, cnbbf=cnbbf):
                for dc in range(DC):
                    te.matmul(ps_[:, :], sns_ap[mt][dc], xns[dc][:, :],
                              start=(dc == 0), stop=False)
                return te.matmul(ps_[:, :], cs["ones_row_bf"][0:1, :], cnbbf[0:1, :],
                                 start=False, stop=True)
            if mt % 2 == 1:
                @p.step("scalar", reads=[p.ps[0], p.ps[1]],
                        writes=[combt[mt - 1], combt[mt]])
                def _(sc, mt=mt):
                    sc.activation(out=combt[mt - 1][:, :], in_=p.ps[0][:, :], func=AF.Exp)
                    return sc.activation(out=combt[mt][:, :], in_=p.ps[1][:, :], func=AF.Exp)

                @p.step("tensor", reads=[combt[mt - 1], combt[mt], cs["ones_col_bf"]],
                        writes=[p.ps[5]])
                def _(te, mt=mt):
                    te.matmul(p.ps[5][0:1, :], cs["ones_col_bf"][:, :], combt[mt - 1][:, :],
                              start=(mt == 1), stop=False)
                    return te.matmul(p.ps[5][0:1, :], cs["ones_col_bf"][:, :], combt[mt][:, :],
                                     start=False, stop=(mt == 15))
        znr = p.ring("lnrows", 5, [1, 512], F32)[0]

        @p.step("vector", reads=[p.ps[5]], writes=[znr])
        def _(v, znr=znr):
            return v.tensor_copy(out=znr[:, :], in_=p.ps[5][0:1, :])

        @p.step("gpsimd", reads=[znr], dma=True)
        def _(g, znr=znr, a=a):
            return g.dma_start(out=zn_d[0:1, a:a + 512], in_=znr[:, :])
        for dh in range(2):
            for mt in range(16):
                eos = [eot[(mt % 2) * DC + dc] for dc in range(4)]
                for dt_ in range(4):
                    dtile_ = dh * 4 + dt_

                    @p.step("sync", writes=[eos[dt_]], dma=True)
                    def _(sy, mt=mt, dtile_=dtile_, eos=eos, dt_=dt_):
                        return sy.dma_start(out=eos[dt_][:, :],
                                            in_=eo_tm[mt * 128:(mt + 1) * 128,
                                                      dtile_ * 128:(dtile_ + 1) * 128])

                @p.step("tensor", reads=eos + [combt[mt]], writes=[p.ps[0], p.ps[1], p.ps[2], p.ps[3]])
                def _(te, mt=mt, eos=eos):
                    last = None
                    for dt_ in range(4):
                        last = te.matmul(p.ps[dt_][:, :], eos[dt_][:, :], combt[mt][:, :],
                                         start=(mt == 0), stop=(mt == 15))
                    return last
            for dt_ in range(4):
                dtile_ = dh * 4 + dt_
                oo = oout[dt_ % 3]

                @p.step("scalar", reads=[p.ps[dt_]], writes=[oo])
                def _(sc, dt_=dt_, oo=oo):
                    return sc.activation(out=oo[:, :], in_=p.ps[dt_][:, :], func=AF.Copy)

                @p.step("gpsimd", reads=[oo], dma=True)
                def _(g, oo=oo, dtile_=dtile_, a=a):
                    return g.dma_start(out=cc_in[dtile_ * 128:(dtile_ + 1) * 128, a:a + 512],
                                       in_=oo[:, :])
            p.barrier()
    p.barrier()
    collective_pair(p, cc_in, cc_out)
    collective_pair(p, zn_d, zn_o)

    znrow = p.ring("lnrows", 5, [1, 512], F32)[1:3]
    zbc = p.ring("adT", 4, [128, 512], F32)[0]
    xt7 = p.ring("swX16", 32, [128, 512], BF16)[0:2]
    ct7 = p.ring("swX16", 32, [128, 512], BF16)[2:4]
    f7 = p.ring("adT", 4, [128, 512], F32)[1:3]
    t7 = [p.ring("adT", 4, [128, 512], F32)[3], p.ring("f32buf", 2, [128, D], F32)[0][:, 0:512]]
    for ci in range(8):
        a = ci * 512
        rz = znrow[ci % 2]

        @p.step("sync", writes=[rz], dma=True)
        def _(sy, a=a, rz=rz):
            return sy.dma_start(out=rz[:, :], in_=zn_o[0:1, a:a + 512])

        @p.step("vector", reads=[rz], writes=[rz])
        def _(v, rz=rz):
            return v.reciprocal(out=rz[:, :], in_=rz[:, :])

        @p.step("tensor", reads=[rz, cs["ones_row_f"]], writes=[p.ps[4]])
        def _(te, rz=rz):
            return te.matmul(p.ps[4][:, :], cs["ones_row_f"][0:1, :], rz[:, :],
                             start=True, stop=True)

        @p.step("vector", reads=[p.ps[4]], writes=[zbc])
        def _(v):
            return v.tensor_copy(out=zbc[:, :], in_=p.ps[4][:, :])
        for dc in range(DC):
            xv, cv, fv, tv = xt7[dc % 2], ct7[dc % 2], f7[dc % 2], t7[dc % 2]

            @p.step("sync", writes=[xv], dma=True)
            def _(sy, dc=dc, a=a, xv=xv):
                return sy.dma_start(out=xv[:, :], in_=x3[dc * 128:(dc + 1) * 128, a:a + 512])

            @p.step("sync", writes=[cv], dma=True)
            def _(sy, dc=dc, a=a, cv=cv):
                return sy.dma_start(out=cv[:, :], in_=cc_out[dc * 128:(dc + 1) * 128, a:a + 512])

            @p.step("vector", reads=[cv, zbc], writes=[tv])
            def _(v, cv=cv, tv=tv):
                return v.tensor_mul(out=tv[:, :], in0=cv[:, :], in1=zbc[:, :])

            @p.step("vector", reads=[tv, xv], writes=[fv])
            def _(v, tv=tv, xv=xv, fv=fv):
                return v.tensor_add(out=fv[:, :], in0=tv[:, :], in1=xv[:, :])
            @p.step("tensor", reads=[fv, identf], writes=[p.ps[5], p.ps[6]])
            def _(te, fv=fv):
                last = None
                for q in range(4):
                    tgt = p.ps[5] if q < 2 else p.ps[6]
                    last = te.transpose(tgt[:, (q % 2) * 128:(q % 2) * 128 + 128],
                                        fv[:, q * 128:(q + 1) * 128], identf[:, :])
                return last
            ov = [p.ring("f32buf", 2, [128, D], F32)[0][:, 512:1024],
                  p.ring("f32buf", 2, [128, D], F32)[1][:, 0:512]][dc % 2]

            @p.step("vector", reads=[p.ps[5], p.ps[6]], writes=[ov])
            def _(v, ov=ov):
                v.tensor_copy(out=ov[:, 0:256], in_=p.ps[5][:, 0:256])
                return v.tensor_copy(out=ov[:, 256:512], in_=p.ps[6][:, 0:256])
            @p.step("gpsimd", reads=[ov], dma=True)
            def _(g, ov=ov, dc=dc, a=a):
                return [g.dma_start(out=out_tm[a + q * 128:a + (q + 1) * 128,
                                               dc * 128:(dc + 1) * 128],
                                    in_=ov[:, q * 128:(q + 1) * 128]) for q in range(4)]
    return p.done() if standalone else p


def make_inputs_D(inp, consts, core, xn_fm, xn_tm, x3):
    b, j = core // 2, core % 2
    d = {"xn_fm": xn_fm, "xn_tm": xn_tm, "x3": x3,
         "identf_in": consts["ident"],
         "sgam": inp["L1_ffn_slot_norm_gamma"].reshape(1, D).astype(np.float32),
         "c_ident": consts["ident"], "c_ones_col": consts["ones_col"],
         "c_ones_row": consts["ones_row"]}
    for e in range(4):
        eg = 4 * j + e
        d[f"se_{e}"] = np.ascontiguousarray(inp["L1_ffn_slot_embeds"][eg])
        d[f"w1_{e}"] = np.ascontiguousarray(inp["L1_ffn_w1"][eg])
        d[f"b1_{e}"] = np.ascontiguousarray(inp["L1_ffn_b1"][eg].reshape(4096, 1))
        d[f"w2_{e}"] = np.ascontiguousarray(inp["L1_ffn_w2"][eg])
        d[f"b2_{e}"] = np.ascontiguousarray(inp["L1_ffn_b2"][eg].reshape(1, D))
    return d



def build_fused(debug=False):
    p = P("F")
    x1 = p.dram("s_x1", [D, S])
    hs1 = p.dram("s_hs1", [D, S])
    mla = p.dram("s_mla", [D, S])
    comb = p.dram("s_comb", [8, S])
    x2 = p.dram("s_x2", [D, S])
    hs2 = p.dram("s_hs2", [D, S])
    x3 = p.dram("s_x3", [D, S])
    xnf = p.dram("s_xnf", [D, S])
    xnt = p.dram("s_xnt", [S, D])
    out = p.outp("out", [S, D], F32)
    build_A(p=p, pre={"x1": x1, "hs1": hs1}, pfx="A_")
    p.barrier()
    build_B(p=p, pre={"hs1": hs1, "mla": mla, "comb": comb}, pfx="B_")
    p.barrier()
    build_B2(p=p, pre={"mla": mla, "comb": comb, "x1": x1, "hs1": hs1,
                       "x2": x2, "hs2": hs2}, pfx="B2_")
    p.barrier()
    build_C(p=p, pre={"hs2": hs2, "x2": x2, "x3": x3, "xn_fm": xnf,
                      "xn_tm": xnt}, pfx="C_")
    p.barrier()
    build_D(p=p, pre={"xn_fm": xnf, "xn_tm": xnt, "x3": x3, "out": out}, pfx="D_")
    if debug:
        p.barrier()
        for nm, t in (("d_x1", x1), ("d_hs1", hs1), ("d_mla", mla),
                      ("d_comb", comb), ("d_x2", x2), ("d_hs2", hs2),
                      ("d_x3", x3), ("d_xnf", xnf), ("d_xnt", xnt)) + tuple(
                          ("d_" + k, v) for k, v in p._d_dump.items()):
            dbg = p.outp(nm, list(t.shape), BF16)

            @p.step("gpsimd", dma=True)
            def _(g, dbg=dbg, t=t):
                return g.dma_start(out=dbg[:, :], in_=t[:, :])
        p.barrier()
    return p.done()


def make_inputs_fused(inp, consts, core):
    d = {}
    for pfx, fn, extra in (("A_", make_inputs_A, ()),
                           ("B_", make_inputs_B, (None,)),
                           ("B2_", make_inputs_B2, (None, None, None, None)),
                           ("C_", make_inputs_C, (None, None)),
                           ("D_", make_inputs_D, (None, None, None))):
        sub = fn(inp, consts, core, *extra)
        for k, v in sub.items():
            if v is None:
                continue
            if k.startswith("c_"):
                d[k] = v
            else:
                d[pfx + k] = v
    return d


from concourse.bass_utils import run_bass_kernel_spmd

_BUILD_CACHE = {}


def _get(name, fn):
    if name not in _BUILD_CACHE:
        _BUILD_CACHE[name] = fn()
    return _BUILD_CACHE[name]


def kernel(**inputs):
    consts = host_constants()
    cores = list(range(8))
    nc = _get("F", build_fused)
    im = [make_inputs_fused(inputs, consts, c) for c in cores]
    r = run_bass_kernel_spmd(nc, im, core_ids=cores).results
    out = np.stack([np.asarray(r[2 * b]["out"], dtype=np.float32) for b in range(4)])
    return out
